# version 09
# baseline (speedup 1.0000x reference)
import math
from contextlib import ExitStack

import numpy as np
import ml_dtypes
import concourse.bass as bass
import concourse.mybir as mybir
from concourse.bass_utils import run_bass_kernel_spmd

F32 = mybir.dt.float32
BF16 = mybir.dt.bfloat16
AF = mybir.ActivationFunctionType
ALU = mybir.AluOpType
AX = mybir.AxisListType

FULL = dict(D=2048, T=8192, H=16)
TINY = dict(D=512, T=2048, H=4)

C_SHIFT = 12.0


def mkcfg(base):
    c = dict(base)
    D, T, H = c["D"], c["T"], c["H"]
    c["DH"] = 128
    assert H * c["DH"] == D
    c["DC"] = D // 128
    c["OWN"] = T // 4
    assert c["OWN"] == D
    c["QT"] = T // 16
    assert c["QT"] % 128 == 0
    c["NQS"] = c["QT"] // 128
    c["NSW"] = 4
    c["NKS"] = T // 512
    c["CH"] = c["QT"]
    c["NCH"] = T // c["CH"]
    c["NSUB"] = c["CH"] // 128
    c["S"], c["R"] = 32, 16
    c["J"] = c["S"] * c["R"]
    c["JC"] = c["J"] // 128
    c["GR"] = min(512, c["OWN"])
    c["NG"] = c["OWN"] // c["GR"]
    c["ORT"] = c["OWN"] // 128
    assert c["ORT"] == c["DC"]
    c["SCALE"] = 1.0 / math.sqrt(c["DH"])
    return c


class Sched:
    ENG = ("sy", "gq", "pe", "act", "dve")

    def __init__(self):
        self.ops = {e: [] for e in self.ENG}
        from collections import defaultdict
        self.n = defaultdict(int)
        for s in ("pe", "act", "dve"):
            self.n[s] = 0
        self.chain_last = {"act": 0, "dve": 0, "gq": 0}
        self.chain_sem = {"act": "act", "dve": "dve", "gq": "pool"}

    def add(self, eng, fn, waits=(), inc=None, chain=False):
        w = [x for x in waits if x is not None and x[1] > 0]
        if chain and eng in self.chain_last:
            if self.chain_last[eng] > 0:
                w.append((self.chain_sem[eng], self.chain_last[eng]))
        tok = None
        if inc is not None:
            sem, k = inc
            self.n[sem] += k
            tok = (sem, self.n[sem])
            if eng in self.chain_last and sem == self.chain_sem.get(eng):
                self.chain_last[eng] = self.n[sem]
        self.ops[eng].append((fn, w, inc))
        return tok

    def tok(self, sem):
        return (sem, self.n[sem])

    def barrier(self):
        snap = dict(self.n)
        for e in self.ENG:
            self.add(e, None, waits=[(s, v) for s, v in snap.items()])
        return snap


def build_nc(cfg):
    D, T, H, DH = cfg["D"], cfg["T"], cfg["H"], cfg["DH"]
    DC, OWN, QT, NQS = cfg["DC"], cfg["OWN"], cfg["QT"], cfg["NQS"]
    NSW, NKS, CH, NCH, NSUB = cfg["NSW"], cfg["NKS"], cfg["CH"], cfg["NCH"], cfg["NSUB"]
    S, J, JC = cfg["S"], cfg["J"], cfg["JC"]
    GR, NG, ORT, SCALE = cfg["GR"], cfg["NG"], cfg["ORT"], cfg["SCALE"]
    NVC = D // 512
    NSUBK = T // 128

    nc = bass.Bass("TRN2", target_bir_lowering=False)

    xb = nc.dram_tensor("xb", [T, D], F32, kind="ExternalInput").ap()
    xo = nc.dram_tensor("xo", [OWN, D], F32, kind="ExternalInput").ap()
    xq = nc.dram_tensor("xq", [OWN, D], F32, kind="ExternalInput").ap()
    wq = nc.dram_tensor("wq", [D, D], F32, kind="ExternalInput").ap()
    wk = nc.dram_tensor("wk", [D, D], F32, kind="ExternalInput").ap()
    wv = nc.dram_tensor("wv", [D, D], F32, kind="ExternalInput").ap()
    wo = nc.dram_tensor("wo", [D, D], F32, kind="ExternalInput").ap()
    bq_i = nc.dram_tensor("bq", [128, DC], F32, kind="ExternalInput").ap()
    bk_i = nc.dram_tensor("bk", [128, DC], F32, kind="ExternalInput").ap()
    bv_i = nc.dram_tensor("bv", [128, DC], F32, kind="ExternalInput").ap()
    bo_i = nc.dram_tensor("bo_bc", [128, D], F32, kind="ExternalInput").ap()
    wr_i = nc.dram_tensor("wr_t", [D, S], F32, kind="ExternalInput").ap()
    u_fl = nc.dram_tensor("u_flat", [D, J], F32, kind="ExternalInput").ap()
    v_fl = nc.dram_tensor("v_flat", [J, D], F32, kind="ExternalInput").ap()
    at_i = nc.dram_tensor("attr_t", [S, S], F32, kind="ExternalInput").ap()
    wa_i = nc.dram_tensor("wa_t", [S, 1], F32, kind="ExternalInput").ap()
    mk_i = nc.dram_tensor("masks", [128, NKS * QT], BF16, kind="ExternalInput").ap()
    id_i = nc.dram_tensor("ident", [128, 128], BF16, kind="ExternalInput").ap()
    of_i = nc.dram_tensor("ones_f", [1, 128], F32, kind="ExternalInput").ap()
    oc_i = nc.dram_tensor("ones_c", [128, 1], BF16, kind="ExternalInput").ap()
    em_i = nc.dram_tensor("e_mat", [S, 128 * JC], BF16, kind="ExternalInput").ap()

    out = nc.dram_tensor("out", [OWN, D], F32, kind="ExternalOutput").ap()

    kt_scr = nc.dram_tensor("kt_scr", [D, T], BF16, kind="Internal").ap()
    v_scr = nc.dram_tensor("v_scr", [T, D], BF16, kind="Internal").ap()
    qt_scr = nc.dram_tensor("qt_scr", [D, OWN], BF16, kind="Internal").ap()
    xnt_scr = nc.dram_tensor("xnt_scr", [D, T], BF16, kind="Internal").ap()

    with ExitStack() as ctx:
        e = ctx.enter_context

        pa = e(nc.sbuf_tensor("pa", [128, DC * OWN], BF16))
        PW_COLS = max(DC * D, 2 * T + 2 * 128 * NSUBK)
        pw = e(nc.sbuf_tensor("pw", [128, PW_COLS], BF16))
        pm = e(nc.sbuf_tensor("pm", [128, max(DC * J, NKS * QT, 2 * (D + 1024 + 3 * 512))], BF16))
        pv = e(nc.sbuf_tensor("pv", [128, max(JC * D, JC * GR + OWN + 2 * D)], BF16))
        pg = e(nc.sbuf_tensor("pg", [128, JC * OWN], BF16))
        xs = [e(nc.sbuf_tensor(f"xs{i}", [128, max(D, OWN)], BF16)) for i in range(4)]
        pt = [e(nc.sbuf_tensor(f"pt{i}", [128, QT], BF16)) for i in range(3)]
        wrk = e(nc.sbuf_tensor("wrk", [128, 64], F32))
        scw = e(nc.sbuf_tensor("scw", [128, 8 * S], F32))
        gps = e(nc.sbuf_tensor("gps", [128, ORT], F32))
        bqs = e(nc.sbuf_tensor("bqs", [128, DC], F32))
        bks = e(nc.sbuf_tensor("bks", [128, DC], F32))
        bvs = e(nc.sbuf_tensor("bvs", [128, DC], F32))
        wrs = e(nc.sbuf_tensor("wrs", [128, DC * S], BF16))
        ems = e(nc.sbuf_tensor("ems", [S, 128 * JC], BF16))
        ids = e(nc.sbuf_tensor("ids", [128, 128], BF16))
        onf = e(nc.sbuf_tensor("onf", [1, 128], F32))
        onc = e(nc.sbuf_tensor("onc", [128, 1], BF16))
        ats = e(nc.sbuf_tensor("ats", [S, S], F32))
        was = e(nc.sbuf_tensor("was", [S, 1], F32))
        awb = e(nc.sbuf_tensor("awb", [128, S], F32))
        aws = e(nc.sbuf_tensor("aws", [1, S], F32))
        cst = e(nc.sbuf_tensor("cst", [128, 1], F32))

        pf = [e(nc.psum_tensor(f"pf{i}", [128, 512], F32)) for i in range(4)]
        psc = [e(nc.psum_tensor(f"psc{i}", [128, 512], F32)) for i in range(2)]
        pb = [e(nc.psum_tensor(f"pb{i}", [128, 1024], BF16)) for i in range(2)]

        block_holder = []

        S_ = Sched()

        def xnt_own(dc, r0, r1):
            return pa[:, dc * OWN + r0: dc * OWN + r1]

        XTC_OFF = [0, 2 * DC * CH]
        STG_OFF = [DC * CH, 3 * DC * CH]

        def xtc(buf, dc, r0, r1):
            o = XTC_OFF[buf]
            return pa[:, o + dc * CH + r0: o + dc * CH + r1]

        def stg(buf, kc, c0, c1):
            o = STG_OFF[buf]
            return pa[:, o + kc * CH + c0: o + kc * CH + c1]

        def stgv(buf, rs, c0, c1):
            o = STG_OFF[buf]
            return pa[:, o + rs * D + c0: o + rs * D + c1]

        def ac(rt, f0, f1):
            return pa[:, rt * D + f0: rt * D + f1]

        def acT(fc, r0, r1):
            return pa[:, fc * OWN + r0: fc * OWN + r1]

        def wchunk(dc, n0, n1):
            return pw[:, dc * D + n0: dc * D + n1]

        def kth(b, k0, k1):
            return pw[:, b * T + k0: b * T + k1]

        VH0 = 2 * T

        def vh128(b, sub):
            o = VH0 + b * (128 * NSUBK) + sub * 128
            return pw[:, o: o + 128]

        def vh_view(b):
            o = VH0 + b * (128 * NSUBK)
            return pw[:, o: o + NSUBK * 128].rearrange("p (s c) -> p s c", c=128)

        def uchunk(dc, j0, j1):
            return pm[:, dc * J + j0: dc * J + j1]

        def mask_j(j):
            return pm[:, j * QT:(j + 1) * QT]

        pmf = pm.bitcast(F32)

        def bo_b(c0, c1):
            return pmf[:, c0:c1]

        def p6x(b):
            return pmf[:, D + b * 512: D + (b + 1) * 512]

        def p6t(i):
            return pmf[:, D + 1024 + i * 512: D + 1024 + (i + 1) * 512]

        def vflat(jc, n0, n1):
            return pv[:, jc * D + n0: jc * D + n1]

        def gexp(jc):
            return pv[:, jc * GR:(jc + 1) * GR]

        GT0 = JC * GR

        def gT(r0, r1):
            return pv[0:S, GT0 + r0: GT0 + r1]

        def gut(jc, r0, r1):
            return pg[:, jc * OWN + r0: jc * OWN + r1]

        PVJ = JC * GR + OWN
        sqj = pv[:, PVJ: PVJ + D]
        cpj = pv[:, PVJ + D: PVJ + 2 * D]
        qst = pg[:, 0: 2 * GR]

        def w1(i):
            return wrk[:, i:i + 1]

        A = S_.add

        def dma(eng, sem, outap, inap, waits=()):
            return A(eng, lambda E, o=outap, i=inap: E.dma_start(out=o, in_=i),
                     waits=waits, inc=(sem, 16))

        tok_consts = None
        for o, i in [(ids[:], id_i), (onf[:], of_i), (onc[:], oc_i),
                     (bqs[:], bq_i), (bks[:], bk_i),
                     (bvs[:], bv_i), (ats[:], at_i), (was[:], wa_i),
                     (ems[:], em_i)]:
            tok_consts = dma("sy", "cst", o, i)
        tok_wrs = dma("gq", "wrs", wrs.rearrange("p (dc s) -> p dc s", s=S),
                      wr_i.rearrange("(dc p) s -> p dc s", p=128))

        tok_wq = dma("gq", "w", pw[:, 0:DC * D].rearrange("p (dc n) -> p dc n", n=D),
                     wq.rearrange("(dc p) n -> p dc n", p=128))
        tok_u = dma("gq", "u", pm[:, 0:DC * J].rearrange("p (dc j) -> p dc j", j=J),
                    u_fl.rearrange("(dc p) j -> p dc j", p=128))
        tok_cst = A("dve", lambda E: E.memset(cst[:], -C_SHIFT),
                    inc=("dve", 1), chain=True)

        ln_i = [0]
        ln_trs = {}
        ln_norm = {}

        def ln_stats(src_rows, extra_waits=()):
            i = ln_i[0]
            ln_i[0] += 1
            beta = i % 4
            b2 = i % 2
            xbuf = xs[beta]
            t_x = dma("gq", f"x{beta}", xbuf[:, 0:D], src_rows,
                      waits=[ln_trs.get(i - 4)] + list(extra_waits))
            t_sq = A("act", lambda E, xb_=xbuf, b=b2: E.activation(
                out=sqj, in_=xb_[:, 0:D], func=AF.Square,
                accum_out=wrk[:, 8 + b:9 + b]), waits=[t_x], inc=("act", 1), chain=True)
            t_sm = A("dve", lambda E, xb_=xbuf, b=b2: E.tensor_scalar(
                cpj, xb_[:, 0:D], 1.0, None, op0=ALU.mult, op1=ALU.add,
                accum_out=wrk[:, 10 + b:11 + b]), waits=[t_x], inc=("dve", 1), chain=True)
            A("dve", lambda E, b=b2: E.tensor_scalar(
                w1(12 + b), w1(10 + b), 1.0 / D, None, op0=ALU.mult),
              waits=[t_sm], inc=("dve", 1), chain=True)
            A("dve", lambda E, b=b2: E.tensor_scalar(
                w1(14 + b), w1(8 + b), 1.0 / D, None, op0=ALU.mult),
              waits=[t_sq], inc=("dve", 1), chain=True)
            A("dve", lambda E, b=b2: E.tensor_tensor(
                w1(16 + b), w1(12 + b), w1(12 + b), op=ALU.mult),
              inc=("dve", 1), chain=True)
            A("dve", lambda E, b=b2: E.tensor_tensor(
                w1(18 + b), w1(14 + b), w1(16 + b), op=ALU.subtract),
                inc=("dve", 1), chain=True)
            t_var = A("dve", lambda E, b=b2: E.tensor_scalar(
                w1(18 + b), w1(18 + b), 1e-5, None, op0=ALU.add),
                inc=("dve", 1), chain=True)
            A("act", lambda E, b=b2: E.activation(
                out=w1(20 + b), in_=w1(18 + b), func=AF.Ln),
              waits=[t_var], inc=("act", 1), chain=True)
            t_rstd = A("act", lambda E, b=b2: E.activation(
                out=w1(22 + b), in_=w1(20 + b), func=AF.Exp, scale=-0.5),
                inc=("act", 1), chain=True)
            ln_norm[i] = A("dve", lambda E, xb_=xbuf, b=b2: E.tensor_scalar(
                xb_[:, 0:D], xb_[:, 0:D], w1(12 + b), w1(22 + b),
                op0=ALU.subtract, op1=ALU.mult),
                waits=[t_rstd], inc=("dve", 1), chain=True)
            return i

        def ln_tr(i, dest_fn, dest_r0):
            beta = i % 4
            xbuf = xs[beta]
            t_norm = ln_norm[i]
            ev_toks = []
            tr_toks = []
            for dc in range(DC):
                war = ev_toks[dc - 2] if dc >= 2 else ln_tr.last_ev.get(dc % 2)
                t_tr = A("pe", lambda E, xb_=xbuf, dc=dc: E.transpose(
                    pb[dc % 2][:, 0:128], xb_[:, dc * 128:(dc + 1) * 128], ids[:]),
                    waits=[t_norm, war, tok_consts], inc=("pe", 1))
                tr_toks.append(t_tr)
                t_ev = A("dve", lambda E, dc=dc, r0=dest_r0, df=dest_fn: E.tensor_copy(
                    df(dc, r0, r0 + 128), pb[dc % 2][:, 0:128]),
                    waits=[t_tr], inc=("dve", 1), chain=True)
                ev_toks.append(t_ev)
            ln_tr.last_ev = {0: ev_toks[-2], 1: ev_toks[-1]}
            ln_trs[i] = tr_toks[-1]
            return ev_toks[-1]

        ln_tr.last_ev = {}

        def ln_subtile(src_rows, dest_fn, dest_r0, extra_waits=()):
            i = ln_stats(src_rows, extra_waits)
            ev = ln_tr(i, dest_fn, dest_r0)
            return ev, ln_trs[i]

        q_mm_last = None
        qev = {}
        qld = {}
        NCH0 = OWN // CH
        LNQ = DC // NSUB
        ev_of = {}
        st_ids = {}
        ids0 = [ln_stats(xq[sb * 128:(sb + 1) * 128, :]) for sb in range(NSUB)]
        ev_of[0] = max(ln_tr(i, lambda dc, a, bb: xtc(0, dc, a, bb),
                             sb * 128) for sb, i in enumerate(ids0))
        for c in range(NCH0):
            if c + 1 < NCH0:
                st_ids[c + 1] = [ln_stats(
                    xq[(c + 1) * CH + sb * 128:(c + 1) * CH + (sb + 1) * 128, :])
                    for sb in range(NSUB)]
            for qc in range(DC):
                n = c * DC + qc
                waits = [ev_of[c], tok_wq]
                if n >= 4:
                    waits.append(qev[n - 4])

                def mk_qmm(qc=qc, n=n, c=c):
                    def f(E):
                        for dc in range(DC):
                            ins = E.matmul(pf[n % 4][:, 0:CH],
                                           wchunk(dc, qc * 128, (qc + 1) * 128),
                                           xtc(c % 2, dc, 0, CH),
                                           start=(dc == 0), stop=(dc == DC - 1))
                        return ins
                    return f
                q_mm_last = A("pe", mk_qmm(), waits=waits, inc=("pe", 1))
                w2 = [q_mm_last]
                if n >= 2:
                    w2.append(qld[n - 2])
                qev[n] = A("act", lambda E, qc=qc, n=n: E.activation(
                    out=qst[:, (n % 2) * GR:(n % 2) * GR + CH],
                    in_=pf[n % 4][:, 0:CH], func=AF.Identity,
                    bias=bqs[:, qc:qc + 1]), waits=w2, inc=("act", 1), chain=True)
                qld[n] = dma(
                    "sy", f"q{n % 2}",
                    qt_scr.rearrange("(qc p) r -> p qc r", p=128)[:, qc, c * CH:(c + 1) * CH],
                    qst[:, (n % 2) * GR:(n % 2) * GR + CH],
                    waits=[qev[n]])
                if c + 1 < NCH0 and qc % LNQ == LNQ - 1:
                    sb = qc // LNQ
                    ev = ln_tr(st_ids[c + 1][sb],
                               lambda dc, a, bb, c=c: xtc((c + 1) % 2, dc, a, bb),
                               sb * 128)
                    if sb == NSUB - 1:
                        ev_of[c + 1] = ev
        tok_p2b_pe = S_.tok("pe")
        tok_p2b_q0 = S_.tok("q0")
        tok_p2b_q1 = S_.tok("q1")

        for i in range(OWN // 128):
            ev, tr = ln_subtile(xo[i * 128:(i + 1) * 128, :], xnt_own, i * 128)
        tok_p1 = S_.tok("dve")

        t1 = A("pe", lambda E: E.matmul(psc[0][0:1, 0:S], was[:], ats[:],
                                        start=True, stop=True),
               waits=[tok_consts, tok_p2b_pe], inc=("pe", 1))
        t2 = A("act", lambda E: E.activation(out=aws[:], in_=psc[0][0:1, 0:S],
                                             func=AF.Exp, scale=-1.0),
               waits=[t1], inc=("act", 1), chain=True)
        t3 = A("dve", lambda E: E.tensor_scalar(aws[:], aws[:], 1.0, None,
                                                op0=ALU.add),
               waits=[t2], inc=("dve", 1), chain=True)
        t4 = A("dve", lambda E: E.reciprocal(aws[:], aws[:]),
               inc=("dve", 1), chain=True)
        t5 = A("pe", lambda E: E.matmul(psc[1][:, 0:S], onf[:], aws[:],
                                        start=True, stop=True),
               waits=[t4], inc=("pe", 1))
        tok_awb = A("dve", lambda E: E.tensor_copy(awb[:], psc[1][:, 0:S]),
                    waits=[t5], inc=("dve", 1), chain=True)

        g_done = {}
        for rt in range(ORT):
            r0 = rt * 128

            def mk_sc(rt=rt, r0=r0):
                def f(E):
                    for dc in range(DC):
                        ins = E.matmul(psc[0][:, 0:S],
                                       xnt_own(dc, r0, r0 + 128),
                                       wrs[:, dc * S:(dc + 1) * S],
                                       start=(dc == 0), stop=(dc == DC - 1))
                    return ins
                return f
            waits = [tok_p1, tok_consts, tok_wrs, t2]
            if rt > 0:
                waits.append(g_done[rt - 1])
            t_sc = A("pe", mk_sc(), waits=waits, inc=("pe", 1))

            t_cp = A("dve", lambda E: E.tensor_copy(scw[:, 0:S], psc[0][:, 0:S]),
                     waits=[t_sc], inc=("dve", 1), chain=True)
            A("dve", lambda E: E.tensor_reduce(w1(24), scw[:, 0:S], axis=AX.X,
                                               op=ALU.max, negate=True),
              inc=("dve", 1), chain=True)
            A("dve", lambda E: E.tensor_scalar(w1(25), w1(24), -1.0, None,
                                               op0=ALU.mult),
              inc=("dve", 1), chain=True)
            A("dve", lambda E: E.tensor_scalar(scw[:, S:2 * S], scw[:, 0:S],
                                               w1(25), 1e9, op0=ALU.is_ge,
                                               op1=ALU.mult),
              inc=("dve", 1), chain=True)
            t_ein = A("dve", lambda E: E.tensor_tensor(
                scw[:, 2 * S:3 * S], scw[:, 0:S], scw[:, S:2 * S],
                op=ALU.subtract), inc=("dve", 1), chain=True)
            t_exp = A("act", lambda E: E.activation(
                out=scw[:, 3 * S:4 * S], in_=scw[:, 0:S], func=AF.Exp,
                bias=w1(24)), waits=[t_ein], inc=("act", 1), chain=True)
            A("dve", lambda E: E.tensor_reduce(w1(26), scw[:, 2 * S:3 * S],
                                               axis=AX.X, op=ALU.max),
              inc=("dve", 1), chain=True)
            A("dve", lambda E: E.tensor_scalar(scw[:, 4 * S:5 * S], scw[:, 0:S],
                                               w1(26), None, op0=ALU.is_ge),
              inc=("dve", 1), chain=True)
            A("dve", lambda E: E.tensor_reduce(w1(27), scw[:, 3 * S:4 * S],
                                               axis=AX.X, op=ALU.add),
              waits=[t_exp], inc=("dve", 1), chain=True)
            A("dve", lambda E: E.tensor_tensor(scw[:, 5 * S:6 * S],
                                               scw[:, 3 * S:4 * S],
                                               scw[:, 4 * S:5 * S], op=ALU.mult),
              inc=("dve", 1), chain=True)
            A("dve", lambda E: E.tensor_reduce(w1(28), scw[:, 5 * S:6 * S],
                                               axis=AX.X, op=ALU.add),
              inc=("dve", 1), chain=True)
            A("dve", lambda E: E.tensor_scalar(w1(29), w1(27), 1e-8, None,
                                               op0=ALU.mult),
              inc=("dve", 1), chain=True)
            A("dve", lambda E: E.tensor_tensor(w1(30), w1(28), w1(29), op=ALU.add),
              inc=("dve", 1), chain=True)
            A("dve", lambda E: E.reciprocal(w1(31), w1(30)),
              inc=("dve", 1), chain=True)
            A("dve", lambda E: E.tensor_scalar(scw[:, 6 * S:7 * S],
                                               scw[:, 5 * S:6 * S], w1(31), None,
                                               op0=ALU.mult),
              inc=("dve", 1), chain=True)
            A("dve", lambda E: E.tensor_tensor(scw[:, 7 * S:8 * S],
                                               scw[:, 6 * S:7 * S], awb[:],
                                               op=ALU.mult),
              waits=[tok_awb], inc=("dve", 1), chain=True)
            A("dve", lambda E: E.tensor_reduce(w1(32), scw[:, 7 * S:8 * S],
                                               axis=AX.X, op=ALU.add),
              inc=("dve", 1), chain=True)
            A("dve", lambda E, rt=rt: E.tensor_scalar(
                gps[:, rt:rt + 1], w1(32), 0.2, 0.9, op0=ALU.mult, op1=ALU.add),
              inc=("dve", 1), chain=True)
            t_gbf = A("dve", lambda E: E.tensor_copy(pt[2][:, 0:S],
                                                     scw[:, 6 * S:7 * S]),
                      inc=("dve", 1), chain=True)
            t_gtr = A("pe", lambda E: E.transpose(pb[0][0:S, 0:128], pt[2][:, 0:S],
                                                  ids[:]),
                      waits=[t_gbf, (g_done.get(rt - 1) if rt else None)],
                      inc=("pe", 1))
            g_done[rt] = A("dve", lambda E, r0=r0: E.tensor_copy(
                gT(r0, r0 + 128), pb[0][0:S, 0:128]),
                waits=[t_gtr], inc=("dve", 1), chain=True)

        gut_toks = {}
        hu_last = None
        for gi in range(NG):
            r0 = gi * GR
            gx = {}
            for jc in range(JC):
                gxw = [g_done[ORT - 1], tok_consts, tok_awb]
                if jc > 0:
                    gxw.append(gx[jc - 1])
                elif gi > 0:
                    gxw.append(gx_prev_last)
                t_gx = A("pe", lambda E, jc=jc, r0=r0: E.matmul(
                    psc[1][:, 0:GR], ems[:, jc * 128:(jc + 1) * 128],
                    gT(r0, r0 + GR), start=True, stop=True),
                    waits=gxw, inc=("pe", 1))
                gx[jc] = A("dve", lambda E, jc=jc: E.tensor_copy(
                    gexp(jc), psc[1][:, 0:GR]),
                    waits=[t_gx], inc=("dve", 1), chain=True)
            gx_prev_last = gx[JC - 1]
            for jc in range(JC):
                n = gi * JC + jc
                waits = [tok_u, tok_p1]
                if n >= 4:
                    waits.append(gut_toks[n - 4])

                def mk_hu(jc=jc, r0=r0, n=n):
                    def f(E):
                        for dc in range(DC):
                            ins = E.matmul(pf[n % 4][:, 0:GR],
                                           uchunk(dc, jc * 128, (jc + 1) * 128),
                                           xnt_own(dc, r0, r0 + GR),
                                           start=(dc == 0), stop=(dc == DC - 1))
                        return ins
                    return f
                hu_last = A("pe", mk_hu(), waits=waits, inc=("pe", 1))
                gut_toks[n] = A("dve", lambda E, jc=jc, r0=r0, n=n: E.tensor_tensor(
                    gut(jc, r0, r0 + GR), pf[n % 4][:, 0:GR], gexp(jc),
                    op=ALU.mult),
                    waits=[hu_last, gx[jc], tok_p2b_q0, tok_p2b_q1],
                    inc=("dve", 1), chain=True)
        tok_p3_dve = S_.tok("dve")
        tok_p3_pe = S_.tok("pe")

        tok_wk = dma("gq", "w", pw[:, 0:DC * D].rearrange("p (dc n) -> p dc n", n=D),
                     wk.rearrange("(dc p) n -> p dc n", p=128),
                     waits=[tok_p2b_pe])

        kev = {}
        kmm = {}
        kdma = {}
        kev_of = {}
        kst_ids = {}
        ids0 = [ln_stats(xb[sb * 128:(sb + 1) * 128, :]) for sb in range(NSUB)]
        kev_of[0] = max(ln_tr(i, lambda dc, a, bb: xtc(0, dc, a, bb),
                              sb * 128) for sb, i in enumerate(ids0))
        for c in range(NCH):
            if c + 1 < NCH:
                kst_ids[c + 1] = [ln_stats(
                    xb[(c + 1) * CH + sb * 128:(c + 1) * CH + (sb + 1) * 128, :],
                    extra_waits=[kdma.get(c - 1)]) for sb in range(NSUB)]
            for kc in range(DC):
                waits = [kev_of[c], tok_wk]
                gn = c * DC + kc
                if gn >= 4:
                    waits.append(kev[gn - 4])

                def mk_kmm(c=c, kc=kc):
                    def f(E, _kk=None):
                        for dc in range(DC):
                            ins = E.matmul(pf[kc % 4][:, 0:CH],
                                           wchunk(dc, kc * 128, (kc + 1) * 128),
                                           xtc(c % 2, dc, 0, CH),
                                           start=(dc == 0), stop=(dc == DC - 1))
                        return ins
                    return f
                kmm[gn] = A("pe", mk_kmm(), waits=waits, inc=("pe", 1))
                w2 = [kmm[gn]]
                if c > 1:
                    w2.append(kdma[c - 2])
                kev[gn] = A("act", lambda E, c=c, kc=kc: E.activation(
                    out=stg(c % 2, kc, 0, CH), in_=pf[kc % 4][:, 0:CH],
                    func=AF.Identity,
                    bias=bks[:, kc:kc + 1]), waits=w2, inc=("act", 1), chain=True)
                if c + 1 < NCH and kc % LNQ == LNQ - 1:
                    sb = kc // LNQ
                    ev = ln_tr(kst_ids[c + 1][sb],
                               lambda dc, a, bb, c=c: xtc((c + 1) % 2, dc, a, bb),
                               sb * 128)
                    if sb == NSUB - 1:
                        kev_of[c + 1] = ev
            _kd = dma(
                "sy", f"kx{c % 2}",
                kt_scr.rearrange("(kc p) r -> p kc r", p=128)[:, :, c * CH:(c + 1) * CH],
                pa[:, STG_OFF[c % 2]: STG_OFF[c % 2] + DC * CH
                   ].rearrange("p (kc r) -> p kc r", r=CH),
                waits=[kev[c * DC + DC - 1]])
            t_xd = dma(
                "sy", f"kx{c % 2}",
                xnt_scr.rearrange("(dc p) r -> p dc r", p=128)[:, :, c * CH:(c + 1) * CH],
                pa[:, XTC_OFF[c % 2]: XTC_OFF[c % 2] + DC * CH
                   ].rearrange("p (dc r) -> p dc r", r=CH),
                waits=[kev_of[c]])
            kdma[c] = t_xd
        tok_p2a_ld0 = S_.tok("kx0")
        tok_p2a_ld1 = S_.tok("kx1")
        tok_p2a_pe = S_.tok("pe")

        tok_wv = dma("gq", "w", pw[:, 0:DC * D].rearrange("p (dc n) -> p dc n", n=D),
                     wv.rearrange("(dc p) n -> p dc n", p=128),
                     waits=[tok_p2b_pe, tok_p3_pe, tok_p2a_pe])
        tok_masks = dma("gq", "mk", pm[:, 0:NKS * QT], mk_i, waits=[tok_p3_pe])

        vev = {}
        vdma = {}
        vmm = {}
        vxl = {}
        for c in range(NCH):
            waits = [tok_p2a_ld0, tok_p2a_ld1, tok_p3_dve, tok_p2b_q0, tok_p2b_q1, tok_p2a_pe]
            if c > 1:
                waits.append(vmm[c - 2])
            vxl[c] = dma(
                "sy", f"vx{c % 2}",
                pa[:, XTC_OFF[c % 2]:XTC_OFF[c % 2] + DC * CH
                   ].rearrange("p (dc r) -> p dc r", r=CH),
                xnt_scr.rearrange("(dc p) r -> p dc r", p=128)[:, :, c * CH:(c + 1) * CH],
                waits=waits)
            for rs in range(NSUB):
                for vc in range(NVC):
                    n = rs * NVC + vc
                    gn = c * NSUB * NVC + n
                    waits = [vxl[c], tok_wv]
                    if gn >= 4:
                        waits.append(vev[gn - 4])

                    def mk_vmm(rs=rs, vc=vc, n=n, c=c):
                        def f(E):
                            for dc in range(DC):
                                ins = E.matmul(pf[n % 4][:, 0:512],
                                               xtc(c % 2, dc, rs * 128, (rs + 1) * 128),
                                               wchunk(dc, vc * 512, (vc + 1) * 512),
                                               start=(dc == 0), stop=(dc == DC - 1))
                            return ins
                        return f
                    vmm[c] = A("pe", mk_vmm(), waits=waits, inc=("pe", 1))
                    w2 = [vmm[c]]
                    if c > 1:
                        w2.append(vdma[c - 2])
                    vev[gn] = A("dve", lambda E, rs=rs, vc=vc, n=n, c=c: E.tensor_copy(
                        stgv(c % 2, rs, vc * 512, (vc + 1) * 512), pf[n % 4][:, 0:512]),
                        waits=w2, inc=("dve", 1), chain=True)
            vdma[c] = dma(
                "sy", f"vw{c % 2}",
                v_scr[c * CH:(c + 1) * CH, :].rearrange("(rs p) n -> p rs n", p=128),
                pa[:, STG_OFF[c % 2]: STG_OFF[c % 2] + NSUB * D
                   ].rearrange("p (rs n) -> p rs n", n=D),
                waits=[vev[c * NSUB * NVC + NSUB * NVC - 1]])
        tok_p2c_ld0 = S_.tok("vw0")
        tok_p2c_ld1 = S_.tok("vw1")
        tok_p2c_pe = S_.tok("pe")

        tok_vflat = dma("gq", "vf", pv[:, 0:JC * D].rearrange("p (jc n) -> p jc n", n=D),
                        v_fl.rearrange("(jc p) n -> p jc n", p=128),
                        waits=[tok_p3_dve, ("act", S_.n["act"]), ("dve", S_.n["dve"])])

        SCB = [psc[0], psc[1], pf[2], pf[3]]

        def scb_(k):
            return SCB[k % 4]

        S_ACC0 = max(NKS * QT, DC * J) // 2
        def s_acc(q0, q1):
            return pmf[:, S_ACC0 + q0: S_ACC0 + q1]

        head_pe = {}
        head_ev = {}
        head_ld = {}
        exp_toks = {}
        mask_toks = {}
        pool_toks = {}
        sweep_ev = {}
        lmm_tok = {}
        nvd = (NSUBK + 7) // 8
        for h in range(H):
            beta = h % 2
            waits = [tok_p2a_ld0, tok_p2a_ld1, tok_p2b_q0, tok_p2b_q1,
                     tok_p2c_ld0, tok_p2c_ld1, tok_p2c_pe, tok_p2a_pe,
                     S_.tok("pe") if h == 0 else None]
            if h >= 2:
                waits += [head_pe[h - 2], head_ev[h - 2]]
            t = dma("sy", f"h{beta}", kth(beta, 0, T), kt_scr[h * 128:(h + 1) * 128, :],
                    waits=waits)
            for d in range(nvd):
                s0, s1 = d * 8, min((d + 1) * 8, NSUBK)
                t = dma("sy", f"h{beta}", vh_view(beta)[:, s0:s1, :],
                        v_scr[s0 * 128:s1 * 128, h * 128:(h + 1) * 128
                              ].rearrange("(s p) c -> p s c", p=128))
            RPH = OWN // H
            head_ld[h] = t = dma(
                "sy", f"h{beta}",
                xs[beta][:, 0:DC * 128].rearrange("p (qc r) -> p qc r", r=128),
                qt_scr.rearrange("(qc p) r -> p qc r", p=128)[:, :, h * RPH:(h + 1) * RPH])

            for sw in range(1, NSW + 1):
                nk = sw * NKS
                outq = []
                prev_ev = sweep_ev.get((h, sw - 1)) or sweep_ev.get((h - 1, NSW))
                t_ms = A("gq", lambda E: E.memset(s_acc(0, QT), 0.0),
                         waits=[prev_ev, tok_u], inc=("pool", 1), chain=True)
                for k in range(nk):
                    if k >= 4:
                        war = exp_toks[(h, sw, k - 4)]
                    elif sw > 1:
                        war = exp_toks[(h, sw - 1, (sw - 1) * NKS - 4 + k)]
                    elif h > 0:
                        war = exp_toks[(h - 1, NSW, NSW * NKS - 4 + k)]
                    else:
                        war = None
                    sw_waits = [head_ld[h], war]
                    if h == 0 and sw == 1 and k in (2, 3):
                        sw_waits.append(tok_p3_dve)
                    JJ = QT // H
                    t_sc = A("pe", lambda E, b=beta, k=k, sw=sw, JJ=JJ: E.matmul(
                        scb_(k)[:, 0:QT], kth(b, k * 128, (k + 1) * 128),
                        xs[b][:, 0:DC * 128].rearrange(
                            "p (qc r) -> p r qc", r=128)[:, (sw - 1) * JJ: sw * JJ, :],
                        start=True, stop=True),
                        waits=sw_waits, inc=("pe", 1))
                    ew = [t_sc, tok_cst]
                    km3 = k - 3
                    if km3 >= 0:
                        ew.append(pool_toks[(h, sw, km3)])
                    elif sw > 1:
                        ew.append(pool_toks[(h, sw - 1, (sw - 1) * NKS + km3)])
                    elif h > 0:
                        ew.append(pool_toks[(h - 1, NSW, NSW * NKS + km3)])
                    exp_toks[(h, sw, k)] = A(
                        "act", lambda E, k=k: E.activation(
                            out=pt[k % 3][:, 0:QT], in_=scb_(k)[:, 0:QT],
                            func=AF.Exp, bias=cst[:], scale=SCALE),
                        waits=ew, inc=("act", 1), chain=True)
                    if k >= nk - NKS:
                        j = k - (nk - NKS)
                        mask_toks[(h, sw, k)] = A(
                            "dve", lambda E, k=k, j=j: E.tensor_tensor(
                                pt[k % 3][:, 0:QT], pt[k % 3][:, 0:QT],
                                mask_j(j), op=ALU.mult),
                            waits=[exp_toks[(h, sw, k)], tok_masks],
                            inc=("dve", 1), chain=True)
                        ptok = mask_toks[(h, sw, k)]
                    else:
                        ptok = exp_toks[(h, sw, k)]
                    pool_toks[(h, sw, k)] = A(
                        "gq", lambda E, k=k: E.tensor_tensor(
                            s_acc(0, QT), s_acc(0, QT), pt[k % 3][:, 0:QT],
                            op=ALU.add),
                        waits=[ptok], inc=("pool", 1), chain=True)
                    pvw = [ptok]
                    if k == 0:
                        if sw > 1:
                            pvw.append(sweep_ev[(h, sw - 1)])
                        elif h > 0:
                            pvw.append(sweep_ev[(h - 1, NSW)])
                        else:
                            pvw.append(tok_p3_dve)

                    def mk_ot(b=beta, k=k, nk=nk):
                        def f(E):
                            return E.matmul(pf[0][:, 0:QT], vh128(b, k),
                                            pt[k % 3][:, 0:QT],
                                            start=(k == 0), stop=(k == nk - 1))
                        return f
                    outq.append((mk_ot(), pvw))
                    while len(outq) > 2:
                        A("pe", *outq.pop(0))
                tpv = None
                for i2, (fn2, w2_) in enumerate(outq):
                    tpv = A("pe", fn2, waits=w2_,
                            inc=("pe", 1) if i2 == len(outq) - 1 else None)
                head_pe[h] = tpv
                sweep_pe = tpv
                t_cast = A("dve", lambda E, nk=nk: E.tensor_copy(
                    pt[nk % 3][:, 0:QT], s_acc(0, QT)),
                    waits=[pool_toks[(h, sw, nk - 1)], sweep_pe],
                    inc=("dve", 1), chain=True)
                lmm_tok[(h, sw)] = A("pe", lambda E, nk=nk: E.matmul(
                    pf[1][0:1, 0:QT], onc[:], pt[nk % 3][:, 0:QT],
                    start=True, stop=True),
                    waits=[t_cast, sweep_ev.get((h, sw - 1)) or
                           sweep_ev.get((h - 1, NSW)) or tok_p3_dve],
                    inc=("pe", 1))
                t_rl = A("dve", lambda E: E.reciprocal(s_acc(0, QT)[0:1, :],
                                                       pf[1][0:1, 0:QT]),
                         waits=[lmm_tok[(h, sw)]], inc=("dve", 1), chain=True)
                t_bc = A("pe", lambda E: E.matmul(pf[1][:, 0:QT], onf[:],
                                                  s_acc(0, QT)[0:1, :],
                                                  start=True, stop=True),
                         waits=[t_rl], inc=("pe", 1))
                t_cp = A("dve", lambda E: E.tensor_copy(s_acc(0, QT), pf[1][:, 0:QT]),
                         waits=[t_bc], inc=("dve", 1), chain=True)
                r0 = (sw - 1) * QT
                A("dve", lambda E, r0=r0, h=h: E.tensor_tensor(
                    acT(h, r0, r0 + QT), pf[0][:, 0:QT], s_acc(0, QT),
                    op=ALU.mult),
                    waits=[t_cp, sweep_pe], inc=("dve", 1), chain=True)
                t_ev = A("dve", lambda E, r0=r0, h=h: E.tensor_scalar(
                    acT(h, r0, r0 + QT), acT(h, r0, r0 + QT), bvs[:, h:h + 1],
                    None, op0=ALU.add),
                    inc=("dve", 1), chain=True)
                sweep_ev[(h, sw)] = t_ev
            head_ev[h] = sweep_ev[(h, NSW)]
        tok_p4_dve = S_.tok("dve")
        tok_p4_pe = S_.tok("pe")
        tok_p4_act = S_.tok("act")

        tok_wo = dma("gq", "w", pw[:, 0:DC * D].rearrange("p (dc n) -> p dc n", n=D),
                     wo.rearrange("(dc p) n -> p dc n", p=128),
                     waits=[tok_p4_pe])

        tok_bo = dma("sy", "bo", bo_b(0, D), bo_i,
                     waits=[tok_p4_dve, tok_p4_act])

        e1t = {}
        e2t = {}
        e3t = {}
        stt = {}
        xlt = {}
        for rt in range(ORT):
            for cc in range(NVC):
                n = rt * NVC + cc
                xw = [tok_bo]
                if n >= 2:
                    xw.append(e3t[n - 2])
                xlt[n] = dma("sy", f"px{n % 2}", p6x(n % 2),
                             xo[rt * 128:(rt + 1) * 128, cc * 512:(cc + 1) * 512],
                             waits=xw)
                mw = [tok_p3_dve, tok_vflat, tok_wo, tok_p4_dve]
                if n >= 2:
                    mw += [e1t[n - 2], e2t[n - 2]]

                def mk_f(rt=rt, cc=cc, n=n):
                    def f(E):
                        for jc in range(JC):
                            ins = E.matmul(pf[n % 2][:, 0:512],
                                           gut(jc, rt * 128, (rt + 1) * 128),
                                           vflat(jc, cc * 512, (cc + 1) * 512),
                                           start=(jc == 0), stop=(jc == JC - 1))
                        return ins
                    return f
                t_f = A("pe", mk_f(), waits=mw, inc=("pe", 1))

                def mk_o(rt=rt, cc=cc, n=n):
                    def f(E):
                        for fc in range(DC):
                            ins = E.matmul(pf[2 + n % 2][:, 0:512],
                                           acT(fc, rt * 128, (rt + 1) * 128),
                                           wchunk(fc, cc * 512, (cc + 1) * 512),
                                           start=(fc == 0), stop=(fc == DC - 1))
                        return ins
                    return f
                t_o = A("pe", mk_o(), inc=("pe", 1))

                w1_ = [t_f, tok_p3_dve]
                if n >= 3:
                    w1_.append(stt[n - 3])
                e1t[n] = A("act", lambda E, rt=rt, n=n: E.activation(
                    out=p6t(n % 3), in_=pf[n % 2][:, 0:512], func=AF.Copy,
                    scale=gps[:, rt:rt + 1]), waits=w1_, inc=("act", 1),
                    chain=True)
                e2t[n] = A("dve", lambda E, n=n: E.tensor_tensor(
                    p6t(n % 3), p6t(n % 3), pf[2 + n % 2][:, 0:512], op=ALU.add),
                    waits=[e1t[n], t_o], inc=("dve", 1), chain=True)
                A("dve", lambda E, n=n: E.tensor_tensor(
                    p6t(n % 3), p6t(n % 3), p6x(n % 2), op=ALU.add),
                  waits=[xlt[n]], inc=("dve", 1), chain=True)
                e3t[n] = A("dve", lambda E, n=n, cc=cc: E.tensor_tensor(
                    p6t(n % 3), p6t(n % 3), bo_b(cc * 512, (cc + 1) * 512),
                    op=ALU.add), waits=[tok_bo], inc=("dve", 1), chain=True)
                stt[n] = dma("sy", f"o{n % 3}", out[rt * 128:(rt + 1) * 128, cc * 512:(cc + 1) * 512],
                             p6t(n % 3), waits=[e3t[n]])

        final = dict(S_.n)
        S_.add("gq", None, waits=[(s, v) for s, v in final.items()])

        semnames = set(S_.n.keys())
        for ename in S_.ENG:
            for fn, waits, inc in S_.ops[ename]:
                for sem, val in waits:
                    semnames.add(sem)
        sems = {name: e(nc.semaphore("s_" + name)) for name in sorted(semnames)}
        block = e(nc.Block())

        def emit_wm(engobj, ename):
            water = {}
            for fn, waits, inc in S_.ops[ename]:
                for sem, val in waits:
                    if water.get(sem, 0) >= val:
                        continue
                    engobj.wait_ge(sems[sem], val)
                    water[sem] = val
                if fn is None:
                    continue
                ins = fn(engobj)
                if inc is not None:
                    ins.then_inc(sems[inc[0]], inc[1])

        @block.sync
        def _(E):
            emit_wm(E, "sy")

        @block.gpsimd
        def _(E):
            emit_wm(E, "gq")

        @block.tensor
        def _(E):
            emit_wm(E, "pe")

        @block.vector
        def _(E):
            emit_wm(E, "dve")

        @block.scalar
        def _(E):
            emit_wm(E, "act")

    return nc


def host_inputs(cfg, core, ins):
    D, T = cfg["D"], cfg["T"]
    OWN, QT, NSW, NKS, DC, JC, S = (cfg["OWN"], cfg["QT"], cfg["NSW"],
                                    cfg["NKS"], cfg["DC"], cfg["JC"], cfg["S"])
    b, g = core // 4, core % 4
    x = ins["L0_attn_x"]
    xb = np.ascontiguousarray(x[b], dtype=np.float32)
    blocks = [4 * s + g for s in range(NSW)]
    xo = np.ascontiguousarray(
        np.concatenate([x[b][blk * QT:(blk + 1) * QT] for blk in blocks], axis=0),
        dtype=np.float32)
    H = cfg["H"]
    TPH, QPH = cfg["T"] // H, QT // H
    qrows = []
    for h in range(H):
        for s in range(NSW):
            base = h * TPH + (QT * (4 * s + g)) // H
            qrows.extend(range(base, base + QPH))
    xq = np.ascontiguousarray(x[b][np.array(qrows)], dtype=np.float32)
    masks = np.zeros((128, NKS * QT), dtype=ml_dtypes.bfloat16)
    kl = np.arange(128)[:, None]
    q = np.arange(QT)[None, :]
    for j in range(NKS):
        masks[:, j * QT:(j + 1) * QT] = (j * 128 + kl <= g * QT + q).astype(
            ml_dtypes.bfloat16)
    e_mat = np.zeros((S, 128 * JC), dtype=ml_dtypes.bfloat16)
    for jj in range(128 * JC):
        e_mat[jj // 16, jj] = 1
    U = np.asarray(ins["L0_ffn_U"], np.float32)
    m = {
        "xb": xb,
        "xo": xo,
        "xq": xq,
        "wq": ins["L0_attn_Wq"], "wk": ins["L0_attn_Wk"],
        "wv": ins["L0_attn_Wv"], "wo": ins["L0_attn_Wo"],
        "bq": np.ascontiguousarray(np.asarray(ins["L0_attn_bq"], np.float32).reshape(DC, 128).T),
        "bk": np.ascontiguousarray(np.asarray(ins["L0_attn_bk"], np.float32).reshape(DC, 128).T),
        "bv": np.ascontiguousarray(np.asarray(ins["L0_attn_bv"], np.float32).reshape(DC, 128).T),
        "bo_bc": np.ascontiguousarray(np.broadcast_to(
            np.asarray(ins["L0_attn_bo"], np.float32).reshape(1, D), (128, D))),
        "wr_t": np.ascontiguousarray(np.asarray(ins["L0_ffn_Wr"], np.float32).T),
        "u_flat": np.ascontiguousarray(U.transpose(1, 0, 2).reshape(D, -1)),
        "v_flat": np.ascontiguousarray(np.asarray(ins["L0_ffn_V"], np.float32).reshape(-1, D)),
        "attr_t": np.ascontiguousarray(np.asarray(ins["L0_ffn_attr"], np.float32).T),
        "wa_t": np.ascontiguousarray(np.asarray(ins["L0_ffn_Wa"], np.float32).reshape(1, S).T),
        "masks": masks,
        "ident": np.eye(128, dtype=ml_dtypes.bfloat16),
        "ones_f": np.ones((1, 128), np.float32),
        "ones_c": np.ones((128, 1), dtype=ml_dtypes.bfloat16),
        "e_mat": e_mat,
    }
    return {k: np.ascontiguousarray(np.asarray(v)) for k, v in m.items()}


def assemble(cfg, results):
    D, T, QT, NSW = cfg["D"], cfg["T"], cfg["QT"], cfg["NSW"]
    full = np.zeros((2, T, D), np.float32)
    for core in range(8):
        b, g = core // 4, core % 4
        o = np.asarray(results[core]["out"], np.float32)
        for s in range(NSW):
            blk = 4 * s + g
            full[b, blk * QT:(blk + 1) * QT] = o[s * QT:(s + 1) * QT]
    return full


def kernel(**inputs):
    cfg = mkcfg(FULL)
    nc = build_nc(cfg)
    in_maps = [host_inputs(cfg, c, inputs) for c in range(8)]
    res = run_bass_kernel_spmd(nc, in_maps, core_ids=list(range(8)))
    return assemble(cfg, res.results)


# version 10
# speedup vs baseline: 1.0125x; 1.0125x over previous
import math
from contextlib import ExitStack

import numpy as np
import ml_dtypes
import concourse.bass as bass
import concourse.mybir as mybir
from concourse.bass_utils import run_bass_kernel_spmd

F32 = mybir.dt.float32
BF16 = mybir.dt.bfloat16
AF = mybir.ActivationFunctionType
ALU = mybir.AluOpType
AX = mybir.AxisListType

FULL = dict(D=2048, T=8192, H=16)
TINY = dict(D=512, T=2048, H=4)

C_SHIFT = 12.0


def mkcfg(base):
    c = dict(base)
    D, T, H = c["D"], c["T"], c["H"]
    c["DH"] = 128
    assert H * c["DH"] == D
    c["DC"] = D // 128
    c["OWN"] = T // 4
    assert c["OWN"] == D
    c["QT"] = T // 16
    assert c["QT"] % 128 == 0
    c["NQS"] = c["QT"] // 128
    c["NSW"] = 4
    c["NKS"] = T // 512
    c["CH"] = c["QT"]
    c["NCH"] = T // c["CH"]
    c["NSUB"] = c["CH"] // 128
    c["S"], c["R"] = 32, 16
    c["J"] = c["S"] * c["R"]
    c["JC"] = c["J"] // 128
    c["GR"] = min(512, c["OWN"])
    c["NG"] = c["OWN"] // c["GR"]
    c["ORT"] = c["OWN"] // 128
    assert c["ORT"] == c["DC"]
    c["SCALE"] = 1.0 / math.sqrt(c["DH"])
    return c


class Sched:
    ENG = ("sy", "gq", "pe", "act", "dve")

    def __init__(self):
        self.ops = {e: [] for e in self.ENG}
        from collections import defaultdict
        self.n = defaultdict(int)
        for s in ("pe", "act", "dve"):
            self.n[s] = 0
        self.chain_last = {"act": 0, "dve": 0, "gq": 0}
        self.chain_sem = {"act": "act", "dve": "dve", "gq": "pool"}

    def add(self, eng, fn, waits=(), inc=None, chain=False):
        w = [x for x in waits if x is not None and x[1] > 0]
        if chain and eng in self.chain_last:
            if self.chain_last[eng] > 0:
                w.append((self.chain_sem[eng], self.chain_last[eng]))
        tok = None
        if inc is not None:
            sem, k = inc
            self.n[sem] += k
            tok = (sem, self.n[sem])
            if eng in self.chain_last and sem == self.chain_sem.get(eng):
                self.chain_last[eng] = self.n[sem]
        self.ops[eng].append((fn, w, inc))
        return tok

    def tok(self, sem):
        return (sem, self.n[sem])

    def barrier(self):
        snap = dict(self.n)
        for e in self.ENG:
            self.add(e, None, waits=[(s, v) for s, v in snap.items()])
        return snap


def build_nc(cfg):
    D, T, H, DH = cfg["D"], cfg["T"], cfg["H"], cfg["DH"]
    DC, OWN, QT, NQS = cfg["DC"], cfg["OWN"], cfg["QT"], cfg["NQS"]
    NSW, NKS, CH, NCH, NSUB = cfg["NSW"], cfg["NKS"], cfg["CH"], cfg["NCH"], cfg["NSUB"]
    S, J, JC = cfg["S"], cfg["J"], cfg["JC"]
    GR, NG, ORT, SCALE = cfg["GR"], cfg["NG"], cfg["ORT"], cfg["SCALE"]
    NVC = D // 512
    NSUBK = T // 128

    nc = bass.Bass("TRN2", target_bir_lowering=False)

    xb = nc.dram_tensor("xb", [T, D], F32, kind="ExternalInput").ap()
    xo = nc.dram_tensor("xo", [OWN, D], F32, kind="ExternalInput").ap()
    xq = nc.dram_tensor("xq", [OWN, D], F32, kind="ExternalInput").ap()
    wq = nc.dram_tensor("wq", [D, D], F32, kind="ExternalInput").ap()
    wk = nc.dram_tensor("wk", [D, D], F32, kind="ExternalInput").ap()
    wv = nc.dram_tensor("wv", [D, D], F32, kind="ExternalInput").ap()
    wo = nc.dram_tensor("wo", [D, D], F32, kind="ExternalInput").ap()
    bq_i = nc.dram_tensor("bq", [128, DC], F32, kind="ExternalInput").ap()
    bk_i = nc.dram_tensor("bk", [128, DC], F32, kind="ExternalInput").ap()
    bv_i = nc.dram_tensor("bv", [128, DC], F32, kind="ExternalInput").ap()
    bo_i = nc.dram_tensor("bo_bc", [128, D], F32, kind="ExternalInput").ap()
    wr_i = nc.dram_tensor("wr_t", [D, S], F32, kind="ExternalInput").ap()
    u_fl = nc.dram_tensor("u_flat", [D, J], F32, kind="ExternalInput").ap()
    v_fl = nc.dram_tensor("v_flat", [J, D], F32, kind="ExternalInput").ap()
    at_i = nc.dram_tensor("attr_t", [S, S], F32, kind="ExternalInput").ap()
    wa_i = nc.dram_tensor("wa_t", [S, 1], F32, kind="ExternalInput").ap()
    mk_i = nc.dram_tensor("masks", [128, NKS * QT], BF16, kind="ExternalInput").ap()
    id_i = nc.dram_tensor("ident", [128, 128], BF16, kind="ExternalInput").ap()
    of_i = nc.dram_tensor("ones_f", [1, 128], F32, kind="ExternalInput").ap()
    oc_i = nc.dram_tensor("ones_c", [128, 1], BF16, kind="ExternalInput").ap()
    em_i = nc.dram_tensor("e_mat", [S, 128 * JC], BF16, kind="ExternalInput").ap()

    out = nc.dram_tensor("out", [OWN, D], F32, kind="ExternalOutput").ap()

    kt_scr = nc.dram_tensor("kt_scr", [D, T], BF16, kind="Internal").ap()
    v_scr = nc.dram_tensor("v_scr", [T, D], BF16, kind="Internal").ap()
    qt_scr = nc.dram_tensor("qt_scr", [D, OWN], BF16, kind="Internal").ap()
    xnt_scr = nc.dram_tensor("xnt_scr", [D, T], BF16, kind="Internal").ap()

    with ExitStack() as ctx:
        e = ctx.enter_context

        pa = e(nc.sbuf_tensor("pa", [128, DC * OWN], BF16))
        PW_COLS = max(DC * D, 2 * T + 2 * 128 * NSUBK)
        pw = e(nc.sbuf_tensor("pw", [128, PW_COLS], BF16))
        pm = e(nc.sbuf_tensor("pm", [128, max(DC * J, NKS * QT, 2 * (D + 1024 + 3 * 512)) + 4 * QT], BF16))
        pv = e(nc.sbuf_tensor("pv", [128, max(JC * D, JC * GR + OWN + 2 * D)], BF16))
        pg = e(nc.sbuf_tensor("pg", [128, JC * OWN], BF16))
        xs = [e(nc.sbuf_tensor(f"xs{i}", [128, max(D, OWN)], BF16)) for i in range(4)]
        pt = [e(nc.sbuf_tensor(f"pt{i}", [128, QT], BF16)) for i in range(3)]
        wrk = e(nc.sbuf_tensor("wrk", [128, 64], F32))
        scw = e(nc.sbuf_tensor("scw", [128, 8 * S], F32))
        gps = e(nc.sbuf_tensor("gps", [128, ORT], F32))
        bqs = e(nc.sbuf_tensor("bqs", [128, DC], F32))
        bks = e(nc.sbuf_tensor("bks", [128, DC], F32))
        bvs = e(nc.sbuf_tensor("bvs", [128, DC], F32))
        wrs = e(nc.sbuf_tensor("wrs", [128, DC * S], BF16))
        ems = e(nc.sbuf_tensor("ems", [S, 128 * JC], BF16))
        ids = e(nc.sbuf_tensor("ids", [128, 128], BF16))
        onf = e(nc.sbuf_tensor("onf", [1, 128], F32))
        onc = e(nc.sbuf_tensor("onc", [128, 1], BF16))
        ats = e(nc.sbuf_tensor("ats", [S, S], F32))
        was = e(nc.sbuf_tensor("was", [S, 1], F32))
        awb = e(nc.sbuf_tensor("awb", [128, S], F32))
        aws = e(nc.sbuf_tensor("aws", [1, S], F32))
        cst = e(nc.sbuf_tensor("cst", [128, 1], F32))

        pf = [e(nc.psum_tensor(f"pf{i}", [128, 512], F32)) for i in range(4)]
        psc = [e(nc.psum_tensor(f"psc{i}", [128, 512], F32)) for i in range(2)]
        pb = [e(nc.psum_tensor(f"pb{i}", [128, 1024], BF16)) for i in range(2)]

        block_holder = []

        S_ = Sched()

        def xnt_own(dc, r0, r1):
            return pa[:, dc * OWN + r0: dc * OWN + r1]

        XTC_OFF = [0, 2 * DC * CH]
        STG_OFF = [DC * CH, 3 * DC * CH]

        def xtc(buf, dc, r0, r1):
            o = XTC_OFF[buf]
            return pa[:, o + dc * CH + r0: o + dc * CH + r1]

        def stg(buf, kc, c0, c1):
            o = STG_OFF[buf]
            return pa[:, o + kc * CH + c0: o + kc * CH + c1]

        def stgv(buf, rs, c0, c1):
            o = STG_OFF[buf]
            return pa[:, o + rs * D + c0: o + rs * D + c1]

        def ac(rt, f0, f1):
            return pa[:, rt * D + f0: rt * D + f1]

        def acT(fc, r0, r1):
            return pa[:, fc * OWN + r0: fc * OWN + r1]

        def wchunk(dc, n0, n1):
            return pw[:, dc * D + n0: dc * D + n1]

        def kth(b, k0, k1):
            return pw[:, b * T + k0: b * T + k1]

        VH0 = 2 * T

        def vh128(b, sub):
            o = VH0 + b * (128 * NSUBK) + sub * 128
            return pw[:, o: o + 128]

        def vh_view(b):
            o = VH0 + b * (128 * NSUBK)
            return pw[:, o: o + NSUBK * 128].rearrange("p (s c) -> p s c", c=128)

        def uchunk(dc, j0, j1):
            return pm[:, dc * J + j0: dc * J + j1]

        def mask_j(j):
            return pm[:, j * QT:(j + 1) * QT]

        pmf = pm.bitcast(F32)

        def bo_b(c0, c1):
            return pmf[:, c0:c1]

        def p6x(b):
            return pmf[:, D + b * 512: D + (b + 1) * 512]

        def p6t(i):
            return pmf[:, D + 1024 + i * 512: D + 1024 + (i + 1) * 512]

        def vflat(jc, n0, n1):
            return pv[:, jc * D + n0: jc * D + n1]

        def gexp(jc):
            return pv[:, jc * GR:(jc + 1) * GR]

        GT0 = JC * GR

        def gT(r0, r1):
            return pv[0:S, GT0 + r0: GT0 + r1]

        def gut(jc, r0, r1):
            return pg[:, jc * OWN + r0: jc * OWN + r1]

        PVJ = JC * GR + OWN
        sqj = pv[:, PVJ: PVJ + D]
        cpj = pv[:, PVJ + D: PVJ + 2 * D]
        qst = pg[:, 0: 2 * GR]

        def w1(i):
            return wrk[:, i:i + 1]

        A = S_.add

        def dma(eng, sem, outap, inap, waits=()):
            return A(eng, lambda E, o=outap, i=inap: E.dma_start(out=o, in_=i),
                     waits=waits, inc=(sem, 16))

        tok_consts = None
        for o, i in [(ids[:], id_i), (onf[:], of_i), (onc[:], oc_i),
                     (bqs[:], bq_i), (bks[:], bk_i),
                     (bvs[:], bv_i), (ats[:], at_i), (was[:], wa_i),
                     (ems[:], em_i)]:
            tok_consts = dma("sy", "cst", o, i)
        tok_wrs = dma("gq", "wrs", wrs.rearrange("p (dc s) -> p dc s", s=S),
                      wr_i.rearrange("(dc p) s -> p dc s", p=128))

        tok_wq = dma("gq", "w", pw[:, 0:DC * D].rearrange("p (dc n) -> p dc n", n=D),
                     wq.rearrange("(dc p) n -> p dc n", p=128))
        tok_u = dma("gq", "u", pm[:, 0:DC * J].rearrange("p (dc j) -> p dc j", j=J),
                    u_fl.rearrange("(dc p) j -> p dc j", p=128))
        tok_cst = A("dve", lambda E: E.memset(cst[:], -C_SHIFT),
                    inc=("dve", 1), chain=True)

        ln_i = [0]
        ln_trs = {}
        ln_norm = {}

        def ln_stats(src_rows, extra_waits=()):
            i = ln_i[0]
            ln_i[0] += 1
            beta = i % 4
            b2 = i % 2
            xbuf = xs[beta]
            t_x = dma("gq", f"x{beta}", xbuf[:, 0:D], src_rows,
                      waits=[ln_trs.get(i - 4)] + list(extra_waits))
            t_sq = A("act", lambda E, xb_=xbuf, b=b2: E.activation(
                out=sqj, in_=xb_[:, 0:D], func=AF.Square,
                accum_out=wrk[:, 8 + b:9 + b]), waits=[t_x], inc=("act", 1), chain=True)
            t_sm = A("dve", lambda E, xb_=xbuf, b=b2: E.tensor_scalar(
                cpj, xb_[:, 0:D], 1.0, None, op0=ALU.mult, op1=ALU.add,
                accum_out=wrk[:, 10 + b:11 + b]), waits=[t_x], inc=("dve", 1), chain=True)
            A("dve", lambda E, b=b2: E.tensor_scalar(
                w1(12 + b), w1(10 + b), 1.0 / D, None, op0=ALU.mult),
              waits=[t_sm], inc=("dve", 1), chain=True)
            A("dve", lambda E, b=b2: E.tensor_scalar(
                w1(14 + b), w1(8 + b), 1.0 / D, None, op0=ALU.mult),
              waits=[t_sq], inc=("dve", 1), chain=True)
            A("dve", lambda E, b=b2: E.tensor_tensor(
                w1(16 + b), w1(12 + b), w1(12 + b), op=ALU.mult),
              inc=("dve", 1), chain=True)
            A("dve", lambda E, b=b2: E.tensor_tensor(
                w1(18 + b), w1(14 + b), w1(16 + b), op=ALU.subtract),
                inc=("dve", 1), chain=True)
            t_var = A("dve", lambda E, b=b2: E.tensor_scalar(
                w1(18 + b), w1(18 + b), 1e-5, None, op0=ALU.add),
                inc=("dve", 1), chain=True)
            A("act", lambda E, b=b2: E.activation(
                out=w1(20 + b), in_=w1(18 + b), func=AF.Ln),
              waits=[t_var], inc=("act", 1), chain=True)
            t_rstd = A("act", lambda E, b=b2: E.activation(
                out=w1(22 + b), in_=w1(20 + b), func=AF.Exp, scale=-0.5),
                inc=("act", 1), chain=True)
            ln_norm[i] = A("dve", lambda E, xb_=xbuf, b=b2: E.tensor_scalar(
                xb_[:, 0:D], xb_[:, 0:D], w1(12 + b), w1(22 + b),
                op0=ALU.subtract, op1=ALU.mult),
                waits=[t_rstd], inc=("dve", 1), chain=True)
            return i

        def ln_tr(i, dest_fn, dest_r0):
            beta = i % 4
            xbuf = xs[beta]
            t_norm = ln_norm[i]
            ev_toks = []
            tr_toks = []
            for dc in range(DC):
                war = ev_toks[dc - 2] if dc >= 2 else ln_tr.last_ev.get(dc % 2)
                t_tr = A("pe", lambda E, xb_=xbuf, dc=dc: E.transpose(
                    pb[dc % 2][:, 0:128], xb_[:, dc * 128:(dc + 1) * 128], ids[:]),
                    waits=[t_norm, war, tok_consts], inc=("pe", 1))
                tr_toks.append(t_tr)
                t_ev = A("dve", lambda E, dc=dc, r0=dest_r0, df=dest_fn: E.tensor_copy(
                    df(dc, r0, r0 + 128), pb[dc % 2][:, 0:128]),
                    waits=[t_tr], inc=("dve", 1), chain=True)
                ev_toks.append(t_ev)
            ln_tr.last_ev = {0: ev_toks[-2], 1: ev_toks[-1]}
            ln_trs[i] = tr_toks[-1]
            return ev_toks[-1]

        ln_tr.last_ev = {}

        def ln_subtile(src_rows, dest_fn, dest_r0, extra_waits=()):
            i = ln_stats(src_rows, extra_waits)
            ev = ln_tr(i, dest_fn, dest_r0)
            return ev, ln_trs[i]

        q_mm_last = None
        qev = {}
        qld = {}
        NCH0 = OWN // CH
        LNQ = DC // NSUB
        ev_of = {}
        st_ids = {}
        ids0 = [ln_stats(xq[sb * 128:(sb + 1) * 128, :]) for sb in range(NSUB)]
        ev_of[0] = max(ln_tr(i, lambda dc, a, bb: xtc(0, dc, a, bb),
                             sb * 128) for sb, i in enumerate(ids0))
        for c in range(NCH0):
            if c + 1 < NCH0:
                st_ids[c + 1] = [ln_stats(
                    xq[(c + 1) * CH + sb * 128:(c + 1) * CH + (sb + 1) * 128, :])
                    for sb in range(NSUB)]
            for qc in range(DC):
                n = c * DC + qc
                waits = [ev_of[c], tok_wq]
                if n >= 4:
                    waits.append(qev[n - 4])

                def mk_qmm(qc=qc, n=n, c=c):
                    def f(E):
                        for dc in range(DC):
                            ins = E.matmul(pf[n % 4][:, 0:CH],
                                           wchunk(dc, qc * 128, (qc + 1) * 128),
                                           xtc(c % 2, dc, 0, CH),
                                           start=(dc == 0), stop=(dc == DC - 1))
                        return ins
                    return f
                q_mm_last = A("pe", mk_qmm(), waits=waits, inc=("pe", 1))
                w2 = [q_mm_last]
                if n >= 2:
                    w2.append(qld[n - 2])
                qev[n] = A("act", lambda E, qc=qc, n=n: E.activation(
                    out=qst[:, (n % 2) * GR:(n % 2) * GR + CH],
                    in_=pf[n % 4][:, 0:CH], func=AF.Identity,
                    bias=bqs[:, qc:qc + 1]), waits=w2, inc=("act", 1), chain=True)
                qld[n] = dma(
                    "sy", f"q{n % 2}",
                    qt_scr.rearrange("(qc p) r -> p qc r", p=128)[:, qc, c * CH:(c + 1) * CH],
                    qst[:, (n % 2) * GR:(n % 2) * GR + CH],
                    waits=[qev[n]])
                if c + 1 < NCH0 and qc % LNQ == LNQ - 1:
                    sb = qc // LNQ
                    ev = ln_tr(st_ids[c + 1][sb],
                               lambda dc, a, bb, c=c: xtc((c + 1) % 2, dc, a, bb),
                               sb * 128)
                    if sb == NSUB - 1:
                        ev_of[c + 1] = ev
        tok_p2b_pe = S_.tok("pe")
        tok_p2b_q0 = S_.tok("q0")
        tok_p2b_q1 = S_.tok("q1")

        for i in range(OWN // 128):
            ev, tr = ln_subtile(xo[i * 128:(i + 1) * 128, :], xnt_own, i * 128)
        tok_p1 = S_.tok("dve")

        t1 = A("pe", lambda E: E.matmul(psc[0][0:1, 0:S], was[:], ats[:],
                                        start=True, stop=True),
               waits=[tok_consts, tok_p2b_pe], inc=("pe", 1))
        t2 = A("act", lambda E: E.activation(out=aws[:], in_=psc[0][0:1, 0:S],
                                             func=AF.Exp, scale=-1.0),
               waits=[t1], inc=("act", 1), chain=True)
        t3 = A("dve", lambda E: E.tensor_scalar(aws[:], aws[:], 1.0, None,
                                                op0=ALU.add),
               waits=[t2], inc=("dve", 1), chain=True)
        t4 = A("dve", lambda E: E.reciprocal(aws[:], aws[:]),
               inc=("dve", 1), chain=True)
        t5 = A("pe", lambda E: E.matmul(psc[1][:, 0:S], onf[:], aws[:],
                                        start=True, stop=True),
               waits=[t4], inc=("pe", 1))
        tok_awb = A("dve", lambda E: E.tensor_copy(awb[:], psc[1][:, 0:S]),
                    waits=[t5], inc=("dve", 1), chain=True)

        g_done = {}
        for rt in range(ORT):
            r0 = rt * 128

            def mk_sc(rt=rt, r0=r0):
                def f(E):
                    for dc in range(DC):
                        ins = E.matmul(psc[0][:, 0:S],
                                       xnt_own(dc, r0, r0 + 128),
                                       wrs[:, dc * S:(dc + 1) * S],
                                       start=(dc == 0), stop=(dc == DC - 1))
                    return ins
                return f
            waits = [tok_p1, tok_consts, tok_wrs, t2]
            if rt > 0:
                waits.append(g_done[rt - 1])
            t_sc = A("pe", mk_sc(), waits=waits, inc=("pe", 1))

            t_cp = A("dve", lambda E: E.tensor_copy(scw[:, 0:S], psc[0][:, 0:S]),
                     waits=[t_sc], inc=("dve", 1), chain=True)
            A("dve", lambda E: E.tensor_reduce(w1(24), scw[:, 0:S], axis=AX.X,
                                               op=ALU.max, negate=True),
              inc=("dve", 1), chain=True)
            A("dve", lambda E: E.tensor_scalar(w1(25), w1(24), -1.0, None,
                                               op0=ALU.mult),
              inc=("dve", 1), chain=True)
            A("dve", lambda E: E.tensor_scalar(scw[:, S:2 * S], scw[:, 0:S],
                                               w1(25), 1e9, op0=ALU.is_ge,
                                               op1=ALU.mult),
              inc=("dve", 1), chain=True)
            t_ein = A("dve", lambda E: E.tensor_tensor(
                scw[:, 2 * S:3 * S], scw[:, 0:S], scw[:, S:2 * S],
                op=ALU.subtract), inc=("dve", 1), chain=True)
            t_exp = A("act", lambda E: E.activation(
                out=scw[:, 3 * S:4 * S], in_=scw[:, 0:S], func=AF.Exp,
                bias=w1(24)), waits=[t_ein], inc=("act", 1), chain=True)
            A("dve", lambda E: E.tensor_reduce(w1(26), scw[:, 2 * S:3 * S],
                                               axis=AX.X, op=ALU.max),
              inc=("dve", 1), chain=True)
            A("dve", lambda E: E.tensor_scalar(scw[:, 4 * S:5 * S], scw[:, 0:S],
                                               w1(26), None, op0=ALU.is_ge),
              inc=("dve", 1), chain=True)
            A("dve", lambda E: E.tensor_reduce(w1(27), scw[:, 3 * S:4 * S],
                                               axis=AX.X, op=ALU.add),
              waits=[t_exp], inc=("dve", 1), chain=True)
            A("dve", lambda E: E.tensor_tensor(scw[:, 5 * S:6 * S],
                                               scw[:, 3 * S:4 * S],
                                               scw[:, 4 * S:5 * S], op=ALU.mult),
              inc=("dve", 1), chain=True)
            A("dve", lambda E: E.tensor_reduce(w1(28), scw[:, 5 * S:6 * S],
                                               axis=AX.X, op=ALU.add),
              inc=("dve", 1), chain=True)
            A("dve", lambda E: E.tensor_scalar(w1(29), w1(27), 1e-8, None,
                                               op0=ALU.mult),
              inc=("dve", 1), chain=True)
            A("dve", lambda E: E.tensor_tensor(w1(30), w1(28), w1(29), op=ALU.add),
              inc=("dve", 1), chain=True)
            A("dve", lambda E: E.reciprocal(w1(31), w1(30)),
              inc=("dve", 1), chain=True)
            A("dve", lambda E: E.tensor_scalar(scw[:, 6 * S:7 * S],
                                               scw[:, 5 * S:6 * S], w1(31), None,
                                               op0=ALU.mult),
              inc=("dve", 1), chain=True)
            A("dve", lambda E: E.tensor_tensor(scw[:, 7 * S:8 * S],
                                               scw[:, 6 * S:7 * S], awb[:],
                                               op=ALU.mult),
              waits=[tok_awb], inc=("dve", 1), chain=True)
            A("dve", lambda E: E.tensor_reduce(w1(32), scw[:, 7 * S:8 * S],
                                               axis=AX.X, op=ALU.add),
              inc=("dve", 1), chain=True)
            A("dve", lambda E, rt=rt: E.tensor_scalar(
                gps[:, rt:rt + 1], w1(32), 0.2, 0.9, op0=ALU.mult, op1=ALU.add),
              inc=("dve", 1), chain=True)
            t_gbf = A("dve", lambda E: E.tensor_copy(pt[2][:, 0:S],
                                                     scw[:, 6 * S:7 * S]),
                      inc=("dve", 1), chain=True)
            t_gtr = A("pe", lambda E: E.transpose(pb[0][0:S, 0:128], pt[2][:, 0:S],
                                                  ids[:]),
                      waits=[t_gbf, (g_done.get(rt - 1) if rt else None)],
                      inc=("pe", 1))
            g_done[rt] = A("dve", lambda E, r0=r0: E.tensor_copy(
                gT(r0, r0 + 128), pb[0][0:S, 0:128]),
                waits=[t_gtr], inc=("dve", 1), chain=True)

        gut_toks = {}
        hu_last = None
        for gi in range(NG):
            r0 = gi * GR
            gx = {}
            for jc in range(JC):
                gxw = [g_done[ORT - 1], tok_consts, tok_awb]
                if jc > 0:
                    gxw.append(gx[jc - 1])
                elif gi > 0:
                    gxw.append(gx_prev_last)
                t_gx = A("pe", lambda E, jc=jc, r0=r0: E.matmul(
                    psc[1][:, 0:GR], ems[:, jc * 128:(jc + 1) * 128],
                    gT(r0, r0 + GR), start=True, stop=True),
                    waits=gxw, inc=("pe", 1))
                gx[jc] = A("dve", lambda E, jc=jc: E.tensor_copy(
                    gexp(jc), psc[1][:, 0:GR]),
                    waits=[t_gx], inc=("dve", 1), chain=True)
            gx_prev_last = gx[JC - 1]
            for jc in range(JC):
                n = gi * JC + jc
                waits = [tok_u, tok_p1]
                if n >= 4:
                    waits.append(gut_toks[n - 4])

                def mk_hu(jc=jc, r0=r0, n=n):
                    def f(E):
                        for dc in range(DC):
                            ins = E.matmul(pf[n % 4][:, 0:GR],
                                           uchunk(dc, jc * 128, (jc + 1) * 128),
                                           xnt_own(dc, r0, r0 + GR),
                                           start=(dc == 0), stop=(dc == DC - 1))
                        return ins
                    return f
                hu_last = A("pe", mk_hu(), waits=waits, inc=("pe", 1))
                gut_toks[n] = A("dve", lambda E, jc=jc, r0=r0, n=n: E.tensor_tensor(
                    gut(jc, r0, r0 + GR), pf[n % 4][:, 0:GR], gexp(jc),
                    op=ALU.mult),
                    waits=[hu_last, gx[jc], tok_p2b_q0, tok_p2b_q1],
                    inc=("dve", 1), chain=True)
        tok_p3_dve = S_.tok("dve")
        tok_p3_pe = S_.tok("pe")

        tok_wk = dma("gq", "w", pw[:, 0:DC * D].rearrange("p (dc n) -> p dc n", n=D),
                     wk.rearrange("(dc p) n -> p dc n", p=128),
                     waits=[tok_p2b_pe])

        kev = {}
        kmm = {}
        kdma = {}
        kev_of = {}
        kst_ids = {}
        ids0 = [ln_stats(xb[sb * 128:(sb + 1) * 128, :]) for sb in range(NSUB)]
        kev_of[0] = max(ln_tr(i, lambda dc, a, bb: xtc(0, dc, a, bb),
                              sb * 128) for sb, i in enumerate(ids0))
        for c in range(NCH):
            if c + 1 < NCH:
                kst_ids[c + 1] = [ln_stats(
                    xb[(c + 1) * CH + sb * 128:(c + 1) * CH + (sb + 1) * 128, :],
                    extra_waits=[kdma.get(c - 1)]) for sb in range(NSUB)]
            for kc in range(DC):
                waits = [kev_of[c], tok_wk]
                gn = c * DC + kc
                if gn >= 4:
                    waits.append(kev[gn - 4])

                def mk_kmm(c=c, kc=kc):
                    def f(E, _kk=None):
                        for dc in range(DC):
                            ins = E.matmul(pf[kc % 4][:, 0:CH],
                                           wchunk(dc, kc * 128, (kc + 1) * 128),
                                           xtc(c % 2, dc, 0, CH),
                                           start=(dc == 0), stop=(dc == DC - 1))
                        return ins
                    return f
                kmm[gn] = A("pe", mk_kmm(), waits=waits, inc=("pe", 1))
                w2 = [kmm[gn]]
                if c > 1:
                    w2.append(kdma[c - 2])
                kev[gn] = A("act", lambda E, c=c, kc=kc: E.activation(
                    out=stg(c % 2, kc, 0, CH), in_=pf[kc % 4][:, 0:CH],
                    func=AF.Identity,
                    bias=bks[:, kc:kc + 1]), waits=w2, inc=("act", 1), chain=True)
                if c + 1 < NCH and kc % LNQ == LNQ - 1:
                    sb = kc // LNQ
                    ev = ln_tr(kst_ids[c + 1][sb],
                               lambda dc, a, bb, c=c: xtc((c + 1) % 2, dc, a, bb),
                               sb * 128)
                    if sb == NSUB - 1:
                        kev_of[c + 1] = ev
            _kd = dma(
                "sy", f"kx{c % 2}",
                kt_scr.rearrange("(kc p) r -> p kc r", p=128)[:, :, c * CH:(c + 1) * CH],
                pa[:, STG_OFF[c % 2]: STG_OFF[c % 2] + DC * CH
                   ].rearrange("p (kc r) -> p kc r", r=CH),
                waits=[kev[c * DC + DC - 1]])
            t_xd = dma(
                "sy", f"kx{c % 2}",
                xnt_scr.rearrange("(dc p) r -> p dc r", p=128)[:, :, c * CH:(c + 1) * CH],
                pa[:, XTC_OFF[c % 2]: XTC_OFF[c % 2] + DC * CH
                   ].rearrange("p (dc r) -> p dc r", r=CH),
                waits=[kev_of[c]])
            kdma[c] = t_xd
        tok_p2a_ld0 = S_.tok("kx0")
        tok_p2a_ld1 = S_.tok("kx1")
        tok_p2a_pe = S_.tok("pe")

        tok_wv = dma("gq", "w", pw[:, 0:DC * D].rearrange("p (dc n) -> p dc n", n=D),
                     wv.rearrange("(dc p) n -> p dc n", p=128),
                     waits=[tok_p2b_pe, tok_p3_pe, tok_p2a_pe])
        tok_masks = dma("gq", "mk", pm[:, 0:NKS * QT], mk_i, waits=[tok_p3_pe])

        vev = {}
        vdma = {}
        vmm = {}
        vxl = {}
        for c in range(NCH):
            waits = [tok_p2a_ld0, tok_p2a_ld1, tok_p3_dve, tok_p2b_q0, tok_p2b_q1, tok_p2a_pe]
            if c > 1:
                waits.append(vmm[c - 2])
            vxl[c] = dma(
                "sy", f"vx{c % 2}",
                pa[:, XTC_OFF[c % 2]:XTC_OFF[c % 2] + DC * CH
                   ].rearrange("p (dc r) -> p dc r", r=CH),
                xnt_scr.rearrange("(dc p) r -> p dc r", p=128)[:, :, c * CH:(c + 1) * CH],
                waits=waits)
            for rs in range(NSUB):
                for vc in range(NVC):
                    n = rs * NVC + vc
                    gn = c * NSUB * NVC + n
                    waits = [vxl[c], tok_wv]
                    if gn >= 4:
                        waits.append(vev[gn - 4])

                    def mk_vmm(rs=rs, vc=vc, n=n, c=c):
                        def f(E):
                            for dc in range(DC):
                                ins = E.matmul(pf[n % 4][:, 0:512],
                                               xtc(c % 2, dc, rs * 128, (rs + 1) * 128),
                                               wchunk(dc, vc * 512, (vc + 1) * 512),
                                               start=(dc == 0), stop=(dc == DC - 1))
                            return ins
                        return f
                    vmm[c] = A("pe", mk_vmm(), waits=waits, inc=("pe", 1))
                    w2 = [vmm[c]]
                    if c > 1:
                        w2.append(vdma[c - 2])
                    vev[gn] = A("dve", lambda E, rs=rs, vc=vc, n=n, c=c: E.tensor_copy(
                        stgv(c % 2, rs, vc * 512, (vc + 1) * 512), pf[n % 4][:, 0:512]),
                        waits=w2, inc=("dve", 1), chain=True)
            vdma[c] = dma(
                "sy", f"vw{c % 2}",
                v_scr[c * CH:(c + 1) * CH, :].rearrange("(rs p) n -> p rs n", p=128),
                pa[:, STG_OFF[c % 2]: STG_OFF[c % 2] + NSUB * D
                   ].rearrange("p (rs n) -> p rs n", n=D),
                waits=[vev[c * NSUB * NVC + NSUB * NVC - 1]])
        tok_p2c_ld0 = S_.tok("vw0")
        tok_p2c_ld1 = S_.tok("vw1")
        tok_p2c_pe = S_.tok("pe")

        tok_vflat = dma("gq", "vf", pv[:, 0:JC * D].rearrange("p (jc n) -> p jc n", n=D),
                        v_fl.rearrange("(jc p) n -> p jc n", p=128),
                        waits=[tok_p3_dve, ("act", S_.n["act"]), ("dve", S_.n["dve"])])

        SCB = [psc[0], psc[1], pf[2], pf[3]]

        def scb_(k):
            return SCB[k % 4]

        S_ACC0 = max(NKS * QT, DC * J, 2 * (D + 1024 + 3 * 512)) // 2

        def s_acc(q0, q1):
            return pmf[:, S_ACC0 + q0: S_ACC0 + q1]

        def s_acc2(q0, q1):
            return pmf[:, S_ACC0 + QT + q0: S_ACC0 + QT + q1]

        head_pe = {}
        head_ev = {}
        head_ld = {}
        exp_toks = {}
        mask_toks = {}
        pool_toks = {}
        sweep_ev = {}
        lmm_tok = {}
        nvd = (NSUBK + 7) // 8
        for h in range(H):
            beta = h % 2
            waits = [tok_p2a_ld0, tok_p2a_ld1, tok_p2b_q0, tok_p2b_q1,
                     tok_p2c_ld0, tok_p2c_ld1, tok_p2c_pe, tok_p2a_pe,
                     S_.tok("pe") if h == 0 else None]
            if h >= 2:
                waits += [head_pe[h - 2], head_ev[h - 2]]
            t = dma("sy", f"h{beta}", kth(beta, 0, T), kt_scr[h * 128:(h + 1) * 128, :],
                    waits=waits)
            for d in range(nvd):
                s0, s1 = d * 8, min((d + 1) * 8, NSUBK)
                t = dma("sy", f"h{beta}", vh_view(beta)[:, s0:s1, :],
                        v_scr[s0 * 128:s1 * 128, h * 128:(h + 1) * 128
                              ].rearrange("(s p) c -> p s c", p=128))
            RPH = OWN // H
            head_ld[h] = t = dma(
                "sy", f"h{beta}",
                xs[beta][:, 0:DC * 128].rearrange("p (qc r) -> p qc r", r=128),
                qt_scr.rearrange("(qc p) r -> p qc r", p=128)[:, :, h * RPH:(h + 1) * RPH])

            for sw in range(1, NSW + 1):
                nk = sw * NKS
                outq = []
                prev_ev = sweep_ev.get((h, sw - 1)) or sweep_ev.get((h - 1, NSW))
                t_ms = A("gq", lambda E: E.memset(s_acc(0, QT), 0.0),
                         waits=[prev_ev, tok_u], inc=("pool", 1), chain=True)
                t_ms2 = A("dve", lambda E: E.memset(s_acc2(0, QT), 0.0),
                          waits=[prev_ev, tok_u], inc=("dve", 1), chain=True)
                for k in range(nk):
                    if k >= 4:
                        war = exp_toks[(h, sw, k - 4)]
                    elif sw > 1:
                        war = exp_toks[(h, sw - 1, (sw - 1) * NKS - 4 + k)]
                    elif h > 0:
                        war = exp_toks[(h - 1, NSW, NSW * NKS - 4 + k)]
                    else:
                        war = None
                    sw_waits = [head_ld[h], war]
                    if h == 0 and sw == 1 and k in (2, 3):
                        sw_waits.append(tok_p3_dve)
                    JJ = QT // H
                    t_sc = A("pe", lambda E, b=beta, k=k, sw=sw, JJ=JJ: E.matmul(
                        scb_(k)[:, 0:QT], kth(b, k * 128, (k + 1) * 128),
                        xs[b][:, 0:DC * 128].rearrange(
                            "p (qc r) -> p r qc", r=128)[:, (sw - 1) * JJ: sw * JJ, :],
                        start=True, stop=True),
                        waits=sw_waits, inc=("pe", 1))
                    ew = [t_sc, tok_cst]
                    km3 = k - 3
                    if km3 >= 0:
                        ew.append(pool_toks[(h, sw, km3)])
                    elif sw > 1:
                        ew.append(pool_toks[(h, sw - 1, (sw - 1) * NKS + km3)])
                    elif h > 0:
                        ew.append(pool_toks[(h - 1, NSW, NSW * NKS + km3)])
                    exp_toks[(h, sw, k)] = A(
                        "act", lambda E, k=k: E.activation(
                            out=pt[k % 3][:, 0:QT], in_=scb_(k)[:, 0:QT],
                            func=AF.Exp, bias=cst[:], scale=SCALE),
                        waits=ew, inc=("act", 1), chain=True)
                    if k >= nk - NKS:
                        j = k - (nk - NKS)
                        mask_toks[(h, sw, k)] = A(
                            "dve", lambda E, k=k, j=j: E.tensor_tensor(
                                pt[k % 3][:, 0:QT], pt[k % 3][:, 0:QT],
                                mask_j(j), op=ALU.mult),
                            waits=[exp_toks[(h, sw, k)], tok_masks],
                            inc=("dve", 1), chain=True)
                        ptok = mask_toks[(h, sw, k)]
                    else:
                        ptok = exp_toks[(h, sw, k)]
                    if k % 2 == 0:
                        pool_toks[(h, sw, k)] = A(
                            "gq", lambda E, k=k: E.tensor_tensor(
                                s_acc(0, QT), s_acc(0, QT), pt[k % 3][:, 0:QT],
                                op=ALU.add),
                            waits=[ptok], inc=("pool", 1), chain=True)
                    else:
                        pool_toks[(h, sw, k)] = A(
                            "dve", lambda E, k=k: E.tensor_tensor(
                                s_acc2(0, QT), s_acc2(0, QT), pt[k % 3][:, 0:QT],
                                op=ALU.add),
                            waits=[ptok], inc=("dve", 1), chain=True)
                    pvw = [ptok]
                    if k == 0:
                        if sw > 1:
                            pvw.append(sweep_ev[(h, sw - 1)])
                        elif h > 0:
                            pvw.append(sweep_ev[(h - 1, NSW)])
                        else:
                            pvw.append(tok_p3_dve)

                    def mk_ot(b=beta, k=k, nk=nk):
                        def f(E):
                            return E.matmul(pf[0][:, 0:QT], vh128(b, k),
                                            pt[k % 3][:, 0:QT],
                                            start=(k == 0), stop=(k == nk - 1))
                        return f
                    outq.append((mk_ot(), pvw))
                    while len(outq) > 2:
                        A("pe", *outq.pop(0))
                tpv = None
                for i2, (fn2, w2_) in enumerate(outq):
                    tpv = A("pe", fn2, waits=w2_,
                            inc=("pe", 1) if i2 == len(outq) - 1 else None)
                head_pe[h] = tpv
                sweep_pe = tpv
                A("dve", lambda E: E.tensor_tensor(
                    s_acc2(0, QT), s_acc2(0, QT), s_acc(0, QT), op=ALU.add),
                    waits=[pool_toks[(h, sw, nk - 1)],
                           pool_toks[(h, sw, nk - 2)], sweep_pe],
                    inc=("dve", 1), chain=True)
                t_cast = A("dve", lambda E, nk=nk: E.tensor_copy(
                    pt[nk % 3][:, 0:QT], s_acc2(0, QT)),
                    inc=("dve", 1), chain=True)
                lmm_tok[(h, sw)] = A("pe", lambda E, nk=nk: E.matmul(
                    pf[1][0:1, 0:QT], onc[:], pt[nk % 3][:, 0:QT],
                    start=True, stop=True),
                    waits=[t_cast, sweep_ev.get((h, sw - 1)) or
                           sweep_ev.get((h - 1, NSW)) or tok_p3_dve],
                    inc=("pe", 1))
                t_rl = A("dve", lambda E: E.reciprocal(s_acc(0, QT)[0:1, :],
                                                       pf[1][0:1, 0:QT]),
                         waits=[lmm_tok[(h, sw)]], inc=("dve", 1), chain=True)
                t_bc = A("pe", lambda E: E.matmul(pf[1][:, 0:QT], onf[:],
                                                  s_acc(0, QT)[0:1, :],
                                                  start=True, stop=True),
                         waits=[t_rl], inc=("pe", 1))
                t_cp = A("dve", lambda E: E.tensor_copy(s_acc(0, QT), pf[1][:, 0:QT]),
                         waits=[t_bc], inc=("dve", 1), chain=True)
                r0 = (sw - 1) * QT
                A("dve", lambda E, r0=r0, h=h: E.tensor_tensor(
                    acT(h, r0, r0 + QT), pf[0][:, 0:QT], s_acc(0, QT),
                    op=ALU.mult),
                    waits=[t_cp, sweep_pe], inc=("dve", 1), chain=True)
                t_ev = A("dve", lambda E, r0=r0, h=h: E.tensor_scalar(
                    acT(h, r0, r0 + QT), acT(h, r0, r0 + QT), bvs[:, h:h + 1],
                    None, op0=ALU.add),
                    inc=("dve", 1), chain=True)
                sweep_ev[(h, sw)] = t_ev
            head_ev[h] = sweep_ev[(h, NSW)]
        tok_p4_dve = S_.tok("dve")
        tok_p4_pe = S_.tok("pe")
        tok_p4_act = S_.tok("act")

        tok_wo = dma("gq", "w", pw[:, 0:DC * D].rearrange("p (dc n) -> p dc n", n=D),
                     wo.rearrange("(dc p) n -> p dc n", p=128),
                     waits=[tok_p4_pe])

        tok_bo = dma("sy", "bo", bo_b(0, D), bo_i,
                     waits=[tok_p4_dve, tok_p4_act])

        e1t = {}
        e2t = {}
        e3t = {}
        stt = {}
        xlt = {}
        for rt in range(ORT):
            for cc in range(NVC):
                n = rt * NVC + cc
                xw = [tok_bo]
                if n >= 2:
                    xw.append(e3t[n - 2])
                xlt[n] = dma("sy", f"px{n % 2}", p6x(n % 2),
                             xo[rt * 128:(rt + 1) * 128, cc * 512:(cc + 1) * 512],
                             waits=xw)
                mw = [tok_p3_dve, tok_vflat, tok_wo, tok_p4_dve]
                if n >= 2:
                    mw += [e1t[n - 2], e2t[n - 2]]

                def mk_f(rt=rt, cc=cc, n=n):
                    def f(E):
                        for jc in range(JC):
                            ins = E.matmul(pf[n % 2][:, 0:512],
                                           gut(jc, rt * 128, (rt + 1) * 128),
                                           vflat(jc, cc * 512, (cc + 1) * 512),
                                           start=(jc == 0), stop=(jc == JC - 1))
                        return ins
                    return f
                t_f = A("pe", mk_f(), waits=mw, inc=("pe", 1))

                def mk_o(rt=rt, cc=cc, n=n):
                    def f(E):
                        for fc in range(DC):
                            ins = E.matmul(pf[2 + n % 2][:, 0:512],
                                           acT(fc, rt * 128, (rt + 1) * 128),
                                           wchunk(fc, cc * 512, (cc + 1) * 512),
                                           start=(fc == 0), stop=(fc == DC - 1))
                        return ins
                    return f
                t_o = A("pe", mk_o(), inc=("pe", 1))

                w1_ = [t_f, tok_p3_dve]
                if n >= 3:
                    w1_.append(stt[n - 3])
                e1t[n] = A("act", lambda E, rt=rt, n=n: E.activation(
                    out=p6t(n % 3), in_=pf[n % 2][:, 0:512], func=AF.Copy,
                    scale=gps[:, rt:rt + 1]), waits=w1_, inc=("act", 1),
                    chain=True)
                e2t[n] = A("dve", lambda E, n=n: E.tensor_tensor(
                    p6t(n % 3), p6t(n % 3), pf[2 + n % 2][:, 0:512], op=ALU.add),
                    waits=[e1t[n], t_o], inc=("dve", 1), chain=True)
                A("dve", lambda E, n=n: E.tensor_tensor(
                    p6t(n % 3), p6t(n % 3), p6x(n % 2), op=ALU.add),
                  waits=[xlt[n]], inc=("dve", 1), chain=True)
                e3t[n] = A("dve", lambda E, n=n, cc=cc: E.tensor_tensor(
                    p6t(n % 3), p6t(n % 3), bo_b(cc * 512, (cc + 1) * 512),
                    op=ALU.add), waits=[tok_bo], inc=("dve", 1), chain=True)
                stt[n] = dma("sy", f"o{n % 3}", out[rt * 128:(rt + 1) * 128, cc * 512:(cc + 1) * 512],
                             p6t(n % 3), waits=[e3t[n]])

        final = dict(S_.n)
        S_.add("gq", None, waits=[(s, v) for s, v in final.items()])

        semnames = set(S_.n.keys())
        for ename in S_.ENG:
            for fn, waits, inc in S_.ops[ename]:
                for sem, val in waits:
                    semnames.add(sem)
        sems = {name: e(nc.semaphore("s_" + name)) for name in sorted(semnames)}
        block = e(nc.Block())

        def emit_wm(engobj, ename):
            water = {}
            for fn, waits, inc in S_.ops[ename]:
                for sem, val in waits:
                    if water.get(sem, 0) >= val:
                        continue
                    engobj.wait_ge(sems[sem], val)
                    water[sem] = val
                if fn is None:
                    continue
                ins = fn(engobj)
                if inc is not None:
                    ins.then_inc(sems[inc[0]], inc[1])

        @block.sync
        def _(E):
            emit_wm(E, "sy")

        @block.gpsimd
        def _(E):
            emit_wm(E, "gq")

        @block.tensor
        def _(E):
            emit_wm(E, "pe")

        @block.vector
        def _(E):
            emit_wm(E, "dve")

        @block.scalar
        def _(E):
            emit_wm(E, "act")

    return nc


def host_inputs(cfg, core, ins):
    D, T = cfg["D"], cfg["T"]
    OWN, QT, NSW, NKS, DC, JC, S = (cfg["OWN"], cfg["QT"], cfg["NSW"],
                                    cfg["NKS"], cfg["DC"], cfg["JC"], cfg["S"])
    b, g = core // 4, core % 4
    x = ins["L0_attn_x"]
    xb = np.ascontiguousarray(x[b], dtype=np.float32)
    blocks = [4 * s + g for s in range(NSW)]
    xo = np.ascontiguousarray(
        np.concatenate([x[b][blk * QT:(blk + 1) * QT] for blk in blocks], axis=0),
        dtype=np.float32)
    H = cfg["H"]
    TPH, QPH = cfg["T"] // H, QT // H
    qrows = []
    for h in range(H):
        for s in range(NSW):
            base = h * TPH + (QT * (4 * s + g)) // H
            qrows.extend(range(base, base + QPH))
    xq = np.ascontiguousarray(x[b][np.array(qrows)], dtype=np.float32)
    masks = np.zeros((128, NKS * QT), dtype=ml_dtypes.bfloat16)
    kl = np.arange(128)[:, None]
    q = np.arange(QT)[None, :]
    for j in range(NKS):
        masks[:, j * QT:(j + 1) * QT] = (j * 128 + kl <= g * QT + q).astype(
            ml_dtypes.bfloat16)
    e_mat = np.zeros((S, 128 * JC), dtype=ml_dtypes.bfloat16)
    for jj in range(128 * JC):
        e_mat[jj // 16, jj] = 1
    U = np.asarray(ins["L0_ffn_U"], np.float32)
    m = {
        "xb": xb,
        "xo": xo,
        "xq": xq,
        "wq": ins["L0_attn_Wq"], "wk": ins["L0_attn_Wk"],
        "wv": ins["L0_attn_Wv"], "wo": ins["L0_attn_Wo"],
        "bq": np.ascontiguousarray(np.asarray(ins["L0_attn_bq"], np.float32).reshape(DC, 128).T),
        "bk": np.ascontiguousarray(np.asarray(ins["L0_attn_bk"], np.float32).reshape(DC, 128).T),
        "bv": np.ascontiguousarray(np.asarray(ins["L0_attn_bv"], np.float32).reshape(DC, 128).T),
        "bo_bc": np.ascontiguousarray(np.broadcast_to(
            np.asarray(ins["L0_attn_bo"], np.float32).reshape(1, D), (128, D))),
        "wr_t": np.ascontiguousarray(np.asarray(ins["L0_ffn_Wr"], np.float32).T),
        "u_flat": np.ascontiguousarray(U.transpose(1, 0, 2).reshape(D, -1)),
        "v_flat": np.ascontiguousarray(np.asarray(ins["L0_ffn_V"], np.float32).reshape(-1, D)),
        "attr_t": np.ascontiguousarray(np.asarray(ins["L0_ffn_attr"], np.float32).T),
        "wa_t": np.ascontiguousarray(np.asarray(ins["L0_ffn_Wa"], np.float32).reshape(1, S).T),
        "masks": masks,
        "ident": np.eye(128, dtype=ml_dtypes.bfloat16),
        "ones_f": np.ones((1, 128), np.float32),
        "ones_c": np.ones((128, 1), dtype=ml_dtypes.bfloat16),
        "e_mat": e_mat,
    }
    return {k: np.ascontiguousarray(np.asarray(v)) for k, v in m.items()}


def assemble(cfg, results):
    D, T, QT, NSW = cfg["D"], cfg["T"], cfg["QT"], cfg["NSW"]
    full = np.zeros((2, T, D), np.float32)
    for core in range(8):
        b, g = core // 4, core % 4
        o = np.asarray(results[core]["out"], np.float32)
        for s in range(NSW):
            blk = 4 * s + g
            full[b, blk * QT:(blk + 1) * QT] = o[s * QT:(s + 1) * QT]
    return full


def kernel(**inputs):
    cfg = mkcfg(FULL)
    nc = build_nc(cfg)
    in_maps = [host_inputs(cfg, c, inputs) for c in range(8)]
    res = run_bass_kernel_spmd(nc, in_maps, core_ids=list(range(8)))
    return assemble(cfg, res.results)


# version 12
# speedup vs baseline: 1.2010x; 1.1862x over previous
import math
from contextlib import ExitStack

import numpy as np
import ml_dtypes
import concourse.bass as bass
import concourse.mybir as mybir
from concourse.bass_utils import run_bass_kernel_spmd

F32 = mybir.dt.float32
BF16 = mybir.dt.bfloat16
AF = mybir.ActivationFunctionType
ALU = mybir.AluOpType
AX = mybir.AxisListType

FULL = dict(D=2048, T=8192, H=16)
TINY = dict(D=512, T=2048, H=4)

C_SHIFT = 12.0


def mkcfg(base):
    c = dict(base)
    D, T, H = c["D"], c["T"], c["H"]
    c["DH"] = 128
    assert H * c["DH"] == D
    c["DC"] = D // 128
    c["OWN"] = T // 4
    assert c["OWN"] == D
    c["QT"] = T // 16
    assert c["QT"] % 128 == 0
    c["NQS"] = c["QT"] // 128
    c["NSW"] = 4
    c["NKS"] = T // 512
    c["CH"] = c["QT"]
    c["NCH"] = T // c["CH"]
    c["NSUB"] = c["CH"] // 128
    c["S"], c["R"] = 32, 16
    c["J"] = c["S"] * c["R"]
    c["JC"] = c["J"] // 128
    c["GR"] = min(512, c["OWN"])
    c["NG"] = c["OWN"] // c["GR"]
    c["ORT"] = c["OWN"] // 128
    assert c["ORT"] == c["DC"]
    c["SCALE"] = 1.0 / math.sqrt(c["DH"])
    return c


class Sched:
    ENG = ("sy", "gq", "pe", "act", "dve")

    def __init__(self):
        self.ops = {e: [] for e in self.ENG}
        from collections import defaultdict
        self.n = defaultdict(int)
        for s in ("pe", "act", "dve"):
            self.n[s] = 0
        self.chain_last = {"act": 0, "dve": 0, "gq": 0}
        self.chain_sem = {"act": "act", "dve": "dve", "gq": "pool"}

    def add(self, eng, fn, waits=(), inc=None, chain=False):
        w = [x for x in waits if x is not None and x[1] > 0]
        if chain and eng in self.chain_last:
            if self.chain_last[eng] > 0:
                w.append((self.chain_sem[eng], self.chain_last[eng]))
        tok = None
        if inc is not None:
            sem, k = inc
            self.n[sem] += k
            tok = (sem, self.n[sem])
            if eng in self.chain_last and sem == self.chain_sem.get(eng):
                self.chain_last[eng] = self.n[sem]
        self.ops[eng].append((fn, w, inc))
        return tok

    def tok(self, sem):
        return (sem, self.n[sem])

    def barrier(self):
        snap = dict(self.n)
        for e in self.ENG:
            self.add(e, None, waits=[(s, v) for s, v in snap.items()])
        return snap


def build_nc(cfg):
    D, T, H, DH = cfg["D"], cfg["T"], cfg["H"], cfg["DH"]
    DC, OWN, QT, NQS = cfg["DC"], cfg["OWN"], cfg["QT"], cfg["NQS"]
    NSW, NKS, CH, NCH, NSUB = cfg["NSW"], cfg["NKS"], cfg["CH"], cfg["NCH"], cfg["NSUB"]
    S, J, JC = cfg["S"], cfg["J"], cfg["JC"]
    GR, NG, ORT, SCALE = cfg["GR"], cfg["NG"], cfg["ORT"], cfg["SCALE"]
    NVC = D // 512
    NSUBK = T // 128

    nc = bass.Bass("TRN2", target_bir_lowering=False)

    xb = nc.dram_tensor("xb", [T, D], F32, kind="ExternalInput").ap()
    xo = nc.dram_tensor("xo", [OWN, D], F32, kind="ExternalInput").ap()
    xq = nc.dram_tensor("xq", [OWN, D], F32, kind="ExternalInput").ap()
    wq = nc.dram_tensor("wq", [D, D], F32, kind="ExternalInput").ap()
    wk = nc.dram_tensor("wk", [D, D], F32, kind="ExternalInput").ap()
    wv = nc.dram_tensor("wv", [D, D], F32, kind="ExternalInput").ap()
    wo = nc.dram_tensor("wo", [D, D], F32, kind="ExternalInput").ap()
    bq_i = nc.dram_tensor("bq", [128, DC], F32, kind="ExternalInput").ap()
    bk_i = nc.dram_tensor("bk", [128, DC], F32, kind="ExternalInput").ap()
    bv_i = nc.dram_tensor("bv", [128, DC], F32, kind="ExternalInput").ap()
    bo_i = nc.dram_tensor("bo_bc", [128, D], F32, kind="ExternalInput").ap()
    wr_i = nc.dram_tensor("wr_t", [D, S], F32, kind="ExternalInput").ap()
    u_fl = nc.dram_tensor("u_flat", [D, J], F32, kind="ExternalInput").ap()
    v_fl = nc.dram_tensor("v_flat", [J, D], F32, kind="ExternalInput").ap()
    at_i = nc.dram_tensor("attr_t", [S, S], F32, kind="ExternalInput").ap()
    wa_i = nc.dram_tensor("wa_t", [S, 1], F32, kind="ExternalInput").ap()
    mk_i = nc.dram_tensor("masks", [128, NKS * QT], BF16, kind="ExternalInput").ap()
    id_i = nc.dram_tensor("ident", [128, 128], BF16, kind="ExternalInput").ap()
    of_i = nc.dram_tensor("ones_f", [1, 128], F32, kind="ExternalInput").ap()
    oc_i = nc.dram_tensor("ones_c", [128, 1], BF16, kind="ExternalInput").ap()
    em_i = nc.dram_tensor("e_mat", [S, 128 * JC], BF16, kind="ExternalInput").ap()

    out = nc.dram_tensor("out", [OWN, D], F32, kind="ExternalOutput").ap()

    kt_scr = nc.dram_tensor("kt_scr", [D, T], BF16, kind="Internal").ap()
    v_scr = nc.dram_tensor("v_scr", [T, D], BF16, kind="Internal").ap()
    qt_scr = nc.dram_tensor("qt_scr", [D, OWN], BF16, kind="Internal").ap()
    xnt_scr = nc.dram_tensor("xnt_scr", [D, T], BF16, kind="Internal").ap()

    with ExitStack() as ctx:
        e = ctx.enter_context

        pa = e(nc.sbuf_tensor("pa", [128, DC * OWN], BF16))
        PW_COLS = max(DC * D, 2 * T + 2 * 128 * NSUBK)
        pw = e(nc.sbuf_tensor("pw", [128, PW_COLS], BF16))
        pm = e(nc.sbuf_tensor("pm", [128, max(DC * J, NKS * QT, 2 * (D + 1024 + 3 * 512)) + 4 * QT], BF16))
        pv = e(nc.sbuf_tensor("pv", [128, max(JC * D, JC * GR + OWN + 2 * D)], BF16))
        pg = e(nc.sbuf_tensor("pg", [128, JC * OWN], BF16))
        xs = [e(nc.sbuf_tensor(f"xs{i}", [128, max(D, OWN)], BF16)) for i in range(4)]
        pt = [e(nc.sbuf_tensor(f"pt{i}", [128, QT], BF16)) for i in range(3)]
        wrk = e(nc.sbuf_tensor("wrk", [128, 64], F32))
        scw = e(nc.sbuf_tensor("scw", [128, 8 * S], F32))
        gps = e(nc.sbuf_tensor("gps", [128, ORT], F32))
        bqs = e(nc.sbuf_tensor("bqs", [128, DC], F32))
        bks = e(nc.sbuf_tensor("bks", [128, DC], F32))
        bvs = e(nc.sbuf_tensor("bvs", [128, DC], F32))
        wrs = e(nc.sbuf_tensor("wrs", [128, DC * S], BF16))
        ems = e(nc.sbuf_tensor("ems", [S, 128 * JC], BF16))
        ids = e(nc.sbuf_tensor("ids", [128, 128], BF16))
        onf = e(nc.sbuf_tensor("onf", [1, 128], F32))
        onc = e(nc.sbuf_tensor("onc", [128, 1], BF16))
        ats = e(nc.sbuf_tensor("ats", [S, S], F32))
        was = e(nc.sbuf_tensor("was", [S, 1], F32))
        awb = e(nc.sbuf_tensor("awb", [128, S], F32))
        aws = e(nc.sbuf_tensor("aws", [1, S], F32))
        cst = e(nc.sbuf_tensor("cst", [128, 1], F32))

        pf = [e(nc.psum_tensor(f"pf{i}", [128, 512], F32)) for i in range(4)]
        psc = [e(nc.psum_tensor(f"psc{i}", [128, 512], F32)) for i in range(2)]
        pb = [e(nc.psum_tensor(f"pb{i}", [128, 1024], BF16)) for i in range(2)]

        block_holder = []

        S_ = Sched()

        def xnt_own(dc, r0, r1):
            return pa[:, dc * OWN + r0: dc * OWN + r1]

        XTC_OFF = [0, 2 * DC * CH]
        STG_OFF = [DC * CH, 3 * DC * CH]

        def xtc(buf, dc, r0, r1):
            o = XTC_OFF[buf]
            return pa[:, o + dc * CH + r0: o + dc * CH + r1]

        def stg(buf, kc, c0, c1):
            o = STG_OFF[buf]
            return pa[:, o + kc * CH + c0: o + kc * CH + c1]

        def stgv(buf, rs, c0, c1):
            o = STG_OFF[buf]
            return pa[:, o + rs * D + c0: o + rs * D + c1]

        def ac(rt, f0, f1):
            return pa[:, rt * D + f0: rt * D + f1]

        def acT(fc, r0, r1):
            return pa[:, fc * OWN + r0: fc * OWN + r1]

        def wchunk(dc, n0, n1):
            return pw[:, dc * D + n0: dc * D + n1]

        def kth(b, k0, k1):
            return pw[:, b * T + k0: b * T + k1]

        VH0 = 2 * T

        def vh128(b, sub):
            o = VH0 + b * (128 * NSUBK) + sub * 128
            return pw[:, o: o + 128]

        def vh_view(b):
            o = VH0 + b * (128 * NSUBK)
            return pw[:, o: o + NSUBK * 128].rearrange("p (s c) -> p s c", c=128)

        def uchunk(dc, j0, j1):
            return pm[:, dc * J + j0: dc * J + j1]

        def mask_j(j):
            return pm[:, j * QT:(j + 1) * QT]

        pmf = pm.bitcast(F32)

        def bo_b(c0, c1):
            return pmf[:, c0:c1]

        def p6x(b):
            return pmf[:, D + b * 512: D + (b + 1) * 512]

        def p6t(i):
            return pmf[:, D + 1024 + i * 512: D + 1024 + (i + 1) * 512]

        def vflat(jc, n0, n1):
            return pv[:, jc * D + n0: jc * D + n1]

        def gexp(jc):
            return pv[:, jc * GR:(jc + 1) * GR]

        GT0 = JC * GR

        def gT(r0, r1):
            return pv[0:S, GT0 + r0: GT0 + r1]

        def gut(jc, r0, r1):
            return pg[:, jc * OWN + r0: jc * OWN + r1]

        PVJ = JC * GR + OWN
        sqj = pv[:, PVJ: PVJ + D]
        cpj = pv[:, PVJ + D: PVJ + 2 * D]
        qst = pg[:, 0: 2 * GR]
        sbf = xs[3]
        rlb = xs[2].bitcast(F32)

        def w1(i):
            return wrk[:, i:i + 1]

        A = S_.add

        def dma(eng, sem, outap, inap, waits=()):
            return A(eng, lambda E, o=outap, i=inap: E.dma_start(out=o, in_=i),
                     waits=waits, inc=(sem, 16))

        tok_consts = None
        for o, i in [(ids[:], id_i), (onf[:], of_i), (onc[:], oc_i),
                     (bqs[:], bq_i), (bks[:], bk_i),
                     (bvs[:], bv_i), (ats[:], at_i), (was[:], wa_i),
                     (ems[:], em_i)]:
            tok_consts = dma("sy", "cst", o, i)
        tok_wrs = dma("gq", "wrs", wrs.rearrange("p (dc s) -> p dc s", s=S),
                      wr_i.rearrange("(dc p) s -> p dc s", p=128))

        tok_wq = dma("gq", "w", pw[:, 0:DC * D].rearrange("p (dc n) -> p dc n", n=D),
                     wq.rearrange("(dc p) n -> p dc n", p=128))
        tok_u = dma("gq", "u", pm[:, 0:DC * J].rearrange("p (dc j) -> p dc j", j=J),
                    u_fl.rearrange("(dc p) j -> p dc j", p=128))
        tok_cst = A("dve", lambda E: E.memset(cst[:], -C_SHIFT),
                    inc=("dve", 1), chain=True)

        ln_i = [0]
        ln_trs = {}
        ln_norm = {}

        def ln_stats(src_rows, extra_waits=()):
            i = ln_i[0]
            ln_i[0] += 1
            beta = i % 4
            b2 = i % 2
            xbuf = xs[beta]
            t_x = dma("gq", f"x{beta}", xbuf[:, 0:D], src_rows,
                      waits=[ln_trs.get(i - 4)] + list(extra_waits))
            t_sq = A("act", lambda E, xb_=xbuf, b=b2: E.activation(
                out=sqj, in_=xb_[:, 0:D], func=AF.Square,
                accum_out=wrk[:, 8 + b:9 + b]), waits=[t_x], inc=("act", 1), chain=True)
            t_sm = A("dve", lambda E, xb_=xbuf, b=b2: E.tensor_scalar(
                cpj, xb_[:, 0:D], 1.0, None, op0=ALU.mult, op1=ALU.add,
                accum_out=wrk[:, 10 + b:11 + b]), waits=[t_x], inc=("dve", 1), chain=True)
            A("dve", lambda E, b=b2: E.tensor_scalar(
                w1(12 + b), w1(10 + b), 1.0 / D, None, op0=ALU.mult),
              waits=[t_sm], inc=("dve", 1), chain=True)
            A("dve", lambda E, b=b2: E.tensor_scalar(
                w1(14 + b), w1(8 + b), 1.0 / D, None, op0=ALU.mult),
              waits=[t_sq], inc=("dve", 1), chain=True)
            A("dve", lambda E, b=b2: E.tensor_tensor(
                w1(16 + b), w1(12 + b), w1(12 + b), op=ALU.mult),
              inc=("dve", 1), chain=True)
            A("dve", lambda E, b=b2: E.tensor_tensor(
                w1(18 + b), w1(14 + b), w1(16 + b), op=ALU.subtract),
                inc=("dve", 1), chain=True)
            t_var = A("dve", lambda E, b=b2: E.tensor_scalar(
                w1(18 + b), w1(18 + b), 1e-5, None, op0=ALU.add),
                inc=("dve", 1), chain=True)
            A("act", lambda E, b=b2: E.activation(
                out=w1(20 + b), in_=w1(18 + b), func=AF.Ln),
              waits=[t_var], inc=("act", 1), chain=True)
            t_rstd = A("act", lambda E, b=b2: E.activation(
                out=w1(22 + b), in_=w1(20 + b), func=AF.Exp, scale=-0.5),
                inc=("act", 1), chain=True)
            ln_norm[i] = A("dve", lambda E, xb_=xbuf, b=b2: E.tensor_scalar(
                xb_[:, 0:D], xb_[:, 0:D], w1(12 + b), w1(22 + b),
                op0=ALU.subtract, op1=ALU.mult),
                waits=[t_rstd], inc=("dve", 1), chain=True)
            return i

        def ln_tr(i, dest_fn, dest_r0):
            beta = i % 4
            xbuf = xs[beta]
            t_norm = ln_norm[i]
            ev_toks = []
            tr_toks = []
            for dc in range(DC):
                war = ev_toks[dc - 2] if dc >= 2 else ln_tr.last_ev.get(dc % 2)
                t_tr = A("pe", lambda E, xb_=xbuf, dc=dc: E.transpose(
                    pb[dc % 2][:, 0:128], xb_[:, dc * 128:(dc + 1) * 128], ids[:]),
                    waits=[t_norm, war, tok_consts], inc=("pe", 1))
                tr_toks.append(t_tr)
                t_ev = A("dve", lambda E, dc=dc, r0=dest_r0, df=dest_fn: E.tensor_copy(
                    df(dc, r0, r0 + 128), pb[dc % 2][:, 0:128]),
                    waits=[t_tr], inc=("dve", 1), chain=True)
                ev_toks.append(t_ev)
            ln_tr.last_ev = {0: ev_toks[-2], 1: ev_toks[-1]}
            ln_trs[i] = tr_toks[-1]
            return ev_toks[-1]

        ln_tr.last_ev = {}

        def ln_subtile(src_rows, dest_fn, dest_r0, extra_waits=()):
            i = ln_stats(src_rows, extra_waits)
            ev = ln_tr(i, dest_fn, dest_r0)
            return ev, ln_trs[i]

        q_mm_last = None
        qev = {}
        qld = {}
        NCH0 = OWN // CH
        LNQ = DC // NSUB
        ev_of = {}
        st_ids = {}
        ids0 = [ln_stats(xq[sb * 128:(sb + 1) * 128, :]) for sb in range(NSUB)]
        ev_of[0] = max(ln_tr(i, lambda dc, a, bb: xtc(0, dc, a, bb),
                             sb * 128) for sb, i in enumerate(ids0))
        for c in range(NCH0):
            if c + 1 < NCH0:
                st_ids[c + 1] = [ln_stats(
                    xq[(c + 1) * CH + sb * 128:(c + 1) * CH + (sb + 1) * 128, :])
                    for sb in range(NSUB)]
            for qc in range(DC):
                n = c * DC + qc
                waits = [ev_of[c], tok_wq]
                if n >= 4:
                    waits.append(qev[n - 4])

                def mk_qmm(qc=qc, n=n, c=c):
                    def f(E):
                        for dc in range(DC):
                            ins = E.matmul(pf[n % 4][:, 0:CH],
                                           wchunk(dc, qc * 128, (qc + 1) * 128),
                                           xtc(c % 2, dc, 0, CH),
                                           start=(dc == 0), stop=(dc == DC - 1))
                        return ins
                    return f
                q_mm_last = A("pe", mk_qmm(), waits=waits, inc=("pe", 1))
                w2 = [q_mm_last]
                if n >= 2:
                    w2.append(qld[n - 2])
                qev[n] = A("act", lambda E, qc=qc, n=n: E.activation(
                    out=qst[:, (n % 2) * GR:(n % 2) * GR + CH],
                    in_=pf[n % 4][:, 0:CH], func=AF.Identity,
                    bias=bqs[:, qc:qc + 1]), waits=w2, inc=("act", 1), chain=True)
                qld[n] = dma(
                    "sy", f"q{n % 2}",
                    qt_scr.rearrange("(qc p) r -> p qc r", p=128)[:, qc, c * CH:(c + 1) * CH],
                    qst[:, (n % 2) * GR:(n % 2) * GR + CH],
                    waits=[qev[n]])
                if c + 1 < NCH0 and qc % LNQ == LNQ - 1:
                    sb = qc // LNQ
                    ev = ln_tr(st_ids[c + 1][sb],
                               lambda dc, a, bb, c=c: xtc((c + 1) % 2, dc, a, bb),
                               sb * 128)
                    if sb == NSUB - 1:
                        ev_of[c + 1] = ev
        tok_p2b_pe = S_.tok("pe")
        tok_p2b_q0 = S_.tok("q0")
        tok_p2b_q1 = S_.tok("q1")

        for i in range(OWN // 128):
            ev, tr = ln_subtile(xo[i * 128:(i + 1) * 128, :], xnt_own, i * 128)
        tok_p1 = S_.tok("dve")

        t1 = A("pe", lambda E: E.matmul(psc[0][0:1, 0:S], was[:], ats[:],
                                        start=True, stop=True),
               waits=[tok_consts, tok_p2b_pe], inc=("pe", 1))
        t2 = A("act", lambda E: E.activation(out=aws[:], in_=psc[0][0:1, 0:S],
                                             func=AF.Exp, scale=-1.0),
               waits=[t1], inc=("act", 1), chain=True)
        t3 = A("dve", lambda E: E.tensor_scalar(aws[:], aws[:], 1.0, None,
                                                op0=ALU.add),
               waits=[t2], inc=("dve", 1), chain=True)
        t4 = A("dve", lambda E: E.reciprocal(aws[:], aws[:]),
               inc=("dve", 1), chain=True)
        t5 = A("pe", lambda E: E.matmul(psc[1][:, 0:S], onf[:], aws[:],
                                        start=True, stop=True),
               waits=[t4], inc=("pe", 1))
        tok_awb = A("dve", lambda E: E.tensor_copy(awb[:], psc[1][:, 0:S]),
                    waits=[t5], inc=("dve", 1), chain=True)

        g_done = {}
        for rt in range(ORT):
            r0 = rt * 128

            def mk_sc(rt=rt, r0=r0):
                def f(E):
                    for dc in range(DC):
                        ins = E.matmul(psc[0][:, 0:S],
                                       xnt_own(dc, r0, r0 + 128),
                                       wrs[:, dc * S:(dc + 1) * S],
                                       start=(dc == 0), stop=(dc == DC - 1))
                    return ins
                return f
            waits = [tok_p1, tok_consts, tok_wrs, t2]
            if rt > 0:
                waits.append(g_done[rt - 1])
            t_sc = A("pe", mk_sc(), waits=waits, inc=("pe", 1))

            t_cp = A("dve", lambda E: E.tensor_copy(scw[:, 0:S], psc[0][:, 0:S]),
                     waits=[t_sc], inc=("dve", 1), chain=True)
            A("dve", lambda E: E.tensor_reduce(w1(24), scw[:, 0:S], axis=AX.X,
                                               op=ALU.max, negate=True),
              inc=("dve", 1), chain=True)
            A("dve", lambda E: E.tensor_scalar(w1(25), w1(24), -1.0, None,
                                               op0=ALU.mult),
              inc=("dve", 1), chain=True)
            A("dve", lambda E: E.tensor_scalar(scw[:, S:2 * S], scw[:, 0:S],
                                               w1(25), 1e9, op0=ALU.is_ge,
                                               op1=ALU.mult),
              inc=("dve", 1), chain=True)
            t_ein = A("dve", lambda E: E.tensor_tensor(
                scw[:, 2 * S:3 * S], scw[:, 0:S], scw[:, S:2 * S],
                op=ALU.subtract), inc=("dve", 1), chain=True)
            t_exp = A("act", lambda E: E.activation(
                out=scw[:, 3 * S:4 * S], in_=scw[:, 0:S], func=AF.Exp,
                bias=w1(24)), waits=[t_ein], inc=("act", 1), chain=True)
            A("dve", lambda E: E.tensor_reduce(w1(26), scw[:, 2 * S:3 * S],
                                               axis=AX.X, op=ALU.max),
              inc=("dve", 1), chain=True)
            A("dve", lambda E: E.tensor_scalar(scw[:, 4 * S:5 * S], scw[:, 0:S],
                                               w1(26), None, op0=ALU.is_ge),
              inc=("dve", 1), chain=True)
            A("dve", lambda E: E.tensor_reduce(w1(27), scw[:, 3 * S:4 * S],
                                               axis=AX.X, op=ALU.add),
              waits=[t_exp], inc=("dve", 1), chain=True)
            A("dve", lambda E: E.tensor_tensor(scw[:, 5 * S:6 * S],
                                               scw[:, 3 * S:4 * S],
                                               scw[:, 4 * S:5 * S], op=ALU.mult),
              inc=("dve", 1), chain=True)
            A("dve", lambda E: E.tensor_reduce(w1(28), scw[:, 5 * S:6 * S],
                                               axis=AX.X, op=ALU.add),
              inc=("dve", 1), chain=True)
            A("dve", lambda E: E.tensor_scalar(w1(29), w1(27), 1e-8, None,
                                               op0=ALU.mult),
              inc=("dve", 1), chain=True)
            A("dve", lambda E: E.tensor_tensor(w1(30), w1(28), w1(29), op=ALU.add),
              inc=("dve", 1), chain=True)
            A("dve", lambda E: E.reciprocal(w1(31), w1(30)),
              inc=("dve", 1), chain=True)
            A("dve", lambda E: E.tensor_scalar(scw[:, 6 * S:7 * S],
                                               scw[:, 5 * S:6 * S], w1(31), None,
                                               op0=ALU.mult),
              inc=("dve", 1), chain=True)
            A("dve", lambda E: E.tensor_tensor(scw[:, 7 * S:8 * S],
                                               scw[:, 6 * S:7 * S], awb[:],
                                               op=ALU.mult),
              waits=[tok_awb], inc=("dve", 1), chain=True)
            A("dve", lambda E: E.tensor_reduce(w1(32), scw[:, 7 * S:8 * S],
                                               axis=AX.X, op=ALU.add),
              inc=("dve", 1), chain=True)
            A("dve", lambda E, rt=rt: E.tensor_scalar(
                gps[:, rt:rt + 1], w1(32), 0.2, 0.9, op0=ALU.mult, op1=ALU.add),
              inc=("dve", 1), chain=True)
            t_gbf = A("dve", lambda E: E.tensor_copy(pt[2][:, 0:S],
                                                     scw[:, 6 * S:7 * S]),
                      inc=("dve", 1), chain=True)
            t_gtr = A("pe", lambda E: E.transpose(pb[0][0:S, 0:128], pt[2][:, 0:S],
                                                  ids[:]),
                      waits=[t_gbf, (g_done.get(rt - 1) if rt else None)],
                      inc=("pe", 1))
            g_done[rt] = A("dve", lambda E, r0=r0: E.tensor_copy(
                gT(r0, r0 + 128), pb[0][0:S, 0:128]),
                waits=[t_gtr], inc=("dve", 1), chain=True)

        gut_toks = {}
        hu_last = None
        for gi in range(NG):
            r0 = gi * GR
            gx = {}
            for jc in range(JC):
                gxw = [g_done[ORT - 1], tok_consts, tok_awb]
                if jc > 0:
                    gxw.append(gx[jc - 1])
                elif gi > 0:
                    gxw.append(gx_prev_last)
                t_gx = A("pe", lambda E, jc=jc, r0=r0: E.matmul(
                    psc[1][:, 0:GR], ems[:, jc * 128:(jc + 1) * 128],
                    gT(r0, r0 + GR), start=True, stop=True),
                    waits=gxw, inc=("pe", 1))
                gx[jc] = A("dve", lambda E, jc=jc: E.tensor_copy(
                    gexp(jc), psc[1][:, 0:GR]),
                    waits=[t_gx], inc=("dve", 1), chain=True)
            gx_prev_last = gx[JC - 1]
            for jc in range(JC):
                n = gi * JC + jc
                waits = [tok_u, tok_p1]
                if n >= 4:
                    waits.append(gut_toks[n - 4])

                def mk_hu(jc=jc, r0=r0, n=n):
                    def f(E):
                        for dc in range(DC):
                            ins = E.matmul(pf[n % 4][:, 0:GR],
                                           uchunk(dc, jc * 128, (jc + 1) * 128),
                                           xnt_own(dc, r0, r0 + GR),
                                           start=(dc == 0), stop=(dc == DC - 1))
                        return ins
                    return f
                hu_last = A("pe", mk_hu(), waits=waits, inc=("pe", 1))
                gut_toks[n] = A("dve", lambda E, jc=jc, r0=r0, n=n: E.tensor_tensor(
                    gut(jc, r0, r0 + GR), pf[n % 4][:, 0:GR], gexp(jc),
                    op=ALU.mult),
                    waits=[hu_last, gx[jc], tok_p2b_q0, tok_p2b_q1],
                    inc=("dve", 1), chain=True)
        tok_p3_dve = S_.tok("dve")
        tok_p3_pe = S_.tok("pe")

        tok_wk = dma("gq", "w", pw[:, 0:DC * D].rearrange("p (dc n) -> p dc n", n=D),
                     wk.rearrange("(dc p) n -> p dc n", p=128),
                     waits=[tok_p2b_pe])

        kev = {}
        kmm = {}
        kdma = {}
        kev_of = {}
        kst_ids = {}
        ids0 = [ln_stats(xb[sb * 128:(sb + 1) * 128, :]) for sb in range(NSUB)]
        kev_of[0] = max(ln_tr(i, lambda dc, a, bb: xtc(0, dc, a, bb),
                              sb * 128) for sb, i in enumerate(ids0))
        for c in range(NCH):
            if c + 1 < NCH:
                kst_ids[c + 1] = [ln_stats(
                    xb[(c + 1) * CH + sb * 128:(c + 1) * CH + (sb + 1) * 128, :],
                    extra_waits=[kdma.get(c - 1)]) for sb in range(NSUB)]
            for kc in range(DC):
                waits = [kev_of[c], tok_wk]
                gn = c * DC + kc
                if gn >= 4:
                    waits.append(kev[gn - 4])

                def mk_kmm(c=c, kc=kc):
                    def f(E, _kk=None):
                        for dc in range(DC):
                            ins = E.matmul(pf[kc % 4][:, 0:CH],
                                           wchunk(dc, kc * 128, (kc + 1) * 128),
                                           xtc(c % 2, dc, 0, CH),
                                           start=(dc == 0), stop=(dc == DC - 1))
                        return ins
                    return f
                kmm[gn] = A("pe", mk_kmm(), waits=waits, inc=("pe", 1))
                w2 = [kmm[gn]]
                if c > 1:
                    w2.append(kdma[c - 2])
                kev[gn] = A("act", lambda E, c=c, kc=kc: E.activation(
                    out=stg(c % 2, kc, 0, CH), in_=pf[kc % 4][:, 0:CH],
                    func=AF.Identity,
                    bias=bks[:, kc:kc + 1]), waits=w2, inc=("act", 1), chain=True)
                if c + 1 < NCH and kc % LNQ == LNQ - 1:
                    sb = kc // LNQ
                    ev = ln_tr(kst_ids[c + 1][sb],
                               lambda dc, a, bb, c=c: xtc((c + 1) % 2, dc, a, bb),
                               sb * 128)
                    if sb == NSUB - 1:
                        kev_of[c + 1] = ev
            _kd = dma(
                "sy", f"kx{c % 2}",
                kt_scr.rearrange("(kc p) r -> p kc r", p=128)[:, :, c * CH:(c + 1) * CH],
                pa[:, STG_OFF[c % 2]: STG_OFF[c % 2] + DC * CH
                   ].rearrange("p (kc r) -> p kc r", r=CH),
                waits=[kev[c * DC + DC - 1]])
            t_xd = dma(
                "sy", f"kx{c % 2}",
                xnt_scr.rearrange("(dc p) r -> p dc r", p=128)[:, :, c * CH:(c + 1) * CH],
                pa[:, XTC_OFF[c % 2]: XTC_OFF[c % 2] + DC * CH
                   ].rearrange("p (dc r) -> p dc r", r=CH),
                waits=[kev_of[c]])
            kdma[c] = t_xd
        tok_p2a_ld0 = S_.tok("kx0")
        tok_p2a_ld1 = S_.tok("kx1")
        tok_p2a_pe = S_.tok("pe")

        tok_wv = dma("gq", "w", pw[:, 0:DC * D].rearrange("p (dc n) -> p dc n", n=D),
                     wv.rearrange("(dc p) n -> p dc n", p=128),
                     waits=[tok_p2b_pe, tok_p3_pe, tok_p2a_pe])
        tok_masks = dma("gq", "mk", pm[:, 0:NKS * QT], mk_i, waits=[tok_p3_pe])

        vev = {}
        vdma = {}
        vmm = {}
        vxl = {}
        for c in range(NCH):
            waits = [tok_p2a_ld0, tok_p2a_ld1, tok_p3_dve, tok_p2b_q0, tok_p2b_q1, tok_p2a_pe]
            if c > 1:
                waits.append(vmm[c - 2])
            vxl[c] = dma(
                "sy", f"vx{c % 2}",
                pa[:, XTC_OFF[c % 2]:XTC_OFF[c % 2] + DC * CH
                   ].rearrange("p (dc r) -> p dc r", r=CH),
                xnt_scr.rearrange("(dc p) r -> p dc r", p=128)[:, :, c * CH:(c + 1) * CH],
                waits=waits)
            for rs in range(NSUB):
                for vc in range(NVC):
                    n = rs * NVC + vc
                    gn = c * NSUB * NVC + n
                    waits = [vxl[c], tok_wv]
                    if gn >= 4:
                        waits.append(vev[gn - 4])

                    def mk_vmm(rs=rs, vc=vc, n=n, c=c):
                        def f(E):
                            for dc in range(DC):
                                ins = E.matmul(pf[n % 4][:, 0:512],
                                               xtc(c % 2, dc, rs * 128, (rs + 1) * 128),
                                               wchunk(dc, vc * 512, (vc + 1) * 512),
                                               start=(dc == 0), stop=(dc == DC - 1))
                            return ins
                        return f
                    vmm[c] = A("pe", mk_vmm(), waits=waits, inc=("pe", 1))
                    w2 = [vmm[c]]
                    if c > 1:
                        w2.append(vdma[c - 2])
                    vev[gn] = A("dve", lambda E, rs=rs, vc=vc, n=n, c=c: E.tensor_copy(
                        stgv(c % 2, rs, vc * 512, (vc + 1) * 512), pf[n % 4][:, 0:512]),
                        waits=w2, inc=("dve", 1), chain=True)
            vdma[c] = dma(
                "sy", f"vw{c % 2}",
                v_scr[c * CH:(c + 1) * CH, :].rearrange("(rs p) n -> p rs n", p=128),
                pa[:, STG_OFF[c % 2]: STG_OFF[c % 2] + NSUB * D
                   ].rearrange("p (rs n) -> p rs n", n=D),
                waits=[vev[c * NSUB * NVC + NSUB * NVC - 1]])
        tok_p2c_ld0 = S_.tok("vw0")
        tok_p2c_ld1 = S_.tok("vw1")
        tok_p2c_pe = S_.tok("pe")

        tok_vflat = dma("gq", "vf", pv[:, 0:JC * D].rearrange("p (jc n) -> p jc n", n=D),
                        v_fl.rearrange("(jc p) n -> p jc n", p=128),
                        waits=[tok_p3_dve, ("act", S_.n["act"]), ("dve", S_.n["dve"])])

        tok_x23 = ("x2", S_.n["x2"])
        tok_x23b = ("x3", S_.n["x3"])

        SCB = [psc[0], psc[1]]

        def scb_(k):
            return SCB[k % 2]

        def po_(w):
            return pf[0] if w == 0 else pf[2]

        def pl_(w):
            return pf[1] if w == 0 else pf[3]

        S_ACC0 = max(NKS * QT, DC * J, 2 * (D + 1024 + 3 * 512)) // 2

        def s_acc(q0, q1):
            return pmf[:, S_ACC0 + q0: S_ACC0 + q1]

        def s_acc2(q0, q1):
            return pmf[:, S_ACC0 + QT + q0: S_ACC0 + QT + q1]

        head_pe = {}
        head_ev = {}
        head_ld = {}
        exp_toks = {}
        mask_toks = {}
        pool_toks = {}
        sweep_ev = {}
        ev_by_gsw = {}
        cast_tok = {}
        lmm_tok = {}
        cp_by_gsw = {}
        epi_jobs = []
        gk_c = [0]
        nvd = (NSUBK + 7) // 8

        def flush_epi():
            while epi_jobs:
                epi_jobs.pop(0)()

        for h in range(H):
            beta = h % 2
            waits = [tok_p2a_ld0, tok_p2a_ld1, tok_p2b_q0, tok_p2b_q1,
                     tok_p2c_ld0, tok_p2c_ld1, tok_p2c_pe, tok_p2a_pe,
                     S_.tok("pe") if h == 0 else None]
            if h >= 2:
                waits += [head_pe[h - 2], sweep_ev[(h - 2, NSW)]]
            t = dma("sy", f"h{beta}", kth(beta, 0, T), kt_scr[h * 128:(h + 1) * 128, :],
                    waits=waits)
            for d in range(nvd):
                s0, s1 = d * 8, min((d + 1) * 8, NSUBK)
                t = dma("sy", f"h{beta}", vh_view(beta)[:, s0:s1, :],
                        v_scr[s0 * 128:s1 * 128, h * 128:(h + 1) * 128
                              ].rearrange("(s p) c -> p s c", p=128))
            RPH = OWN // H
            head_ld[h] = t = dma(
                "sy", f"h{beta}",
                xs[beta][:, 0:DC * 128].rearrange("p (qc r) -> p qc r", r=128),
                qt_scr.rearrange("(qc p) r -> p qc r", p=128)[:, :, h * RPH:(h + 1) * RPH])

            for sw in range(1, NSW + 1):
                gsw = h * NSW + sw - 1
                w_par = gsw % 2
                nk = sw * NKS
                outq = []
                prev_cast = cast_tok.get(gsw - 1)
                A("gq", lambda E: E.memset(s_acc(0, QT), 0.0),
                  waits=[prev_cast, tok_u], inc=("pool", 1), chain=True)
                A("dve", lambda E: E.memset(s_acc2(0, QT), 0.0),
                  waits=[prev_cast, tok_u], inc=("dve", 1), chain=True)
                for k in range(nk):
                    gk = gk_c[0]
                    gk_c[0] += 1
                    war = exp_toks.get(gk - 2)
                    sw_waits = [head_ld[h], war]
                    if gk in (0, 1):
                        sw_waits.append(tok_p3_dve)
                    JJ = QT // H
                    t_sc = A("pe", lambda E, b=beta, k=k, gk=gk, sw=sw, JJ=JJ: E.matmul(
                        scb_(gk)[:, 0:QT], kth(b, k * 128, (k + 1) * 128),
                        xs[b][:, 0:DC * 128].rearrange(
                            "p (qc r) -> p r qc", r=128)[:, (sw - 1) * JJ: sw * JJ, :],
                        start=True, stop=True),
                        waits=sw_waits, inc=("pe", 1))
                    if k == 2:
                        flush_epi()
                    ew = [t_sc, tok_cst, pool_toks.get(gk - 3)]
                    exp_toks[gk] = A(
                        "act", lambda E, gk=gk: E.activation(
                            out=pt[gk % 3][:, 0:QT], in_=scb_(gk)[:, 0:QT],
                            func=AF.Exp, bias=cst[:], scale=SCALE),
                        waits=ew, inc=("act", 1), chain=True)
                    if k >= nk - NKS:
                        j = k - (nk - NKS)
                        mask_toks[gk] = A(
                            "dve", lambda E, gk=gk, j=j: E.tensor_tensor(
                                pt[gk % 3][:, 0:QT], pt[gk % 3][:, 0:QT],
                                mask_j(j), op=ALU.mult),
                            waits=[exp_toks[gk], tok_masks],
                            inc=("dve", 1), chain=True)
                        ptok = mask_toks[gk]
                    else:
                        ptok = exp_toks[gk]
                    if k % 2 == 0:
                        pool_toks[gk] = A(
                            "gq", lambda E, gk=gk: E.tensor_tensor(
                                s_acc(0, QT), s_acc(0, QT), pt[gk % 3][:, 0:QT],
                                op=ALU.add),
                            waits=[ptok], inc=("pool", 1), chain=True)
                    else:
                        pool_toks[gk] = A(
                            "dve", lambda E, gk=gk: E.tensor_tensor(
                                s_acc2(0, QT), s_acc2(0, QT), pt[gk % 3][:, 0:QT],
                                op=ALU.add),
                            waits=[ptok], inc=("dve", 1), chain=True)
                    pvw = [ptok]
                    if k == 0:
                        pvw.append(ev_by_gsw.get(gsw - 2, tok_p3_dve))

                    def mk_ot(b=beta, k=k, gk=gk, nk=nk, w_par=w_par):
                        def f(E):
                            return E.matmul(po_(w_par)[:, 0:QT], vh128(b, k),
                                            pt[gk % 3][:, 0:QT],
                                            start=(k == 0), stop=(k == nk - 1))
                        return f
                    outq.append((mk_ot(), pvw))
                    while len(outq) > 2:
                        A("pe", *outq.pop(0))
                tpv = None
                for i2, (fn2, w2_) in enumerate(outq):
                    tpv = A("pe", fn2, waits=w2_,
                            inc=("pe", 1) if i2 == len(outq) - 1 else None)
                head_pe[h] = tpv
                sweep_pe = tpv
                A("dve", lambda E: E.tensor_tensor(
                    s_acc2(0, QT), s_acc2(0, QT), s_acc(0, QT), op=ALU.add),
                    waits=[pool_toks[gk_c[0] - 1],
                           pool_toks[gk_c[0] - 2],
                           lmm_tok.get(gsw - 1)],
                    inc=("dve", 1), chain=True)
                cast_tok[gsw] = A("dve", lambda E: E.tensor_copy(
                    sbf[:, 0:QT], s_acc2(0, QT)),
                    waits=[tok_x23, tok_x23b], inc=("dve", 1), chain=True)

                def mk_epi(h=h, sw=sw, gsw=gsw, w_par=w_par,
                           sweep_pe=sweep_pe, tcast=cast_tok[gsw]):
                    def job():
                        t_lm = A("pe", lambda E: E.matmul(
                            pl_(w_par)[0:1, 0:QT], onc[:], sbf[:, 0:QT],
                            start=True, stop=True),
                            waits=[tcast, cp_by_gsw.get(gsw - 2, tok_p3_dve)],
                            inc=("pe", 1))
                        lmm_tok[gsw] = t_lm
                        t_rl = A("dve", lambda E: E.reciprocal(
                            rlb[0:1, 0:QT], pl_(w_par)[0:1, 0:QT]),
                            waits=[t_lm, tok_x23, tok_x23b], inc=("dve", 1), chain=True)
                        t_bc = A("pe", lambda E: E.matmul(
                            pl_(w_par)[:, 0:QT], onf[:], rlb[0:1, 0:QT],
                            start=True, stop=True),
                            waits=[t_rl], inc=("pe", 1))
                        t_cp = A("dve", lambda E: E.tensor_copy(
                            rlb[:, 0:QT], pl_(w_par)[:, 0:QT]),
                            waits=[t_bc], inc=("dve", 1), chain=True)
                        cp_by_gsw[gsw] = t_cp
                        r0 = (sw - 1) * QT
                        A("dve", lambda E, r0=r0, h=h: E.tensor_tensor(
                            acT(h, r0, r0 + QT), po_(w_par)[:, 0:QT],
                            rlb[:, 0:QT], op=ALU.mult),
                            waits=[t_cp, sweep_pe], inc=("dve", 1), chain=True)
                        t_ev = A("dve", lambda E, r0=r0, h=h: E.tensor_scalar(
                            acT(h, r0, r0 + QT), acT(h, r0, r0 + QT),
                            bvs[:, h:h + 1], None, op0=ALU.add),
                            inc=("dve", 1), chain=True)
                        sweep_ev[(h, sw)] = t_ev
                        ev_by_gsw[gsw] = t_ev
                    return job
                epi_jobs.append(mk_epi())
        flush_epi()
        tok_p4_dve = S_.tok("dve")
        tok_p4_pe = S_.tok("pe")
        tok_p4_act = S_.tok("act")

        tok_wo = dma("gq", "w", pw[:, 0:DC * D].rearrange("p (dc n) -> p dc n", n=D),
                     wo.rearrange("(dc p) n -> p dc n", p=128),
                     waits=[tok_p4_pe])

        tok_bo = dma("sy", "bo", bo_b(0, D), bo_i,
                     waits=[tok_p4_dve, tok_p4_act])

        e1t = {}
        e2t = {}
        e3t = {}
        stt = {}
        xlt = {}
        for rt in range(ORT):
            for cc in range(NVC):
                n = rt * NVC + cc
                xw = [tok_bo]
                if n >= 2:
                    xw.append(e3t[n - 2])
                xlt[n] = dma("sy", f"px{n % 2}", p6x(n % 2),
                             xo[rt * 128:(rt + 1) * 128, cc * 512:(cc + 1) * 512],
                             waits=xw)
                mw = [tok_p3_dve, tok_vflat, tok_wo, tok_p4_dve]
                if n >= 2:
                    mw += [e1t[n - 2], e2t[n - 2]]

                def mk_f(rt=rt, cc=cc, n=n):
                    def f(E):
                        for jc in range(JC):
                            ins = E.matmul(pf[n % 2][:, 0:512],
                                           gut(jc, rt * 128, (rt + 1) * 128),
                                           vflat(jc, cc * 512, (cc + 1) * 512),
                                           start=(jc == 0), stop=(jc == JC - 1))
                        return ins
                    return f
                t_f = A("pe", mk_f(), waits=mw, inc=("pe", 1))

                def mk_o(rt=rt, cc=cc, n=n):
                    def f(E):
                        for fc in range(DC):
                            ins = E.matmul(pf[2 + n % 2][:, 0:512],
                                           acT(fc, rt * 128, (rt + 1) * 128),
                                           wchunk(fc, cc * 512, (cc + 1) * 512),
                                           start=(fc == 0), stop=(fc == DC - 1))
                        return ins
                    return f
                t_o = A("pe", mk_o(), inc=("pe", 1))

                w1_ = [t_f, tok_p3_dve]
                if n >= 3:
                    w1_.append(stt[n - 3])
                e1t[n] = A("act", lambda E, rt=rt, n=n: E.activation(
                    out=p6t(n % 3), in_=pf[n % 2][:, 0:512], func=AF.Copy,
                    scale=gps[:, rt:rt + 1]), waits=w1_, inc=("act", 1),
                    chain=True)
                e2t[n] = A("dve", lambda E, n=n: E.tensor_tensor(
                    p6t(n % 3), p6t(n % 3), pf[2 + n % 2][:, 0:512], op=ALU.add),
                    waits=[e1t[n], t_o], inc=("dve", 1), chain=True)
                A("dve", lambda E, n=n: E.tensor_tensor(
                    p6t(n % 3), p6t(n % 3), p6x(n % 2), op=ALU.add),
                  waits=[xlt[n]], inc=("dve", 1), chain=True)
                e3t[n] = A("dve", lambda E, n=n, cc=cc: E.tensor_tensor(
                    p6t(n % 3), p6t(n % 3), bo_b(cc * 512, (cc + 1) * 512),
                    op=ALU.add), waits=[tok_bo], inc=("dve", 1), chain=True)
                stt[n] = dma("sy", f"o{n % 3}", out[rt * 128:(rt + 1) * 128, cc * 512:(cc + 1) * 512],
                             p6t(n % 3), waits=[e3t[n]])

        final = dict(S_.n)
        S_.add("gq", None, waits=[(s, v) for s, v in final.items()])

        semnames = set(S_.n.keys())
        for ename in S_.ENG:
            for fn, waits, inc in S_.ops[ename]:
                for sem, val in waits:
                    semnames.add(sem)
        sems = {name: e(nc.semaphore("s_" + name)) for name in sorted(semnames)}
        block = e(nc.Block())

        def emit_wm(engobj, ename):
            water = {}
            for fn, waits, inc in S_.ops[ename]:
                for sem, val in waits:
                    if water.get(sem, 0) >= val:
                        continue
                    engobj.wait_ge(sems[sem], val)
                    water[sem] = val
                if fn is None:
                    continue
                ins = fn(engobj)
                if inc is not None:
                    ins.then_inc(sems[inc[0]], inc[1])

        @block.sync
        def _(E):
            emit_wm(E, "sy")

        @block.gpsimd
        def _(E):
            emit_wm(E, "gq")

        @block.tensor
        def _(E):
            emit_wm(E, "pe")

        @block.vector
        def _(E):
            emit_wm(E, "dve")

        @block.scalar
        def _(E):
            emit_wm(E, "act")

    return nc


def host_inputs(cfg, core, ins):
    D, T = cfg["D"], cfg["T"]
    OWN, QT, NSW, NKS, DC, JC, S = (cfg["OWN"], cfg["QT"], cfg["NSW"],
                                    cfg["NKS"], cfg["DC"], cfg["JC"], cfg["S"])
    b, g = core // 4, core % 4
    x = ins["L0_attn_x"]
    xb = np.ascontiguousarray(x[b], dtype=np.float32)
    blocks = [4 * s + g for s in range(NSW)]
    xo = np.ascontiguousarray(
        np.concatenate([x[b][blk * QT:(blk + 1) * QT] for blk in blocks], axis=0),
        dtype=np.float32)
    H = cfg["H"]
    TPH, QPH = cfg["T"] // H, QT // H
    qrows = []
    for h in range(H):
        for s in range(NSW):
            base = h * TPH + (QT * (4 * s + g)) // H
            qrows.extend(range(base, base + QPH))
    xq = np.ascontiguousarray(x[b][np.array(qrows)], dtype=np.float32)
    masks = np.zeros((128, NKS * QT), dtype=ml_dtypes.bfloat16)
    kl = np.arange(128)[:, None]
    q = np.arange(QT)[None, :]
    for j in range(NKS):
        masks[:, j * QT:(j + 1) * QT] = (j * 128 + kl <= g * QT + q).astype(
            ml_dtypes.bfloat16)
    e_mat = np.zeros((S, 128 * JC), dtype=ml_dtypes.bfloat16)
    for jj in range(128 * JC):
        e_mat[jj // 16, jj] = 1
    U = np.asarray(ins["L0_ffn_U"], np.float32)
    m = {
        "xb": xb,
        "xo": xo,
        "xq": xq,
        "wq": ins["L0_attn_Wq"], "wk": ins["L0_attn_Wk"],
        "wv": ins["L0_attn_Wv"], "wo": ins["L0_attn_Wo"],
        "bq": np.ascontiguousarray(np.asarray(ins["L0_attn_bq"], np.float32).reshape(DC, 128).T),
        "bk": np.ascontiguousarray(np.asarray(ins["L0_attn_bk"], np.float32).reshape(DC, 128).T),
        "bv": np.ascontiguousarray(np.asarray(ins["L0_attn_bv"], np.float32).reshape(DC, 128).T),
        "bo_bc": np.ascontiguousarray(np.broadcast_to(
            np.asarray(ins["L0_attn_bo"], np.float32).reshape(1, D), (128, D))),
        "wr_t": np.ascontiguousarray(np.asarray(ins["L0_ffn_Wr"], np.float32).T),
        "u_flat": np.ascontiguousarray(U.transpose(1, 0, 2).reshape(D, -1)),
        "v_flat": np.ascontiguousarray(np.asarray(ins["L0_ffn_V"], np.float32).reshape(-1, D)),
        "attr_t": np.ascontiguousarray(np.asarray(ins["L0_ffn_attr"], np.float32).T),
        "wa_t": np.ascontiguousarray(np.asarray(ins["L0_ffn_Wa"], np.float32).reshape(1, S).T),
        "masks": masks,
        "ident": np.eye(128, dtype=ml_dtypes.bfloat16),
        "ones_f": np.ones((1, 128), np.float32),
        "ones_c": np.ones((128, 1), dtype=ml_dtypes.bfloat16),
        "e_mat": e_mat,
    }
    return {k: np.ascontiguousarray(np.asarray(v)) for k, v in m.items()}


def assemble(cfg, results):
    D, T, QT, NSW = cfg["D"], cfg["T"], cfg["QT"], cfg["NSW"]
    full = np.zeros((2, T, D), np.float32)
    for core in range(8):
        b, g = core // 4, core % 4
        o = np.asarray(results[core]["out"], np.float32)
        for s in range(NSW):
            blk = 4 * s + g
            full[b, blk * QT:(blk + 1) * QT] = o[s * QT:(s + 1) * QT]
    return full


def kernel(**inputs):
    cfg = mkcfg(FULL)
    nc = build_nc(cfg)
    in_maps = [host_inputs(cfg, c, inputs) for c in range(8)]
    res = run_bass_kernel_spmd(nc, in_maps, core_ids=list(range(8)))
    return assemble(cfg, res.results)


# version 13
# speedup vs baseline: 1.2686x; 1.0563x over previous
import math
from contextlib import ExitStack

import numpy as np
import ml_dtypes
import concourse.bass as bass
import concourse.mybir as mybir
from concourse.bass_utils import run_bass_kernel_spmd

F32 = mybir.dt.float32
BF16 = mybir.dt.bfloat16
AF = mybir.ActivationFunctionType
ALU = mybir.AluOpType
AX = mybir.AxisListType

FULL = dict(D=2048, T=8192, H=16)
TINY = dict(D=512, T=2048, H=4)

C_SHIFT = 12.0


def mkcfg(base):
    c = dict(base)
    D, T, H = c["D"], c["T"], c["H"]
    c["DH"] = 128
    assert H * c["DH"] == D
    c["DC"] = D // 128
    c["OWN"] = T // 4
    assert c["OWN"] == D
    c["QT"] = T // 16
    assert c["QT"] % 128 == 0
    c["NQS"] = c["QT"] // 128
    c["NSW"] = 4
    c["NKS"] = T // 512
    c["CH"] = c["QT"]
    c["NCH"] = T // c["CH"]
    c["NSUB"] = c["CH"] // 128
    c["S"], c["R"] = 32, 16
    c["J"] = c["S"] * c["R"]
    c["JC"] = c["J"] // 128
    c["GR"] = min(512, c["OWN"])
    c["NG"] = c["OWN"] // c["GR"]
    c["ORT"] = c["OWN"] // 128
    assert c["ORT"] == c["DC"]
    c["SCALE"] = 1.0 / math.sqrt(c["DH"])
    return c


class Sched:
    ENG = ("sy", "gq", "pe", "act", "dve")

    def __init__(self):
        self.ops = {e: [] for e in self.ENG}
        from collections import defaultdict
        self.n = defaultdict(int)
        for s in ("pe", "act", "dve"):
            self.n[s] = 0
        self.chain_last = {"act": 0, "dve": 0}

    def add(self, eng, fn, waits=(), inc=None, chain=False):
        w = [x for x in waits if x is not None and x[1] > 0]
        if chain and eng in self.chain_last:
            if self.chain_last[eng] > 0:
                w.append((eng, self.chain_last[eng]))
        tok = None
        if inc is not None:
            sem, k = inc
            self.n[sem] += k
            tok = (sem, self.n[sem])
            if eng in self.chain_last and sem == eng:
                self.chain_last[eng] = self.n[sem]
        self.ops[eng].append((fn, w, inc))
        return tok

    def tok(self, sem):
        return (sem, self.n[sem])

    def barrier(self):
        snap = dict(self.n)
        for e in self.ENG:
            self.add(e, None, waits=[(s, v) for s, v in snap.items()])
        return snap


def build_nc(cfg):
    D, T, H, DH = cfg["D"], cfg["T"], cfg["H"], cfg["DH"]
    DC, OWN, QT, NQS = cfg["DC"], cfg["OWN"], cfg["QT"], cfg["NQS"]
    NSW, NKS, CH, NCH, NSUB = cfg["NSW"], cfg["NKS"], cfg["CH"], cfg["NCH"], cfg["NSUB"]
    S, J, JC = cfg["S"], cfg["J"], cfg["JC"]
    GR, NG, ORT, SCALE = cfg["GR"], cfg["NG"], cfg["ORT"], cfg["SCALE"]
    NVC = D // 512
    NSUBK = T // 128

    nc = bass.Bass("TRN2", target_bir_lowering=False)

    xb = nc.dram_tensor("xb", [T, D], F32, kind="ExternalInput").ap()
    xo = nc.dram_tensor("xo", [OWN, D], F32, kind="ExternalInput").ap()
    xq = nc.dram_tensor("xq", [OWN, D], F32, kind="ExternalInput").ap()
    wq = nc.dram_tensor("wq", [D, D], F32, kind="ExternalInput").ap()
    wk = nc.dram_tensor("wk", [D, D], F32, kind="ExternalInput").ap()
    wv = nc.dram_tensor("wv", [D, D], F32, kind="ExternalInput").ap()
    wo = nc.dram_tensor("wo", [D, D], F32, kind="ExternalInput").ap()
    bq_i = nc.dram_tensor("bq", [128, DC], F32, kind="ExternalInput").ap()
    bk_i = nc.dram_tensor("bk", [128, DC], F32, kind="ExternalInput").ap()
    bv_i = nc.dram_tensor("bv", [128, DC], F32, kind="ExternalInput").ap()
    bo_i = nc.dram_tensor("bo_bc", [128, D], F32, kind="ExternalInput").ap()
    wr_i = nc.dram_tensor("wr_t", [D, S], F32, kind="ExternalInput").ap()
    u_fl = nc.dram_tensor("u_flat", [D, J], F32, kind="ExternalInput").ap()
    v_fl = nc.dram_tensor("v_flat", [J, D], F32, kind="ExternalInput").ap()
    at_i = nc.dram_tensor("attr_t", [S, S], F32, kind="ExternalInput").ap()
    wa_i = nc.dram_tensor("wa_t", [S, 1], F32, kind="ExternalInput").ap()
    mk_i = nc.dram_tensor("masks", [128, NKS * QT], BF16, kind="ExternalInput").ap()
    id_i = nc.dram_tensor("ident", [128, 128], BF16, kind="ExternalInput").ap()
    of_i = nc.dram_tensor("ones_f", [1, 128], F32, kind="ExternalInput").ap()
    em_i = nc.dram_tensor("e_mat", [S, 128 * JC], BF16, kind="ExternalInput").ap()

    out = nc.dram_tensor("out", [OWN, D], F32, kind="ExternalOutput").ap()

    kt_scr = nc.dram_tensor("kt_scr", [D, T], BF16, kind="Internal").ap()
    v_scr = nc.dram_tensor("v_scr", [T, D], BF16, kind="Internal").ap()
    qt_scr = nc.dram_tensor("qt_scr", [D, OWN], BF16, kind="Internal").ap()
    xnt_scr = nc.dram_tensor("xnt_scr", [D, T], BF16, kind="Internal").ap()

    with ExitStack() as ctx:
        e = ctx.enter_context

        pa = e(nc.sbuf_tensor("pa", [128, DC * OWN], BF16))
        PW_COLS = max(DC * D, 2 * T + 2 * 129 * NSUBK)
        pw = e(nc.sbuf_tensor("pw", [128, PW_COLS], BF16))
        pm = e(nc.sbuf_tensor("pm", [128, max(DC * J, NKS * QT, 2 * (D + 1024 + 3 * 512))], BF16))
        pv = e(nc.sbuf_tensor("pv", [128, max(JC * D, JC * GR + OWN + 2 * D)], BF16))
        pg = e(nc.sbuf_tensor("pg", [128, JC * OWN], BF16))
        xs = [e(nc.sbuf_tensor(f"xs{i}", [128, max(D, OWN)], BF16)) for i in range(4)]
        ptp = [e(nc.sbuf_tensor(f"ptp{i}", [128, 2 * QT], BF16)) for i in range(2)]
        pt = [ptp[0][:, 0:QT], ptp[0][:, QT:2 * QT], ptp[1][:, 0:QT]]
        wrk = e(nc.sbuf_tensor("wrk", [128, 64], F32))
        scw = e(nc.sbuf_tensor("scw", [128, 8 * S], F32))
        gps = e(nc.sbuf_tensor("gps", [128, ORT], F32))
        bqs = e(nc.sbuf_tensor("bqs", [128, DC], F32))
        bks = e(nc.sbuf_tensor("bks", [128, DC], F32))
        bvs = e(nc.sbuf_tensor("bvs", [128, DC], F32))
        wrs = e(nc.sbuf_tensor("wrs", [128, DC * S], BF16))
        ems = e(nc.sbuf_tensor("ems", [S, 128 * JC], BF16))
        ids = e(nc.sbuf_tensor("ids", [128, 128], BF16))
        onf = e(nc.sbuf_tensor("onf", [1, 128], F32))
        ats = e(nc.sbuf_tensor("ats", [S, S], F32))
        was = e(nc.sbuf_tensor("was", [S, 1], F32))
        awb = e(nc.sbuf_tensor("awb", [128, S], F32))
        aws = e(nc.sbuf_tensor("aws", [1, S], F32))
        cst = e(nc.sbuf_tensor("cst", [128, 1], F32))

        pf = [e(nc.psum_tensor(f"pf{i}", [128, 512], F32)) for i in range(4)]
        scp = [e(nc.psum_tensor(f"scp{i}", [128, 1024], F32)) for i in range(2)]
        scp0_bf = scp[0].bitcast(BF16)

        def pbv(i, a, b):
            return scp0_bf[:, i * 1024 + a: i * 1024 + b]

        block_holder = []

        S_ = Sched()

        def xnt_own(dc, r0, r1):
            return pa[:, dc * OWN + r0: dc * OWN + r1]

        XTC_OFF = [0, 2 * DC * CH]
        STG_OFF = [DC * CH, 3 * DC * CH]

        def xtc(buf, dc, r0, r1):
            o = XTC_OFF[buf]
            return pa[:, o + dc * CH + r0: o + dc * CH + r1]

        def stg(buf, kc, c0, c1):
            o = STG_OFF[buf]
            return pa[:, o + kc * CH + c0: o + kc * CH + c1]

        def stgv(buf, rs, c0, c1):
            o = STG_OFF[buf]
            return pa[:, o + rs * D + c0: o + rs * D + c1]

        def ac(rt, f0, f1):
            return pa[:, rt * D + f0: rt * D + f1]

        def acT(fc, r0, r1):
            return pa[:, fc * OWN + r0: fc * OWN + r1]

        def wchunk(dc, n0, n1):
            return pw[:, dc * D + n0: dc * D + n1]

        def kth(b, k0, k1):
            return pw[:, b * T + k0: b * T + k1]

        VH0 = 2 * T

        def vh(b, sub):
            o = VH0 + b * (129 * NSUBK) + sub * 129
            return pw[:, o: o + 129]

        def vh_view(b):
            o = VH0 + b * (129 * NSUBK)
            return pw[:, o: o + NSUBK * 129].rearrange("p (s c) -> p s c", c=129)

        def uchunk(dc, j0, j1):
            return pm[:, dc * J + j0: dc * J + j1]

        def mask_j(j):
            return pm[:, j * QT:(j + 1) * QT]

        pmf = pm.bitcast(F32)

        def bo_b(c0, c1):
            return pmf[:, c0:c1]

        def p6x(b):
            return pmf[:, D + b * 512: D + (b + 1) * 512]

        def p6t(i):
            return pmf[:, D + 1024 + i * 512: D + 1024 + (i + 1) * 512]

        def vflat(jc, n0, n1):
            return pv[:, jc * D + n0: jc * D + n1]

        def gexp(jc):
            return pv[:, jc * GR:(jc + 1) * GR]

        GT0 = JC * GR

        def gT(r0, r1):
            return pv[0:S, GT0 + r0: GT0 + r1]

        def gut(jc, r0, r1):
            return pg[:, jc * OWN + r0: jc * OWN + r1]

        PVJ = JC * GR + OWN
        sqj = pv[:, PVJ: PVJ + D]
        cpj = pv[:, PVJ + D: PVJ + 2 * D]
        qst = pg[:, 0: 2 * GR]

        def w1(i):
            return wrk[:, i:i + 1]

        A = S_.add

        def dma(eng, sem, outap, inap, waits=()):
            return A(eng, lambda E, o=outap, i=inap: E.dma_start(out=o, in_=i),
                     waits=waits, inc=(sem, 16))

        tok_consts = None
        for o, i in [(ids[:], id_i), (onf[:], of_i), (bqs[:], bq_i), (bks[:], bk_i),
                     (bvs[:], bv_i), (ats[:], at_i), (was[:], wa_i),
                     (ems[:], em_i)]:
            tok_consts = dma("sy", "cst", o, i)
        tok_wrs = dma("gq", "wrs", wrs.rearrange("p (dc s) -> p dc s", s=S),
                      wr_i.rearrange("(dc p) s -> p dc s", p=128))

        tok_wq = dma("gq", "w", pw[:, 0:DC * D].rearrange("p (dc n) -> p dc n", n=D),
                     wq.rearrange("(dc p) n -> p dc n", p=128))
        tok_u = dma("gq", "u", pm[:, 0:DC * J].rearrange("p (dc j) -> p dc j", j=J),
                    u_fl.rearrange("(dc p) j -> p dc j", p=128))
        tok_cst = A("dve", lambda E: E.memset(cst[:], -C_SHIFT),
                    inc=("dve", 1), chain=True)

        ln_i = [0]
        ln_trs = {}
        ln_norm = {}

        def ln_stats(src_rows, extra_waits=()):
            i = ln_i[0]
            ln_i[0] += 1
            beta = i % 4
            b2 = i % 2
            xbuf = xs[beta]
            t_x = dma("gq", f"x{beta}", xbuf[:, 0:D], src_rows,
                      waits=[ln_trs.get(i - 4)] + list(extra_waits))
            t_sq = A("act", lambda E, xb_=xbuf, b=b2: E.activation(
                out=sqj, in_=xb_[:, 0:D], func=AF.Square,
                accum_out=wrk[:, 8 + b:9 + b]), waits=[t_x], inc=("act", 1), chain=True)
            t_sm = A("dve", lambda E, xb_=xbuf, b=b2: E.tensor_scalar(
                cpj, xb_[:, 0:D], 1.0, None, op0=ALU.mult, op1=ALU.add,
                accum_out=wrk[:, 10 + b:11 + b]), waits=[t_x], inc=("dve", 1), chain=True)
            A("dve", lambda E, b=b2: E.tensor_scalar(
                w1(12 + b), w1(10 + b), 1.0 / D, None, op0=ALU.mult),
              waits=[t_sm], inc=("dve", 1), chain=True)
            A("dve", lambda E, b=b2: E.tensor_scalar(
                w1(14 + b), w1(8 + b), 1.0 / D, None, op0=ALU.mult),
              waits=[t_sq], inc=("dve", 1), chain=True)
            A("dve", lambda E, b=b2: E.tensor_tensor(
                w1(16 + b), w1(12 + b), w1(12 + b), op=ALU.mult),
              inc=("dve", 1), chain=True)
            A("dve", lambda E, b=b2: E.tensor_tensor(
                w1(18 + b), w1(14 + b), w1(16 + b), op=ALU.subtract),
                inc=("dve", 1), chain=True)
            t_var = A("dve", lambda E, b=b2: E.tensor_scalar(
                w1(18 + b), w1(18 + b), 1e-5, None, op0=ALU.add),
                inc=("dve", 1), chain=True)
            A("act", lambda E, b=b2: E.activation(
                out=w1(20 + b), in_=w1(18 + b), func=AF.Ln),
              waits=[t_var], inc=("act", 1), chain=True)
            t_rstd = A("act", lambda E, b=b2: E.activation(
                out=w1(22 + b), in_=w1(20 + b), func=AF.Exp, scale=-0.5),
                inc=("act", 1), chain=True)
            ln_norm[i] = A("dve", lambda E, xb_=xbuf, b=b2: E.tensor_scalar(
                xb_[:, 0:D], xb_[:, 0:D], w1(12 + b), w1(22 + b),
                op0=ALU.subtract, op1=ALU.mult),
                waits=[t_rstd], inc=("dve", 1), chain=True)
            return i

        def ln_tr(i, dest_fn, dest_r0):
            beta = i % 4
            xbuf = xs[beta]
            t_norm = ln_norm[i]
            ev_toks = []
            tr_toks = []
            for dc in range(DC):
                war = ev_toks[dc - 2] if dc >= 2 else ln_tr.last_ev.get(dc % 2)
                t_tr = A("pe", lambda E, xb_=xbuf, dc=dc: E.transpose(
                    pbv(dc % 2, 0, 128), xb_[:, dc * 128:(dc + 1) * 128], ids[:]),
                    waits=[t_norm, war, tok_consts], inc=("pe", 1))
                tr_toks.append(t_tr)
                t_ev = A("dve", lambda E, dc=dc, r0=dest_r0, df=dest_fn: E.tensor_copy(
                    df(dc, r0, r0 + 128), pbv(dc % 2, 0, 128)),
                    waits=[t_tr], inc=("dve", 1), chain=True)
                ev_toks.append(t_ev)
            ln_tr.last_ev = {0: ev_toks[-2], 1: ev_toks[-1]}
            ln_trs[i] = tr_toks[-1]
            return ev_toks[-1]

        ln_tr.last_ev = {}

        def ln_subtile(src_rows, dest_fn, dest_r0, extra_waits=()):
            i = ln_stats(src_rows, extra_waits)
            ev = ln_tr(i, dest_fn, dest_r0)
            return ev, ln_trs[i]

        q_mm_last = None
        qev = {}
        qld = {}
        NCH0 = OWN // CH
        LNQ = DC // NSUB
        ev_of = {}
        st_ids = {}
        ids0 = [ln_stats(xq[sb * 128:(sb + 1) * 128, :]) for sb in range(NSUB)]
        ev_of[0] = max(ln_tr(i, lambda dc, a, bb: xtc(0, dc, a, bb),
                             sb * 128) for sb, i in enumerate(ids0))
        for c in range(NCH0):
            if c + 1 < NCH0:
                st_ids[c + 1] = [ln_stats(
                    xq[(c + 1) * CH + sb * 128:(c + 1) * CH + (sb + 1) * 128, :])
                    for sb in range(NSUB)]
            for qc in range(DC):
                n = c * DC + qc
                waits = [ev_of[c], tok_wq]
                if n >= 4:
                    waits.append(qev[n - 4])

                def mk_qmm(qc=qc, n=n, c=c):
                    def f(E):
                        for dc in range(DC):
                            ins = E.matmul(pf[n % 4][:, 0:CH],
                                           wchunk(dc, qc * 128, (qc + 1) * 128),
                                           xtc(c % 2, dc, 0, CH),
                                           start=(dc == 0), stop=(dc == DC - 1))
                        return ins
                    return f
                q_mm_last = A("pe", mk_qmm(), waits=waits, inc=("pe", 1))
                w2 = [q_mm_last]
                if n >= 2:
                    w2.append(qld[n - 2])
                qev[n] = A("act", lambda E, qc=qc, n=n: E.activation(
                    out=qst[:, (n % 2) * GR:(n % 2) * GR + CH],
                    in_=pf[n % 4][:, 0:CH], func=AF.Identity,
                    bias=bqs[:, qc:qc + 1]), waits=w2, inc=("act", 1), chain=True)
                qld[n] = dma(
                    "sy", f"q{n % 2}",
                    qt_scr.rearrange("(qc p) r -> p qc r", p=128)[:, qc, c * CH:(c + 1) * CH],
                    qst[:, (n % 2) * GR:(n % 2) * GR + CH],
                    waits=[qev[n]])
                if c + 1 < NCH0 and qc % LNQ == LNQ - 1:
                    sb = qc // LNQ
                    ev = ln_tr(st_ids[c + 1][sb],
                               lambda dc, a, bb, c=c: xtc((c + 1) % 2, dc, a, bb),
                               sb * 128)
                    if sb == NSUB - 1:
                        ev_of[c + 1] = ev
        tok_p2b_pe = S_.tok("pe")
        tok_p2b_q0 = S_.tok("q0")
        tok_p2b_q1 = S_.tok("q1")

        for i in range(OWN // 128):
            ev, tr = ln_subtile(xo[i * 128:(i + 1) * 128, :], xnt_own, i * 128)
        tok_p1 = S_.tok("dve")

        t1 = A("pe", lambda E: E.matmul(scp[0][0:1, 0:S], was[:], ats[:],
                                        start=True, stop=True),
               waits=[tok_consts, tok_p2b_pe, tok_p1], inc=("pe", 1))
        t2 = A("act", lambda E: E.activation(out=aws[:], in_=scp[0][0:1, 0:S],
                                             func=AF.Exp, scale=-1.0),
               waits=[t1], inc=("act", 1), chain=True)
        t3 = A("dve", lambda E: E.tensor_scalar(aws[:], aws[:], 1.0, None,
                                                op0=ALU.add),
               waits=[t2], inc=("dve", 1), chain=True)
        t4 = A("dve", lambda E: E.reciprocal(aws[:], aws[:]),
               inc=("dve", 1), chain=True)
        t5 = A("pe", lambda E: E.matmul(scp[1][:, 0:S], onf[:], aws[:],
                                        start=True, stop=True),
               waits=[t4], inc=("pe", 1))
        tok_awb = A("dve", lambda E: E.tensor_copy(awb[:], scp[1][:, 0:S]),
                    waits=[t5], inc=("dve", 1), chain=True)

        g_done = {}
        for rt in range(ORT):
            r0 = rt * 128

            def mk_sc(rt=rt, r0=r0):
                def f(E):
                    for dc in range(DC):
                        ins = E.matmul(scp[0][:, 0:S],
                                       xnt_own(dc, r0, r0 + 128),
                                       wrs[:, dc * S:(dc + 1) * S],
                                       start=(dc == 0), stop=(dc == DC - 1))
                    return ins
                return f
            waits = [tok_p1, tok_consts, tok_wrs, t2]
            if rt > 0:
                waits.append(g_done[rt - 1])
            t_sc = A("pe", mk_sc(), waits=waits, inc=("pe", 1))

            t_cp = A("dve", lambda E: E.tensor_copy(scw[:, 0:S], scp[0][:, 0:S]),
                     waits=[t_sc], inc=("dve", 1), chain=True)
            A("dve", lambda E: E.tensor_reduce(w1(24), scw[:, 0:S], axis=AX.X,
                                               op=ALU.max, negate=True),
              inc=("dve", 1), chain=True)
            A("dve", lambda E: E.tensor_scalar(w1(25), w1(24), -1.0, None,
                                               op0=ALU.mult),
              inc=("dve", 1), chain=True)
            A("dve", lambda E: E.tensor_scalar(scw[:, S:2 * S], scw[:, 0:S],
                                               w1(25), 1e9, op0=ALU.is_ge,
                                               op1=ALU.mult),
              inc=("dve", 1), chain=True)
            t_ein = A("dve", lambda E: E.tensor_tensor(
                scw[:, 2 * S:3 * S], scw[:, 0:S], scw[:, S:2 * S],
                op=ALU.subtract), inc=("dve", 1), chain=True)
            t_exp = A("act", lambda E: E.activation(
                out=scw[:, 3 * S:4 * S], in_=scw[:, 0:S], func=AF.Exp,
                bias=w1(24)), waits=[t_ein], inc=("act", 1), chain=True)
            A("dve", lambda E: E.tensor_reduce(w1(26), scw[:, 2 * S:3 * S],
                                               axis=AX.X, op=ALU.max),
              inc=("dve", 1), chain=True)
            A("dve", lambda E: E.tensor_scalar(scw[:, 4 * S:5 * S], scw[:, 0:S],
                                               w1(26), None, op0=ALU.is_ge),
              inc=("dve", 1), chain=True)
            A("dve", lambda E: E.tensor_reduce(w1(27), scw[:, 3 * S:4 * S],
                                               axis=AX.X, op=ALU.add),
              waits=[t_exp], inc=("dve", 1), chain=True)
            A("dve", lambda E: E.tensor_tensor(scw[:, 5 * S:6 * S],
                                               scw[:, 3 * S:4 * S],
                                               scw[:, 4 * S:5 * S], op=ALU.mult),
              inc=("dve", 1), chain=True)
            A("dve", lambda E: E.tensor_reduce(w1(28), scw[:, 5 * S:6 * S],
                                               axis=AX.X, op=ALU.add),
              inc=("dve", 1), chain=True)
            A("dve", lambda E: E.tensor_scalar(w1(29), w1(27), 1e-8, None,
                                               op0=ALU.mult),
              inc=("dve", 1), chain=True)
            A("dve", lambda E: E.tensor_tensor(w1(30), w1(28), w1(29), op=ALU.add),
              inc=("dve", 1), chain=True)
            A("dve", lambda E: E.reciprocal(w1(31), w1(30)),
              inc=("dve", 1), chain=True)
            A("dve", lambda E: E.tensor_scalar(scw[:, 6 * S:7 * S],
                                               scw[:, 5 * S:6 * S], w1(31), None,
                                               op0=ALU.mult),
              inc=("dve", 1), chain=True)
            A("dve", lambda E: E.tensor_tensor(scw[:, 7 * S:8 * S],
                                               scw[:, 6 * S:7 * S], awb[:],
                                               op=ALU.mult),
              waits=[tok_awb], inc=("dve", 1), chain=True)
            A("dve", lambda E: E.tensor_reduce(w1(32), scw[:, 7 * S:8 * S],
                                               axis=AX.X, op=ALU.add),
              inc=("dve", 1), chain=True)
            A("dve", lambda E, rt=rt: E.tensor_scalar(
                gps[:, rt:rt + 1], w1(32), 0.2, 0.9, op0=ALU.mult, op1=ALU.add),
              inc=("dve", 1), chain=True)
            t_gbf = A("dve", lambda E: E.tensor_copy(ptp[1][:, 0:S],
                                                     scw[:, 6 * S:7 * S]),
                      inc=("dve", 1), chain=True)
            t_gtr = A("pe", lambda E: E.transpose(scp0_bf[0:S, 1024:1152], ptp[1][:, 0:S],
                                                  ids[:]),
                      waits=[t_gbf, (g_done.get(rt - 1) if rt else None)],
                      inc=("pe", 1))
            g_done[rt] = A("dve", lambda E, r0=r0: E.tensor_copy(
                gT(r0, r0 + 128), scp0_bf[0:S, 1024:1152]),
                waits=[t_gtr], inc=("dve", 1), chain=True)

        gut_toks = {}
        hu_last = None
        for gi in range(NG):
            r0 = gi * GR
            gx = {}
            for jc in range(JC):
                gxw = [g_done[ORT - 1], tok_consts, tok_awb]
                if jc > 0:
                    gxw.append(gx[jc - 1])
                elif gi > 0:
                    gxw.append(gx_prev_last)
                t_gx = A("pe", lambda E, jc=jc, r0=r0: E.matmul(
                    scp[1][:, 0:GR], ems[:, jc * 128:(jc + 1) * 128],
                    gT(r0, r0 + GR), start=True, stop=True),
                    waits=gxw, inc=("pe", 1))
                gx[jc] = A("dve", lambda E, jc=jc: E.tensor_copy(
                    gexp(jc), scp[1][:, 0:GR]),
                    waits=[t_gx], inc=("dve", 1), chain=True)
            gx_prev_last = gx[JC - 1]
            for jc in range(JC):
                n = gi * JC + jc
                waits = [tok_u, tok_p1]
                if n >= 4:
                    waits.append(gut_toks[n - 4])

                def mk_hu(jc=jc, r0=r0, n=n):
                    def f(E):
                        for dc in range(DC):
                            ins = E.matmul(pf[n % 4][:, 0:GR],
                                           uchunk(dc, jc * 128, (jc + 1) * 128),
                                           xnt_own(dc, r0, r0 + GR),
                                           start=(dc == 0), stop=(dc == DC - 1))
                        return ins
                    return f
                hu_last = A("pe", mk_hu(), waits=waits, inc=("pe", 1))
                gut_toks[n] = A("dve", lambda E, jc=jc, r0=r0, n=n: E.tensor_tensor(
                    gut(jc, r0, r0 + GR), pf[n % 4][:, 0:GR], gexp(jc),
                    op=ALU.mult),
                    waits=[hu_last, gx[jc], tok_p2b_q0, tok_p2b_q1],
                    inc=("dve", 1), chain=True)
        tok_p3_dve = S_.tok("dve")
        tok_p3_pe = S_.tok("pe")

        tok_wk = dma("gq", "w", pw[:, 0:DC * D].rearrange("p (dc n) -> p dc n", n=D),
                     wk.rearrange("(dc p) n -> p dc n", p=128),
                     waits=[tok_p2b_pe])

        kev = {}
        kmm = {}
        kdma = {}
        kev_of = {}
        kst_ids = {}
        ids0 = [ln_stats(xb[sb * 128:(sb + 1) * 128, :]) for sb in range(NSUB)]
        kev_of[0] = max(ln_tr(i, lambda dc, a, bb: xtc(0, dc, a, bb),
                              sb * 128) for sb, i in enumerate(ids0))
        for c in range(NCH):
            if c + 1 < NCH:
                kst_ids[c + 1] = [ln_stats(
                    xb[(c + 1) * CH + sb * 128:(c + 1) * CH + (sb + 1) * 128, :],
                    extra_waits=[kdma.get(c - 1)]) for sb in range(NSUB)]
            for kc in range(DC):
                waits = [kev_of[c], tok_wk]
                gn = c * DC + kc
                if gn >= 4:
                    waits.append(kev[gn - 4])

                def mk_kmm(c=c, kc=kc):
                    def f(E, _kk=None):
                        for dc in range(DC):
                            ins = E.matmul(pf[kc % 4][:, 0:CH],
                                           wchunk(dc, kc * 128, (kc + 1) * 128),
                                           xtc(c % 2, dc, 0, CH),
                                           start=(dc == 0), stop=(dc == DC - 1))
                        return ins
                    return f
                kmm[gn] = A("pe", mk_kmm(), waits=waits, inc=("pe", 1))
                w2 = [kmm[gn]]
                if c > 1:
                    w2.append(kdma[c - 2])
                kev[gn] = A("act", lambda E, c=c, kc=kc: E.activation(
                    out=stg(c % 2, kc, 0, CH), in_=pf[kc % 4][:, 0:CH],
                    func=AF.Identity,
                    bias=bks[:, kc:kc + 1]), waits=w2, inc=("act", 1), chain=True)
                if c + 1 < NCH and kc % LNQ == LNQ - 1:
                    sb = kc // LNQ
                    ev = ln_tr(kst_ids[c + 1][sb],
                               lambda dc, a, bb, c=c: xtc((c + 1) % 2, dc, a, bb),
                               sb * 128)
                    if sb == NSUB - 1:
                        kev_of[c + 1] = ev
            _kd = dma(
                "sy", f"kx{c % 2}",
                kt_scr.rearrange("(kc p) r -> p kc r", p=128)[:, :, c * CH:(c + 1) * CH],
                pa[:, STG_OFF[c % 2]: STG_OFF[c % 2] + DC * CH
                   ].rearrange("p (kc r) -> p kc r", r=CH),
                waits=[kev[c * DC + DC - 1]])
            t_xd = dma(
                "sy", f"kx{c % 2}",
                xnt_scr.rearrange("(dc p) r -> p dc r", p=128)[:, :, c * CH:(c + 1) * CH],
                pa[:, XTC_OFF[c % 2]: XTC_OFF[c % 2] + DC * CH
                   ].rearrange("p (dc r) -> p dc r", r=CH),
                waits=[kev_of[c]])
            kdma[c] = t_xd
        tok_p2a_ld0 = S_.tok("kx0")
        tok_p2a_ld1 = S_.tok("kx1")
        tok_p2a_pe = S_.tok("pe")

        tok_wv = dma("gq", "w", pw[:, 0:DC * D].rearrange("p (dc n) -> p dc n", n=D),
                     wv.rearrange("(dc p) n -> p dc n", p=128),
                     waits=[tok_p2b_pe, tok_p3_pe, tok_p2a_pe])
        tok_masks = dma("gq", "mk", pm[:, 0:NKS * QT], mk_i, waits=[tok_p3_pe])

        vev = {}
        vdma = {}
        vmm = {}
        vxl = {}
        for c in range(NCH):
            waits = [tok_p2a_ld0, tok_p2a_ld1, tok_p3_dve, tok_p2b_q0, tok_p2b_q1, tok_p2a_pe]
            if c > 1:
                waits.append(vmm[c - 2])
            vxl[c] = dma(
                "sy", f"vx{c % 2}",
                pa[:, XTC_OFF[c % 2]:XTC_OFF[c % 2] + DC * CH
                   ].rearrange("p (dc r) -> p dc r", r=CH),
                xnt_scr.rearrange("(dc p) r -> p dc r", p=128)[:, :, c * CH:(c + 1) * CH],
                waits=waits)
            for rs in range(NSUB):
                for vc in range(NVC):
                    n = rs * NVC + vc
                    gn = c * NSUB * NVC + n
                    waits = [vxl[c], tok_wv]
                    if gn >= 4:
                        waits.append(vev[gn - 4])

                    def mk_vmm(rs=rs, vc=vc, n=n, c=c):
                        def f(E):
                            for dc in range(DC):
                                ins = E.matmul(pf[n % 4][:, 0:512],
                                               xtc(c % 2, dc, rs * 128, (rs + 1) * 128),
                                               wchunk(dc, vc * 512, (vc + 1) * 512),
                                               start=(dc == 0), stop=(dc == DC - 1))
                            return ins
                        return f
                    vmm[c] = A("pe", mk_vmm(), waits=waits, inc=("pe", 1))
                    w2 = [vmm[c]]
                    if c > 1:
                        w2.append(vdma[c - 2])
                    vev[gn] = A("dve", lambda E, rs=rs, vc=vc, n=n, c=c: E.tensor_copy(
                        stgv(c % 2, rs, vc * 512, (vc + 1) * 512), pf[n % 4][:, 0:512]),
                        waits=w2, inc=("dve", 1), chain=True)
            vdma[c] = dma(
                "sy", f"vw{c % 2}",
                v_scr[c * CH:(c + 1) * CH, :].rearrange("(rs p) n -> p rs n", p=128),
                pa[:, STG_OFF[c % 2]: STG_OFF[c % 2] + NSUB * D
                   ].rearrange("p (rs n) -> p rs n", n=D),
                waits=[vev[c * NSUB * NVC + NSUB * NVC - 1]])
        tok_p2c_ld0 = S_.tok("vw0")
        tok_p2c_ld1 = S_.tok("vw1")
        tok_p2c_pe = S_.tok("pe")

        tok_vflat = dma("gq", "vf", pv[:, 0:JC * D].rearrange("p (jc n) -> p jc n", n=D),
                        v_fl.rearrange("(jc p) n -> p jc n", p=128),
                        waits=[tok_p3_dve, ("act", S_.n["act"]), ("dve", S_.n["dve"])])

        t_o0 = A("dve", lambda E: E.memset(vh_view(0)[:, :, 128:129], 1.0),
                 waits=[tok_p2c_pe], inc=("dve", 1), chain=True)
        tok_vones = A("dve", lambda E: E.memset(vh_view(1)[:, :, 128:129], 1.0),
                      inc=("dve", 1), chain=True)

        tok_pre_attn_dve = ("dve", S_.n["dve"])

        def pso_(j):
            return pf[j // 2][:, (j % 2) * 256:(j % 2) * 256 + 129]

        def pso_d(j):
            return pf[j // 2][:, (j % 2) * 256:(j % 2) * 256 + 128]

        def pso_l(j):
            return pf[j // 2][:, (j % 2) * 256 + 128:(j % 2) * 256 + 129]

        def scb_(k, q0, q1):
            P = (k // 2) % 2
            o = (k % 2) * QT
            return scp[P][:, o + q0: o + q1]

        def ptp_(k, q0, q1):
            P = (k // 2) % 2
            o = (k % 2) * QT
            return ptp[P][:, o + q0: o + q1]

        head_pe = {}
        head_ev = {}
        head_ld = {}
        exp_toks = {}
        mask_toks = {}
        sweep_ev = {}
        nvd = (NSUBK + 7) // 8
        for h in range(H):
            beta = h % 2
            waits = [tok_p2a_ld0, tok_p2a_ld1, tok_p2b_q0, tok_p2b_q1, tok_p2c_ld0, tok_p2c_ld1, tok_vones, tok_p2a_pe,
                     S_.tok("pe") if h == 0 else None]
            if h >= 2:
                waits += [head_pe[h - 2], head_ev[h - 2]]
            t = dma("sy", f"h{beta}", kth(beta, 0, T), kt_scr[h * 128:(h + 1) * 128, :],
                    waits=waits)
            for d in range(nvd):
                s0, s1 = d * 8, min((d + 1) * 8, NSUBK)
                t = dma("sy", f"h{beta}", vh_view(beta)[:, s0:s1, 0:128],
                        v_scr[s0 * 128:s1 * 128, h * 128:(h + 1) * 128
                              ].rearrange("(s p) c -> p s c", p=128))
            RPH = OWN // H
            head_ld[h] = t = dma(
                "sy", f"h{beta}",
                xs[beta][:, 0:DC * 128].rearrange("p (qc r) -> p qc r", r=128),
                qt_scr.rearrange("(qc p) r -> p qc r", p=128)[:, :, h * RPH:(h + 1) * RPH])

            for sw in range(1, NSW + 1):
                nk = sw * NKS
                pvq = []
                for k in range(nk):
                    if k >= 4:
                        war = exp_toks[(h, sw, k - 4)]
                    elif sw > 1:
                        war = exp_toks[(h, sw - 1, (sw - 1) * NKS - 4 + k)]
                    elif h > 0:
                        war = exp_toks[(h - 1, NSW, NSW * NKS - 4 + k)]
                    else:
                        war = None
                    sw_waits = [head_ld[h], war]
                    if h == 0 and sw == 1 and k in (0, 1, 2, 3):
                        sw_waits.append(tok_pre_attn_dve)
                    JJ = QT // H
                    t_sc = A("pe", lambda E, b=beta, k=k, sw=sw, JJ=JJ: E.matmul(
                        scb_(k, 0, QT), kth(b, k * 128, (k + 1) * 128),
                        xs[b][:, 0:DC * 128].rearrange(
                            "p (qc r) -> p r qc", r=128)[:, (sw - 1) * JJ: sw * JJ, :],
                        start=True, stop=True),
                        waits=sw_waits, inc=("pe", 1))
                    if k % 2 == 1:
                        ew = [t_sc]
                        for km in (k - 4, k - 3):
                            if km >= nk - NKS and km >= 0:
                                ew.append(mask_toks[(h, sw, km)])
                            elif km < 0 and (sw > 1 or h > 0):
                                psw = (h, sw - 1) if sw > 1 else (h - 1, NSW)
                                kk2 = psw[1] * NKS + km
                                if (psw[0], psw[1], kk2) in mask_toks:
                                    ew.append(mask_toks[(psw[0], psw[1], kk2)])
                        tex = A("act", lambda E, k=k: E.activation(
                            out=ptp_(k - 1, 0, 2 * QT), in_=scb_(k - 1, 0, 2 * QT),
                            func=AF.Exp, bias=cst[:], scale=SCALE),
                            waits=ew + [tok_cst], inc=("act", 1), chain=True)
                        exp_toks[(h, sw, k - 1)] = tex
                        exp_toks[(h, sw, k)] = tex
                        for km in (k - 1, k):
                            if km >= nk - NKS:
                                j = km - (nk - NKS)
                                mask_toks[(h, sw, km)] = A(
                                    "dve", lambda E, km=km, j=j: E.tensor_tensor(
                                        ptp_(km, 0, QT), ptp_(km, 0, QT),
                                        mask_j(j), op=ALU.mult),
                                    waits=[tex, tok_masks],
                                    inc=("dve", 1), chain=True)
                        for km in (k - 1, k):
                            if km >= nk - NKS:
                                pvw = [mask_toks[(h, sw, km)]]
                            else:
                                pvw = [tex]
                            if km == 0:
                                if sw > 1:
                                    pvw.append(sweep_ev[(h, sw - 1)])
                                elif h > 0:
                                    pvw.append(sweep_ev[(h - 1, NSW)])
                                else:
                                    pvw.append(tok_p3_dve)

                            def mk_pv(b=beta, km=km, nk=nk):
                                def f(E):
                                    for j in range(NQS):
                                        ins = E.matmul(pso_(j),
                                                       ptp_(km, j * 128, (j + 1) * 128),
                                                       vh(b, km),
                                                       start=(km == 0 and j % 2 == 0),
                                                       stop=(km == nk - 1 and
                                                             (j % 2 == 1 or j == NQS - 1)),
                                                       skip_group_check=True)
                                    return ins
                                return f
                            pvq.append((mk_pv(), pvw))
                    while len(pvq) > 2:
                        A("pe", *pvq.pop(0))
                tpv = None
                for i2, (fn2, w2_) in enumerate(pvq):
                    tpv = A("pe", fn2, waits=w2_,
                            inc=("pe", 1) if i2 == len(pvq) - 1 else None)
                head_pe[h] = tpv
                sweep_pe = tpv
                t_ev = None
                for j in range(NQS):
                    rt = ((sw - 1) * QT + j * 128) // 128
                    A("dve", lambda E, j=j: E.reciprocal(w1(40 + j),
                                                         pso_l(j)),
                      waits=[sweep_pe], inc=("dve", 1), chain=True)
                    t_ev = A("dve", lambda E, j=j, rt=rt, h=h: E.tensor_scalar(
                        ac(rt, h * 128, (h + 1) * 128), pso_d(j),
                        w1(40 + j), None, op0=ALU.mult),
                        inc=("dve", 1), chain=True)
                sweep_ev[(h, sw)] = t_ev
            head_ev[h] = sweep_ev[(h, NSW)]
        tok_p4_dve = S_.tok("dve")
        tok_p4_pe = S_.tok("pe")
        tok_p4_act = S_.tok("act")

        tok_wo = dma("gq", "w", pw[:, 0:DC * D].rearrange("p (dc n) -> p dc n", n=D),
                     wo.rearrange("(dc p) n -> p dc n", p=128),
                     waits=[tok_p4_pe])

        pairs = [(a, b) for a in range(ORT) for b in range(a, DC)]
        p5_ev = {}
        for idx, (a, b) in enumerate(pairs):
            par = idx % 2
            n_tr = 1 if a == b else 2
            waits = [tok_p4_dve, tok_consts]
            if idx >= 2:
                waits.append(p5_ev[idx - 2])
            else:
                waits.append(tok_p4_act)

            def mk_tr(a=a, b=b, par=par, n_tr=n_tr):
                def f(E):
                    ins = E.transpose(pbv(par, 0, 128), ac(a, b * 128, (b + 1) * 128), ids[:])
                    if n_tr == 2:
                        ins = E.transpose(pbv(par, 128, 256), ac(b, a * 128, (a + 1) * 128), ids[:])
                    return ins
                return f
            t_tr = A("pe", mk_tr(), waits=waits, inc=("pe", n_tr))
            t_e1 = A("dve", lambda E, a=a, b=b, par=par: E.tensor_scalar(
                acT(b, a * 128, (a + 1) * 128), pbv(par, 0, 128),
                bvs[:, b:b + 1], None, op0=ALU.add),
                waits=[t_tr], inc=("dve", 1), chain=True)
            if a != b:
                t_e1 = A("dve", lambda E, a=a, b=b, par=par: E.tensor_scalar(
                    acT(a, b * 128, (b + 1) * 128), pbv(par, 128, 256),
                    bvs[:, a:a + 1], None, op0=ALU.add),
                    inc=("dve", 1), chain=True)
            p5_ev[idx] = t_e1
        tok_p5_dve = S_.tok("dve")

        tok_bo = dma("sy", "bo", bo_b(0, D), bo_i,
                     waits=[tok_p4_dve, tok_p4_act])

        e1t = {}
        e2t = {}
        e3t = {}
        stt = {}
        xlt = {}
        for rt in range(ORT):
            for cc in range(NVC):
                n = rt * NVC + cc
                xw = [tok_bo]
                if n >= 2:
                    xw.append(e3t[n - 2])
                xlt[n] = dma("sy", f"px{n % 2}", p6x(n % 2),
                             xo[rt * 128:(rt + 1) * 128, cc * 512:(cc + 1) * 512],
                             waits=xw)
                mw = [tok_p3_dve, tok_vflat, tok_wo, tok_p5_dve]
                if n >= 2:
                    mw += [e1t[n - 2], e2t[n - 2]]

                def mk_f(rt=rt, cc=cc, n=n):
                    def f(E):
                        for jc in range(JC):
                            ins = E.matmul(pf[n % 2][:, 0:512],
                                           gut(jc, rt * 128, (rt + 1) * 128),
                                           vflat(jc, cc * 512, (cc + 1) * 512),
                                           start=(jc == 0), stop=(jc == JC - 1))
                        return ins
                    return f
                t_f = A("pe", mk_f(), waits=mw, inc=("pe", 1))

                def mk_o(rt=rt, cc=cc, n=n):
                    def f(E):
                        for fc in range(DC):
                            ins = E.matmul(pf[2 + n % 2][:, 0:512],
                                           acT(fc, rt * 128, (rt + 1) * 128),
                                           wchunk(fc, cc * 512, (cc + 1) * 512),
                                           start=(fc == 0), stop=(fc == DC - 1))
                        return ins
                    return f
                t_o = A("pe", mk_o(), inc=("pe", 1))

                w1_ = [t_f, tok_p3_dve]
                if n >= 3:
                    w1_.append(stt[n - 3])
                e1t[n] = A("act", lambda E, rt=rt, n=n: E.activation(
                    out=p6t(n % 3), in_=pf[n % 2][:, 0:512], func=AF.Copy,
                    scale=gps[:, rt:rt + 1]), waits=w1_, inc=("act", 1),
                    chain=True)
                e2t[n] = A("dve", lambda E, n=n: E.tensor_tensor(
                    p6t(n % 3), p6t(n % 3), pf[2 + n % 2][:, 0:512], op=ALU.add),
                    waits=[e1t[n], t_o], inc=("dve", 1), chain=True)
                A("dve", lambda E, n=n: E.tensor_tensor(
                    p6t(n % 3), p6t(n % 3), p6x(n % 2), op=ALU.add),
                  waits=[xlt[n]], inc=("dve", 1), chain=True)
                e3t[n] = A("dve", lambda E, n=n, cc=cc: E.tensor_tensor(
                    p6t(n % 3), p6t(n % 3), bo_b(cc * 512, (cc + 1) * 512),
                    op=ALU.add), waits=[tok_bo], inc=("dve", 1), chain=True)
                stt[n] = dma("sy", f"o{n % 3}", out[rt * 128:(rt + 1) * 128, cc * 512:(cc + 1) * 512],
                             p6t(n % 3), waits=[e3t[n]])

        final = dict(S_.n)
        S_.add("gq", None, waits=[(s, v) for s, v in final.items()])

        semnames = set(S_.n.keys())
        for ename in S_.ENG:
            for fn, waits, inc in S_.ops[ename]:
                for sem, val in waits:
                    semnames.add(sem)
        sems = {name: e(nc.semaphore("s_" + name)) for name in sorted(semnames)}
        block = e(nc.Block())

        def emit_wm(engobj, ename):
            water = {}
            for fn, waits, inc in S_.ops[ename]:
                for sem, val in waits:
                    if water.get(sem, 0) >= val:
                        continue
                    engobj.wait_ge(sems[sem], val)
                    water[sem] = val
                if fn is None:
                    continue
                ins = fn(engobj)
                if inc is not None:
                    ins.then_inc(sems[inc[0]], inc[1])

        @block.sync
        def _(E):
            emit_wm(E, "sy")

        @block.gpsimd
        def _(E):
            emit_wm(E, "gq")

        @block.tensor
        def _(E):
            emit_wm(E, "pe")

        @block.vector
        def _(E):
            emit_wm(E, "dve")

        @block.scalar
        def _(E):
            emit_wm(E, "act")

    return nc


def host_inputs(cfg, core, ins):
    D, T = cfg["D"], cfg["T"]
    OWN, QT, NSW, NKS, DC, JC, S = (cfg["OWN"], cfg["QT"], cfg["NSW"],
                                    cfg["NKS"], cfg["DC"], cfg["JC"], cfg["S"])
    b, g = core // 4, core % 4
    x = ins["L0_attn_x"]
    xb = np.ascontiguousarray(x[b], dtype=np.float32)
    blocks = [4 * s + g for s in range(NSW)]
    xo = np.ascontiguousarray(
        np.concatenate([x[b][blk * QT:(blk + 1) * QT] for blk in blocks], axis=0),
        dtype=np.float32)
    H = cfg["H"]
    TPH, QPH = cfg["T"] // H, QT // H
    qrows = []
    for h in range(H):
        for s in range(NSW):
            base = h * TPH + (QT * (4 * s + g)) // H
            qrows.extend(range(base, base + QPH))
    xq = np.ascontiguousarray(x[b][np.array(qrows)], dtype=np.float32)
    masks = np.zeros((128, NKS * QT), dtype=ml_dtypes.bfloat16)
    kl = np.arange(128)[:, None]
    q = np.arange(QT)[None, :]
    for j in range(NKS):
        masks[:, j * QT:(j + 1) * QT] = (j * 128 + kl <= g * QT + q).astype(
            ml_dtypes.bfloat16)
    e_mat = np.zeros((S, 128 * JC), dtype=ml_dtypes.bfloat16)
    for jj in range(128 * JC):
        e_mat[jj // 16, jj] = 1
    U = np.asarray(ins["L0_ffn_U"], np.float32)
    m = {
        "xb": xb,
        "xo": xo,
        "xq": xq,
        "wq": ins["L0_attn_Wq"], "wk": ins["L0_attn_Wk"],
        "wv": ins["L0_attn_Wv"], "wo": ins["L0_attn_Wo"],
        "bq": np.ascontiguousarray(np.asarray(ins["L0_attn_bq"], np.float32).reshape(DC, 128).T),
        "bk": np.ascontiguousarray(np.asarray(ins["L0_attn_bk"], np.float32).reshape(DC, 128).T),
        "bv": np.ascontiguousarray(np.asarray(ins["L0_attn_bv"], np.float32).reshape(DC, 128).T),
        "bo_bc": np.ascontiguousarray(np.broadcast_to(
            np.asarray(ins["L0_attn_bo"], np.float32).reshape(1, D), (128, D))),
        "wr_t": np.ascontiguousarray(np.asarray(ins["L0_ffn_Wr"], np.float32).T),
        "u_flat": np.ascontiguousarray(U.transpose(1, 0, 2).reshape(D, -1)),
        "v_flat": np.ascontiguousarray(np.asarray(ins["L0_ffn_V"], np.float32).reshape(-1, D)),
        "attr_t": np.ascontiguousarray(np.asarray(ins["L0_ffn_attr"], np.float32).T),
        "wa_t": np.ascontiguousarray(np.asarray(ins["L0_ffn_Wa"], np.float32).reshape(1, S).T),
        "masks": masks,
        "ident": np.eye(128, dtype=ml_dtypes.bfloat16),
        "ones_f": np.ones((1, 128), np.float32),
        "e_mat": e_mat,
    }
    return {k: np.ascontiguousarray(np.asarray(v)) for k, v in m.items()}


def assemble(cfg, results):
    D, T, QT, NSW = cfg["D"], cfg["T"], cfg["QT"], cfg["NSW"]
    full = np.zeros((2, T, D), np.float32)
    for core in range(8):
        b, g = core // 4, core % 4
        o = np.asarray(results[core]["out"], np.float32)
        for s in range(NSW):
            blk = 4 * s + g
            full[b, blk * QT:(blk + 1) * QT] = o[s * QT:(s + 1) * QT]
    return full


def kernel(**inputs):
    cfg = mkcfg(FULL)
    nc = build_nc(cfg)
    in_maps = [host_inputs(cfg, c, inputs) for c in range(8)]
    res = run_bass_kernel_spmd(nc, in_maps, core_ids=list(range(8)))
    return assemble(cfg, res.results)


# version 14
# speedup vs baseline: 1.4942x; 1.1778x over previous
import math
from contextlib import ExitStack

import numpy as np
import ml_dtypes
import concourse.bass as bass
import concourse.mybir as mybir
from concourse.bass_utils import run_bass_kernel_spmd

F32 = mybir.dt.float32
BF16 = mybir.dt.bfloat16
AF = mybir.ActivationFunctionType
ALU = mybir.AluOpType
AX = mybir.AxisListType

FULL = dict(D=2048, T=8192, H=16)
TINY = dict(D=512, T=2048, H=4)

C_SHIFT = 12.0


def mkcfg(base):
    c = dict(base)
    D, T, H = c["D"], c["T"], c["H"]
    c["DH"] = 128
    assert H * c["DH"] == D
    c["DC"] = D // 128
    c["OWN"] = T // 4
    assert c["OWN"] == D
    c["QT"] = T // 16
    assert c["QT"] % 128 == 0
    c["NQS"] = c["QT"] // 128
    c["NSW"] = 4
    c["NKS"] = T // 512
    c["CH"] = c["QT"]
    c["NCH"] = T // c["CH"]
    c["NSUB"] = c["CH"] // 128
    c["S"], c["R"] = 32, 16
    c["J"] = c["S"] * c["R"]
    c["JC"] = c["J"] // 128
    c["GR"] = min(512, c["OWN"])
    c["NG"] = c["OWN"] // c["GR"]
    c["ORT"] = c["OWN"] // 128
    assert c["ORT"] == c["DC"]
    c["SCALE"] = 1.0 / math.sqrt(c["DH"])
    return c


class Sched:
    ENG = ("sy", "gq", "pe", "act", "dve")

    def __init__(self):
        self.ops = {e: [] for e in self.ENG}
        from collections import defaultdict
        self.n = defaultdict(int)
        for s in ("pe", "act", "dve"):
            self.n[s] = 0
        self.chain_last = {"act": 0, "dve": 0}

    def add(self, eng, fn, waits=(), inc=None, chain=False):
        w = [x for x in waits if x is not None and x[1] > 0]
        if chain and eng in self.chain_last:
            if self.chain_last[eng] > 0:
                w.append((eng, self.chain_last[eng]))
        tok = None
        if inc is not None:
            sem, k = inc
            self.n[sem] += k
            tok = (sem, self.n[sem])
            if eng in self.chain_last and sem == eng:
                self.chain_last[eng] = self.n[sem]
        self.ops[eng].append((fn, w, inc))
        return tok

    def tok(self, sem):
        return (sem, self.n[sem])

    def barrier(self):
        snap = dict(self.n)
        for e in self.ENG:
            self.add(e, None, waits=[(s, v) for s, v in snap.items()])
        return snap


def build_nc(cfg):
    D, T, H, DH = cfg["D"], cfg["T"], cfg["H"], cfg["DH"]
    DC, OWN, QT, NQS = cfg["DC"], cfg["OWN"], cfg["QT"], cfg["NQS"]
    NSW, NKS, CH, NCH, NSUB = cfg["NSW"], cfg["NKS"], cfg["CH"], cfg["NCH"], cfg["NSUB"]
    S, J, JC = cfg["S"], cfg["J"], cfg["JC"]
    GR, NG, ORT, SCALE = cfg["GR"], cfg["NG"], cfg["ORT"], cfg["SCALE"]
    NVC = D // 512
    NSUBK = T // 128

    nc = bass.Bass("TRN2", target_bir_lowering=False)

    xb = nc.dram_tensor("xb", [T, D], F32, kind="ExternalInput").ap()
    xo = nc.dram_tensor("xo", [OWN, D], F32, kind="ExternalInput").ap()
    xq = nc.dram_tensor("xq", [OWN, D], F32, kind="ExternalInput").ap()
    wq = nc.dram_tensor("wq", [D, D], F32, kind="ExternalInput").ap()
    wk = nc.dram_tensor("wk", [D, D], F32, kind="ExternalInput").ap()
    wv = nc.dram_tensor("wv", [D, D], F32, kind="ExternalInput").ap()
    wo = nc.dram_tensor("wo", [D, D], F32, kind="ExternalInput").ap()
    bq_i = nc.dram_tensor("bq", [128, DC], F32, kind="ExternalInput").ap()
    bk_i = nc.dram_tensor("bk", [128, DC], F32, kind="ExternalInput").ap()
    bv_i = nc.dram_tensor("bv", [128, DC], F32, kind="ExternalInput").ap()
    bo_i = nc.dram_tensor("bo_bc", [128, D], F32, kind="ExternalInput").ap()
    wr_i = nc.dram_tensor("wr_t", [D, S], F32, kind="ExternalInput").ap()
    u_fl = nc.dram_tensor("u_flat", [D, J], F32, kind="ExternalInput").ap()
    v_fl = nc.dram_tensor("v_flat", [J, D], F32, kind="ExternalInput").ap()
    at_i = nc.dram_tensor("attr_t", [S, S], F32, kind="ExternalInput").ap()
    wa_i = nc.dram_tensor("wa_t", [S, 1], F32, kind="ExternalInput").ap()
    mk_i = nc.dram_tensor("masks", [128, NKS * QT], BF16, kind="ExternalInput").ap()
    id_i = nc.dram_tensor("ident", [128, 128], BF16, kind="ExternalInput").ap()
    of_i = nc.dram_tensor("ones_f", [1, 128], F32, kind="ExternalInput").ap()
    em_i = nc.dram_tensor("e_mat", [S, 128 * JC], BF16, kind="ExternalInput").ap()

    out = nc.dram_tensor("out", [OWN, D], F32, kind="ExternalOutput").ap()

    kt_scr = nc.dram_tensor("kt_scr", [D, T], BF16, kind="Internal").ap()
    v_scr = nc.dram_tensor("v_scr", [T, D], BF16, kind="Internal").ap()
    qu_scr = nc.dram_tensor("qu_scr", [128, (D // 128) * OWN], BF16, kind="Internal").ap()
    xnt_scr = nc.dram_tensor("xnt_scr", [D, T], BF16, kind="Internal").ap()

    with ExitStack() as ctx:
        e = ctx.enter_context

        pa = e(nc.sbuf_tensor("pa", [128, DC * OWN], BF16))
        PW_COLS = max(DC * D, 2 * T + 2 * 129 * NSUBK)
        pw = e(nc.sbuf_tensor("pw", [128, PW_COLS], BF16))
        pm = e(nc.sbuf_tensor("pm", [128, max(DC * J, NKS * QT, 2 * (D + 1024 + 3 * 512))], BF16))
        pv = e(nc.sbuf_tensor("pv", [128, max(JC * D, JC * GR + OWN + 2 * D)], BF16))
        pg = e(nc.sbuf_tensor("pg", [128, JC * OWN], BF16))
        xs = [e(nc.sbuf_tensor(f"xs{i}", [128, max(D, OWN)], BF16)) for i in range(4)]
        ptp = [e(nc.sbuf_tensor(f"ptp{i}", [128, 2 * QT], BF16)) for i in range(2)]
        pt = [ptp[0][:, 0:QT], ptp[0][:, QT:2 * QT], ptp[1][:, 0:QT]]
        wrk = e(nc.sbuf_tensor("wrk", [128, 64], F32))
        scw = e(nc.sbuf_tensor("scw", [128, 8 * S], F32))
        gps = e(nc.sbuf_tensor("gps", [128, ORT], F32))
        bqs = e(nc.sbuf_tensor("bqs", [128, DC], F32))
        bks = e(nc.sbuf_tensor("bks", [128, DC], F32))
        bvs = e(nc.sbuf_tensor("bvs", [128, DC], F32))
        wrs = e(nc.sbuf_tensor("wrs", [128, DC * S], BF16))
        ems = e(nc.sbuf_tensor("ems", [S, 128 * JC], BF16))
        ids = e(nc.sbuf_tensor("ids", [128, 128], BF16))
        onf = e(nc.sbuf_tensor("onf", [1, 128], F32))
        ats = e(nc.sbuf_tensor("ats", [S, S], F32))
        was = e(nc.sbuf_tensor("was", [S, 1], F32))
        awb = e(nc.sbuf_tensor("awb", [128, S], F32))
        aws = e(nc.sbuf_tensor("aws", [1, S], F32))
        cst = e(nc.sbuf_tensor("cst", [128, 1], F32))

        pf = [e(nc.psum_tensor(f"pf{i}", [128, 512], F32)) for i in range(4)]
        scp = [e(nc.psum_tensor(f"scp{i}", [128, 1024], F32)) for i in range(2)]
        scp0_bf = scp[0].bitcast(BF16)

        def pbv(i, a, b):
            return scp0_bf[:, i * 1024 + a: i * 1024 + b]

        block_holder = []

        S_ = Sched()

        def xnt_own(dc, r0, r1):
            return pa[:, dc * OWN + r0: dc * OWN + r1]

        XTC_OFF = [0, 2 * DC * CH]
        STG_OFF = [DC * CH, 3 * DC * CH]

        def xtc(buf, dc, r0, r1):
            o = XTC_OFF[buf]
            return pa[:, o + dc * CH + r0: o + dc * CH + r1]

        def stg(buf, kc, c0, c1):
            o = STG_OFF[buf]
            return pa[:, o + kc * CH + c0: o + kc * CH + c1]

        def stgv(buf, rs, c0, c1):
            o = STG_OFF[buf]
            return pa[:, o + rs * D + c0: o + rs * D + c1]

        def ac(rt, f0, f1):
            return pa[:, rt * D + f0: rt * D + f1]

        def acT(fc, r0, r1):
            return pa[:, fc * OWN + r0: fc * OWN + r1]

        def wchunk(dc, n0, n1):
            return pw[:, dc * D + n0: dc * D + n1]

        def kth(b, k0, k1):
            return pw[:, b * T + k0: b * T + k1]

        VH0 = 2 * T

        def vh(b, sub):
            o = VH0 + b * (129 * NSUBK) + sub * 129
            return pw[:, o: o + 129]

        def vh_view(b):
            o = VH0 + b * (129 * NSUBK)
            return pw[:, o: o + NSUBK * 129].rearrange("p (s c) -> p s c", c=129)

        def uchunk(dc, j0, j1):
            return pm[:, dc * J + j0: dc * J + j1]

        def mask_j(j):
            return pm[:, j * QT:(j + 1) * QT]

        pmf = pm.bitcast(F32)

        def bo_b(c0, c1):
            return pmf[:, c0:c1]

        def p6x(b):
            return pmf[:, D + b * 512: D + (b + 1) * 512]

        def p6t(i):
            return pmf[:, D + 1024 + i * 512: D + 1024 + (i + 1) * 512]

        def vflat(jc, n0, n1):
            return pv[:, jc * D + n0: jc * D + n1]

        def gexp(jc):
            return pv[:, jc * GR:(jc + 1) * GR]

        GT0 = JC * GR

        def gT(r0, r1):
            return pv[0:S, GT0 + r0: GT0 + r1]

        def gut(jc, r0, r1):
            return pg[:, jc * OWN + r0: jc * OWN + r1]

        PVJ = JC * GR + OWN
        sqj = pv[:, PVJ: PVJ + D]
        cpj = pv[:, PVJ + D: PVJ + 2 * D]
        qst = pg[:, 0: 2 * GR]

        def w1(i):
            return wrk[:, i:i + 1]

        A = S_.add

        def dma(eng, sem, outap, inap, waits=()):
            return A(eng, lambda E, o=outap, i=inap: E.dma_start(out=o, in_=i),
                     waits=waits, inc=(sem, 16))

        tok_consts = None
        for o, i in [(ids[:], id_i), (onf[:], of_i), (bqs[:], bq_i), (bks[:], bk_i),
                     (bvs[:], bv_i), (ats[:], at_i), (was[:], wa_i),
                     (ems[:], em_i)]:
            tok_consts = dma("sy", "cst", o, i)
        tok_wrs = dma("gq", "wrs", wrs.rearrange("p (dc s) -> p dc s", s=S),
                      wr_i.rearrange("(dc p) s -> p dc s", p=128))

        tok_wq = dma("gq", "w", pw[:, 0:DC * D].rearrange("p (dc n) -> p dc n", n=D),
                     wq.rearrange("(dc p) n -> p dc n", p=128))
        tok_u = dma("gq", "u", pm[:, 0:DC * J].rearrange("p (dc j) -> p dc j", j=J),
                    u_fl.rearrange("(dc p) j -> p dc j", p=128))
        tok_cst = A("dve", lambda E: E.memset(cst[:], -C_SHIFT),
                    inc=("dve", 1), chain=True)

        ln_i = [0]
        ln_trs = {}
        ln_norm = {}

        def ln_stats(src_rows, extra_waits=()):
            i = ln_i[0]
            ln_i[0] += 1
            beta = i % 4
            b2 = i % 2
            xbuf = xs[beta]
            t_x = dma("gq", f"x{beta}", xbuf[:, 0:D], src_rows,
                      waits=[ln_trs.get(i - 4)] + list(extra_waits))
            t_sq = A("act", lambda E, xb_=xbuf, b=b2: E.activation(
                out=sqj, in_=xb_[:, 0:D], func=AF.Square,
                accum_out=wrk[:, 8 + b:9 + b]), waits=[t_x], inc=("act", 1), chain=True)
            t_sm = A("dve", lambda E, xb_=xbuf, b=b2: E.tensor_scalar(
                cpj, xb_[:, 0:D], 1.0, None, op0=ALU.mult, op1=ALU.add,
                accum_out=wrk[:, 10 + b:11 + b]), waits=[t_x], inc=("dve", 1), chain=True)
            A("dve", lambda E, b=b2: E.tensor_scalar(
                w1(12 + b), w1(10 + b), 1.0 / D, None, op0=ALU.mult),
              waits=[t_sm], inc=("dve", 1), chain=True)
            A("dve", lambda E, b=b2: E.tensor_scalar(
                w1(14 + b), w1(8 + b), 1.0 / D, None, op0=ALU.mult),
              waits=[t_sq], inc=("dve", 1), chain=True)
            A("dve", lambda E, b=b2: E.tensor_tensor(
                w1(16 + b), w1(12 + b), w1(12 + b), op=ALU.mult),
              inc=("dve", 1), chain=True)
            A("dve", lambda E, b=b2: E.tensor_tensor(
                w1(18 + b), w1(14 + b), w1(16 + b), op=ALU.subtract),
                inc=("dve", 1), chain=True)
            t_var = A("dve", lambda E, b=b2: E.tensor_scalar(
                w1(18 + b), w1(18 + b), 1e-5, None, op0=ALU.add),
                inc=("dve", 1), chain=True)
            A("act", lambda E, b=b2: E.activation(
                out=w1(20 + b), in_=w1(18 + b), func=AF.Ln),
              waits=[t_var], inc=("act", 1), chain=True)
            t_rstd = A("act", lambda E, b=b2: E.activation(
                out=w1(22 + b), in_=w1(20 + b), func=AF.Exp, scale=-0.5),
                inc=("act", 1), chain=True)
            ln_norm[i] = A("dve", lambda E, xb_=xbuf, b=b2: E.tensor_scalar(
                xb_[:, 0:D], xb_[:, 0:D], w1(12 + b), w1(22 + b),
                op0=ALU.subtract, op1=ALU.mult),
                waits=[t_rstd], inc=("dve", 1), chain=True)
            return i

        def ln_tr(i, dest_fn, dest_r0):
            beta = i % 4
            xbuf = xs[beta]
            t_norm = ln_norm[i]
            ev_toks = []
            tr_toks = []
            for dc in range(DC):
                war = ev_toks[dc - 2] if dc >= 2 else ln_tr.last_ev.get(dc % 2)
                t_tr = A("pe", lambda E, xb_=xbuf, dc=dc: E.transpose(
                    pbv(dc % 2, 0, 128), xb_[:, dc * 128:(dc + 1) * 128], ids[:]),
                    waits=[t_norm, war, tok_consts], inc=("pe", 1))
                tr_toks.append(t_tr)
                t_ev = A("dve", lambda E, dc=dc, r0=dest_r0, df=dest_fn: E.tensor_copy(
                    df(dc, r0, r0 + 128), pbv(dc % 2, 0, 128)),
                    waits=[t_tr], inc=("dve", 1), chain=True)
                ev_toks.append(t_ev)
            ln_tr.last_ev = {0: ev_toks[-2], 1: ev_toks[-1]}
            ln_trs[i] = tr_toks[-1]
            return ev_toks[-1]

        ln_tr.last_ev = {}

        def ln_subtile(src_rows, dest_fn, dest_r0, extra_waits=()):
            i = ln_stats(src_rows, extra_waits)
            ev = ln_tr(i, dest_fn, dest_r0)
            return ev, ln_trs[i]

        q_mm_last = None
        qev = {}
        qld = {}
        NCH0 = OWN // CH
        LNQ = DC // NSUB
        ev_of = {}
        st_ids = {}
        qdma = {}
        ids0 = [ln_stats(xq[sb * 128:(sb + 1) * 128, :]) for sb in range(NSUB)]
        ev_of[0] = max(ln_tr(i, lambda dc, a, bb: xtc(0, dc, a, bb),
                             sb * 128) for sb, i in enumerate(ids0))
        for c in range(NCH0):
            if c + 1 < NCH0:
                st_ids[c + 1] = [ln_stats(
                    xq[(c + 1) * CH + sb * 128:(c + 1) * CH + (sb + 1) * 128, :])
                    for sb in range(NSUB)]
            for qc in range(DC):
                n = c * DC + qc
                waits = [ev_of[c], tok_wq]
                if n >= 4:
                    waits.append(qev[n - 4])

                def mk_qmm(qc=qc, n=n, c=c):
                    def f(E):
                        for dc in range(DC):
                            ins = E.matmul(pf[n % 4][:, 0:CH],
                                           wchunk(dc, qc * 128, (qc + 1) * 128),
                                           xtc(c % 2, dc, 0, CH),
                                           start=(dc == 0), stop=(dc == DC - 1))
                        return ins
                    return f
                q_mm_last = A("pe", mk_qmm(), waits=waits, inc=("pe", 1))
                w2 = [q_mm_last]
                if c > 1 and qc == 0:
                    w2.append(qdma[c - 2])
                qev[n] = A("act", lambda E, qc=qc, n=n, c=c: E.activation(
                    out=pa[:, STG_OFF[c % 2]: STG_OFF[c % 2] + DC * CH
                           ].rearrange("p (c2 r) -> p c2 r", r=DC)[:, :, qc:qc + 1],
                    in_=pf[n % 4][:, 0:CH], func=AF.Identity,
                    bias=bqs[:, qc:qc + 1]), waits=w2, inc=("act", 1), chain=True)
                if c + 1 < NCH0 and qc % LNQ == LNQ - 1:
                    sb = qc // LNQ
                    ev_last, _ = ln_subtile(
                        xq[(c + 1) * CH + sb * 128:(c + 1) * CH + (sb + 1) * 128, :],
                        lambda dc, a, bb, c=c: xtc((c + 1) % 2, dc, a, bb),
                        sb * 128) if False else (None, None)
                    ev = ln_tr(st_ids[c + 1][sb],
                               lambda dc, a, bb, c=c: xtc((c + 1) % 2, dc, a, bb),
                               sb * 128)
                    if sb == NSUB - 1:
                        ev_of[c + 1] = ev
            qdma[c] = dma(
                "sy", f"q{c % 2}",
                qu_scr[:, c * DC * CH:(c + 1) * DC * CH],
                pa[:, STG_OFF[c % 2]: STG_OFF[c % 2] + DC * CH],
                waits=[qev[c * DC + DC - 1]])
        tok_p2b_pe = S_.tok("pe")
        tok_p2b_q0 = S_.tok("q0")
        tok_p2b_q1 = S_.tok("q1")

        S_.add("dve", None, waits=[("q0", S_.n["q0"]), ("q1", S_.n["q1"])])
        S_.add("pe", None, waits=[("q0", S_.n["q0"]), ("q1", S_.n["q1"])])
        for i in range(OWN // 128):
            ev, tr = ln_subtile(xo[i * 128:(i + 1) * 128, :], xnt_own, i * 128)
        tok_p1 = S_.tok("dve")

        t1 = A("pe", lambda E: E.matmul(scp[0][0:1, 0:S], was[:], ats[:],
                                        start=True, stop=True),
               waits=[tok_consts, tok_p2b_pe, tok_p1], inc=("pe", 1))
        t2 = A("act", lambda E: E.activation(out=aws[:], in_=scp[0][0:1, 0:S],
                                             func=AF.Exp, scale=-1.0),
               waits=[t1], inc=("act", 1), chain=True)
        t3 = A("dve", lambda E: E.tensor_scalar(aws[:], aws[:], 1.0, None,
                                                op0=ALU.add),
               waits=[t2], inc=("dve", 1), chain=True)
        t4 = A("dve", lambda E: E.reciprocal(aws[:], aws[:]),
               inc=("dve", 1), chain=True)
        t5 = A("pe", lambda E: E.matmul(scp[1][:, 0:S], onf[:], aws[:],
                                        start=True, stop=True),
               waits=[t4], inc=("pe", 1))
        tok_awb = A("dve", lambda E: E.tensor_copy(awb[:], scp[1][:, 0:S]),
                    waits=[t5], inc=("dve", 1), chain=True)

        g_done = {}
        for rt in range(ORT):
            r0 = rt * 128

            def mk_sc(rt=rt, r0=r0):
                def f(E):
                    for dc in range(DC):
                        ins = E.matmul(scp[0][:, 0:S],
                                       xnt_own(dc, r0, r0 + 128),
                                       wrs[:, dc * S:(dc + 1) * S],
                                       start=(dc == 0), stop=(dc == DC - 1))
                    return ins
                return f
            waits = [tok_p1, tok_consts, tok_wrs, t2]
            if rt > 0:
                waits.append(g_done[rt - 1])
            t_sc = A("pe", mk_sc(), waits=waits, inc=("pe", 1))

            t_cp = A("dve", lambda E: E.tensor_copy(scw[:, 0:S], scp[0][:, 0:S]),
                     waits=[t_sc], inc=("dve", 1), chain=True)
            A("dve", lambda E: E.tensor_reduce(w1(24), scw[:, 0:S], axis=AX.X,
                                               op=ALU.max, negate=True),
              inc=("dve", 1), chain=True)
            A("dve", lambda E: E.tensor_scalar(w1(25), w1(24), -1.0, None,
                                               op0=ALU.mult),
              inc=("dve", 1), chain=True)
            A("dve", lambda E: E.tensor_scalar(scw[:, S:2 * S], scw[:, 0:S],
                                               w1(25), 1e9, op0=ALU.is_ge,
                                               op1=ALU.mult),
              inc=("dve", 1), chain=True)
            t_ein = A("dve", lambda E: E.tensor_tensor(
                scw[:, 2 * S:3 * S], scw[:, 0:S], scw[:, S:2 * S],
                op=ALU.subtract), inc=("dve", 1), chain=True)
            t_exp = A("act", lambda E: E.activation(
                out=scw[:, 3 * S:4 * S], in_=scw[:, 0:S], func=AF.Exp,
                bias=w1(24)), waits=[t_ein], inc=("act", 1), chain=True)
            A("dve", lambda E: E.tensor_reduce(w1(26), scw[:, 2 * S:3 * S],
                                               axis=AX.X, op=ALU.max),
              inc=("dve", 1), chain=True)
            A("dve", lambda E: E.tensor_scalar(scw[:, 4 * S:5 * S], scw[:, 0:S],
                                               w1(26), None, op0=ALU.is_ge),
              inc=("dve", 1), chain=True)
            A("dve", lambda E: E.tensor_reduce(w1(27), scw[:, 3 * S:4 * S],
                                               axis=AX.X, op=ALU.add),
              waits=[t_exp], inc=("dve", 1), chain=True)
            A("dve", lambda E: E.tensor_tensor(scw[:, 5 * S:6 * S],
                                               scw[:, 3 * S:4 * S],
                                               scw[:, 4 * S:5 * S], op=ALU.mult),
              inc=("dve", 1), chain=True)
            A("dve", lambda E: E.tensor_reduce(w1(28), scw[:, 5 * S:6 * S],
                                               axis=AX.X, op=ALU.add),
              inc=("dve", 1), chain=True)
            A("dve", lambda E: E.tensor_scalar(w1(29), w1(27), 1e-8, None,
                                               op0=ALU.mult),
              inc=("dve", 1), chain=True)
            A("dve", lambda E: E.tensor_tensor(w1(30), w1(28), w1(29), op=ALU.add),
              inc=("dve", 1), chain=True)
            A("dve", lambda E: E.reciprocal(w1(31), w1(30)),
              inc=("dve", 1), chain=True)
            A("dve", lambda E: E.tensor_scalar(scw[:, 6 * S:7 * S],
                                               scw[:, 5 * S:6 * S], w1(31), None,
                                               op0=ALU.mult),
              inc=("dve", 1), chain=True)
            A("dve", lambda E: E.tensor_tensor(scw[:, 7 * S:8 * S],
                                               scw[:, 6 * S:7 * S], awb[:],
                                               op=ALU.mult),
              waits=[tok_awb], inc=("dve", 1), chain=True)
            A("dve", lambda E: E.tensor_reduce(w1(32), scw[:, 7 * S:8 * S],
                                               axis=AX.X, op=ALU.add),
              inc=("dve", 1), chain=True)
            A("dve", lambda E, rt=rt: E.tensor_scalar(
                gps[:, rt:rt + 1], w1(32), 0.2, 0.9, op0=ALU.mult, op1=ALU.add),
              inc=("dve", 1), chain=True)
            t_gbf = A("dve", lambda E: E.tensor_copy(ptp[1][:, 0:S],
                                                     scw[:, 6 * S:7 * S]),
                      inc=("dve", 1), chain=True)
            t_gtr = A("pe", lambda E: E.transpose(scp0_bf[0:S, 1024:1152], ptp[1][:, 0:S],
                                                  ids[:]),
                      waits=[t_gbf, (g_done.get(rt - 1) if rt else None)],
                      inc=("pe", 1))
            g_done[rt] = A("dve", lambda E, r0=r0: E.tensor_copy(
                gT(r0, r0 + 128), scp0_bf[0:S, 1024:1152]),
                waits=[t_gtr], inc=("dve", 1), chain=True)

        gut_toks = {}
        hu_last = None
        for gi in range(NG):
            r0 = gi * GR
            gx = {}
            for jc in range(JC):
                gxw = [g_done[ORT - 1], tok_consts, tok_awb]
                if jc > 0:
                    gxw.append(gx[jc - 1])
                elif gi > 0:
                    gxw.append(gx_prev_last)
                t_gx = A("pe", lambda E, jc=jc, r0=r0: E.matmul(
                    scp[1][:, 0:GR], ems[:, jc * 128:(jc + 1) * 128],
                    gT(r0, r0 + GR), start=True, stop=True),
                    waits=gxw, inc=("pe", 1))
                gx[jc] = A("dve", lambda E, jc=jc: E.tensor_copy(
                    gexp(jc), scp[1][:, 0:GR]),
                    waits=[t_gx], inc=("dve", 1), chain=True)
            gx_prev_last = gx[JC - 1]
            for jc in range(JC):
                n = gi * JC + jc
                waits = [tok_u, tok_p1]
                if n >= 4:
                    waits.append(gut_toks[n - 4])

                def mk_hu(jc=jc, r0=r0, n=n):
                    def f(E):
                        for dc in range(DC):
                            ins = E.matmul(pf[n % 4][:, 0:GR],
                                           uchunk(dc, jc * 128, (jc + 1) * 128),
                                           xnt_own(dc, r0, r0 + GR),
                                           start=(dc == 0), stop=(dc == DC - 1))
                        return ins
                    return f
                hu_last = A("pe", mk_hu(), waits=waits, inc=("pe", 1))
                gut_toks[n] = A("dve", lambda E, jc=jc, r0=r0, n=n: E.tensor_tensor(
                    gut(jc, r0, r0 + GR), pf[n % 4][:, 0:GR], gexp(jc),
                    op=ALU.mult),
                    waits=[hu_last, gx[jc], tok_p2b_q0, tok_p2b_q1],
                    inc=("dve", 1), chain=True)
        tok_p3_dve = S_.tok("dve")
        tok_p3_pe = S_.tok("pe")

        tok_wk = dma("gq", "w", pw[:, 0:DC * D].rearrange("p (dc n) -> p dc n", n=D),
                     wk.rearrange("(dc p) n -> p dc n", p=128),
                     waits=[tok_p2b_pe])

        kev = {}
        kmm = {}
        kdma = {}
        kev_of = {}
        kst_ids = {}
        ids0 = [ln_stats(xb[sb * 128:(sb + 1) * 128, :]) for sb in range(NSUB)]
        kev_of[0] = max(ln_tr(i, lambda dc, a, bb: xtc(0, dc, a, bb),
                              sb * 128) for sb, i in enumerate(ids0))
        for c in range(NCH):
            if c + 1 < NCH:
                kst_ids[c + 1] = [ln_stats(
                    xb[(c + 1) * CH + sb * 128:(c + 1) * CH + (sb + 1) * 128, :],
                    extra_waits=[kdma.get(c - 1)]) for sb in range(NSUB)]
            for kc in range(DC):
                waits = [kev_of[c], tok_wk]
                gn = c * DC + kc
                if gn >= 4:
                    waits.append(kev[gn - 4])

                def mk_kmm(c=c, kc=kc):
                    def f(E, _kk=None):
                        for dc in range(DC):
                            ins = E.matmul(pf[kc % 4][:, 0:CH],
                                           wchunk(dc, kc * 128, (kc + 1) * 128),
                                           xtc(c % 2, dc, 0, CH),
                                           start=(dc == 0), stop=(dc == DC - 1))
                        return ins
                    return f
                kmm[gn] = A("pe", mk_kmm(), waits=waits, inc=("pe", 1))
                w2 = [kmm[gn]]
                if c > 1:
                    w2.append(kdma[c - 2])
                kev[gn] = A("act", lambda E, c=c, kc=kc: E.activation(
                    out=stg(c % 2, kc, 0, CH), in_=pf[kc % 4][:, 0:CH],
                    func=AF.Identity,
                    bias=bks[:, kc:kc + 1]), waits=w2, inc=("act", 1), chain=True)
                if c + 1 < NCH and kc % LNQ == LNQ - 1:
                    sb = kc // LNQ
                    ev = ln_tr(kst_ids[c + 1][sb],
                               lambda dc, a, bb, c=c: xtc((c + 1) % 2, dc, a, bb),
                               sb * 128)
                    if sb == NSUB - 1:
                        kev_of[c + 1] = ev
            _kd = dma(
                "sy", f"kx{c % 2}",
                kt_scr.rearrange("(kc p) r -> p kc r", p=128)[:, :, c * CH:(c + 1) * CH],
                pa[:, STG_OFF[c % 2]: STG_OFF[c % 2] + DC * CH
                   ].rearrange("p (kc r) -> p kc r", r=CH),
                waits=[kev[c * DC + DC - 1]])
            t_xd = dma(
                "sy", f"kx{c % 2}",
                xnt_scr.rearrange("(dc p) r -> p dc r", p=128)[:, :, c * CH:(c + 1) * CH],
                pa[:, XTC_OFF[c % 2]: XTC_OFF[c % 2] + DC * CH
                   ].rearrange("p (dc r) -> p dc r", r=CH),
                waits=[kev_of[c]])
            kdma[c] = t_xd
        tok_p2a_ld0 = S_.tok("kx0")
        tok_p2a_ld1 = S_.tok("kx1")
        tok_p2a_pe = S_.tok("pe")

        tok_wv = dma("gq", "w", pw[:, 0:DC * D].rearrange("p (dc n) -> p dc n", n=D),
                     wv.rearrange("(dc p) n -> p dc n", p=128),
                     waits=[tok_p2b_pe, tok_p3_pe, tok_p2a_pe])
        tok_masks = dma("gq", "mk", pm[:, 0:NKS * QT], mk_i, waits=[tok_p3_pe])

        vev = {}
        vdma = {}
        vmm = {}
        vxl = {}
        for c in range(NCH):
            waits = [tok_p2a_ld0, tok_p2a_ld1, tok_p3_dve, tok_p2b_q0, tok_p2b_q1, tok_p2a_pe]
            if c > 1:
                waits.append(vmm[c - 2])
            vxl[c] = dma(
                "sy", f"vx{c % 2}",
                pa[:, XTC_OFF[c % 2]:XTC_OFF[c % 2] + DC * CH
                   ].rearrange("p (dc r) -> p dc r", r=CH),
                xnt_scr.rearrange("(dc p) r -> p dc r", p=128)[:, :, c * CH:(c + 1) * CH],
                waits=waits)
            for rs in range(NSUB):
                for vc in range(NVC):
                    n = rs * NVC + vc
                    gn = c * NSUB * NVC + n
                    waits = [vxl[c], tok_wv]
                    if gn >= 4:
                        waits.append(vev[gn - 4])

                    def mk_vmm(rs=rs, vc=vc, n=n, c=c):
                        def f(E):
                            for dc in range(DC):
                                ins = E.matmul(pf[n % 4][:, 0:512],
                                               xtc(c % 2, dc, rs * 128, (rs + 1) * 128),
                                               wchunk(dc, vc * 512, (vc + 1) * 512),
                                               start=(dc == 0), stop=(dc == DC - 1))
                            return ins
                        return f
                    vmm[c] = A("pe", mk_vmm(), waits=waits, inc=("pe", 1))
                    w2 = [vmm[c]]
                    if c > 1:
                        w2.append(vdma[c - 2])
                    vev[gn] = A("dve", lambda E, rs=rs, vc=vc, n=n, c=c: E.tensor_copy(
                        stgv(c % 2, rs, vc * 512, (vc + 1) * 512), pf[n % 4][:, 0:512]),
                        waits=w2, inc=("dve", 1), chain=True)
            vdma[c] = dma(
                "sy", f"vw{c % 2}",
                v_scr[c * CH:(c + 1) * CH, :].rearrange("(rs p) n -> p rs n", p=128),
                pa[:, STG_OFF[c % 2]: STG_OFF[c % 2] + NSUB * D
                   ].rearrange("p (rs n) -> p rs n", n=D),
                waits=[vev[c * NSUB * NVC + NSUB * NVC - 1]])
        tok_p2c_ld0 = S_.tok("vw0")
        tok_p2c_ld1 = S_.tok("vw1")
        tok_p2c_pe = S_.tok("pe")

        tok_vflat = dma("gq", "vf", pv[:, 0:JC * D].rearrange("p (jc n) -> p jc n", n=D),
                        v_fl.rearrange("(jc p) n -> p jc n", p=128),
                        waits=[tok_p3_dve, ("act", S_.n["act"]), ("dve", S_.n["dve"])])

        t_o0 = A("dve", lambda E: E.memset(vh_view(0)[:, :, 128:129], 1.0),
                 waits=[tok_p2c_pe], inc=("dve", 1), chain=True)
        tok_vones = A("dve", lambda E: E.memset(vh_view(1)[:, :, 128:129], 1.0),
                      inc=("dve", 1), chain=True)

        tok_pre_attn_dve = ("dve", S_.n["dve"])

        def pso_(j):
            return pf[j // 2][:, (j % 2) * 256:(j % 2) * 256 + 129]

        def pso_d(j):
            return pf[j // 2][:, (j % 2) * 256:(j % 2) * 256 + 128]

        def pso_l(j):
            return pf[j // 2][:, (j % 2) * 256 + 128:(j % 2) * 256 + 129]

        def scb_(k, q0, q1):
            P = (k // 2) % 2
            o = (k % 2) * QT
            return scp[P][:, o + q0: o + q1]

        def ptp_(k, q0, q1):
            P = (k // 2) % 2
            o = (k % 2) * QT
            return ptp[P][:, o + q0: o + q1]

        head_pe = {}
        head_ev = {}
        head_ld = {}
        exp_toks = {}
        mask_toks = {}
        sweep_ev = {}
        nvd = (NSUBK + 7) // 8
        for h in range(H):
            beta = h % 2
            waits = [tok_p2a_ld0, tok_p2a_ld1, tok_p2b_q0, tok_p2b_q1, tok_p2c_ld0, tok_p2c_ld1, tok_vones, tok_p2a_pe,
                     S_.tok("pe") if h == 0 else None]
            if h >= 2:
                waits += [head_pe[h - 2], head_ev[h - 2]]
            t = dma("sy", f"h{beta}", kth(beta, 0, T), kt_scr[h * 128:(h + 1) * 128, :],
                    waits=waits)
            for d in range(nvd):
                s0, s1 = d * 8, min((d + 1) * 8, NSUBK)
                t = dma("sy", f"h{beta}", vh_view(beta)[:, s0:s1, 0:128],
                        v_scr[s0 * 128:s1 * 128, h * 128:(h + 1) * 128
                              ].rearrange("(s p) c -> p s c", p=128))
            head_ld[h] = t = dma(
                "sy", f"h{beta}",
                xs[beta][:, 0:OWN],
                qu_scr[:, h * OWN:(h + 1) * OWN])

            for sw in range(1, NSW + 1):
                nk = sw * NKS
                pvq = []
                for k in range(nk):
                    if k >= 4:
                        war = exp_toks[(h, sw, k - 4)]
                    elif sw > 1:
                        war = exp_toks[(h, sw - 1, (sw - 1) * NKS - 4 + k)]
                    elif h > 0:
                        war = exp_toks[(h - 1, NSW, NSW * NKS - 4 + k)]
                    else:
                        war = None
                    sw_waits = [head_ld[h], war]
                    if h == 0 and sw == 1 and k in (0, 1, 2, 3):
                        sw_waits.append(tok_pre_attn_dve)
                    t_sc = A("pe", lambda E, b=beta, k=k, sw=sw: E.matmul(
                        scb_(k, 0, QT), kth(b, k * 128, (k + 1) * 128),
                        xs[b][:, (sw - 1) * QT: sw * QT],
                        start=True, stop=True),
                        waits=sw_waits, inc=("pe", 1))
                    if k % 2 == 1:
                        ew = [t_sc]
                        for km in (k - 4, k - 3):
                            if km >= nk - NKS and km >= 0:
                                ew.append(mask_toks[(h, sw, km)])
                            elif km < 0 and (sw > 1 or h > 0):
                                psw = (h, sw - 1) if sw > 1 else (h - 1, NSW)
                                kk2 = psw[1] * NKS + km
                                if (psw[0], psw[1], kk2) in mask_toks:
                                    ew.append(mask_toks[(psw[0], psw[1], kk2)])
                        tex = A("act", lambda E, k=k: E.activation(
                            out=ptp_(k - 1, 0, 2 * QT), in_=scb_(k - 1, 0, 2 * QT),
                            func=AF.Exp, bias=cst[:], scale=SCALE),
                            waits=ew + [tok_cst], inc=("act", 1), chain=True)
                        exp_toks[(h, sw, k - 1)] = tex
                        exp_toks[(h, sw, k)] = tex
                        for km in (k - 1, k):
                            if km >= nk - NKS:
                                j = km - (nk - NKS)
                                mask_toks[(h, sw, km)] = A(
                                    "dve", lambda E, km=km, j=j: E.tensor_tensor(
                                        ptp_(km, 0, QT), ptp_(km, 0, QT),
                                        mask_j(j), op=ALU.mult),
                                    waits=[tex, tok_masks],
                                    inc=("dve", 1), chain=True)
                        for km in (k - 1, k):
                            if km >= nk - NKS:
                                pvw = [mask_toks[(h, sw, km)]]
                            else:
                                pvw = [tex]
                            if km == 0:
                                if sw > 1:
                                    pvw.append(sweep_ev[(h, sw - 1)])
                                elif h > 0:
                                    pvw.append(sweep_ev[(h - 1, NSW)])
                                else:
                                    pvw.append(tok_p3_dve)

                            def mk_pv(b=beta, km=km, nk=nk):
                                def f(E):
                                    for j in range(NQS):
                                        ins = E.matmul(pso_(j),
                                                       ptp_(km, j * 128, (j + 1) * 128),
                                                       vh(b, km),
                                                       start=(km == 0 and j % 2 == 0),
                                                       stop=(km == nk - 1 and
                                                             (j % 2 == 1 or j == NQS - 1)),
                                                       skip_group_check=True)
                                    return ins
                                return f
                            pvq.append((mk_pv(), pvw))
                    while len(pvq) > 2:
                        A("pe", *pvq.pop(0))
                tpv = None
                for i2, (fn2, w2_) in enumerate(pvq):
                    tpv = A("pe", fn2, waits=w2_,
                            inc=("pe", 1) if i2 == len(pvq) - 1 else None)
                head_pe[h] = tpv
                sweep_pe = tpv
                t_ev = None
                for j in range(NQS):
                    rt = ((sw - 1) * QT + j * 128) // 128
                    A("dve", lambda E, j=j: E.reciprocal(w1(40 + j),
                                                         pso_l(j)),
                      waits=[sweep_pe], inc=("dve", 1), chain=True)
                    t_ev = A("dve", lambda E, j=j, rt=rt, h=h: E.tensor_scalar(
                        ac(rt, h * 128, (h + 1) * 128), pso_d(j),
                        w1(40 + j), None, op0=ALU.mult),
                        inc=("dve", 1), chain=True)
                sweep_ev[(h, sw)] = t_ev
            head_ev[h] = sweep_ev[(h, NSW)]
        tok_p4_dve = S_.tok("dve")
        tok_p4_pe = S_.tok("pe")
        tok_p4_act = S_.tok("act")

        tok_wo = dma("gq", "w", pw[:, 0:DC * D].rearrange("p (dc n) -> p dc n", n=D),
                     wo.rearrange("(dc p) n -> p dc n", p=128),
                     waits=[tok_p4_pe])

        pairs = [(a, b) for a in range(ORT) for b in range(a, DC)]
        p5_ev = {}
        for idx, (a, b) in enumerate(pairs):
            par = idx % 2
            n_tr = 1 if a == b else 2
            waits = [tok_p4_dve, tok_consts]
            if idx >= 2:
                waits.append(p5_ev[idx - 2])
            else:
                waits.append(tok_p4_act)

            def mk_tr(a=a, b=b, par=par, n_tr=n_tr):
                def f(E):
                    ins = E.transpose(pbv(par, 0, 128), ac(a, b * 128, (b + 1) * 128), ids[:])
                    if n_tr == 2:
                        ins = E.transpose(pbv(par, 128, 256), ac(b, a * 128, (a + 1) * 128), ids[:])
                    return ins
                return f
            t_tr = A("pe", mk_tr(), waits=waits, inc=("pe", n_tr))
            t_e1 = A("dve", lambda E, a=a, b=b, par=par: E.tensor_scalar(
                acT(b, a * 128, (a + 1) * 128), pbv(par, 0, 128),
                bvs[:, b:b + 1], None, op0=ALU.add),
                waits=[t_tr], inc=("dve", 1), chain=True)
            if a != b:
                t_e1 = A("dve", lambda E, a=a, b=b, par=par: E.tensor_scalar(
                    acT(a, b * 128, (b + 1) * 128), pbv(par, 128, 256),
                    bvs[:, a:a + 1], None, op0=ALU.add),
                    inc=("dve", 1), chain=True)
            p5_ev[idx] = t_e1
        tok_p5_dve = S_.tok("dve")

        tok_bo = dma("sy", "bo", bo_b(0, D), bo_i,
                     waits=[tok_p4_dve, tok_p4_act])

        e1t = {}
        e2t = {}
        e3t = {}
        stt = {}
        xlt = {}
        for rt in range(ORT):
            for cc in range(NVC):
                n = rt * NVC + cc
                xw = [tok_bo]
                if n >= 2:
                    xw.append(e3t[n - 2])
                xlt[n] = dma("sy", f"px{n % 2}", p6x(n % 2),
                             xo[rt * 128:(rt + 1) * 128, cc * 512:(cc + 1) * 512],
                             waits=xw)
                mw = [tok_p3_dve, tok_vflat, tok_wo, tok_p5_dve]
                if n >= 2:
                    mw += [e1t[n - 2], e2t[n - 2]]

                def mk_f(rt=rt, cc=cc, n=n):
                    def f(E):
                        for jc in range(JC):
                            ins = E.matmul(pf[n % 2][:, 0:512],
                                           gut(jc, rt * 128, (rt + 1) * 128),
                                           vflat(jc, cc * 512, (cc + 1) * 512),
                                           start=(jc == 0), stop=(jc == JC - 1))
                        return ins
                    return f
                t_f = A("pe", mk_f(), waits=mw, inc=("pe", 1))

                def mk_o(rt=rt, cc=cc, n=n):
                    def f(E):
                        for fc in range(DC):
                            ins = E.matmul(pf[2 + n % 2][:, 0:512],
                                           acT(fc, rt * 128, (rt + 1) * 128),
                                           wchunk(fc, cc * 512, (cc + 1) * 512),
                                           start=(fc == 0), stop=(fc == DC - 1))
                        return ins
                    return f
                t_o = A("pe", mk_o(), inc=("pe", 1))

                w1_ = [t_f, tok_p3_dve]
                if n >= 3:
                    w1_.append(stt[n - 3])
                e1t[n] = A("act", lambda E, rt=rt, n=n: E.activation(
                    out=p6t(n % 3), in_=pf[n % 2][:, 0:512], func=AF.Copy,
                    scale=gps[:, rt:rt + 1]), waits=w1_, inc=("act", 1),
                    chain=True)
                e2t[n] = A("dve", lambda E, n=n: E.tensor_tensor(
                    p6t(n % 3), p6t(n % 3), pf[2 + n % 2][:, 0:512], op=ALU.add),
                    waits=[e1t[n], t_o], inc=("dve", 1), chain=True)
                A("dve", lambda E, n=n: E.tensor_tensor(
                    p6t(n % 3), p6t(n % 3), p6x(n % 2), op=ALU.add),
                  waits=[xlt[n]], inc=("dve", 1), chain=True)
                e3t[n] = A("dve", lambda E, n=n, cc=cc: E.tensor_tensor(
                    p6t(n % 3), p6t(n % 3), bo_b(cc * 512, (cc + 1) * 512),
                    op=ALU.add), waits=[tok_bo], inc=("dve", 1), chain=True)
                stt[n] = dma("sy", f"o{n % 3}", out[rt * 128:(rt + 1) * 128, cc * 512:(cc + 1) * 512],
                             p6t(n % 3), waits=[e3t[n]])

        final = dict(S_.n)
        S_.add("gq", None, waits=[(s, v) for s, v in final.items()])

        semnames = set(S_.n.keys())
        for ename in S_.ENG:
            for fn, waits, inc in S_.ops[ename]:
                for sem, val in waits:
                    semnames.add(sem)
        sems = {name: e(nc.semaphore("s_" + name)) for name in sorted(semnames)}
        block = e(nc.Block())

        def emit_wm(engobj, ename):
            water = {}
            for fn, waits, inc in S_.ops[ename]:
                for sem, val in waits:
                    if water.get(sem, 0) >= val:
                        continue
                    engobj.wait_ge(sems[sem], val)
                    water[sem] = val
                if fn is None:
                    continue
                ins = fn(engobj)
                if inc is not None:
                    ins.then_inc(sems[inc[0]], inc[1])

        @block.sync
        def _(E):
            emit_wm(E, "sy")

        @block.gpsimd
        def _(E):
            emit_wm(E, "gq")

        @block.tensor
        def _(E):
            emit_wm(E, "pe")

        @block.vector
        def _(E):
            emit_wm(E, "dve")

        @block.scalar
        def _(E):
            emit_wm(E, "act")

    return nc


def host_inputs(cfg, core, ins):
    D, T = cfg["D"], cfg["T"]
    OWN, QT, NSW, NKS, DC, JC, S = (cfg["OWN"], cfg["QT"], cfg["NSW"],
                                    cfg["NKS"], cfg["DC"], cfg["JC"], cfg["S"])
    b, g = core // 4, core % 4
    x = ins["L0_attn_x"]
    xb = np.ascontiguousarray(x[b], dtype=np.float32)
    blocks = [4 * s + g for s in range(NSW)]
    xo = np.ascontiguousarray(
        np.concatenate([x[b][blk * QT:(blk + 1) * QT] for blk in blocks], axis=0),
        dtype=np.float32)
    H = cfg["H"]
    TPH, QPH = cfg["T"] // H, QT // H
    qrows = []
    for h in range(H):
        for s in range(NSW):
            base = h * TPH + (QT * (4 * s + g)) // H
            qrows.extend(range(base, base + QPH))
    xq = np.ascontiguousarray(x[b][np.array(qrows)], dtype=np.float32)
    masks = np.zeros((128, NKS * QT), dtype=ml_dtypes.bfloat16)
    kl = np.arange(128)[:, None]
    q = np.arange(QT)[None, :]
    for j in range(NKS):
        masks[:, j * QT:(j + 1) * QT] = (j * 128 + kl <= g * QT + q).astype(
            ml_dtypes.bfloat16)
    e_mat = np.zeros((S, 128 * JC), dtype=ml_dtypes.bfloat16)
    for jj in range(128 * JC):
        e_mat[jj // 16, jj] = 1
    U = np.asarray(ins["L0_ffn_U"], np.float32)
    m = {
        "xb": xb,
        "xo": xo,
        "xq": xq,
        "wq": ins["L0_attn_Wq"], "wk": ins["L0_attn_Wk"],
        "wv": ins["L0_attn_Wv"], "wo": ins["L0_attn_Wo"],
        "bq": np.ascontiguousarray(np.asarray(ins["L0_attn_bq"], np.float32).reshape(DC, 128).T),
        "bk": np.ascontiguousarray(np.asarray(ins["L0_attn_bk"], np.float32).reshape(DC, 128).T),
        "bv": np.ascontiguousarray(np.asarray(ins["L0_attn_bv"], np.float32).reshape(DC, 128).T),
        "bo_bc": np.ascontiguousarray(np.broadcast_to(
            np.asarray(ins["L0_attn_bo"], np.float32).reshape(1, D), (128, D))),
        "wr_t": np.ascontiguousarray(np.asarray(ins["L0_ffn_Wr"], np.float32).T),
        "u_flat": np.ascontiguousarray(U.transpose(1, 0, 2).reshape(D, -1)),
        "v_flat": np.ascontiguousarray(np.asarray(ins["L0_ffn_V"], np.float32).reshape(-1, D)),
        "attr_t": np.ascontiguousarray(np.asarray(ins["L0_ffn_attr"], np.float32).T),
        "wa_t": np.ascontiguousarray(np.asarray(ins["L0_ffn_Wa"], np.float32).reshape(1, S).T),
        "masks": masks,
        "ident": np.eye(128, dtype=ml_dtypes.bfloat16),
        "ones_f": np.ones((1, 128), np.float32),
        "e_mat": e_mat,
    }
    return {k: np.ascontiguousarray(np.asarray(v)) for k, v in m.items()}


def assemble(cfg, results):
    D, T, QT, NSW = cfg["D"], cfg["T"], cfg["QT"], cfg["NSW"]
    full = np.zeros((2, T, D), np.float32)
    for core in range(8):
        b, g = core // 4, core % 4
        o = np.asarray(results[core]["out"], np.float32)
        for s in range(NSW):
            blk = 4 * s + g
            full[b, blk * QT:(blk + 1) * QT] = o[s * QT:(s + 1) * QT]
    return full


def kernel(**inputs):
    cfg = mkcfg(FULL)
    nc = build_nc(cfg)
    in_maps = [host_inputs(cfg, c, inputs) for c in range(8)]
    res = run_bass_kernel_spmd(nc, in_maps, core_ids=list(range(8)))
    return assemble(cfg, res.results)


# version 15
# speedup vs baseline: 1.5064x; 1.0082x over previous
import math
from contextlib import ExitStack

import numpy as np
import ml_dtypes
import concourse.bass as bass
import concourse.mybir as mybir
from concourse.bass_utils import run_bass_kernel_spmd

F32 = mybir.dt.float32
BF16 = mybir.dt.bfloat16
AF = mybir.ActivationFunctionType
ALU = mybir.AluOpType
AX = mybir.AxisListType

FULL = dict(D=2048, T=8192, H=16)
TINY = dict(D=512, T=2048, H=4)

C_SHIFT = 12.0


def mkcfg(base):
    c = dict(base)
    D, T, H = c["D"], c["T"], c["H"]
    c["DH"] = 128
    assert H * c["DH"] == D
    c["DC"] = D // 128
    c["OWN"] = T // 4
    assert c["OWN"] == D
    c["QT"] = T // 16
    assert c["QT"] % 128 == 0
    c["NQS"] = c["QT"] // 128
    c["NSW"] = 4
    c["NKS"] = T // 512
    c["CH"] = c["QT"]
    c["NCH"] = T // c["CH"]
    c["NSUB"] = c["CH"] // 128
    c["S"], c["R"] = 32, 16
    c["J"] = c["S"] * c["R"]
    c["JC"] = c["J"] // 128
    c["GR"] = min(512, c["OWN"])
    c["NG"] = c["OWN"] // c["GR"]
    c["ORT"] = c["OWN"] // 128
    assert c["ORT"] == c["DC"]
    c["SCALE"] = 1.0 / math.sqrt(c["DH"])
    return c


class Sched:
    ENG = ("sy", "gq", "pe", "act", "dve")

    def __init__(self):
        self.ops = {e: [] for e in self.ENG}
        from collections import defaultdict
        self.n = defaultdict(int)
        for s in ("pe", "act", "dve"):
            self.n[s] = 0
        self.chain_last = {"act": 0, "dve": 0}

    def add(self, eng, fn, waits=(), inc=None, chain=False):
        w = [x for x in waits if x is not None and x[1] > 0]
        if chain and eng in self.chain_last:
            if self.chain_last[eng] > 0:
                w.append((eng, self.chain_last[eng]))
        tok = None
        if inc is not None:
            sem, k = inc
            self.n[sem] += k
            tok = (sem, self.n[sem])
            if eng in self.chain_last and sem == eng:
                self.chain_last[eng] = self.n[sem]
        self.ops[eng].append((fn, w, inc))
        return tok

    def tok(self, sem):
        return (sem, self.n[sem])

    def barrier(self):
        snap = dict(self.n)
        for e in self.ENG:
            self.add(e, None, waits=[(s, v) for s, v in snap.items()])
        return snap


def build_nc(cfg):
    D, T, H, DH = cfg["D"], cfg["T"], cfg["H"], cfg["DH"]
    DC, OWN, QT, NQS = cfg["DC"], cfg["OWN"], cfg["QT"], cfg["NQS"]
    NSW, NKS, CH, NCH, NSUB = cfg["NSW"], cfg["NKS"], cfg["CH"], cfg["NCH"], cfg["NSUB"]
    S, J, JC = cfg["S"], cfg["J"], cfg["JC"]
    GR, NG, ORT, SCALE = cfg["GR"], cfg["NG"], cfg["ORT"], cfg["SCALE"]
    NVC = D // 512
    NSUBK = T // 128

    nc = bass.Bass("TRN2", target_bir_lowering=False)

    xb = nc.dram_tensor("xb", [T, D], F32, kind="ExternalInput").ap()
    xo = nc.dram_tensor("xo", [OWN, D], F32, kind="ExternalInput").ap()
    xq = nc.dram_tensor("xq", [OWN, D], F32, kind="ExternalInput").ap()
    wq = nc.dram_tensor("wq", [D, D], F32, kind="ExternalInput").ap()
    wk = nc.dram_tensor("wk", [D, D], F32, kind="ExternalInput").ap()
    wv = nc.dram_tensor("wv", [D, D], F32, kind="ExternalInput").ap()
    wo = nc.dram_tensor("wo", [D, D], F32, kind="ExternalInput").ap()
    bq_i = nc.dram_tensor("bq", [128, DC], F32, kind="ExternalInput").ap()
    bk_i = nc.dram_tensor("bk", [128, DC], F32, kind="ExternalInput").ap()
    bv_i = nc.dram_tensor("bv", [128, DC], F32, kind="ExternalInput").ap()
    bo_i = nc.dram_tensor("bo_bc", [128, D], F32, kind="ExternalInput").ap()
    wr_i = nc.dram_tensor("wr_t", [D, S], F32, kind="ExternalInput").ap()
    u_fl = nc.dram_tensor("u_flat", [D, J], F32, kind="ExternalInput").ap()
    v_fl = nc.dram_tensor("v_flat", [J, D], F32, kind="ExternalInput").ap()
    at_i = nc.dram_tensor("attr_t", [S, S], F32, kind="ExternalInput").ap()
    wa_i = nc.dram_tensor("wa_t", [S, 1], F32, kind="ExternalInput").ap()
    mk_i = nc.dram_tensor("masks", [128, NKS * QT], BF16, kind="ExternalInput").ap()
    id_i = nc.dram_tensor("ident", [128, 128], BF16, kind="ExternalInput").ap()
    of_i = nc.dram_tensor("ones_f", [1, 128], F32, kind="ExternalInput").ap()
    em_i = nc.dram_tensor("e_mat", [S, 128 * JC], BF16, kind="ExternalInput").ap()

    out = nc.dram_tensor("out", [OWN, D], F32, kind="ExternalOutput").ap()

    kt_scr = nc.dram_tensor("kt_scr", [D, T], BF16, kind="Internal").ap()
    v_scr = nc.dram_tensor("v_scr", [T, D], BF16, kind="Internal").ap()
    qu_scr = nc.dram_tensor("qu_scr", [128, (D // 128) * OWN], BF16, kind="Internal").ap()
    xnt_scr = nc.dram_tensor("xnt_scr", [D, T], BF16, kind="Internal").ap()

    with ExitStack() as ctx:
        e = ctx.enter_context

        pa = e(nc.sbuf_tensor("pa", [128, DC * OWN], BF16))
        PW_COLS = max(DC * D, 2 * T + 2 * 129 * NSUBK)
        pw = e(nc.sbuf_tensor("pw", [128, PW_COLS], BF16))
        pm = e(nc.sbuf_tensor("pm", [128, max(DC * J, NKS * QT, 2 * (D + 1024 + 3 * 512))], BF16))
        pv = e(nc.sbuf_tensor("pv", [128, max(JC * D, JC * GR + OWN + 2 * D)], BF16))
        pg = e(nc.sbuf_tensor("pg", [128, JC * OWN], BF16))
        xs = [e(nc.sbuf_tensor(f"xs{i}", [128, max(D, OWN)], BF16)) for i in range(4)]
        ptp = [e(nc.sbuf_tensor(f"ptp{i}", [128, 2 * QT], BF16)) for i in range(2)]
        pt = [ptp[0][:, 0:QT], ptp[0][:, QT:2 * QT], ptp[1][:, 0:QT]]
        wrk = e(nc.sbuf_tensor("wrk", [128, 64], F32))
        scw = e(nc.sbuf_tensor("scw", [128, 8 * S], F32))
        gps = e(nc.sbuf_tensor("gps", [128, ORT], F32))
        bqs = e(nc.sbuf_tensor("bqs", [128, DC], F32))
        bks = e(nc.sbuf_tensor("bks", [128, DC], F32))
        bvs = e(nc.sbuf_tensor("bvs", [128, DC], F32))
        wrs = e(nc.sbuf_tensor("wrs", [128, DC * S], BF16))
        ems = e(nc.sbuf_tensor("ems", [S, 128 * JC], BF16))
        ids = e(nc.sbuf_tensor("ids", [128, 128], BF16))
        onf = e(nc.sbuf_tensor("onf", [1, 128], F32))
        ats = e(nc.sbuf_tensor("ats", [S, S], F32))
        was = e(nc.sbuf_tensor("was", [S, 1], F32))
        awb = e(nc.sbuf_tensor("awb", [128, S], F32))
        aws = e(nc.sbuf_tensor("aws", [1, S], F32))
        cst = e(nc.sbuf_tensor("cst", [128, 1], F32))

        pf = [e(nc.psum_tensor(f"pf{i}", [128, 512], F32)) for i in range(4)]
        scp = [e(nc.psum_tensor(f"scp{i}", [128, 1024], F32)) for i in range(2)]
        scp0_bf = scp[0].bitcast(BF16)

        def pbv(i, a, b):
            return scp0_bf[:, i * 1024 + a: i * 1024 + b]

        block_holder = []

        S_ = Sched()

        def xnt_own(dc, r0, r1):
            return pa[:, dc * OWN + r0: dc * OWN + r1]

        XTC_OFF = [0, 2 * DC * CH]
        STG_OFF = [DC * CH, 3 * DC * CH]

        def xtc(buf, dc, r0, r1):
            o = XTC_OFF[buf]
            return pa[:, o + dc * CH + r0: o + dc * CH + r1]

        def stg(buf, kc, c0, c1):
            o = STG_OFF[buf]
            return pa[:, o + kc * CH + c0: o + kc * CH + c1]

        def stgv(buf, rs, c0, c1):
            o = STG_OFF[buf]
            return pa[:, o + rs * D + c0: o + rs * D + c1]

        def ac(rt, f0, f1):
            return pa[:, rt * D + f0: rt * D + f1]

        def acT(fc, r0, r1):
            return pa[:, fc * OWN + r0: fc * OWN + r1]

        def wchunk(dc, n0, n1):
            return pw[:, dc * D + n0: dc * D + n1]

        def kth(b, k0, k1):
            return pw[:, b * T + k0: b * T + k1]

        VH0 = 2 * T

        def vh(b, sub):
            o = VH0 + b * (129 * NSUBK) + sub * 129
            return pw[:, o: o + 129]

        def vh_view(b):
            o = VH0 + b * (129 * NSUBK)
            return pw[:, o: o + NSUBK * 129].rearrange("p (s c) -> p s c", c=129)

        def uchunk(dc, j0, j1):
            return pm[:, dc * J + j0: dc * J + j1]

        def mask_j(j):
            return pm[:, j * QT:(j + 1) * QT]

        pmf = pm.bitcast(F32)

        def bo_b(c0, c1):
            return pmf[:, c0:c1]

        def p6x(b):
            return pmf[:, D + b * 512: D + (b + 1) * 512]

        def p6t(i):
            return pmf[:, D + 1024 + i * 512: D + 1024 + (i + 1) * 512]

        def vflat(jc, n0, n1):
            return pv[:, jc * D + n0: jc * D + n1]

        def gexp(jc):
            return pv[:, jc * GR:(jc + 1) * GR]

        GT0 = JC * GR

        def gT(r0, r1):
            return pv[0:S, GT0 + r0: GT0 + r1]

        def gut(jc, r0, r1):
            return pg[:, jc * OWN + r0: jc * OWN + r1]

        PVJ = JC * GR + OWN
        sqj = pv[:, PVJ: PVJ + D]
        cpj = pv[:, PVJ + D: PVJ + 2 * D]
        qst = pg[:, 0: 2 * GR]

        def w1(i):
            return wrk[:, i:i + 1]

        A = S_.add

        def dma(eng, sem, outap, inap, waits=()):
            return A(eng, lambda E, o=outap, i=inap: E.dma_start(out=o, in_=i),
                     waits=waits, inc=(sem, 16))

        tok_consts = None
        for o, i in [(ids[:], id_i), (onf[:], of_i), (bqs[:], bq_i), (bks[:], bk_i),
                     (bvs[:], bv_i), (ats[:], at_i), (was[:], wa_i),
                     (ems[:], em_i)]:
            tok_consts = dma("sy", "cst", o, i)
        tok_wrs = dma("gq", "wrs", wrs.rearrange("p (dc s) -> p dc s", s=S),
                      wr_i.rearrange("(dc p) s -> p dc s", p=128))

        tok_wq = dma("gq", "w", pw[:, 0:DC * D].rearrange("p (dc n) -> p dc n", n=D),
                     wq.rearrange("(dc p) n -> p dc n", p=128))
        tok_u = dma("gq", "u", pm[:, 0:DC * J].rearrange("p (dc j) -> p dc j", j=J),
                    u_fl.rearrange("(dc p) j -> p dc j", p=128))
        tok_cst = A("dve", lambda E: E.memset(cst[:], -C_SHIFT),
                    inc=("dve", 1), chain=True)

        ln_i = [0]
        ln_trs = {}
        ln_norm = {}
        ln_sq = {}
        ln_ex2 = {}
        ln_lnt = {}

        def ln_head(src_rows, extra_waits=()):
            i = ln_i[0]
            ln_i[0] += 1
            beta = i % 4
            b2 = i % 2
            xbuf = xs[beta]
            t_x = dma("gq", f"x{beta}", xbuf[:, 0:D], src_rows,
                      waits=[ln_trs.get(i - 4)] + list(extra_waits))
            A("dve", lambda E, xb_=xbuf, b=b2: E.tensor_scalar(
                cpj, xb_[:, 0:D], 1.0, None, op0=ALU.mult, op1=ALU.add,
                accum_out=wrk[:, 10 + b:11 + b]), waits=[t_x], inc=("dve", 1),
                chain=True)
            ln_sq[i] = A("act", lambda E, xb_=xbuf, b=b2: E.activation(
                out=sqj, in_=xb_[:, 0:D], func=AF.Square,
                accum_out=wrk[:, 8 + b:9 + b]),
                waits=[t_x, ln_ex2.get(i - 2)], inc=("act", 1), chain=True)
            return i

        def ln_mid(i):
            b2 = i % 2
            A("dve", lambda E, b=b2: E.tensor_scalar(
                w1(12 + b), w1(10 + b), 1.0 / D, None, op0=ALU.mult),
              inc=("dve", 1), chain=True)
            ln_ex2[i] = A("dve", lambda E, b=b2: E.tensor_scalar(
                w1(14 + b), w1(8 + b), 1.0 / D, None, op0=ALU.mult),
                waits=[ln_sq[i]], inc=("dve", 1), chain=True)
            A("dve", lambda E, b=b2: E.tensor_tensor(
                w1(16 + b), w1(12 + b), w1(12 + b), op=ALU.mult),
              inc=("dve", 1), chain=True)
            A("dve", lambda E, b=b2: E.tensor_tensor(
                w1(18 + b), w1(14 + b), w1(16 + b), op=ALU.subtract),
                waits=[ln_lnt.get(i - 2)], inc=("dve", 1), chain=True)
            t_var = A("dve", lambda E, b=b2: E.tensor_scalar(
                w1(18 + b), w1(18 + b), 1e-5, None, op0=ALU.add),
                inc=("dve", 1), chain=True)
            ln_lnt[i] = A("act", lambda E, b=b2: E.activation(
                out=w1(20 + b), in_=w1(18 + b), func=AF.Ln),
              waits=[t_var], inc=("act", 1), chain=True)
            return A("act", lambda E, b=b2: E.activation(
                out=w1(22 + b), in_=w1(20 + b), func=AF.Exp, scale=-0.5),
                waits=[ln_norm.get(i - 2)], inc=("act", 1), chain=True)

        def ln_tail(i, t_rstd):
            b2 = i % 2
            xbuf = xs[i % 4]
            ln_norm[i] = A("dve", lambda E, xb_=xbuf, b=b2: E.tensor_scalar(
                xb_[:, 0:D], xb_[:, 0:D], w1(12 + b), w1(22 + b),
                op0=ALU.subtract, op1=ALU.mult),
                waits=[t_rstd], inc=("dve", 1), chain=True)

        class LnBatch:

            def __init__(self, srcs, extra_waits=()):
                self.srcs = srcs
                self.extra = list(extra_waits)
                self.ids = []
                self.rstds = {}
                self.k = 0

            def step(self):
                j = self.k
                self.k += 1
                n = len(self.srcs)
                if j < n:
                    self.ids.append(ln_head(self.srcs[j],
                                            self.extra if j == 0 else ()))
                if 1 <= j <= n:
                    self.rstds[j - 1] = ln_mid(self.ids[j - 1])
                if j >= 2 and j - 2 < n:
                    ln_tail(self.ids[j - 2], self.rstds[j - 2])

            def finish(self):
                while self.k < len(self.srcs) + 2:
                    self.step()
                return self.ids

        def ln_stats_batch(srcs, extra_waits=()):
            return LnBatch(srcs, extra_waits).finish()

        def ln_tr(i, dest_fn, dest_r0):
            beta = i % 4
            xbuf = xs[beta]
            t_norm = ln_norm[i]
            ev_toks = []
            tr_toks = []
            for dc in range(DC):
                war = ev_toks[dc - 2] if dc >= 2 else ln_tr.last_ev.get(dc % 2)
                t_tr = A("pe", lambda E, xb_=xbuf, dc=dc: E.transpose(
                    pbv(dc % 2, 0, 128), xb_[:, dc * 128:(dc + 1) * 128], ids[:]),
                    waits=[t_norm, war, tok_consts], inc=("pe", 1))
                tr_toks.append(t_tr)
                t_ev = A("dve", lambda E, dc=dc, r0=dest_r0, df=dest_fn: E.tensor_copy(
                    df(dc, r0, r0 + 128), pbv(dc % 2, 0, 128)),
                    waits=[t_tr], inc=("dve", 1), chain=True)
                ev_toks.append(t_ev)
            ln_tr.last_ev = {0: ev_toks[-2], 1: ev_toks[-1]}
            ln_trs[i] = tr_toks[-1]
            return ev_toks[-1]

        ln_tr.last_ev = {}

        q_mm_last = None
        qev = {}
        qld = {}
        NCH0 = OWN // CH
        LNQ = DC // NSUB
        ev_of = {}
        st_ids = {}
        qdma = {}
        ids0 = ln_stats_batch([xq[sb * 128:(sb + 1) * 128, :]
                               for sb in range(NSUB)])
        ev_of[0] = max(ln_tr(i, lambda dc, a, bb: xtc(0, dc, a, bb),
                             sb * 128) for sb, i in enumerate(ids0))
        for c in range(NCH0):
            nb0 = None
            if c + 1 < NCH0:
                nb0 = LnBatch(
                    [xq[(c + 1) * CH + sb * 128:(c + 1) * CH + (sb + 1) * 128, :]
                     for sb in range(NSUB)])
                st_ids[c + 1] = nb0.ids
            for qc in range(DC):
                if nb0 is not None and qc < NSUB + 2:
                    nb0.step()
                n = c * DC + qc
                waits = [ev_of[c], tok_wq]
                if n >= 4:
                    waits.append(qev[n - 4])

                def mk_qmm(qc=qc, n=n, c=c):
                    def f(E):
                        for dc in range(DC):
                            ins = E.matmul(pf[n % 4][:, 0:CH],
                                           wchunk(dc, qc * 128, (qc + 1) * 128),
                                           xtc(c % 2, dc, 0, CH),
                                           start=(dc == 0), stop=(dc == DC - 1))
                        return ins
                    return f
                q_mm_last = A("pe", mk_qmm(), waits=waits, inc=("pe", 1))
                w2 = [q_mm_last]
                if c > 1 and qc == 0:
                    w2.append(qdma[c - 2])
                qev[n] = A("act", lambda E, qc=qc, n=n, c=c: E.activation(
                    out=pa[:, STG_OFF[c % 2]: STG_OFF[c % 2] + DC * CH
                           ].rearrange("p (c2 r) -> p c2 r", r=DC)[:, :, qc:qc + 1],
                    in_=pf[n % 4][:, 0:CH], func=AF.Identity,
                    bias=bqs[:, qc:qc + 1]), waits=w2, inc=("act", 1), chain=True)
                if c + 1 < NCH0 and qc % LNQ == LNQ - 1:
                    sb = qc // LNQ
                    while nb0.k < sb + 3:
                        nb0.step()
                    ev = ln_tr(st_ids[c + 1][sb],
                               lambda dc, a, bb, c=c: xtc((c + 1) % 2, dc, a, bb),
                               sb * 128)
                    if sb == NSUB - 1:
                        ev_of[c + 1] = ev
            if nb0 is not None:
                nb0.finish()
            qdma[c] = dma(
                "sy", f"q{c % 2}",
                qu_scr[:, c * DC * CH:(c + 1) * DC * CH],
                pa[:, STG_OFF[c % 2]: STG_OFF[c % 2] + DC * CH],
                waits=[qev[c * DC + DC - 1]])
        tok_p2b_pe = S_.tok("pe")
        tok_p2b_q0 = S_.tok("q0")
        tok_p2b_q1 = S_.tok("q1")

        S_.add("dve", None, waits=[("q0", S_.n["q0"]), ("q1", S_.n["q1"])])
        S_.add("pe", None, waits=[("q0", S_.n["q0"]), ("q1", S_.n["q1"])])
        NB1 = OWN // 128 // NSUB
        p1_ids = {0: ln_stats_batch(
            [xo[sb * 128:(sb + 1) * 128, :] for sb in range(NSUB)])}
        for c in range(NB1):
            nb = None
            if c + 1 < NB1:
                nb = LnBatch([xo[((c + 1) * NSUB + sb) * 128:
                                 ((c + 1) * NSUB + sb + 1) * 128, :]
                              for sb in range(NSUB)])
            for sb, i in enumerate(p1_ids[c]):
                ln_tr(i, xnt_own, (c * NSUB + sb) * 128)
                if nb is not None:
                    nb.step()
            if nb is not None:
                p1_ids[c + 1] = nb.finish()
        tok_p1 = S_.tok("dve")

        t1 = A("pe", lambda E: E.matmul(scp[0][0:1, 0:S], was[:], ats[:],
                                        start=True, stop=True),
               waits=[tok_consts, tok_p2b_pe, tok_p1], inc=("pe", 1))
        t2 = A("act", lambda E: E.activation(out=aws[:], in_=scp[0][0:1, 0:S],
                                             func=AF.Exp, scale=-1.0),
               waits=[t1], inc=("act", 1), chain=True)
        t3 = A("dve", lambda E: E.tensor_scalar(aws[:], aws[:], 1.0, None,
                                                op0=ALU.add),
               waits=[t2], inc=("dve", 1), chain=True)
        t4 = A("dve", lambda E: E.reciprocal(aws[:], aws[:]),
               inc=("dve", 1), chain=True)
        t5 = A("pe", lambda E: E.matmul(scp[1][:, 0:S], onf[:], aws[:],
                                        start=True, stop=True),
               waits=[t4], inc=("pe", 1))
        tok_awb = A("dve", lambda E: E.tensor_copy(awb[:], scp[1][:, 0:S]),
                    waits=[t5], inc=("dve", 1), chain=True)

        g_done = {}
        for rt in range(ORT):
            r0 = rt * 128

            def mk_sc(rt=rt, r0=r0):
                def f(E):
                    for dc in range(DC):
                        ins = E.matmul(scp[0][:, 0:S],
                                       xnt_own(dc, r0, r0 + 128),
                                       wrs[:, dc * S:(dc + 1) * S],
                                       start=(dc == 0), stop=(dc == DC - 1))
                    return ins
                return f
            waits = [tok_p1, tok_consts, tok_wrs, t2]
            if rt > 0:
                waits.append(g_done[rt - 1])
            t_sc = A("pe", mk_sc(), waits=waits, inc=("pe", 1))

            t_cp = A("dve", lambda E: E.tensor_copy(scw[:, 0:S], scp[0][:, 0:S]),
                     waits=[t_sc], inc=("dve", 1), chain=True)
            A("dve", lambda E: E.tensor_reduce(w1(24), scw[:, 0:S], axis=AX.X,
                                               op=ALU.max, negate=True),
              inc=("dve", 1), chain=True)
            A("dve", lambda E: E.tensor_scalar(w1(25), w1(24), -1.0, None,
                                               op0=ALU.mult),
              inc=("dve", 1), chain=True)
            A("dve", lambda E: E.tensor_scalar(scw[:, S:2 * S], scw[:, 0:S],
                                               w1(25), 1e9, op0=ALU.is_ge,
                                               op1=ALU.mult),
              inc=("dve", 1), chain=True)
            t_ein = A("dve", lambda E: E.tensor_tensor(
                scw[:, 2 * S:3 * S], scw[:, 0:S], scw[:, S:2 * S],
                op=ALU.subtract), inc=("dve", 1), chain=True)
            t_exp = A("act", lambda E: E.activation(
                out=scw[:, 3 * S:4 * S], in_=scw[:, 0:S], func=AF.Exp,
                bias=w1(24)), waits=[t_ein], inc=("act", 1), chain=True)
            A("dve", lambda E: E.tensor_reduce(w1(26), scw[:, 2 * S:3 * S],
                                               axis=AX.X, op=ALU.max),
              inc=("dve", 1), chain=True)
            A("dve", lambda E: E.tensor_scalar(scw[:, 4 * S:5 * S], scw[:, 0:S],
                                               w1(26), None, op0=ALU.is_ge),
              inc=("dve", 1), chain=True)
            A("dve", lambda E: E.tensor_reduce(w1(27), scw[:, 3 * S:4 * S],
                                               axis=AX.X, op=ALU.add),
              waits=[t_exp], inc=("dve", 1), chain=True)
            A("dve", lambda E: E.tensor_tensor(scw[:, 5 * S:6 * S],
                                               scw[:, 3 * S:4 * S],
                                               scw[:, 4 * S:5 * S], op=ALU.mult),
              inc=("dve", 1), chain=True)
            A("dve", lambda E: E.tensor_reduce(w1(28), scw[:, 5 * S:6 * S],
                                               axis=AX.X, op=ALU.add),
              inc=("dve", 1), chain=True)
            A("dve", lambda E: E.tensor_scalar(w1(29), w1(27), 1e-8, None,
                                               op0=ALU.mult),
              inc=("dve", 1), chain=True)
            A("dve", lambda E: E.tensor_tensor(w1(30), w1(28), w1(29), op=ALU.add),
              inc=("dve", 1), chain=True)
            A("dve", lambda E: E.reciprocal(w1(31), w1(30)),
              inc=("dve", 1), chain=True)
            A("dve", lambda E: E.tensor_scalar(scw[:, 6 * S:7 * S],
                                               scw[:, 5 * S:6 * S], w1(31), None,
                                               op0=ALU.mult),
              inc=("dve", 1), chain=True)
            A("dve", lambda E: E.tensor_tensor(scw[:, 7 * S:8 * S],
                                               scw[:, 6 * S:7 * S], awb[:],
                                               op=ALU.mult),
              waits=[tok_awb], inc=("dve", 1), chain=True)
            A("dve", lambda E: E.tensor_reduce(w1(32), scw[:, 7 * S:8 * S],
                                               axis=AX.X, op=ALU.add),
              inc=("dve", 1), chain=True)
            A("dve", lambda E, rt=rt: E.tensor_scalar(
                gps[:, rt:rt + 1], w1(32), 0.2, 0.9, op0=ALU.mult, op1=ALU.add),
              inc=("dve", 1), chain=True)
            t_gbf = A("dve", lambda E: E.tensor_copy(ptp[1][:, 0:S],
                                                     scw[:, 6 * S:7 * S]),
                      inc=("dve", 1), chain=True)
            t_gtr = A("pe", lambda E: E.transpose(scp0_bf[0:S, 1024:1152], ptp[1][:, 0:S],
                                                  ids[:]),
                      waits=[t_gbf, (g_done.get(rt - 1) if rt else None)],
                      inc=("pe", 1))
            g_done[rt] = A("dve", lambda E, r0=r0: E.tensor_copy(
                gT(r0, r0 + 128), scp0_bf[0:S, 1024:1152]),
                waits=[t_gtr], inc=("dve", 1), chain=True)

        gut_toks = {}
        hu_last = None
        for gi in range(NG):
            r0 = gi * GR
            gx = {}
            for jc in range(JC):
                gxw = [g_done[ORT - 1], tok_consts, tok_awb]
                if jc > 0:
                    gxw.append(gx[jc - 1])
                elif gi > 0:
                    gxw.append(gx_prev_last)
                t_gx = A("pe", lambda E, jc=jc, r0=r0: E.matmul(
                    scp[1][:, 0:GR], ems[:, jc * 128:(jc + 1) * 128],
                    gT(r0, r0 + GR), start=True, stop=True),
                    waits=gxw, inc=("pe", 1))
                gx[jc] = A("dve", lambda E, jc=jc: E.tensor_copy(
                    gexp(jc), scp[1][:, 0:GR]),
                    waits=[t_gx], inc=("dve", 1), chain=True)
            gx_prev_last = gx[JC - 1]
            for jc in range(JC):
                n = gi * JC + jc
                waits = [tok_u, tok_p1]
                if n >= 4:
                    waits.append(gut_toks[n - 4])

                def mk_hu(jc=jc, r0=r0, n=n):
                    def f(E):
                        for dc in range(DC):
                            ins = E.matmul(pf[n % 4][:, 0:GR],
                                           uchunk(dc, jc * 128, (jc + 1) * 128),
                                           xnt_own(dc, r0, r0 + GR),
                                           start=(dc == 0), stop=(dc == DC - 1))
                        return ins
                    return f
                hu_last = A("pe", mk_hu(), waits=waits, inc=("pe", 1))
                gut_toks[n] = A("dve", lambda E, jc=jc, r0=r0, n=n: E.tensor_tensor(
                    gut(jc, r0, r0 + GR), pf[n % 4][:, 0:GR], gexp(jc),
                    op=ALU.mult),
                    waits=[hu_last, gx[jc], tok_p2b_q0, tok_p2b_q1],
                    inc=("dve", 1), chain=True)
        tok_p3_dve = S_.tok("dve")
        tok_p3_pe = S_.tok("pe")

        tok_wk = dma("gq", "w", pw[:, 0:DC * D].rearrange("p (dc n) -> p dc n", n=D),
                     wk.rearrange("(dc p) n -> p dc n", p=128),
                     waits=[tok_p2b_pe])

        kev = {}
        kmm = {}
        kdma = {}
        kev_of = {}
        kst_ids = {}
        ids0 = ln_stats_batch([xb[sb * 128:(sb + 1) * 128, :]
                               for sb in range(NSUB)])
        kev_of[0] = max(ln_tr(i, lambda dc, a, bb: xtc(0, dc, a, bb),
                              sb * 128) for sb, i in enumerate(ids0))
        for c in range(NCH):
            nbk = None
            if c + 1 < NCH:
                nbk = LnBatch(
                    [xb[(c + 1) * CH + sb * 128:(c + 1) * CH + (sb + 1) * 128, :]
                     for sb in range(NSUB)],
                    extra_waits=[kdma.get(c - 1)])
                kst_ids[c + 1] = nbk.ids
            for kc in range(DC):
                if nbk is not None and kc < NSUB + 2:
                    nbk.step()
                waits = [kev_of[c], tok_wk]
                gn = c * DC + kc
                if gn >= 4:
                    waits.append(kev[gn - 4])

                def mk_kmm(c=c, kc=kc):
                    def f(E, _kk=None):
                        for dc in range(DC):
                            ins = E.matmul(pf[kc % 4][:, 0:CH],
                                           wchunk(dc, kc * 128, (kc + 1) * 128),
                                           xtc(c % 2, dc, 0, CH),
                                           start=(dc == 0), stop=(dc == DC - 1))
                        return ins
                    return f
                kmm[gn] = A("pe", mk_kmm(), waits=waits, inc=("pe", 1))
                w2 = [kmm[gn]]
                if c > 1:
                    w2.append(kdma[c - 2])
                kev[gn] = A("act", lambda E, c=c, kc=kc: E.activation(
                    out=stg(c % 2, kc, 0, CH), in_=pf[kc % 4][:, 0:CH],
                    func=AF.Identity,
                    bias=bks[:, kc:kc + 1]), waits=w2, inc=("act", 1), chain=True)
                if c + 1 < NCH and kc % LNQ == LNQ - 1:
                    sb = kc // LNQ
                    while nbk.k < sb + 3:
                        nbk.step()
                    ev = ln_tr(kst_ids[c + 1][sb],
                               lambda dc, a, bb, c=c: xtc((c + 1) % 2, dc, a, bb),
                               sb * 128)
                    if sb == NSUB - 1:
                        kev_of[c + 1] = ev
            if nbk is not None:
                nbk.finish()
            _kd = dma(
                "sy", f"kx{c % 2}",
                kt_scr.rearrange("(kc p) r -> p kc r", p=128)[:, :, c * CH:(c + 1) * CH],
                pa[:, STG_OFF[c % 2]: STG_OFF[c % 2] + DC * CH
                   ].rearrange("p (kc r) -> p kc r", r=CH),
                waits=[kev[c * DC + DC - 1]])
            t_xd = dma(
                "sy", f"kx{c % 2}",
                xnt_scr.rearrange("(dc p) r -> p dc r", p=128)[:, :, c * CH:(c + 1) * CH],
                pa[:, XTC_OFF[c % 2]: XTC_OFF[c % 2] + DC * CH
                   ].rearrange("p (dc r) -> p dc r", r=CH),
                waits=[kev_of[c]])
            kdma[c] = t_xd
        tok_p2a_ld0 = S_.tok("kx0")
        tok_p2a_ld1 = S_.tok("kx1")
        tok_p2a_pe = S_.tok("pe")

        tok_wv = dma("gq", "w", pw[:, 0:DC * D].rearrange("p (dc n) -> p dc n", n=D),
                     wv.rearrange("(dc p) n -> p dc n", p=128),
                     waits=[tok_p2b_pe, tok_p3_pe, tok_p2a_pe])
        tok_masks = dma("gq", "mk", pm[:, 0:NKS * QT], mk_i, waits=[tok_p3_pe])

        vev = {}
        vdma = {}
        vmm = {}
        vxl = {}
        for c in range(NCH):
            waits = [tok_p2a_ld0, tok_p2a_ld1, tok_p3_dve, tok_p2b_q0, tok_p2b_q1, tok_p2a_pe]
            if c > 1:
                waits.append(vmm[c - 2])
            vxl[c] = dma(
                "sy", f"vx{c % 2}",
                pa[:, XTC_OFF[c % 2]:XTC_OFF[c % 2] + DC * CH
                   ].rearrange("p (dc r) -> p dc r", r=CH),
                xnt_scr.rearrange("(dc p) r -> p dc r", p=128)[:, :, c * CH:(c + 1) * CH],
                waits=waits)
            for rs in range(NSUB):
                for vc in range(NVC):
                    n = rs * NVC + vc
                    gn = c * NSUB * NVC + n
                    waits = [vxl[c], tok_wv]
                    if gn >= 4:
                        waits.append(vev[gn - 4])

                    def mk_vmm(rs=rs, vc=vc, n=n, c=c):
                        def f(E):
                            for dc in range(DC):
                                ins = E.matmul(pf[n % 4][:, 0:512],
                                               xtc(c % 2, dc, rs * 128, (rs + 1) * 128),
                                               wchunk(dc, vc * 512, (vc + 1) * 512),
                                               start=(dc == 0), stop=(dc == DC - 1))
                            return ins
                        return f
                    vmm[c] = A("pe", mk_vmm(), waits=waits, inc=("pe", 1))
                    w2 = [vmm[c]]
                    if c > 1:
                        w2.append(vdma[c - 2])
                    vev[gn] = A("dve", lambda E, rs=rs, vc=vc, n=n, c=c: E.tensor_copy(
                        stgv(c % 2, rs, vc * 512, (vc + 1) * 512), pf[n % 4][:, 0:512]),
                        waits=w2, inc=("dve", 1), chain=True)
            vdma[c] = dma(
                "sy", f"vw{c % 2}",
                v_scr[c * CH:(c + 1) * CH, :].rearrange("(rs p) n -> p rs n", p=128),
                pa[:, STG_OFF[c % 2]: STG_OFF[c % 2] + NSUB * D
                   ].rearrange("p (rs n) -> p rs n", n=D),
                waits=[vev[c * NSUB * NVC + NSUB * NVC - 1]])
        tok_p2c_ld0 = S_.tok("vw0")
        tok_p2c_ld1 = S_.tok("vw1")
        tok_p2c_pe = S_.tok("pe")

        tok_vflat = dma("gq", "vf", pv[:, 0:JC * D].rearrange("p (jc n) -> p jc n", n=D),
                        v_fl.rearrange("(jc p) n -> p jc n", p=128),
                        waits=[tok_p3_dve, ("act", S_.n["act"]), ("dve", S_.n["dve"])])

        t_o0 = A("dve", lambda E: E.memset(vh_view(0)[:, :, 128:129], 1.0),
                 waits=[tok_p2c_pe], inc=("dve", 1), chain=True)
        tok_vones = A("dve", lambda E: E.memset(vh_view(1)[:, :, 128:129], 1.0),
                      inc=("dve", 1), chain=True)

        tok_pre_attn_dve = ("dve", S_.n["dve"])

        def pso_(j):
            return pf[j // 2][:, (j % 2) * 256:(j % 2) * 256 + 129]

        def pso_d(j):
            return pf[j // 2][:, (j % 2) * 256:(j % 2) * 256 + 128]

        def pso_l(j):
            return pf[j // 2][:, (j % 2) * 256 + 128:(j % 2) * 256 + 129]

        def scb_(k, q0, q1):
            P = (k // 2) % 2
            o = (k % 2) * QT
            return scp[P][:, o + q0: o + q1]

        def ptp_(k, q0, q1):
            P = (k // 2) % 2
            o = (k % 2) * QT
            return ptp[P][:, o + q0: o + q1]

        head_pe = {}
        head_ev = {}
        head_ld = {}
        exp_toks = {}
        mask_toks = {}
        sweep_ev = {}
        nvd = (NSUBK + 7) // 8
        for h in range(H):
            beta = h % 2
            waits = [tok_p2a_ld0, tok_p2a_ld1, tok_p2b_q0, tok_p2b_q1, tok_p2c_ld0, tok_p2c_ld1, tok_vones, tok_p2a_pe,
                     S_.tok("pe") if h == 0 else None]
            if h >= 2:
                waits += [head_pe[h - 2], head_ev[h - 2]]
            t = dma("sy", f"h{beta}", kth(beta, 0, T), kt_scr[h * 128:(h + 1) * 128, :],
                    waits=waits)
            for d in range(nvd):
                s0, s1 = d * 8, min((d + 1) * 8, NSUBK)
                t = dma("sy", f"h{beta}", vh_view(beta)[:, s0:s1, 0:128],
                        v_scr[s0 * 128:s1 * 128, h * 128:(h + 1) * 128
                              ].rearrange("(s p) c -> p s c", p=128))
            head_ld[h] = t = dma(
                "sy", f"h{beta}",
                xs[beta][:, 0:OWN],
                qu_scr[:, h * OWN:(h + 1) * OWN])

            for sw in range(1, NSW + 1):
                nk = sw * NKS
                pvq = []
                for k in range(nk):
                    if k >= 4:
                        war = exp_toks[(h, sw, k - 4)]
                    elif sw > 1:
                        war = exp_toks[(h, sw - 1, (sw - 1) * NKS - 4 + k)]
                    elif h > 0:
                        war = exp_toks[(h - 1, NSW, NSW * NKS - 4 + k)]
                    else:
                        war = None
                    sw_waits = [head_ld[h], war]
                    if h == 0 and sw == 1 and k in (0, 1, 2, 3):
                        sw_waits.append(tok_pre_attn_dve)
                    t_sc = A("pe", lambda E, b=beta, k=k, sw=sw: E.matmul(
                        scb_(k, 0, QT), kth(b, k * 128, (k + 1) * 128),
                        xs[b][:, (sw - 1) * QT: sw * QT],
                        start=True, stop=True),
                        waits=sw_waits, inc=("pe", 1))
                    if k % 2 == 1:
                        ew = [t_sc]
                        for km in (k - 4, k - 3):
                            if km >= nk - NKS and km >= 0:
                                ew.append(mask_toks[(h, sw, km)])
                            elif km < 0 and (sw > 1 or h > 0):
                                psw = (h, sw - 1) if sw > 1 else (h - 1, NSW)
                                kk2 = psw[1] * NKS + km
                                if (psw[0], psw[1], kk2) in mask_toks:
                                    ew.append(mask_toks[(psw[0], psw[1], kk2)])
                        tex = A("act", lambda E, k=k: E.activation(
                            out=ptp_(k - 1, 0, 2 * QT), in_=scb_(k - 1, 0, 2 * QT),
                            func=AF.Exp, bias=cst[:], scale=SCALE),
                            waits=ew + [tok_cst], inc=("act", 1), chain=True)
                        exp_toks[(h, sw, k - 1)] = tex
                        exp_toks[(h, sw, k)] = tex
                        for km in (k - 1, k):
                            if km >= nk - NKS:
                                j = km - (nk - NKS)
                                mask_toks[(h, sw, km)] = A(
                                    "dve", lambda E, km=km, j=j: E.tensor_tensor(
                                        ptp_(km, 0, QT), ptp_(km, 0, QT),
                                        mask_j(j), op=ALU.mult),
                                    waits=[tex, tok_masks],
                                    inc=("dve", 1), chain=True)
                        for km in (k - 1, k):
                            if km >= nk - NKS:
                                pvw = [mask_toks[(h, sw, km)]]
                            else:
                                pvw = [tex]
                            if km == 0:
                                if sw > 1:
                                    pvw.append(sweep_ev[(h, sw - 1)])
                                elif h > 0:
                                    pvw.append(sweep_ev[(h - 1, NSW)])
                                else:
                                    pvw.append(tok_p3_dve)

                            def mk_pv(b=beta, km=km, nk=nk):
                                def f(E):
                                    for j in range(NQS):
                                        ins = E.matmul(pso_(j),
                                                       ptp_(km, j * 128, (j + 1) * 128),
                                                       vh(b, km),
                                                       start=(km == 0 and j % 2 == 0),
                                                       stop=(km == nk - 1 and
                                                             (j % 2 == 1 or j == NQS - 1)),
                                                       skip_group_check=True)
                                    return ins
                                return f
                            pvq.append((mk_pv(), pvw))
                    while len(pvq) > 2:
                        A("pe", *pvq.pop(0))
                tpv = None
                for i2, (fn2, w2_) in enumerate(pvq):
                    tpv = A("pe", fn2, waits=w2_,
                            inc=("pe", 1) if i2 == len(pvq) - 1 else None)
                head_pe[h] = tpv
                sweep_pe = tpv
                t_ev = None
                for j in range(NQS):
                    rt = ((sw - 1) * QT + j * 128) // 128
                    A("dve", lambda E, j=j: E.reciprocal(w1(40 + j),
                                                         pso_l(j)),
                      waits=[sweep_pe], inc=("dve", 1), chain=True)
                    t_ev = A("dve", lambda E, j=j, rt=rt, h=h: E.tensor_scalar(
                        ac(rt, h * 128, (h + 1) * 128), pso_d(j),
                        w1(40 + j), None, op0=ALU.mult),
                        inc=("dve", 1), chain=True)
                sweep_ev[(h, sw)] = t_ev
            head_ev[h] = sweep_ev[(h, NSW)]
        tok_p4_dve = S_.tok("dve")
        tok_p4_pe = S_.tok("pe")
        tok_p4_act = S_.tok("act")

        tok_wo = dma("gq", "w", pw[:, 0:DC * D].rearrange("p (dc n) -> p dc n", n=D),
                     wo.rearrange("(dc p) n -> p dc n", p=128),
                     waits=[tok_p4_pe])

        pairs = [(a, b) for a in range(ORT) for b in range(a, DC)]
        p5_ev = {}
        for idx, (a, b) in enumerate(pairs):
            par = idx % 2
            n_tr = 1 if a == b else 2
            waits = [tok_p4_dve, tok_consts]
            if idx >= 2:
                waits.append(p5_ev[idx - 2])
            else:
                waits.append(tok_p4_act)

            def mk_tr(a=a, b=b, par=par, n_tr=n_tr):
                def f(E):
                    ins = E.transpose(pbv(par, 0, 128), ac(a, b * 128, (b + 1) * 128), ids[:])
                    if n_tr == 2:
                        ins = E.transpose(pbv(par, 128, 256), ac(b, a * 128, (a + 1) * 128), ids[:])
                    return ins
                return f
            t_tr = A("pe", mk_tr(), waits=waits, inc=("pe", n_tr))
            t_e1 = A("dve", lambda E, a=a, b=b, par=par: E.tensor_scalar(
                acT(b, a * 128, (a + 1) * 128), pbv(par, 0, 128),
                bvs[:, b:b + 1], None, op0=ALU.add),
                waits=[t_tr], inc=("dve", 1), chain=True)
            if a != b:
                t_e1 = A("dve", lambda E, a=a, b=b, par=par: E.tensor_scalar(
                    acT(a, b * 128, (b + 1) * 128), pbv(par, 128, 256),
                    bvs[:, a:a + 1], None, op0=ALU.add),
                    inc=("dve", 1), chain=True)
            p5_ev[idx] = t_e1
        tok_p5_dve = S_.tok("dve")

        tok_bo = dma("sy", "bo", bo_b(0, D), bo_i,
                     waits=[tok_p4_dve, tok_p4_act])

        e1t = {}
        e2t = {}
        e3t = {}
        stt = {}
        xlt = {}
        for rt in range(ORT):
            for cc in range(NVC):
                n = rt * NVC + cc
                xw = [tok_bo]
                if n >= 2:
                    xw.append(e3t[n - 2])
                xlt[n] = dma("sy", f"px{n % 2}", p6x(n % 2),
                             xo[rt * 128:(rt + 1) * 128, cc * 512:(cc + 1) * 512],
                             waits=xw)
                mw = [tok_p3_dve, tok_vflat, tok_wo, tok_p5_dve]
                if n >= 2:
                    mw += [e1t[n - 2], e2t[n - 2]]

                def mk_f(rt=rt, cc=cc, n=n):
                    def f(E):
                        for jc in range(JC):
                            ins = E.matmul(pf[n % 2][:, 0:512],
                                           gut(jc, rt * 128, (rt + 1) * 128),
                                           vflat(jc, cc * 512, (cc + 1) * 512),
                                           start=(jc == 0), stop=(jc == JC - 1))
                        return ins
                    return f
                t_f = A("pe", mk_f(), waits=mw, inc=("pe", 1))

                def mk_o(rt=rt, cc=cc, n=n):
                    def f(E):
                        for fc in range(DC):
                            ins = E.matmul(pf[2 + n % 2][:, 0:512],
                                           acT(fc, rt * 128, (rt + 1) * 128),
                                           wchunk(fc, cc * 512, (cc + 1) * 512),
                                           start=(fc == 0), stop=(fc == DC - 1))
                        return ins
                    return f
                t_o = A("pe", mk_o(), inc=("pe", 1))

                w1_ = [t_f, tok_p3_dve]
                if n >= 3:
                    w1_.append(stt[n - 3])
                e1t[n] = A("act", lambda E, rt=rt, n=n: E.activation(
                    out=p6t(n % 3), in_=pf[n % 2][:, 0:512], func=AF.Copy,
                    scale=gps[:, rt:rt + 1]), waits=w1_, inc=("act", 1),
                    chain=True)
                e2t[n] = A("dve", lambda E, n=n: E.tensor_tensor(
                    p6t(n % 3), p6t(n % 3), pf[2 + n % 2][:, 0:512], op=ALU.add),
                    waits=[e1t[n], t_o], inc=("dve", 1), chain=True)
                A("dve", lambda E, n=n: E.tensor_tensor(
                    p6t(n % 3), p6t(n % 3), p6x(n % 2), op=ALU.add),
                  waits=[xlt[n]], inc=("dve", 1), chain=True)
                e3t[n] = A("dve", lambda E, n=n, cc=cc: E.tensor_tensor(
                    p6t(n % 3), p6t(n % 3), bo_b(cc * 512, (cc + 1) * 512),
                    op=ALU.add), waits=[tok_bo], inc=("dve", 1), chain=True)
                stt[n] = dma("sy", f"o{n % 3}", out[rt * 128:(rt + 1) * 128, cc * 512:(cc + 1) * 512],
                             p6t(n % 3), waits=[e3t[n]])

        final = dict(S_.n)
        S_.add("gq", None, waits=[(s, v) for s, v in final.items()])

        semnames = set(S_.n.keys())
        for ename in S_.ENG:
            for fn, waits, inc in S_.ops[ename]:
                for sem, val in waits:
                    semnames.add(sem)
        sems = {name: e(nc.semaphore("s_" + name)) for name in sorted(semnames)}
        block = e(nc.Block())

        def emit_wm(engobj, ename):
            water = {}
            for fn, waits, inc in S_.ops[ename]:
                for sem, val in waits:
                    if water.get(sem, 0) >= val:
                        continue
                    engobj.wait_ge(sems[sem], val)
                    water[sem] = val
                if fn is None:
                    continue
                ins = fn(engobj)
                if inc is not None:
                    ins.then_inc(sems[inc[0]], inc[1])

        @block.sync
        def _(E):
            emit_wm(E, "sy")

        @block.gpsimd
        def _(E):
            emit_wm(E, "gq")

        @block.tensor
        def _(E):
            emit_wm(E, "pe")

        @block.vector
        def _(E):
            emit_wm(E, "dve")

        @block.scalar
        def _(E):
            emit_wm(E, "act")

    return nc


def host_inputs(cfg, core, ins):
    D, T = cfg["D"], cfg["T"]
    OWN, QT, NSW, NKS, DC, JC, S = (cfg["OWN"], cfg["QT"], cfg["NSW"],
                                    cfg["NKS"], cfg["DC"], cfg["JC"], cfg["S"])
    b, g = core // 4, core % 4
    x = ins["L0_attn_x"]
    xb = np.ascontiguousarray(x[b], dtype=np.float32)
    blocks = [4 * s + g for s in range(NSW)]
    xo = np.ascontiguousarray(
        np.concatenate([x[b][blk * QT:(blk + 1) * QT] for blk in blocks], axis=0),
        dtype=np.float32)
    H = cfg["H"]
    TPH, QPH = cfg["T"] // H, QT // H
    qrows = []
    for h in range(H):
        for s in range(NSW):
            base = h * TPH + (QT * (4 * s + g)) // H
            qrows.extend(range(base, base + QPH))
    xq = np.ascontiguousarray(x[b][np.array(qrows)], dtype=np.float32)
    masks = np.zeros((128, NKS * QT), dtype=ml_dtypes.bfloat16)
    kl = np.arange(128)[:, None]
    q = np.arange(QT)[None, :]
    for j in range(NKS):
        masks[:, j * QT:(j + 1) * QT] = (j * 128 + kl <= g * QT + q).astype(
            ml_dtypes.bfloat16)
    e_mat = np.zeros((S, 128 * JC), dtype=ml_dtypes.bfloat16)
    for jj in range(128 * JC):
        e_mat[jj // 16, jj] = 1
    U = np.asarray(ins["L0_ffn_U"], np.float32)
    m = {
        "xb": xb,
        "xo": xo,
        "xq": xq,
        "wq": ins["L0_attn_Wq"], "wk": ins["L0_attn_Wk"],
        "wv": ins["L0_attn_Wv"], "wo": ins["L0_attn_Wo"],
        "bq": np.ascontiguousarray(np.asarray(ins["L0_attn_bq"], np.float32).reshape(DC, 128).T),
        "bk": np.ascontiguousarray(np.asarray(ins["L0_attn_bk"], np.float32).reshape(DC, 128).T),
        "bv": np.ascontiguousarray(np.asarray(ins["L0_attn_bv"], np.float32).reshape(DC, 128).T),
        "bo_bc": np.ascontiguousarray(np.broadcast_to(
            np.asarray(ins["L0_attn_bo"], np.float32).reshape(1, D), (128, D))),
        "wr_t": np.ascontiguousarray(np.asarray(ins["L0_ffn_Wr"], np.float32).T),
        "u_flat": np.ascontiguousarray(U.transpose(1, 0, 2).reshape(D, -1)),
        "v_flat": np.ascontiguousarray(np.asarray(ins["L0_ffn_V"], np.float32).reshape(-1, D)),
        "attr_t": np.ascontiguousarray(np.asarray(ins["L0_ffn_attr"], np.float32).T),
        "wa_t": np.ascontiguousarray(np.asarray(ins["L0_ffn_Wa"], np.float32).reshape(1, S).T),
        "masks": masks,
        "ident": np.eye(128, dtype=ml_dtypes.bfloat16),
        "ones_f": np.ones((1, 128), np.float32),
        "e_mat": e_mat,
    }
    return {k: np.ascontiguousarray(np.asarray(v)) for k, v in m.items()}


def assemble(cfg, results):
    D, T, QT, NSW = cfg["D"], cfg["T"], cfg["QT"], cfg["NSW"]
    full = np.zeros((2, T, D), np.float32)
    for core in range(8):
        b, g = core // 4, core % 4
        o = np.asarray(results[core]["out"], np.float32)
        for s in range(NSW):
            blk = 4 * s + g
            full[b, blk * QT:(blk + 1) * QT] = o[s * QT:(s + 1) * QT]
    return full


def kernel(**inputs):
    cfg = mkcfg(FULL)
    nc = build_nc(cfg)
    in_maps = [host_inputs(cfg, c, inputs) for c in range(8)]
    res = run_bass_kernel_spmd(nc, in_maps, core_ids=list(range(8)))
    return assemble(cfg, res.results)


# version 16
# speedup vs baseline: 1.6049x; 1.0654x over previous
import math
from contextlib import ExitStack

import numpy as np
import ml_dtypes
import concourse.bass as bass
import concourse.mybir as mybir
from concourse.bass_utils import run_bass_kernel_spmd

F32 = mybir.dt.float32
BF16 = mybir.dt.bfloat16
AF = mybir.ActivationFunctionType
ALU = mybir.AluOpType
AX = mybir.AxisListType

FULL = dict(D=2048, T=8192, H=16)
TINY = dict(D=512, T=2048, H=4)

C_SHIFT = 12.0


def mkcfg(base):
    c = dict(base)
    D, T, H = c["D"], c["T"], c["H"]
    c["DH"] = 128
    assert H * c["DH"] == D
    c["DC"] = D // 128
    c["OWN"] = T // 4
    assert c["OWN"] == D
    c["QT"] = T // 16
    assert c["QT"] % 128 == 0
    c["NQS"] = c["QT"] // 128
    c["NSW"] = 4
    c["NKS"] = T // 512
    c["CH"] = c["QT"]
    c["NCH"] = T // c["CH"]
    c["NSUB"] = c["CH"] // 128
    c["S"], c["R"] = 32, 16
    c["J"] = c["S"] * c["R"]
    c["JC"] = c["J"] // 128
    c["GR"] = min(512, c["OWN"])
    c["NG"] = c["OWN"] // c["GR"]
    c["ORT"] = c["OWN"] // 128
    assert c["ORT"] == c["DC"]
    c["SCALE"] = 1.0 / math.sqrt(c["DH"])
    return c


class Sched:
    ENG = ("sy", "gq", "pe", "act", "dve")

    def __init__(self):
        self.ops = {e: [] for e in self.ENG}
        from collections import defaultdict
        self.n = defaultdict(int)
        for s in ("pe", "act", "dve"):
            self.n[s] = 0
        self.chain_last = {"act": 0, "dve": 0}

    def add(self, eng, fn, waits=(), inc=None, chain=False):
        w = [x for x in waits if x is not None and x[1] > 0]
        if chain and eng in self.chain_last:
            if self.chain_last[eng] > 0:
                w.append((eng, self.chain_last[eng]))
        tok = None
        if inc is not None:
            sem, k = inc
            self.n[sem] += k
            tok = (sem, self.n[sem])
            if eng in self.chain_last and sem == eng:
                self.chain_last[eng] = self.n[sem]
        self.ops[eng].append((fn, w, inc))
        return tok

    def tok(self, sem):
        return (sem, self.n[sem])

    def barrier(self):
        snap = dict(self.n)
        for e in self.ENG:
            self.add(e, None, waits=[(s, v) for s, v in snap.items()])
        return snap


def build_nc(cfg):
    D, T, H, DH = cfg["D"], cfg["T"], cfg["H"], cfg["DH"]
    DC, OWN, QT, NQS = cfg["DC"], cfg["OWN"], cfg["QT"], cfg["NQS"]
    NSW, NKS, CH, NCH, NSUB = cfg["NSW"], cfg["NKS"], cfg["CH"], cfg["NCH"], cfg["NSUB"]
    S, J, JC = cfg["S"], cfg["J"], cfg["JC"]
    GR, NG, ORT, SCALE = cfg["GR"], cfg["NG"], cfg["ORT"], cfg["SCALE"]
    NVC = D // 512
    NSUBK = T // 128

    nc = bass.Bass("TRN2", target_bir_lowering=False)

    xb = nc.dram_tensor("xb", [T, D], F32, kind="ExternalInput").ap()
    xo = nc.dram_tensor("xo", [OWN, D], F32, kind="ExternalInput").ap()
    xq = nc.dram_tensor("xq", [OWN, D], F32, kind="ExternalInput").ap()
    wq = nc.dram_tensor("wq", [D, D], F32, kind="ExternalInput").ap()
    wk = nc.dram_tensor("wk", [D, D], F32, kind="ExternalInput").ap()
    wv = nc.dram_tensor("wv", [D, D], F32, kind="ExternalInput").ap()
    wo = nc.dram_tensor("wo", [D, D], F32, kind="ExternalInput").ap()
    bq_i = nc.dram_tensor("bq", [128, DC], F32, kind="ExternalInput").ap()
    bk_i = nc.dram_tensor("bk", [128, DC], F32, kind="ExternalInput").ap()
    bv_i = nc.dram_tensor("bv", [128, DC], F32, kind="ExternalInput").ap()
    bo_i = nc.dram_tensor("bo_bc", [128, D], F32, kind="ExternalInput").ap()
    wr_i = nc.dram_tensor("wr_t", [D, S], F32, kind="ExternalInput").ap()
    u_fl = nc.dram_tensor("u_flat", [D, J], F32, kind="ExternalInput").ap()
    v_fl = nc.dram_tensor("v_flat", [J, D], F32, kind="ExternalInput").ap()
    at_i = nc.dram_tensor("attr_t", [S, S], F32, kind="ExternalInput").ap()
    wa_i = nc.dram_tensor("wa_t", [S, 1], F32, kind="ExternalInput").ap()
    mk_i = nc.dram_tensor("masks", [128, NKS * QT], BF16, kind="ExternalInput").ap()
    id_i = nc.dram_tensor("ident", [128, 128], BF16, kind="ExternalInput").ap()
    of_i = nc.dram_tensor("ones_f", [1, 128], F32, kind="ExternalInput").ap()
    em_i = nc.dram_tensor("e_mat", [S, 128 * JC], BF16, kind="ExternalInput").ap()

    out = nc.dram_tensor("out", [OWN, D], F32, kind="ExternalOutput").ap()

    kt_scr = nc.dram_tensor("kt_scr", [D, T], BF16, kind="Internal").ap()
    v_scr = nc.dram_tensor("v_scr", [T, D], BF16, kind="Internal").ap()
    qu_scr = nc.dram_tensor("qu_scr", [128, (D // 128) * OWN], BF16, kind="Internal").ap()
    xnt_scr = nc.dram_tensor("xnt_scr", [D, T], BF16, kind="Internal").ap()

    with ExitStack() as ctx:
        e = ctx.enter_context

        pa = e(nc.sbuf_tensor("pa", [128, DC * OWN], BF16))
        PW_COLS = max(DC * D, 2 * T + 2 * 129 * NSUBK)
        pw = e(nc.sbuf_tensor("pw", [128, PW_COLS], BF16))
        pm = e(nc.sbuf_tensor("pm", [128, max(DC * J, NKS * QT, 2 * (D + 1024 + 3 * 512))], BF16))
        pv = e(nc.sbuf_tensor("pv", [128, max(JC * D, JC * GR + OWN + 2 * D)], BF16))
        pg = e(nc.sbuf_tensor("pg", [128, JC * OWN], BF16))
        xs = [e(nc.sbuf_tensor(f"xs{i}", [128, max(D, OWN)], BF16)) for i in range(4)]
        ptp = [e(nc.sbuf_tensor(f"ptp{i}", [128, 2 * QT], BF16)) for i in range(2)]
        pt = [ptp[0][:, 0:QT], ptp[0][:, QT:2 * QT], ptp[1][:, 0:QT]]
        wrk = e(nc.sbuf_tensor("wrk", [128, 64], F32))
        scw = e(nc.sbuf_tensor("scw", [128, 8 * S], F32))
        gps = e(nc.sbuf_tensor("gps", [128, ORT], F32))
        bqs = e(nc.sbuf_tensor("bqs", [128, DC], F32))
        bks = e(nc.sbuf_tensor("bks", [128, DC], F32))
        bvs = e(nc.sbuf_tensor("bvs", [128, DC], F32))
        wrs = e(nc.sbuf_tensor("wrs", [128, DC * S], BF16))
        ems = e(nc.sbuf_tensor("ems", [S, 128 * JC], BF16))
        ids = e(nc.sbuf_tensor("ids", [128, 128], BF16))
        onf = e(nc.sbuf_tensor("onf", [1, 128], F32))
        ats = e(nc.sbuf_tensor("ats", [S, S], F32))
        was = e(nc.sbuf_tensor("was", [S, 1], F32))
        awb = e(nc.sbuf_tensor("awb", [128, S], F32))
        aws = e(nc.sbuf_tensor("aws", [1, S], F32))
        cst = e(nc.sbuf_tensor("cst", [128, 1], F32))

        pf = [e(nc.psum_tensor(f"pf{i}", [128, 512], F32)) for i in range(4)]
        scp = [e(nc.psum_tensor(f"scp{i}", [128, 1024], F32)) for i in range(2)]
        scp0_bf = scp[0].bitcast(BF16)

        def pbv(i, a, b):
            return scp0_bf[:, i * 1024 + a: i * 1024 + b]

        block_holder = []

        S_ = Sched()

        def xnt_own(dc, r0, r1):
            return pa[:, dc * OWN + r0: dc * OWN + r1]

        XTC_OFF = [0, 2 * DC * CH]
        STG_OFF = [DC * CH, 3 * DC * CH]

        def xtc(buf, dc, r0, r1):
            o = XTC_OFF[buf]
            return pa[:, o + dc * CH + r0: o + dc * CH + r1]

        def stg(buf, kc, c0, c1):
            o = STG_OFF[buf]
            return pa[:, o + kc * CH + c0: o + kc * CH + c1]

        def stgv(buf, rs, c0, c1):
            o = STG_OFF[buf]
            return pa[:, o + rs * D + c0: o + rs * D + c1]

        def ac(rt, f0, f1):
            return pa[:, rt * D + f0: rt * D + f1]

        def acT(fc, r0, r1):
            return pa[:, fc * OWN + r0: fc * OWN + r1]

        def wchunk(dc, n0, n1):
            return pw[:, dc * D + n0: dc * D + n1]

        def kth(b, k0, k1):
            return pw[:, b * T + k0: b * T + k1]

        VH0 = 2 * T

        def vh(b, sub):
            o = VH0 + b * (129 * NSUBK) + sub * 129
            return pw[:, o: o + 129]

        def vh_view(b):
            o = VH0 + b * (129 * NSUBK)
            return pw[:, o: o + NSUBK * 129].rearrange("p (s c) -> p s c", c=129)

        def uchunk(dc, j0, j1):
            return pm[:, dc * J + j0: dc * J + j1]

        def mask_j(j):
            return pm[:, j * QT:(j + 1) * QT]

        pmf = pm.bitcast(F32)

        def bo_b(c0, c1):
            return pmf[:, c0:c1]

        def p6x(b):
            return pmf[:, D + b * 512: D + (b + 1) * 512]

        def p6t(i):
            return pmf[:, D + 1024 + i * 512: D + 1024 + (i + 1) * 512]

        def vflat(jc, n0, n1):
            return pv[:, jc * D + n0: jc * D + n1]

        def gexp(jc):
            return pv[:, jc * GR:(jc + 1) * GR]

        GT0 = JC * GR

        def gT(r0, r1):
            return pv[0:S, GT0 + r0: GT0 + r1]

        def gut(jc, r0, r1):
            return pg[:, jc * OWN + r0: jc * OWN + r1]

        PVJ = JC * GR + OWN
        sqj = pv[:, PVJ: PVJ + D]
        cpj = pv[:, PVJ + D: PVJ + 2 * D]
        qst = pg[:, 0: 2 * GR]

        def w1(i):
            return wrk[:, i:i + 1]

        A = S_.add

        def dma(eng, sem, outap, inap, waits=()):
            return A(eng, lambda E, o=outap, i=inap: E.dma_start(out=o, in_=i),
                     waits=waits, inc=(sem, 16))

        tok_consts = None
        for o, i in [(ids[:], id_i), (onf[:], of_i), (bqs[:], bq_i), (bks[:], bk_i),
                     (bvs[:], bv_i), (ats[:], at_i), (was[:], wa_i),
                     (ems[:], em_i)]:
            tok_consts = dma("sy", "cst", o, i)
        tok_wrs = dma("gq", "wrs", wrs.rearrange("p (dc s) -> p dc s", s=S),
                      wr_i.rearrange("(dc p) s -> p dc s", p=128))

        def wload(src, waits=()):
            dst = pw[:, 0:DC * D].rearrange("p (dc n) -> p dc n", n=D)
            s3 = src.rearrange("(dc p) n -> p dc n", p=128)
            ta = dma("gq", "w", dst[:, 0:DC // 2, :], s3[:, 0:DC // 2, :],
                     waits=waits)
            tb = dma("gq", "w2", dst[:, DC // 2:DC, :], s3[:, DC // 2:DC, :],
                     waits=waits)
            return (ta, tb)

        tok_wq = wload(wq)
        tok_u = dma("gq", "u", pm[:, 0:DC * J].rearrange("p (dc j) -> p dc j", j=J),
                    u_fl.rearrange("(dc p) j -> p dc j", p=128))
        tok_cst = A("dve", lambda E: E.memset(cst[:], -C_SHIFT),
                    inc=("dve", 1), chain=True)

        ln_i = [0]
        ln_trs = {}
        ln_norm = {}
        ln_sq = {}
        ln_ex2 = {}
        ln_lnt = {}

        def ln_head(src_rows, extra_waits=()):
            i = ln_i[0]
            ln_i[0] += 1
            beta = i % 4
            b2 = i % 2
            xbuf = xs[beta]
            t_x = dma("gq", f"x{beta}", xbuf[:, 0:D], src_rows,
                      waits=[ln_trs.get(i - 4)] + list(extra_waits))
            A("dve", lambda E, xb_=xbuf, b=b2: E.tensor_scalar(
                cpj, xb_[:, 0:D], 1.0, None, op0=ALU.mult, op1=ALU.add,
                accum_out=wrk[:, 10 + b:11 + b]), waits=[t_x], inc=("dve", 1),
                chain=True)
            ln_sq[i] = A("act", lambda E, xb_=xbuf, b=b2: E.activation(
                out=sqj, in_=xb_[:, 0:D], func=AF.Square,
                accum_out=wrk[:, 8 + b:9 + b]),
                waits=[t_x, ln_ex2.get(i - 2)], inc=("act", 1), chain=True)
            return i

        def ln_mid(i):
            b2 = i % 2
            A("dve", lambda E, b=b2: E.tensor_scalar(
                w1(12 + b), w1(10 + b), 1.0 / D, None, op0=ALU.mult),
              inc=("dve", 1), chain=True)
            ln_ex2[i] = A("dve", lambda E, b=b2: E.tensor_scalar(
                w1(14 + b), w1(8 + b), 1.0 / D, None, op0=ALU.mult),
                waits=[ln_sq[i]], inc=("dve", 1), chain=True)
            A("dve", lambda E, b=b2: E.tensor_tensor(
                w1(16 + b), w1(12 + b), w1(12 + b), op=ALU.mult),
              inc=("dve", 1), chain=True)
            A("dve", lambda E, b=b2: E.tensor_tensor(
                w1(18 + b), w1(14 + b), w1(16 + b), op=ALU.subtract),
                waits=[ln_lnt.get(i - 2)], inc=("dve", 1), chain=True)
            t_var = A("dve", lambda E, b=b2: E.tensor_scalar(
                w1(18 + b), w1(18 + b), 1e-5, None, op0=ALU.add),
                inc=("dve", 1), chain=True)
            ln_lnt[i] = A("act", lambda E, b=b2: E.activation(
                out=w1(20 + b), in_=w1(18 + b), func=AF.Ln),
              waits=[t_var], inc=("act", 1), chain=True)
            return A("act", lambda E, b=b2: E.activation(
                out=w1(22 + b), in_=w1(20 + b), func=AF.Exp, scale=-0.5),
                waits=[ln_norm.get(i - 2)], inc=("act", 1), chain=True)

        def ln_tail(i, t_rstd):
            b2 = i % 2
            xbuf = xs[i % 4]
            ln_norm[i] = A("dve", lambda E, xb_=xbuf, b=b2: E.tensor_scalar(
                xb_[:, 0:D], xb_[:, 0:D], w1(12 + b), w1(22 + b),
                op0=ALU.subtract, op1=ALU.mult),
                waits=[t_rstd], inc=("dve", 1), chain=True)

        class LnBatch:

            def __init__(self, srcs, extra_waits=()):
                self.srcs = srcs
                self.extra = list(extra_waits)
                self.ids = []
                self.rstds = {}
                self.k = 0

            def step(self):
                j = self.k
                self.k += 1
                n = len(self.srcs)
                if j < n:
                    self.ids.append(ln_head(self.srcs[j],
                                            self.extra if j == 0 else ()))
                if 1 <= j <= n:
                    self.rstds[j - 1] = ln_mid(self.ids[j - 1])
                if j >= 2 and j - 2 < n:
                    ln_tail(self.ids[j - 2], self.rstds[j - 2])

            def finish(self):
                while self.k < len(self.srcs) + 2:
                    self.step()
                return self.ids

        def ln_stats_batch(srcs, extra_waits=()):
            return LnBatch(srcs, extra_waits).finish()

        def ln_tr(i, dest_fn, dest_r0):
            beta = i % 4
            xbuf = xs[beta]
            t_norm = ln_norm[i]
            ev_toks = []
            tr_toks = []
            for dc in range(DC):
                war = ev_toks[dc - 2] if dc >= 2 else ln_tr.last_ev.get(dc % 2)
                t_tr = A("pe", lambda E, xb_=xbuf, dc=dc: E.transpose(
                    pbv(dc % 2, 0, 128), xb_[:, dc * 128:(dc + 1) * 128], ids[:]),
                    waits=[t_norm, war, tok_consts], inc=("pe", 1))
                tr_toks.append(t_tr)
                t_ev = A("dve", lambda E, dc=dc, r0=dest_r0, df=dest_fn: E.tensor_copy(
                    df(dc, r0, r0 + 128), pbv(dc % 2, 0, 128)),
                    waits=[t_tr], inc=("dve", 1), chain=True)
                ev_toks.append(t_ev)
            ln_tr.last_ev = {0: ev_toks[-2], 1: ev_toks[-1]}
            ln_trs[i] = tr_toks[-1]
            return ev_toks[-1]

        ln_tr.last_ev = {}

        q_mm_last = None
        qev = {}
        qld = {}
        NCH0 = OWN // CH
        LNQ = DC // NSUB
        ev_of = {}
        st_ids = {}
        qdma = {}
        ids0 = ln_stats_batch([xq[sb * 128:(sb + 1) * 128, :]
                               for sb in range(NSUB)])
        ev_of[0] = max(ln_tr(i, lambda dc, a, bb: xtc(0, dc, a, bb),
                             sb * 128) for sb, i in enumerate(ids0))
        for c in range(NCH0):
            nb0 = None
            if c + 1 < NCH0:
                nb0 = LnBatch(
                    [xq[(c + 1) * CH + sb * 128:(c + 1) * CH + (sb + 1) * 128, :]
                     for sb in range(NSUB)])
                st_ids[c + 1] = nb0.ids
            for qc in range(DC):
                if nb0 is not None and qc < NSUB + 2:
                    nb0.step()
                n = c * DC + qc
                waits = [ev_of[c], tok_wq[0]]
                if n >= 4:
                    waits.append(qev[n - 4])

                def mk_qmm(qc=qc, n=n, c=c, d0=0, d1=DC):
                    def f(E):
                        for dc in range(d0, d1):
                            ins = E.matmul(pf[n % 4][:, 0:CH],
                                           wchunk(dc, qc * 128, (qc + 1) * 128),
                                           xtc(c % 2, dc, 0, CH),
                                           start=(dc == 0), stop=(dc == DC - 1),
                                           skip_group_check=True)
                        return ins
                    return f
                A("pe", mk_qmm(d0=0, d1=DC // 2), waits=waits, inc=("pe", 1))
                q_mm_last = A("pe", mk_qmm(d0=DC // 2, d1=DC),
                              waits=[tok_wq[1]], inc=("pe", 1))
                w2 = [q_mm_last]
                if c > 1 and qc == 0:
                    w2.append(qdma[c - 2])
                qev[n] = A("act", lambda E, qc=qc, n=n, c=c: E.activation(
                    out=pa[:, STG_OFF[c % 2]: STG_OFF[c % 2] + DC * CH
                           ].rearrange("p (c2 r) -> p c2 r", r=DC)[:, :, qc:qc + 1],
                    in_=pf[n % 4][:, 0:CH], func=AF.Identity,
                    bias=bqs[:, qc:qc + 1]), waits=w2, inc=("act", 1), chain=True)
                if c + 1 < NCH0 and qc % LNQ == LNQ - 1:
                    sb = qc // LNQ
                    while nb0.k < sb + 3:
                        nb0.step()
                    ev = ln_tr(st_ids[c + 1][sb],
                               lambda dc, a, bb, c=c: xtc((c + 1) % 2, dc, a, bb),
                               sb * 128)
                    if sb == NSUB - 1:
                        ev_of[c + 1] = ev
            if nb0 is not None:
                nb0.finish()
            qdma[c] = dma(
                "sy", f"q{c % 2}",
                qu_scr[:, c * DC * CH:(c + 1) * DC * CH],
                pa[:, STG_OFF[c % 2]: STG_OFF[c % 2] + DC * CH],
                waits=[qev[c * DC + DC - 1]])
        tok_p2b_pe = S_.tok("pe")
        tok_p2b_q0 = S_.tok("q0")
        tok_p2b_q1 = S_.tok("q1")

        S_.add("dve", None, waits=[("q0", S_.n["q0"]), ("q1", S_.n["q1"])])
        S_.add("pe", None, waits=[("q0", S_.n["q0"]), ("q1", S_.n["q1"])])
        NB1 = OWN // 128 // NSUB
        p1_ids = {0: ln_stats_batch(
            [xo[sb * 128:(sb + 1) * 128, :] for sb in range(NSUB)])}
        for c in range(NB1):
            nb = None
            if c + 1 < NB1:
                nb = LnBatch([xo[((c + 1) * NSUB + sb) * 128:
                                 ((c + 1) * NSUB + sb + 1) * 128, :]
                              for sb in range(NSUB)])
            for sb, i in enumerate(p1_ids[c]):
                ln_tr(i, xnt_own, (c * NSUB + sb) * 128)
                if nb is not None:
                    nb.step()
            if nb is not None:
                p1_ids[c + 1] = nb.finish()
        tok_p1 = S_.tok("dve")

        t1 = A("pe", lambda E: E.matmul(scp[0][0:1, 0:S], was[:], ats[:],
                                        start=True, stop=True),
               waits=[tok_consts, tok_p2b_pe, tok_p1], inc=("pe", 1))
        t2 = A("act", lambda E: E.activation(out=aws[:], in_=scp[0][0:1, 0:S],
                                             func=AF.Exp, scale=-1.0),
               waits=[t1], inc=("act", 1), chain=True)
        t3 = A("dve", lambda E: E.tensor_scalar(aws[:], aws[:], 1.0, None,
                                                op0=ALU.add),
               waits=[t2], inc=("dve", 1), chain=True)
        t4 = A("dve", lambda E: E.reciprocal(aws[:], aws[:]),
               inc=("dve", 1), chain=True)
        t5 = A("pe", lambda E: E.matmul(scp[1][:, 0:S], onf[:], aws[:],
                                        start=True, stop=True),
               waits=[t4], inc=("pe", 1))
        tok_awb = A("dve", lambda E: E.tensor_copy(awb[:], scp[1][:, 0:S]),
                    waits=[t5], inc=("dve", 1), chain=True)

        g_done = {}
        for rt in range(ORT):
            r0 = rt * 128

            def mk_sc(rt=rt, r0=r0):
                def f(E):
                    for dc in range(DC):
                        ins = E.matmul(scp[0][:, 0:S],
                                       xnt_own(dc, r0, r0 + 128),
                                       wrs[:, dc * S:(dc + 1) * S],
                                       start=(dc == 0), stop=(dc == DC - 1))
                    return ins
                return f
            waits = [tok_p1, tok_consts, tok_wrs, t2]
            if rt > 0:
                waits.append(g_done[rt - 1])
            t_sc = A("pe", mk_sc(), waits=waits, inc=("pe", 1))

            t_cp = A("dve", lambda E: E.tensor_copy(scw[:, 0:S], scp[0][:, 0:S]),
                     waits=[t_sc], inc=("dve", 1), chain=True)
            A("dve", lambda E: E.tensor_reduce(w1(24), scw[:, 0:S], axis=AX.X,
                                               op=ALU.max, negate=True),
              inc=("dve", 1), chain=True)
            A("dve", lambda E: E.tensor_scalar(w1(25), w1(24), -1.0, None,
                                               op0=ALU.mult),
              inc=("dve", 1), chain=True)
            A("dve", lambda E: E.tensor_scalar(scw[:, S:2 * S], scw[:, 0:S],
                                               w1(25), 1e9, op0=ALU.is_ge,
                                               op1=ALU.mult),
              inc=("dve", 1), chain=True)
            t_ein = A("dve", lambda E: E.tensor_tensor(
                scw[:, 2 * S:3 * S], scw[:, 0:S], scw[:, S:2 * S],
                op=ALU.subtract), inc=("dve", 1), chain=True)
            t_exp = A("act", lambda E: E.activation(
                out=scw[:, 3 * S:4 * S], in_=scw[:, 0:S], func=AF.Exp,
                bias=w1(24)), waits=[t_ein], inc=("act", 1), chain=True)
            A("dve", lambda E: E.tensor_reduce(w1(26), scw[:, 2 * S:3 * S],
                                               axis=AX.X, op=ALU.max),
              inc=("dve", 1), chain=True)
            A("dve", lambda E: E.tensor_scalar(scw[:, 4 * S:5 * S], scw[:, 0:S],
                                               w1(26), None, op0=ALU.is_ge),
              inc=("dve", 1), chain=True)
            A("dve", lambda E: E.tensor_reduce(w1(27), scw[:, 3 * S:4 * S],
                                               axis=AX.X, op=ALU.add),
              waits=[t_exp], inc=("dve", 1), chain=True)
            A("dve", lambda E: E.tensor_tensor(scw[:, 5 * S:6 * S],
                                               scw[:, 3 * S:4 * S],
                                               scw[:, 4 * S:5 * S], op=ALU.mult),
              inc=("dve", 1), chain=True)
            A("dve", lambda E: E.tensor_reduce(w1(28), scw[:, 5 * S:6 * S],
                                               axis=AX.X, op=ALU.add),
              inc=("dve", 1), chain=True)
            A("dve", lambda E: E.tensor_scalar(w1(29), w1(27), 1e-8, None,
                                               op0=ALU.mult),
              inc=("dve", 1), chain=True)
            A("dve", lambda E: E.tensor_tensor(w1(30), w1(28), w1(29), op=ALU.add),
              inc=("dve", 1), chain=True)
            A("dve", lambda E: E.reciprocal(w1(31), w1(30)),
              inc=("dve", 1), chain=True)
            A("dve", lambda E: E.tensor_scalar(scw[:, 6 * S:7 * S],
                                               scw[:, 5 * S:6 * S], w1(31), None,
                                               op0=ALU.mult),
              inc=("dve", 1), chain=True)
            A("dve", lambda E: E.tensor_tensor(scw[:, 7 * S:8 * S],
                                               scw[:, 6 * S:7 * S], awb[:],
                                               op=ALU.mult),
              waits=[tok_awb], inc=("dve", 1), chain=True)
            A("dve", lambda E: E.tensor_reduce(w1(32), scw[:, 7 * S:8 * S],
                                               axis=AX.X, op=ALU.add),
              inc=("dve", 1), chain=True)
            A("dve", lambda E, rt=rt: E.tensor_scalar(
                gps[:, rt:rt + 1], w1(32), 0.2, 0.9, op0=ALU.mult, op1=ALU.add),
              inc=("dve", 1), chain=True)
            t_gbf = A("dve", lambda E: E.tensor_copy(ptp[1][:, 0:S],
                                                     scw[:, 6 * S:7 * S]),
                      inc=("dve", 1), chain=True)
            t_gtr = A("pe", lambda E: E.transpose(scp0_bf[0:S, 1024:1152], ptp[1][:, 0:S],
                                                  ids[:]),
                      waits=[t_gbf, (g_done.get(rt - 1) if rt else None)],
                      inc=("pe", 1))
            g_done[rt] = A("dve", lambda E, r0=r0: E.tensor_copy(
                gT(r0, r0 + 128), scp0_bf[0:S, 1024:1152]),
                waits=[t_gtr], inc=("dve", 1), chain=True)

        gut_toks = {}
        hu_last = None
        for gi in range(NG):
            r0 = gi * GR
            gx = {}
            for jc in range(JC):
                gxw = [g_done[ORT - 1], tok_consts, tok_awb]
                if jc > 0:
                    gxw.append(gx[jc - 1])
                elif gi > 0:
                    gxw.append(gx_prev_last)
                t_gx = A("pe", lambda E, jc=jc, r0=r0: E.matmul(
                    scp[1][:, 0:GR], ems[:, jc * 128:(jc + 1) * 128],
                    gT(r0, r0 + GR), start=True, stop=True),
                    waits=gxw, inc=("pe", 1))
                gx[jc] = A("dve", lambda E, jc=jc: E.tensor_copy(
                    gexp(jc), scp[1][:, 0:GR]),
                    waits=[t_gx], inc=("dve", 1), chain=True)
            gx_prev_last = gx[JC - 1]
            for jc in range(JC):
                n = gi * JC + jc
                waits = [tok_u, tok_p1]
                if n >= 4:
                    waits.append(gut_toks[n - 4])

                def mk_hu(jc=jc, r0=r0, n=n):
                    def f(E):
                        for dc in range(DC):
                            ins = E.matmul(pf[n % 4][:, 0:GR],
                                           uchunk(dc, jc * 128, (jc + 1) * 128),
                                           xnt_own(dc, r0, r0 + GR),
                                           start=(dc == 0), stop=(dc == DC - 1))
                        return ins
                    return f
                hu_last = A("pe", mk_hu(), waits=waits, inc=("pe", 1))
                gut_toks[n] = A("dve", lambda E, jc=jc, r0=r0, n=n: E.tensor_tensor(
                    gut(jc, r0, r0 + GR), pf[n % 4][:, 0:GR], gexp(jc),
                    op=ALU.mult),
                    waits=[hu_last, gx[jc], tok_p2b_q0, tok_p2b_q1],
                    inc=("dve", 1), chain=True)
        tok_p3_dve = S_.tok("dve")
        tok_p3_pe = S_.tok("pe")

        tok_wk = wload(wk, waits=[tok_p2b_pe])

        kev = {}
        kmm = {}
        kdma = {}
        kev_of = {}
        kst_ids = {}
        ids0 = ln_stats_batch([xb[sb * 128:(sb + 1) * 128, :]
                               for sb in range(NSUB)])
        kev_of[0] = max(ln_tr(i, lambda dc, a, bb: xtc(0, dc, a, bb),
                              sb * 128) for sb, i in enumerate(ids0))
        for c in range(NCH):
            nbk = None
            if c + 1 < NCH:
                nbk = LnBatch(
                    [xb[(c + 1) * CH + sb * 128:(c + 1) * CH + (sb + 1) * 128, :]
                     for sb in range(NSUB)],
                    extra_waits=[kdma.get(c - 1)])
                kst_ids[c + 1] = nbk.ids
            for kc in range(DC):
                if nbk is not None and kc < NSUB + 2:
                    nbk.step()
                waits = [kev_of[c], *tok_wk]
                gn = c * DC + kc
                if gn >= 4:
                    waits.append(kev[gn - 4])

                def mk_kmm(c=c, kc=kc):
                    def f(E, _kk=None):
                        for dc in range(DC):
                            ins = E.matmul(pf[kc % 4][:, 0:CH],
                                           wchunk(dc, kc * 128, (kc + 1) * 128),
                                           xtc(c % 2, dc, 0, CH),
                                           start=(dc == 0), stop=(dc == DC - 1))
                        return ins
                    return f
                kmm[gn] = A("pe", mk_kmm(), waits=waits, inc=("pe", 1))
                w2 = [kmm[gn]]
                if c > 1:
                    w2.append(kdma[c - 2])
                kev[gn] = A("act", lambda E, c=c, kc=kc: E.activation(
                    out=stg(c % 2, kc, 0, CH), in_=pf[kc % 4][:, 0:CH],
                    func=AF.Identity,
                    bias=bks[:, kc:kc + 1]), waits=w2, inc=("act", 1), chain=True)
                if c + 1 < NCH and kc % LNQ == LNQ - 1:
                    sb = kc // LNQ
                    while nbk.k < sb + 3:
                        nbk.step()
                    ev = ln_tr(kst_ids[c + 1][sb],
                               lambda dc, a, bb, c=c: xtc((c + 1) % 2, dc, a, bb),
                               sb * 128)
                    if sb == NSUB - 1:
                        kev_of[c + 1] = ev
            if nbk is not None:
                nbk.finish()
            _kd = dma(
                "sy", f"kx{c % 2}",
                kt_scr.rearrange("(kc p) r -> p kc r", p=128)[:, :, c * CH:(c + 1) * CH],
                pa[:, STG_OFF[c % 2]: STG_OFF[c % 2] + DC * CH
                   ].rearrange("p (kc r) -> p kc r", r=CH),
                waits=[kev[c * DC + DC - 1]])
            t_xd = dma(
                "sy", f"kx{c % 2}",
                xnt_scr.rearrange("(dc p) r -> p dc r", p=128)[:, :, c * CH:(c + 1) * CH],
                pa[:, XTC_OFF[c % 2]: XTC_OFF[c % 2] + DC * CH
                   ].rearrange("p (dc r) -> p dc r", r=CH),
                waits=[kev_of[c]])
            kdma[c] = t_xd
        tok_p2a_ld0 = S_.tok("kx0")
        tok_p2a_ld1 = S_.tok("kx1")
        tok_p2a_pe = S_.tok("pe")

        tok_wv = wload(wv, waits=[tok_p2b_pe, tok_p3_pe, tok_p2a_pe])
        tok_masks = dma("gq", "mk", pm[:, 0:NKS * QT], mk_i, waits=[tok_p3_pe])

        vev = {}
        vdma = {}
        vmm = {}
        vxl = {}
        for c in range(NCH):
            waits = [tok_p2a_ld0, tok_p2a_ld1, tok_p3_dve, tok_p2b_q0, tok_p2b_q1, tok_p2a_pe]
            if c > 1:
                waits.append(vmm[c - 2])
            vxl[c] = dma(
                "gq", f"vx{c % 2}",
                pa[:, XTC_OFF[c % 2]:XTC_OFF[c % 2] + DC * CH
                   ].rearrange("p (dc r) -> p dc r", r=CH),
                xnt_scr.rearrange("(dc p) r -> p dc r", p=128)[:, :, c * CH:(c + 1) * CH],
                waits=waits)
            for rs in range(NSUB):
                for vc in range(NVC):
                    n = rs * NVC + vc
                    gn = c * NSUB * NVC + n
                    VSTR = min(4, NSUB * NVC)
                    waits = [vxl[c], tok_wv[0]]
                    if gn >= VSTR:
                        waits.append(vev[gn - VSTR])

                    def mk_vmm(rs=rs, vc=vc, n=n, c=c, d0=0, d1=DC):
                        def f(E):
                            for dc in range(d0, d1):
                                ins = E.matmul(pf[n % 4][:, 0:512],
                                               xtc(c % 2, dc, rs * 128, (rs + 1) * 128),
                                               wchunk(dc, vc * 512, (vc + 1) * 512),
                                               start=(dc == 0), stop=(dc == DC - 1),
                                               skip_group_check=True)
                            return ins
                        return f
                    A("pe", mk_vmm(d0=0, d1=DC // 2), waits=waits, inc=("pe", 1))
                    vmm[c] = A("pe", mk_vmm(d0=DC // 2, d1=DC),
                               waits=[tok_wv[1]], inc=("pe", 1))
                    w2 = [vmm[c]]
                    if c > 1:
                        w2.append(vdma[c - 2])
                    vev[gn] = A("dve", lambda E, rs=rs, vc=vc, n=n, c=c: E.tensor_copy(
                        stgv(c % 2, rs, vc * 512, (vc + 1) * 512), pf[n % 4][:, 0:512]),
                        waits=w2, inc=("dve", 1), chain=True)
            vdma[c] = dma(
                "sy", f"vw{c % 2}",
                v_scr[c * CH:(c + 1) * CH, :].rearrange("(rs p) n -> p rs n", p=128),
                pa[:, STG_OFF[c % 2]: STG_OFF[c % 2] + NSUB * D
                   ].rearrange("p (rs n) -> p rs n", n=D),
                waits=[vev[c * NSUB * NVC + NSUB * NVC - 1]])
        tok_p2c_ld0 = S_.tok("vw0")
        tok_p2c_ld1 = S_.tok("vw1")
        tok_p2c_pe = S_.tok("pe")

        tok_vflat = dma("gq", "vf", pv[:, 0:JC * D].rearrange("p (jc n) -> p jc n", n=D),
                        v_fl.rearrange("(jc p) n -> p jc n", p=128),
                        waits=[tok_p3_dve, ("act", S_.n["act"]), ("dve", S_.n["dve"])])

        t_o0 = A("dve", lambda E: E.memset(vh_view(0)[:, :, 128:129], 1.0),
                 waits=[tok_p2c_pe], inc=("dve", 1), chain=True)
        tok_vones = A("dve", lambda E: E.memset(vh_view(1)[:, :, 128:129], 1.0),
                      inc=("dve", 1), chain=True)

        tok_pre_attn_dve = ("dve", S_.n["dve"])

        def pso_(j):
            return pf[j // 2][:, (j % 2) * 256:(j % 2) * 256 + 129]

        def pso_d(j):
            return pf[j // 2][:, (j % 2) * 256:(j % 2) * 256 + 128]

        def pso_l(j):
            return pf[j // 2][:, (j % 2) * 256 + 128:(j % 2) * 256 + 129]

        def scb_(k, q0, q1):
            P = (k // 2) % 2
            o = (k % 2) * QT
            return scp[P][:, o + q0: o + q1]

        def ptp_(k, q0, q1):
            P = (k // 2) % 2
            o = (k % 2) * QT
            return ptp[P][:, o + q0: o + q1]

        head_pe = {}
        head_ev = {}
        head_ld = {}
        exp_toks = {}
        mask_toks = {}
        pv_last = {}
        sweep_ev = {}
        nvd = (NSUBK + 7) // 8
        for h in range(H):
            beta = h % 2
            waits = [tok_p2a_ld0, tok_p2a_ld1, tok_p2b_q0, tok_p2b_q1, tok_p2c_ld0, tok_p2c_ld1, tok_vones, tok_p2a_pe,
                     S_.tok("pe") if h == 0 else None]
            if h >= 2:
                waits += [head_pe[h - 2], head_ev[h - 2]]
            t = dma("sy", f"h{beta}", kth(beta, 0, T), kt_scr[h * 128:(h + 1) * 128, :],
                    waits=waits)
            for d in range(nvd):
                s0, s1 = d * 8, min((d + 1) * 8, NSUBK)
                t = dma("sy", f"h{beta}", vh_view(beta)[:, s0:s1, 0:128],
                        v_scr[s0 * 128:s1 * 128, h * 128:(h + 1) * 128
                              ].rearrange("(s p) c -> p s c", p=128))
            head_ld[h] = t = dma(
                "sy", f"h{beta}",
                xs[beta][:, 0:OWN],
                qu_scr[:, h * OWN:(h + 1) * OWN])

            for sw in range(1, NSW + 1):
                nk = sw * NKS
                pvq = []
                for k in range(nk):
                    if k % 2 == 0:
                        while len(pvq) > 2:
                            fn2, w2_, pr2 = pvq.pop(0)
                            pv_last[pr2] = A("pe", fn2, waits=w2_,
                                             inc=("pe", 1))
                    if k >= 4:
                        war = exp_toks[(h, sw, k - 4)]
                    elif sw > 1:
                        war = exp_toks[(h, sw - 1, (sw - 1) * NKS - 4 + k)]
                    elif h > 0:
                        war = exp_toks[(h - 1, NSW, NSW * NKS - 4 + k)]
                    else:
                        war = None
                    sw_waits = [head_ld[h], war]
                    if h == 0 and sw == 1 and k in (0, 1, 2, 3):
                        sw_waits.append(tok_pre_attn_dve)
                    t_sc = A("pe", lambda E, b=beta, k=k, sw=sw: E.matmul(
                        scb_(k, 0, QT), kth(b, k * 128, (k + 1) * 128),
                        xs[b][:, (sw - 1) * QT: sw * QT],
                        start=True, stop=True),
                        waits=sw_waits, inc=("pe", 1))
                    if k % 2 == 1:
                        ew = [t_sc, pv_last.get((k // 2) % 2)]
                        for km in (k - 4, k - 3):
                            if km >= nk - NKS and km >= 0:
                                ew.append(mask_toks[(h, sw, km)])
                            elif km < 0 and (sw > 1 or h > 0):
                                psw = (h, sw - 1) if sw > 1 else (h - 1, NSW)
                                kk2 = psw[1] * NKS + km
                                if (psw[0], psw[1], kk2) in mask_toks:
                                    ew.append(mask_toks[(psw[0], psw[1], kk2)])
                        tex = A("act", lambda E, k=k: E.activation(
                            out=ptp_(k - 1, 0, 2 * QT), in_=scb_(k - 1, 0, 2 * QT),
                            func=AF.Exp, bias=cst[:], scale=SCALE),
                            waits=ew + [tok_cst], inc=("act", 1), chain=True)
                        exp_toks[(h, sw, k - 1)] = tex
                        exp_toks[(h, sw, k)] = tex
                        for km in (k - 1, k):
                            if km >= nk - NKS:
                                j = km - (nk - NKS)
                                mask_toks[(h, sw, km)] = A(
                                    "dve", lambda E, km=km, j=j: E.tensor_tensor(
                                        ptp_(km, 0, QT), ptp_(km, 0, QT),
                                        mask_j(j), op=ALU.mult),
                                    waits=[tex, tok_masks],
                                    inc=("dve", 1), chain=True)
                        for km in (k - 1, k):
                            if km >= nk - NKS:
                                pvw = [mask_toks[(h, sw, km)]]
                            else:
                                pvw = [tex]
                            if km == 0:
                                if sw > 1:
                                    pvw.append(sweep_ev[(h, sw - 1)])
                                elif h > 0:
                                    pvw.append(sweep_ev[(h - 1, NSW)])
                                else:
                                    pvw.append(tok_p3_dve)

                            def mk_pv(b=beta, km=km, nk=nk):
                                def f(E):
                                    for j in range(NQS):
                                        ins = E.matmul(pso_(j),
                                                       ptp_(km, j * 128, (j + 1) * 128),
                                                       vh(b, km),
                                                       start=(km == 0 and j % 2 == 0),
                                                       stop=(km == nk - 1 and
                                                             (j % 2 == 1 or j == NQS - 1)),
                                                       skip_group_check=True)
                                    return ins
                                return f
                            pvq.append((mk_pv(), pvw, (k // 2) % 2))
                tpv = None
                for fn2, w2_, pr2 in pvq:
                    tpv = A("pe", fn2, waits=w2_, inc=("pe", 1))
                    pv_last[pr2] = tpv
                head_pe[h] = tpv
                sweep_pe = tpv
                t_ev = None
                for j in range(NQS):
                    rt = ((sw - 1) * QT + j * 128) // 128
                    A("dve", lambda E, j=j: E.reciprocal(w1(40 + j),
                                                         pso_l(j)),
                      waits=[sweep_pe], inc=("dve", 1), chain=True)
                    t_ev = A("dve", lambda E, j=j, rt=rt, h=h: E.tensor_scalar(
                        ac(rt, h * 128, (h + 1) * 128), pso_d(j),
                        w1(40 + j), None, op0=ALU.mult),
                        inc=("dve", 1), chain=True)
                sweep_ev[(h, sw)] = t_ev
            head_ev[h] = sweep_ev[(h, NSW)]
        tok_p4_dve = S_.tok("dve")
        tok_p4_pe = S_.tok("pe")
        tok_p4_act = S_.tok("act")

        tok_wo = wload(wo, waits=[tok_p4_pe])

        pairs = [(a, b) for a in range(ORT) for b in range(a, DC)]
        p5_ev = {}
        for idx, (a, b) in enumerate(pairs):
            par = idx % 2
            n_tr = 1 if a == b else 2
            waits = [tok_p4_dve, tok_consts]
            if idx >= 2:
                waits.append(p5_ev[idx - 2])
            else:
                waits.append(tok_p4_act)

            def mk_tr(a=a, b=b, par=par, n_tr=n_tr):
                def f(E):
                    ins = E.transpose(pbv(par, 0, 128), ac(a, b * 128, (b + 1) * 128), ids[:])
                    if n_tr == 2:
                        ins = E.transpose(pbv(par, 128, 256), ac(b, a * 128, (a + 1) * 128), ids[:])
                    return ins
                return f
            t_tr = A("pe", mk_tr(), waits=waits, inc=("pe", n_tr))
            t_e1 = A("dve", lambda E, a=a, b=b, par=par: E.tensor_scalar(
                acT(b, a * 128, (a + 1) * 128), pbv(par, 0, 128),
                bvs[:, b:b + 1], None, op0=ALU.add),
                waits=[t_tr], inc=("dve", 1), chain=True)
            if a != b:
                t_e1 = A("dve", lambda E, a=a, b=b, par=par: E.tensor_scalar(
                    acT(a, b * 128, (b + 1) * 128), pbv(par, 128, 256),
                    bvs[:, a:a + 1], None, op0=ALU.add),
                    inc=("dve", 1), chain=True)
            p5_ev[idx] = t_e1
        tok_p5_dve = S_.tok("dve")

        tok_bo = dma("sy", "bo", bo_b(0, D), bo_i,
                     waits=[tok_p4_dve, tok_p4_act])

        e1t = {}
        e2t = {}
        e3t = {}
        stt = {}
        xlt = {}
        for rt in range(ORT):
            for cc in range(NVC):
                n = rt * NVC + cc
                xw = [tok_bo]
                if n >= 2:
                    xw.append(e3t[n - 2])
                xlt[n] = dma("gq", f"px{n % 2}", p6x(n % 2),
                             xo[rt * 128:(rt + 1) * 128, cc * 512:(cc + 1) * 512],
                             waits=xw)
                mw = [tok_p3_dve, tok_vflat, *tok_wo, tok_p5_dve]
                if n >= 2:
                    mw += [e1t[n - 2], e2t[n - 2]]

                def mk_f(rt=rt, cc=cc, n=n):
                    def f(E):
                        for jc in range(JC):
                            ins = E.matmul(pf[n % 2][:, 0:512],
                                           gut(jc, rt * 128, (rt + 1) * 128),
                                           vflat(jc, cc * 512, (cc + 1) * 512),
                                           start=(jc == 0), stop=(jc == JC - 1))
                        return ins
                    return f
                t_f = A("pe", mk_f(), waits=mw, inc=("pe", 1))

                def mk_o(rt=rt, cc=cc, n=n):
                    def f(E):
                        for fc in range(DC):
                            ins = E.matmul(pf[2 + n % 2][:, 0:512],
                                           acT(fc, rt * 128, (rt + 1) * 128),
                                           wchunk(fc, cc * 512, (cc + 1) * 512),
                                           start=(fc == 0), stop=(fc == DC - 1))
                        return ins
                    return f
                t_o = A("pe", mk_o(), inc=("pe", 1))

                w1_ = [t_f, tok_p3_dve]
                if n >= 3:
                    w1_.append(stt[n - 3])
                e1t[n] = A("act", lambda E, rt=rt, n=n: E.activation(
                    out=p6t(n % 3), in_=pf[n % 2][:, 0:512], func=AF.Copy,
                    scale=gps[:, rt:rt + 1]), waits=w1_, inc=("act", 1),
                    chain=True)
                e2t[n] = A("dve", lambda E, n=n: E.tensor_tensor(
                    p6t(n % 3), p6t(n % 3), pf[2 + n % 2][:, 0:512], op=ALU.add),
                    waits=[e1t[n], t_o], inc=("dve", 1), chain=True)
                A("dve", lambda E, n=n: E.tensor_tensor(
                    p6t(n % 3), p6t(n % 3), p6x(n % 2), op=ALU.add),
                  waits=[xlt[n]], inc=("dve", 1), chain=True)
                e3t[n] = A("dve", lambda E, n=n, cc=cc: E.tensor_tensor(
                    p6t(n % 3), p6t(n % 3), bo_b(cc * 512, (cc + 1) * 512),
                    op=ALU.add), waits=[tok_bo], inc=("dve", 1), chain=True)
                stt[n] = dma("sy", f"o{n % 3}", out[rt * 128:(rt + 1) * 128, cc * 512:(cc + 1) * 512],
                             p6t(n % 3), waits=[e3t[n]])

        final = dict(S_.n)
        S_.add("gq", None, waits=[(s, v) for s, v in final.items()])

        semnames = set(S_.n.keys())
        for ename in S_.ENG:
            for fn, waits, inc in S_.ops[ename]:
                for sem, val in waits:
                    semnames.add(sem)
        sems = {name: e(nc.semaphore("s_" + name)) for name in sorted(semnames)}
        block = e(nc.Block())

        def emit_wm(engobj, ename):
            water = {}
            for fn, waits, inc in S_.ops[ename]:
                for sem, val in waits:
                    if water.get(sem, 0) >= val:
                        continue
                    engobj.wait_ge(sems[sem], val)
                    water[sem] = val
                if fn is None:
                    continue
                ins = fn(engobj)
                if inc is not None:
                    ins.then_inc(sems[inc[0]], inc[1])

        @block.sync
        def _(E):
            emit_wm(E, "sy")

        @block.gpsimd
        def _(E):
            emit_wm(E, "gq")

        @block.tensor
        def _(E):
            emit_wm(E, "pe")

        @block.vector
        def _(E):
            emit_wm(E, "dve")

        @block.scalar
        def _(E):
            emit_wm(E, "act")

    return nc


def host_inputs(cfg, core, ins):
    D, T = cfg["D"], cfg["T"]
    OWN, QT, NSW, NKS, DC, JC, S = (cfg["OWN"], cfg["QT"], cfg["NSW"],
                                    cfg["NKS"], cfg["DC"], cfg["JC"], cfg["S"])
    b, g = core // 4, core % 4
    x = ins["L0_attn_x"]
    xb = np.ascontiguousarray(x[b], dtype=np.float32)
    blocks = [4 * s + g for s in range(NSW)]
    xo = np.ascontiguousarray(
        np.concatenate([x[b][blk * QT:(blk + 1) * QT] for blk in blocks], axis=0),
        dtype=np.float32)
    H = cfg["H"]
    TPH, QPH = cfg["T"] // H, QT // H
    qrows = []
    for h in range(H):
        for s in range(NSW):
            base = h * TPH + (QT * (4 * s + g)) // H
            qrows.extend(range(base, base + QPH))
    xq = np.ascontiguousarray(x[b][np.array(qrows)], dtype=np.float32)
    masks = np.zeros((128, NKS * QT), dtype=ml_dtypes.bfloat16)
    kl = np.arange(128)[:, None]
    q = np.arange(QT)[None, :]
    for j in range(NKS):
        masks[:, j * QT:(j + 1) * QT] = (j * 128 + kl <= g * QT + q).astype(
            ml_dtypes.bfloat16)
    e_mat = np.zeros((S, 128 * JC), dtype=ml_dtypes.bfloat16)
    for jj in range(128 * JC):
        e_mat[jj // 16, jj] = 1
    U = np.asarray(ins["L0_ffn_U"], np.float32)
    m = {
        "xb": xb,
        "xo": xo,
        "xq": xq,
        "wq": ins["L0_attn_Wq"], "wk": ins["L0_attn_Wk"],
        "wv": ins["L0_attn_Wv"], "wo": ins["L0_attn_Wo"],
        "bq": np.ascontiguousarray(np.asarray(ins["L0_attn_bq"], np.float32).reshape(DC, 128).T),
        "bk": np.ascontiguousarray(np.asarray(ins["L0_attn_bk"], np.float32).reshape(DC, 128).T),
        "bv": np.ascontiguousarray(np.asarray(ins["L0_attn_bv"], np.float32).reshape(DC, 128).T),
        "bo_bc": np.ascontiguousarray(np.broadcast_to(
            np.asarray(ins["L0_attn_bo"], np.float32).reshape(1, D), (128, D))),
        "wr_t": np.ascontiguousarray(np.asarray(ins["L0_ffn_Wr"], np.float32).T),
        "u_flat": np.ascontiguousarray(U.transpose(1, 0, 2).reshape(D, -1)),
        "v_flat": np.ascontiguousarray(np.asarray(ins["L0_ffn_V"], np.float32).reshape(-1, D)),
        "attr_t": np.ascontiguousarray(np.asarray(ins["L0_ffn_attr"], np.float32).T),
        "wa_t": np.ascontiguousarray(np.asarray(ins["L0_ffn_Wa"], np.float32).reshape(1, S).T),
        "masks": masks,
        "ident": np.eye(128, dtype=ml_dtypes.bfloat16),
        "ones_f": np.ones((1, 128), np.float32),
        "e_mat": e_mat,
    }
    return {k: np.ascontiguousarray(np.asarray(v)) for k, v in m.items()}


def assemble(cfg, results):
    D, T, QT, NSW = cfg["D"], cfg["T"], cfg["QT"], cfg["NSW"]
    full = np.zeros((2, T, D), np.float32)
    for core in range(8):
        b, g = core // 4, core % 4
        o = np.asarray(results[core]["out"], np.float32)
        for s in range(NSW):
            blk = 4 * s + g
            full[b, blk * QT:(blk + 1) * QT] = o[s * QT:(s + 1) * QT]
    return full


def kernel(**inputs):
    cfg = mkcfg(FULL)
    nc = build_nc(cfg)
    in_maps = [host_inputs(cfg, c, inputs) for c in range(8)]
    res = run_bass_kernel_spmd(nc, in_maps, core_ids=list(range(8)))
    return assemble(cfg, res.results)


# version 17
# speedup vs baseline: 1.6179x; 1.0081x over previous
import math
from contextlib import ExitStack

import numpy as np
import ml_dtypes
import concourse.bass as bass
import concourse.mybir as mybir
from concourse.bass_utils import run_bass_kernel_spmd

F32 = mybir.dt.float32
BF16 = mybir.dt.bfloat16
AF = mybir.ActivationFunctionType
ALU = mybir.AluOpType
AX = mybir.AxisListType

FULL = dict(D=2048, T=8192, H=16)
TINY = dict(D=512, T=2048, H=4)

C_SHIFT = 12.0


def mkcfg(base):
    c = dict(base)
    D, T, H = c["D"], c["T"], c["H"]
    c["DH"] = 128
    assert H * c["DH"] == D
    c["DC"] = D // 128
    c["OWN"] = T // 4
    assert c["OWN"] == D
    c["QT"] = T // 16
    assert c["QT"] % 128 == 0
    c["NQS"] = c["QT"] // 128
    c["NSW"] = 4
    c["NKS"] = T // 512
    c["CH"] = c["QT"]
    c["NCH"] = T // c["CH"]
    c["NSUB"] = c["CH"] // 128
    c["S"], c["R"] = 32, 16
    c["J"] = c["S"] * c["R"]
    c["JC"] = c["J"] // 128
    c["GR"] = min(512, c["OWN"])
    c["NG"] = c["OWN"] // c["GR"]
    c["ORT"] = c["OWN"] // 128
    assert c["ORT"] == c["DC"]
    c["SCALE"] = 1.0 / math.sqrt(c["DH"])
    return c


class Sched:
    ENG = ("sy", "gq", "pe", "act", "dve")

    def __init__(self):
        self.ops = {e: [] for e in self.ENG}
        from collections import defaultdict
        self.n = defaultdict(int)
        for s in ("pe", "act", "dve"):
            self.n[s] = 0
        self.chain_last = {"act": 0, "dve": 0}

    def add(self, eng, fn, waits=(), inc=None, chain=False):
        w = [x for x in waits if x is not None and x[1] > 0]
        if chain and eng in self.chain_last:
            if self.chain_last[eng] > 0:
                w.append((eng, self.chain_last[eng]))
        tok = None
        if inc is not None:
            sem, k = inc
            self.n[sem] += k
            tok = (sem, self.n[sem])
            if eng in self.chain_last and sem == eng:
                self.chain_last[eng] = self.n[sem]
        self.ops[eng].append((fn, w, inc))
        return tok

    def tok(self, sem):
        return (sem, self.n[sem])

    def barrier(self):
        snap = dict(self.n)
        for e in self.ENG:
            self.add(e, None, waits=[(s, v) for s, v in snap.items()])
        return snap


def build_nc(cfg):
    D, T, H, DH = cfg["D"], cfg["T"], cfg["H"], cfg["DH"]
    DC, OWN, QT, NQS = cfg["DC"], cfg["OWN"], cfg["QT"], cfg["NQS"]
    NSW, NKS, CH, NCH, NSUB = cfg["NSW"], cfg["NKS"], cfg["CH"], cfg["NCH"], cfg["NSUB"]
    S, J, JC = cfg["S"], cfg["J"], cfg["JC"]
    GR, NG, ORT, SCALE = cfg["GR"], cfg["NG"], cfg["ORT"], cfg["SCALE"]
    NVC = D // 512
    NSUBK = T // 128

    nc = bass.Bass("TRN2", target_bir_lowering=False)

    xb = nc.dram_tensor("xb", [T, D], F32, kind="ExternalInput").ap()
    xo = nc.dram_tensor("xo", [OWN, D], F32, kind="ExternalInput").ap()
    xq = nc.dram_tensor("xq", [OWN, D], F32, kind="ExternalInput").ap()
    wq = nc.dram_tensor("wq", [D, D], F32, kind="ExternalInput").ap()
    wk = nc.dram_tensor("wk", [D, D], F32, kind="ExternalInput").ap()
    wv = nc.dram_tensor("wv", [D, D], F32, kind="ExternalInput").ap()
    wo = nc.dram_tensor("wo", [D, D], F32, kind="ExternalInput").ap()
    bq_i = nc.dram_tensor("bq", [128, DC], F32, kind="ExternalInput").ap()
    bk_i = nc.dram_tensor("bk", [128, DC], F32, kind="ExternalInput").ap()
    bv_i = nc.dram_tensor("bv", [128, DC], F32, kind="ExternalInput").ap()
    bo_i = nc.dram_tensor("bo_bc", [128, D], F32, kind="ExternalInput").ap()
    wr_i = nc.dram_tensor("wr_t", [D, S], F32, kind="ExternalInput").ap()
    u_fl = nc.dram_tensor("u_flat", [D, J], F32, kind="ExternalInput").ap()
    v_fl = nc.dram_tensor("v_flat", [J, D], F32, kind="ExternalInput").ap()
    at_i = nc.dram_tensor("attr_t", [S, S], F32, kind="ExternalInput").ap()
    wa_i = nc.dram_tensor("wa_t", [S, 1], F32, kind="ExternalInput").ap()
    mk_i = nc.dram_tensor("masks", [128, NKS * QT], BF16, kind="ExternalInput").ap()
    id_i = nc.dram_tensor("ident", [128, 128], BF16, kind="ExternalInput").ap()
    of_i = nc.dram_tensor("ones_f", [1, 128], F32, kind="ExternalInput").ap()
    em_i = nc.dram_tensor("e_mat", [S, 128 * JC], BF16, kind="ExternalInput").ap()

    out = nc.dram_tensor("out", [OWN, D], F32, kind="ExternalOutput").ap()

    kt_scr = nc.dram_tensor("kt_scr", [D, T], BF16, kind="Internal").ap()
    v_scr = nc.dram_tensor("v_scr", [T, D], BF16, kind="Internal").ap()
    qu_scr = nc.dram_tensor("qu_scr", [128, (D // 128) * OWN], BF16, kind="Internal").ap()
    xnt_scr = nc.dram_tensor("xnt_scr", [D, T], BF16, kind="Internal").ap()

    with ExitStack() as ctx:
        e = ctx.enter_context

        pa = e(nc.sbuf_tensor("pa", [128, DC * OWN], BF16))
        PW_COLS = max(DC * D, 2 * T + 2 * 129 * NSUBK)
        pw = e(nc.sbuf_tensor("pw", [128, PW_COLS], BF16))
        pm = e(nc.sbuf_tensor("pm", [128, max(DC * J, NKS * QT, 2 * (D + 1024 + 3 * 512))], BF16))
        pv = e(nc.sbuf_tensor("pv", [128, max(JC * D, JC * GR + OWN + 2 * D)], BF16))
        pg = e(nc.sbuf_tensor("pg", [128, JC * OWN], BF16))
        xs = [e(nc.sbuf_tensor(f"xs{i}", [128, max(D, OWN)], BF16)) for i in range(4)]
        ptp = [e(nc.sbuf_tensor(f"ptp{i}", [128, 2 * QT], BF16)) for i in range(2)]
        pt = [ptp[0][:, 0:QT], ptp[0][:, QT:2 * QT], ptp[1][:, 0:QT]]
        wrk = e(nc.sbuf_tensor("wrk", [128, 64], F32))
        scw = e(nc.sbuf_tensor("scw", [128, 8 * S], F32))
        gps = e(nc.sbuf_tensor("gps", [128, ORT], F32))
        bqs = e(nc.sbuf_tensor("bqs", [128, DC], F32))
        bks = e(nc.sbuf_tensor("bks", [128, DC], F32))
        bvs = e(nc.sbuf_tensor("bvs", [128, DC], F32))
        wrs = e(nc.sbuf_tensor("wrs", [128, DC * S], BF16))
        ems = e(nc.sbuf_tensor("ems", [S, 128 * JC], BF16))
        ids = e(nc.sbuf_tensor("ids", [128, 128], BF16))
        onf = e(nc.sbuf_tensor("onf", [1, 128], F32))
        ats = e(nc.sbuf_tensor("ats", [S, S], F32))
        was = e(nc.sbuf_tensor("was", [S, 1], F32))
        awb = e(nc.sbuf_tensor("awb", [128, S], F32))
        aws = e(nc.sbuf_tensor("aws", [1, S], F32))
        cst = e(nc.sbuf_tensor("cst", [128, 1], F32))

        pf = [e(nc.psum_tensor(f"pf{i}", [128, 512], F32)) for i in range(4)]
        scp = [e(nc.psum_tensor(f"scp{i}", [128, 1024], F32)) for i in range(2)]
        scp0_bf = scp[0].bitcast(BF16)

        def pbv(i, a, b):
            return scp0_bf[:, i * 1024 + a: i * 1024 + b]

        block_holder = []

        S_ = Sched()

        def xnt_own(dc, r0, r1):
            return pa[:, dc * OWN + r0: dc * OWN + r1]

        XTC_OFF = [0, 2 * DC * CH]
        STG_OFF = [DC * CH, 3 * DC * CH]

        def xtc(buf, dc, r0, r1):
            o = XTC_OFF[buf]
            return pa[:, o + dc * CH + r0: o + dc * CH + r1]

        def stg(buf, kc, c0, c1):
            o = STG_OFF[buf]
            return pa[:, o + kc * CH + c0: o + kc * CH + c1]

        def stgv(buf, rs, c0, c1):
            o = STG_OFF[buf]
            return pa[:, o + rs * D + c0: o + rs * D + c1]

        def ac(rt, f0, f1):
            return pa[:, rt * D + f0: rt * D + f1]

        def acT(fc, r0, r1):
            return pa[:, fc * OWN + r0: fc * OWN + r1]

        def wchunk(dc, n0, n1):
            return pw[:, dc * D + n0: dc * D + n1]

        def kth(b, k0, k1):
            return pw[:, b * T + k0: b * T + k1]

        VH0 = 2 * T

        def vh(b, sub):
            o = VH0 + b * (129 * NSUBK) + sub * 129
            return pw[:, o: o + 129]

        def vh_view(b):
            o = VH0 + b * (129 * NSUBK)
            return pw[:, o: o + NSUBK * 129].rearrange("p (s c) -> p s c", c=129)

        def uchunk(dc, j0, j1):
            return pm[:, dc * J + j0: dc * J + j1]

        def mask_j(j):
            return pm[:, j * QT:(j + 1) * QT]

        pmf = pm.bitcast(F32)

        def bo_b(c0, c1):
            return pmf[:, c0:c1]

        def p6x(b):
            return pmf[:, D + b * 512: D + (b + 1) * 512]

        def p6t(i):
            return pmf[:, D + 1024 + i * 512: D + 1024 + (i + 1) * 512]

        def vflat(jc, n0, n1):
            return pv[:, jc * D + n0: jc * D + n1]

        def gexp(jc):
            return pv[:, jc * GR:(jc + 1) * GR]

        GT0 = JC * GR

        def gT(r0, r1):
            return pv[0:S, GT0 + r0: GT0 + r1]

        def gut(jc, r0, r1):
            return pg[:, jc * OWN + r0: jc * OWN + r1]

        PVJ = JC * GR + OWN
        sqj = pv[:, PVJ: PVJ + D]
        cpj = pv[:, PVJ + D: PVJ + 2 * D]
        qst = pg[:, 0: 2 * GR]

        def w1(i):
            return wrk[:, i:i + 1]

        A = S_.add

        def dma(eng, sem, outap, inap, waits=()):
            return A(eng, lambda E, o=outap, i=inap: E.dma_start(out=o, in_=i),
                     waits=waits, inc=(sem, 16))

        tok_consts = None
        for o, i in [(ids[:], id_i), (onf[:], of_i), (bqs[:], bq_i), (bks[:], bk_i),
                     (bvs[:], bv_i), (ats[:], at_i), (was[:], wa_i),
                     (ems[:], em_i)]:
            tok_consts = dma("sy", "cst", o, i)
        tok_wrs = dma("gq", "wrs", wrs.rearrange("p (dc s) -> p dc s", s=S),
                      wr_i.rearrange("(dc p) s -> p dc s", p=128))

        def wload(src, waits=()):
            dst = pw[:, 0:DC * D].rearrange("p (dc n) -> p dc n", n=D)
            s3 = src.rearrange("(dc p) n -> p dc n", p=128)
            ta = dma("gq", "w", dst[:, 0:DC // 2, :], s3[:, 0:DC // 2, :],
                     waits=waits)
            tb = dma("gq", "w2", dst[:, DC // 2:DC, :], s3[:, DC // 2:DC, :],
                     waits=waits)
            return (ta, tb)

        tok_wq = wload(wq)
        tok_u = dma("gq", "u", pm[:, 0:DC * J].rearrange("p (dc j) -> p dc j", j=J),
                    u_fl.rearrange("(dc p) j -> p dc j", p=128))
        tok_cst = A("dve", lambda E: E.memset(cst[:], -C_SHIFT),
                    inc=("dve", 1), chain=True)

        ln_i = [0]
        ln_trs = {}
        ln_norm = {}
        ln_sq = {}
        ln_ex2 = {}
        ln_lnt = {}

        def ln_head(src_rows, extra_waits=()):
            i = ln_i[0]
            ln_i[0] += 1
            beta = i % 4
            b2 = i % 2
            xbuf = xs[beta]
            t_x = dma("gq", f"x{beta}", xbuf[:, 0:D], src_rows,
                      waits=[ln_trs.get(i - 4)] + list(extra_waits))
            A("dve", lambda E, xb_=xbuf, b=b2: E.tensor_scalar(
                cpj, xb_[:, 0:D], 1.0, None, op0=ALU.mult, op1=ALU.add,
                accum_out=wrk[:, 10 + b:11 + b]), waits=[t_x], inc=("dve", 1),
                chain=True)
            ln_sq[i] = A("act", lambda E, xb_=xbuf, b=b2: E.activation(
                out=sqj, in_=xb_[:, 0:D], func=AF.Square,
                accum_out=wrk[:, 8 + b:9 + b]),
                waits=[t_x, ln_ex2.get(i - 2)], inc=("act", 1), chain=True)
            return i

        def ln_mid(i):
            b2 = i % 2
            A("dve", lambda E, b=b2: E.tensor_scalar(
                w1(12 + b), w1(10 + b), 1.0 / D, None, op0=ALU.mult),
              inc=("dve", 1), chain=True)
            ln_ex2[i] = A("dve", lambda E, b=b2: E.tensor_scalar(
                w1(14 + b), w1(8 + b), 1.0 / D, None, op0=ALU.mult),
                waits=[ln_sq[i]], inc=("dve", 1), chain=True)
            A("dve", lambda E, b=b2: E.tensor_tensor(
                w1(16 + b), w1(12 + b), w1(12 + b), op=ALU.mult),
              inc=("dve", 1), chain=True)
            A("dve", lambda E, b=b2: E.tensor_tensor(
                w1(18 + b), w1(14 + b), w1(16 + b), op=ALU.subtract),
                waits=[ln_lnt.get(i - 2)], inc=("dve", 1), chain=True)
            t_var = A("dve", lambda E, b=b2: E.tensor_scalar(
                w1(18 + b), w1(18 + b), 1e-5, None, op0=ALU.add),
                inc=("dve", 1), chain=True)
            ln_lnt[i] = A("act", lambda E, b=b2: E.activation(
                out=w1(20 + b), in_=w1(18 + b), func=AF.Ln),
              waits=[t_var], inc=("act", 1), chain=True)
            return A("act", lambda E, b=b2: E.activation(
                out=w1(22 + b), in_=w1(20 + b), func=AF.Exp, scale=-0.5),
                waits=[ln_norm.get(i - 2)], inc=("act", 1), chain=True)

        def ln_tail(i, t_rstd):
            b2 = i % 2
            xbuf = xs[i % 4]
            ln_norm[i] = A("dve", lambda E, xb_=xbuf, b=b2: E.tensor_scalar(
                xb_[:, 0:D], xb_[:, 0:D], w1(12 + b), w1(22 + b),
                op0=ALU.subtract, op1=ALU.mult),
                waits=[t_rstd], inc=("dve", 1), chain=True)

        class LnBatch:

            def __init__(self, srcs, extra_waits=()):
                self.srcs = srcs
                self.extra = list(extra_waits)
                self.ids = []
                self.rstds = {}
                self.k = 0

            def step(self):
                j = self.k
                self.k += 1
                n = len(self.srcs)
                if j < n:
                    self.ids.append(ln_head(self.srcs[j],
                                            self.extra if j == 0 else ()))
                if 1 <= j <= n:
                    self.rstds[j - 1] = ln_mid(self.ids[j - 1])
                if j >= 2 and j - 2 < n:
                    ln_tail(self.ids[j - 2], self.rstds[j - 2])

            def finish(self):
                while self.k < len(self.srcs) + 2:
                    self.step()
                return self.ids

        def ln_stats_batch(srcs, extra_waits=()):
            return LnBatch(srcs, extra_waits).finish()

        def ln_tr(i, dest_fn, dest_r0):
            beta = i % 4
            xbuf = xs[beta]
            t_norm = ln_norm[i]
            ev_toks = []
            tr_toks = []
            for dc in range(DC):
                war = ev_toks[dc - 2] if dc >= 2 else ln_tr.last_ev.get(dc % 2)
                t_tr = A("pe", lambda E, xb_=xbuf, dc=dc: E.transpose(
                    pbv(dc % 2, 0, 128), xb_[:, dc * 128:(dc + 1) * 128], ids[:]),
                    waits=[t_norm, war, tok_consts], inc=("pe", 1))
                tr_toks.append(t_tr)
                t_ev = A("dve", lambda E, dc=dc, r0=dest_r0, df=dest_fn: E.tensor_copy(
                    df(dc, r0, r0 + 128), pbv(dc % 2, 0, 128)),
                    waits=[t_tr], inc=("dve", 1), chain=True)
                ev_toks.append(t_ev)
            ln_tr.last_ev = {0: ev_toks[-2], 1: ev_toks[-1]}
            ln_trs[i] = tr_toks[-1]
            return ev_toks[-1]

        ln_tr.last_ev = {}

        q_mm_last = None
        qev = {}
        qld = {}
        NCH0 = OWN // CH
        LNQ = DC // NSUB
        ev_of = {}
        st_ids = {}
        qdma = {}
        ids0 = ln_stats_batch([xq[sb * 128:(sb + 1) * 128, :]
                               for sb in range(NSUB)])
        ev_of[0] = max(ln_tr(i, lambda dc, a, bb: xtc(0, dc, a, bb),
                             sb * 128) for sb, i in enumerate(ids0))
        for c in range(NCH0):
            nb0 = None
            if c + 1 < NCH0:
                nb0 = LnBatch(
                    [xq[(c + 1) * CH + sb * 128:(c + 1) * CH + (sb + 1) * 128, :]
                     for sb in range(NSUB)])
                st_ids[c + 1] = nb0.ids
            for qc in range(DC):
                if nb0 is not None and qc < NSUB + 2:
                    nb0.step()
                n = c * DC + qc
                waits = [ev_of[c], tok_wq[0]]
                if n >= 4:
                    waits.append(qev[n - 4])

                def mk_qmm(qc=qc, n=n, c=c, d0=0, d1=DC):
                    def f(E):
                        for dc in range(d0, d1):
                            ins = E.matmul(pf[n % 4][:, 0:CH],
                                           wchunk(dc, qc * 128, (qc + 1) * 128),
                                           xtc(c % 2, dc, 0, CH),
                                           start=(dc == 0), stop=(dc == DC - 1),
                                           skip_group_check=True)
                        return ins
                    return f
                A("pe", mk_qmm(d0=0, d1=DC // 2), waits=waits, inc=("pe", 1))
                q_mm_last = A("pe", mk_qmm(d0=DC // 2, d1=DC),
                              waits=[tok_wq[1]], inc=("pe", 1))
                w2 = [q_mm_last]
                if c > 1 and qc == 0:
                    w2.append(qdma[c - 2])
                qev[n] = A("act", lambda E, qc=qc, n=n, c=c: E.activation(
                    out=pa[:, STG_OFF[c % 2]: STG_OFF[c % 2] + DC * CH
                           ].rearrange("p (c2 r) -> p c2 r", r=DC)[:, :, qc:qc + 1],
                    in_=pf[n % 4][:, 0:CH], func=AF.Identity,
                    bias=bqs[:, qc:qc + 1]), waits=w2, inc=("act", 1), chain=True)
                if c + 1 < NCH0 and qc % LNQ == LNQ - 1:
                    sb = qc // LNQ
                    while nb0.k < sb + 3:
                        nb0.step()
                    ev = ln_tr(st_ids[c + 1][sb],
                               lambda dc, a, bb, c=c: xtc((c + 1) % 2, dc, a, bb),
                               sb * 128)
                    if sb == NSUB - 1:
                        ev_of[c + 1] = ev
            if nb0 is not None:
                nb0.finish()
            qdma[c] = dma(
                "sy", f"q{c % 2}",
                qu_scr[:, c * DC * CH:(c + 1) * DC * CH],
                pa[:, STG_OFF[c % 2]: STG_OFF[c % 2] + DC * CH],
                waits=[qev[c * DC + DC - 1]])
        tok_p2b_pe = S_.tok("pe")
        tok_p2b_q0 = S_.tok("q0")
        tok_p2b_q1 = S_.tok("q1")

        S_.add("dve", None, waits=[("q0", S_.n["q0"]), ("q1", S_.n["q1"])])
        S_.add("pe", None, waits=[("q0", S_.n["q0"]), ("q1", S_.n["q1"])])
        NB1 = OWN // 128 // NSUB
        p1_ids = {0: ln_stats_batch(
            [xo[sb * 128:(sb + 1) * 128, :] for sb in range(NSUB)])}
        for c in range(NB1):
            nb = None
            if c + 1 < NB1:
                nb = LnBatch([xo[((c + 1) * NSUB + sb) * 128:
                                 ((c + 1) * NSUB + sb + 1) * 128, :]
                              for sb in range(NSUB)])
            for sb, i in enumerate(p1_ids[c]):
                ln_tr(i, xnt_own, (c * NSUB + sb) * 128)
                if nb is not None:
                    nb.step()
            if nb is not None:
                p1_ids[c + 1] = nb.finish()
        tok_p1 = S_.tok("dve")

        t1 = A("pe", lambda E: E.matmul(scp[0][0:1, 0:S], was[:], ats[:],
                                        start=True, stop=True),
               waits=[tok_consts, tok_p2b_pe, tok_p1], inc=("pe", 1))
        t2 = A("act", lambda E: E.activation(out=aws[:], in_=scp[0][0:1, 0:S],
                                             func=AF.Exp, scale=-1.0),
               waits=[t1], inc=("act", 1), chain=True)
        t3 = A("dve", lambda E: E.tensor_scalar(aws[:], aws[:], 1.0, None,
                                                op0=ALU.add),
               waits=[t2], inc=("dve", 1), chain=True)
        t4 = A("dve", lambda E: E.reciprocal(aws[:], aws[:]),
               inc=("dve", 1), chain=True)
        t5 = A("pe", lambda E: E.matmul(scp[1][:, 0:S], onf[:], aws[:],
                                        start=True, stop=True),
               waits=[t4], inc=("pe", 1))
        tok_awb = A("dve", lambda E: E.tensor_copy(awb[:], scp[1][:, 0:S]),
                    waits=[t5], inc=("dve", 1), chain=True)

        g_done = {}
        for rt in range(ORT):
            r0 = rt * 128

            def mk_sc(rt=rt, r0=r0):
                def f(E):
                    for dc in range(DC):
                        ins = E.matmul(scp[0][:, 0:S],
                                       xnt_own(dc, r0, r0 + 128),
                                       wrs[:, dc * S:(dc + 1) * S],
                                       start=(dc == 0), stop=(dc == DC - 1))
                    return ins
                return f
            waits = [tok_p1, tok_consts, tok_wrs, t2]
            if rt > 0:
                waits.append(g_done[rt - 1])
            t_sc = A("pe", mk_sc(), waits=waits, inc=("pe", 1))

            t_cp = A("dve", lambda E: E.tensor_copy(scw[:, 0:S], scp[0][:, 0:S]),
                     waits=[t_sc], inc=("dve", 1), chain=True)
            A("dve", lambda E: E.tensor_reduce(w1(24), scw[:, 0:S], axis=AX.X,
                                               op=ALU.max, negate=True),
              inc=("dve", 1), chain=True)
            A("dve", lambda E: E.tensor_scalar(w1(25), w1(24), -1.0, None,
                                               op0=ALU.mult),
              inc=("dve", 1), chain=True)
            A("dve", lambda E: E.tensor_scalar(scw[:, S:2 * S], scw[:, 0:S],
                                               w1(25), 1e9, op0=ALU.is_ge,
                                               op1=ALU.mult),
              inc=("dve", 1), chain=True)
            t_ein = A("dve", lambda E: E.tensor_tensor(
                scw[:, 2 * S:3 * S], scw[:, 0:S], scw[:, S:2 * S],
                op=ALU.subtract), inc=("dve", 1), chain=True)
            t_exp = A("act", lambda E: E.activation(
                out=scw[:, 3 * S:4 * S], in_=scw[:, 0:S], func=AF.Exp,
                bias=w1(24)), waits=[t_ein], inc=("act", 1), chain=True)
            A("dve", lambda E: E.tensor_reduce(w1(26), scw[:, 2 * S:3 * S],
                                               axis=AX.X, op=ALU.max),
              inc=("dve", 1), chain=True)
            A("dve", lambda E: E.tensor_scalar(scw[:, 4 * S:5 * S], scw[:, 0:S],
                                               w1(26), None, op0=ALU.is_ge),
              inc=("dve", 1), chain=True)
            A("dve", lambda E: E.tensor_reduce(w1(27), scw[:, 3 * S:4 * S],
                                               axis=AX.X, op=ALU.add),
              waits=[t_exp], inc=("dve", 1), chain=True)
            A("dve", lambda E: E.tensor_tensor(scw[:, 5 * S:6 * S],
                                               scw[:, 3 * S:4 * S],
                                               scw[:, 4 * S:5 * S], op=ALU.mult),
              inc=("dve", 1), chain=True)
            A("dve", lambda E: E.tensor_reduce(w1(28), scw[:, 5 * S:6 * S],
                                               axis=AX.X, op=ALU.add),
              inc=("dve", 1), chain=True)
            A("dve", lambda E: E.tensor_scalar(w1(29), w1(27), 1e-8, None,
                                               op0=ALU.mult),
              inc=("dve", 1), chain=True)
            A("dve", lambda E: E.tensor_tensor(w1(30), w1(28), w1(29), op=ALU.add),
              inc=("dve", 1), chain=True)
            A("dve", lambda E: E.reciprocal(w1(31), w1(30)),
              inc=("dve", 1), chain=True)
            A("dve", lambda E: E.tensor_scalar(scw[:, 6 * S:7 * S],
                                               scw[:, 5 * S:6 * S], w1(31), None,
                                               op0=ALU.mult),
              inc=("dve", 1), chain=True)
            A("dve", lambda E: E.tensor_tensor(scw[:, 7 * S:8 * S],
                                               scw[:, 6 * S:7 * S], awb[:],
                                               op=ALU.mult),
              waits=[tok_awb], inc=("dve", 1), chain=True)
            A("dve", lambda E: E.tensor_reduce(w1(32), scw[:, 7 * S:8 * S],
                                               axis=AX.X, op=ALU.add),
              inc=("dve", 1), chain=True)
            A("dve", lambda E, rt=rt: E.tensor_scalar(
                gps[:, rt:rt + 1], w1(32), 0.2, 0.9, op0=ALU.mult, op1=ALU.add),
              inc=("dve", 1), chain=True)
            t_gbf = A("dve", lambda E: E.tensor_copy(ptp[1][:, 0:S],
                                                     scw[:, 6 * S:7 * S]),
                      inc=("dve", 1), chain=True)
            t_gtr = A("pe", lambda E: E.transpose(scp0_bf[0:S, 1024:1152], ptp[1][:, 0:S],
                                                  ids[:]),
                      waits=[t_gbf, (g_done.get(rt - 1) if rt else None)],
                      inc=("pe", 1))
            g_done[rt] = A("dve", lambda E, r0=r0: E.tensor_copy(
                gT(r0, r0 + 128), scp0_bf[0:S, 1024:1152]),
                waits=[t_gtr], inc=("dve", 1), chain=True)

        gut_toks = {}
        hu_last = None
        for gi in range(NG):
            r0 = gi * GR
            gx = {}
            for jc in range(JC):
                gxw = [g_done[ORT - 1], tok_consts, tok_awb]
                if jc > 0:
                    gxw.append(gx[jc - 1])
                elif gi > 0:
                    gxw.append(gx_prev_last)
                t_gx = A("pe", lambda E, jc=jc, r0=r0: E.matmul(
                    scp[1][:, 0:GR], ems[:, jc * 128:(jc + 1) * 128],
                    gT(r0, r0 + GR), start=True, stop=True),
                    waits=gxw, inc=("pe", 1))
                gx[jc] = A("dve", lambda E, jc=jc: E.tensor_copy(
                    gexp(jc), scp[1][:, 0:GR]),
                    waits=[t_gx], inc=("dve", 1), chain=True)
            gx_prev_last = gx[JC - 1]
            for jc in range(JC):
                n = gi * JC + jc
                waits = [tok_u, tok_p1]
                if n >= 4:
                    waits.append(gut_toks[n - 4])

                def mk_hu(jc=jc, r0=r0, n=n):
                    def f(E):
                        for dc in range(DC):
                            ins = E.matmul(pf[n % 4][:, 0:GR],
                                           uchunk(dc, jc * 128, (jc + 1) * 128),
                                           xnt_own(dc, r0, r0 + GR),
                                           start=(dc == 0), stop=(dc == DC - 1))
                        return ins
                    return f
                hu_last = A("pe", mk_hu(), waits=waits, inc=("pe", 1))
                gut_toks[n] = A("dve", lambda E, jc=jc, r0=r0, n=n: E.tensor_tensor(
                    gut(jc, r0, r0 + GR), pf[n % 4][:, 0:GR], gexp(jc),
                    op=ALU.mult),
                    waits=[hu_last, gx[jc], tok_p2b_q0, tok_p2b_q1],
                    inc=("dve", 1), chain=True)
        tok_p3_dve = S_.tok("dve")
        tok_p3_pe = S_.tok("pe")

        tok_wk = wload(wk, waits=[tok_p2b_pe])

        kev = {}
        kmm = {}
        kdma = {}
        kev_of = {}
        kst_ids = {}
        ids0 = ln_stats_batch([xb[sb * 128:(sb + 1) * 128, :]
                               for sb in range(NSUB)])
        kev_of[0] = max(ln_tr(i, lambda dc, a, bb: xtc(0, dc, a, bb),
                              sb * 128) for sb, i in enumerate(ids0))
        for c in range(NCH):
            nbk = None
            if c + 1 < NCH:
                nbk = LnBatch(
                    [xb[(c + 1) * CH + sb * 128:(c + 1) * CH + (sb + 1) * 128, :]
                     for sb in range(NSUB)],
                    extra_waits=[kdma.get(c - 1)])
                kst_ids[c + 1] = nbk.ids
            for kc in range(DC):
                if nbk is not None and kc < NSUB + 2:
                    nbk.step()
                waits = [kev_of[c], *tok_wk]
                gn = c * DC + kc
                if gn >= 4:
                    waits.append(kev[gn - 4])

                def mk_kmm(c=c, kc=kc):
                    def f(E, _kk=None):
                        for dc in range(DC):
                            ins = E.matmul(pf[kc % 4][:, 0:CH],
                                           wchunk(dc, kc * 128, (kc + 1) * 128),
                                           xtc(c % 2, dc, 0, CH),
                                           start=(dc == 0), stop=(dc == DC - 1))
                        return ins
                    return f
                kmm[gn] = A("pe", mk_kmm(), waits=waits, inc=("pe", 1))
                w2 = [kmm[gn]]
                if c > 1:
                    w2.append(kdma[c - 2])
                kev[gn] = A("act", lambda E, c=c, kc=kc: E.activation(
                    out=stg(c % 2, kc, 0, CH), in_=pf[kc % 4][:, 0:CH],
                    func=AF.Identity,
                    bias=bks[:, kc:kc + 1]), waits=w2, inc=("act", 1), chain=True)
                if c + 1 < NCH and kc % LNQ == LNQ - 1:
                    sb = kc // LNQ
                    while nbk.k < sb + 3:
                        nbk.step()
                    ev = ln_tr(kst_ids[c + 1][sb],
                               lambda dc, a, bb, c=c: xtc((c + 1) % 2, dc, a, bb),
                               sb * 128)
                    if sb == NSUB - 1:
                        kev_of[c + 1] = ev
            if nbk is not None:
                nbk.finish()
            _kd = dma(
                "sy", f"kx{c % 2}",
                kt_scr.rearrange("(kc p) r -> p kc r", p=128)[:, :, c * CH:(c + 1) * CH],
                pa[:, STG_OFF[c % 2]: STG_OFF[c % 2] + DC * CH
                   ].rearrange("p (kc r) -> p kc r", r=CH),
                waits=[kev[c * DC + DC - 1]])
            t_xd = dma(
                "sy", f"kx{c % 2}",
                xnt_scr.rearrange("(dc p) r -> p dc r", p=128)[:, :, c * CH:(c + 1) * CH],
                pa[:, XTC_OFF[c % 2]: XTC_OFF[c % 2] + DC * CH
                   ].rearrange("p (dc r) -> p dc r", r=CH),
                waits=[kev_of[c]])
            kdma[c] = t_xd
        tok_p2a_ld0 = S_.tok("kx0")
        tok_p2a_ld1 = S_.tok("kx1")
        tok_p2a_pe = S_.tok("pe")

        tok_wv = wload(wv, waits=[tok_p2b_pe, tok_p3_pe, tok_p2a_pe])
        tok_masks = dma("gq", "mk", pm[:, 0:NKS * QT], mk_i, waits=[tok_p3_pe])

        vev = {}
        vdma = {}
        vmm = {}
        vxl = {}

        def emit_vxl(c, extra):
            vxl[c] = dma(
                "sy", f"vx{c % 2}",
                pa[:, XTC_OFF[c % 2]:XTC_OFF[c % 2] + DC * CH
                   ].rearrange("p (dc r) -> p dc r", r=CH),
                xnt_scr.rearrange("(dc p) r -> p dc r", p=128)[:, :, c * CH:(c + 1) * CH],
                waits=[tok_p2a_ld0, tok_p2a_ld1, tok_p3_dve, tok_p2b_q0,
                       tok_p2b_q1, tok_p2a_pe] + extra)

        emit_vxl(0, [])
        if NCH > 1:
            emit_vxl(1, [])
        for c in range(NCH):
            for rs in range(NSUB):
                for vc in range(NVC):
                    n = rs * NVC + vc
                    gn = c * NSUB * NVC + n
                    VSTR = min(4, NSUB * NVC)
                    waits = [vxl[c], tok_wv[0]]
                    if gn >= VSTR:
                        waits.append(vev[gn - VSTR])

                    def mk_vmm(rs=rs, vc=vc, n=n, c=c, d0=0, d1=DC):
                        def f(E):
                            for dc in range(d0, d1):
                                ins = E.matmul(pf[n % 4][:, 0:512],
                                               xtc(c % 2, dc, rs * 128, (rs + 1) * 128),
                                               wchunk(dc, vc * 512, (vc + 1) * 512),
                                               start=(dc == 0), stop=(dc == DC - 1),
                                               skip_group_check=True)
                            return ins
                        return f
                    A("pe", mk_vmm(d0=0, d1=DC // 2), waits=waits, inc=("pe", 1))
                    vmm[c] = A("pe", mk_vmm(d0=DC // 2, d1=DC),
                               waits=[tok_wv[1]], inc=("pe", 1))
                    w2 = [vmm[c]]
                    if c > 1:
                        w2.append(vdma[c - 2])
                    vev[gn] = A("dve", lambda E, rs=rs, vc=vc, n=n, c=c: E.tensor_copy(
                        stgv(c % 2, rs, vc * 512, (vc + 1) * 512), pf[n % 4][:, 0:512]),
                        waits=w2, inc=("dve", 1), chain=True)
            if c + 2 < NCH:
                emit_vxl(c + 2, [vmm[c]])
            vdma[c] = dma(
                "sy", f"vw{c % 2}",
                v_scr[c * CH:(c + 1) * CH, :].rearrange("(rs p) n -> p rs n", p=128),
                pa[:, STG_OFF[c % 2]: STG_OFF[c % 2] + NSUB * D
                   ].rearrange("p (rs n) -> p rs n", n=D),
                waits=[vev[c * NSUB * NVC + NSUB * NVC - 1]])
        tok_p2c_ld0 = S_.tok("vw0")
        tok_p2c_ld1 = S_.tok("vw1")
        tok_p2c_pe = S_.tok("pe")

        tok_vflat = dma("gq", "vf", pv[:, 0:JC * D].rearrange("p (jc n) -> p jc n", n=D),
                        v_fl.rearrange("(jc p) n -> p jc n", p=128),
                        waits=[tok_p3_dve, ("act", S_.n["act"]), ("dve", S_.n["dve"])])

        t_o0 = A("dve", lambda E: E.memset(vh_view(0)[:, :, 128:129], 1.0),
                 waits=[tok_p2c_pe], inc=("dve", 1), chain=True)
        tok_vones = A("dve", lambda E: E.memset(vh_view(1)[:, :, 128:129], 1.0),
                      inc=("dve", 1), chain=True)

        tok_pre_attn_dve = ("dve", S_.n["dve"])

        def pso_(j):
            return pf[j // 2][:, (j % 2) * 256:(j % 2) * 256 + 129]

        def pso_d(j):
            return pf[j // 2][:, (j % 2) * 256:(j % 2) * 256 + 128]

        def pso_l(j):
            return pf[j // 2][:, (j % 2) * 256 + 128:(j % 2) * 256 + 129]

        def scb_(k, q0, q1):
            P = (k // 2) % 2
            o = (k % 2) * QT
            return scp[P][:, o + q0: o + q1]

        def ptp_(k, q0, q1):
            P = (k // 2) % 2
            o = (k % 2) * QT
            return ptp[P][:, o + q0: o + q1]

        head_pe = {}
        head_ev = {}
        head_ld = {}
        exp_toks = {}
        mask_toks = {}
        pv_last = {}
        sweep_ev = {}
        nvd = (NSUBK + 7) // 8
        for h in range(H):
            beta = h % 2
            waits = [tok_p2a_ld0, tok_p2a_ld1, tok_p2b_q0, tok_p2b_q1, tok_p2c_ld0, tok_p2c_ld1, tok_vones, tok_p2a_pe,
                     S_.tok("pe") if h == 0 else None]
            if h >= 2:
                waits += [head_pe[h - 2], head_ev[h - 2]]
            t = dma("sy", f"h{beta}", kth(beta, 0, T), kt_scr[h * 128:(h + 1) * 128, :],
                    waits=waits)
            for d in range(nvd):
                s0, s1 = d * 8, min((d + 1) * 8, NSUBK)
                t = dma("sy", f"h{beta}", vh_view(beta)[:, s0:s1, 0:128],
                        v_scr[s0 * 128:s1 * 128, h * 128:(h + 1) * 128
                              ].rearrange("(s p) c -> p s c", p=128))
            head_ld[h] = t = dma(
                "sy", f"h{beta}",
                xs[beta][:, 0:OWN],
                qu_scr[:, h * OWN:(h + 1) * OWN])

            for sw in range(1, NSW + 1):
                nk = sw * NKS
                pvq = []
                for k in range(nk):
                    if k % 2 == 0:
                        while len(pvq) > 2:
                            fn2, w2_, pr2 = pvq.pop(0)
                            pv_last[pr2] = A("pe", fn2, waits=w2_,
                                             inc=("pe", 1))
                    if k >= 4:
                        war = exp_toks[(h, sw, k - 4)]
                    elif sw > 1:
                        war = exp_toks[(h, sw - 1, (sw - 1) * NKS - 4 + k)]
                    elif h > 0:
                        war = exp_toks[(h - 1, NSW, NSW * NKS - 4 + k)]
                    else:
                        war = None
                    sw_waits = [head_ld[h], war]
                    if h == 0 and sw == 1 and k in (0, 1, 2, 3):
                        sw_waits.append(tok_pre_attn_dve)
                    t_sc = A("pe", lambda E, b=beta, k=k, sw=sw: E.matmul(
                        scb_(k, 0, QT), kth(b, k * 128, (k + 1) * 128),
                        xs[b][:, (sw - 1) * QT: sw * QT],
                        start=True, stop=True),
                        waits=sw_waits, inc=("pe", 1))
                    if k % 2 == 1:
                        ew = [t_sc, pv_last.get((k // 2) % 2)]
                        for km in (k - 4, k - 3):
                            if km >= nk - NKS and km >= 0:
                                ew.append(mask_toks[(h, sw, km)])
                            elif km < 0 and (sw > 1 or h > 0):
                                psw = (h, sw - 1) if sw > 1 else (h - 1, NSW)
                                kk2 = psw[1] * NKS + km
                                if (psw[0], psw[1], kk2) in mask_toks:
                                    ew.append(mask_toks[(psw[0], psw[1], kk2)])
                        tex = A("act", lambda E, k=k: E.activation(
                            out=ptp_(k - 1, 0, 2 * QT), in_=scb_(k - 1, 0, 2 * QT),
                            func=AF.Exp, bias=cst[:], scale=SCALE),
                            waits=ew + [tok_cst], inc=("act", 1), chain=True)
                        exp_toks[(h, sw, k - 1)] = tex
                        exp_toks[(h, sw, k)] = tex
                        for km in (k - 1, k):
                            if km >= nk - NKS:
                                j = km - (nk - NKS)
                                mask_toks[(h, sw, km)] = A(
                                    "dve", lambda E, km=km, j=j: E.tensor_tensor(
                                        ptp_(km, 0, QT), ptp_(km, 0, QT),
                                        mask_j(j), op=ALU.mult),
                                    waits=[tex, tok_masks],
                                    inc=("dve", 1), chain=True)
                        for km in (k - 1, k):
                            if km >= nk - NKS:
                                pvw = [mask_toks[(h, sw, km)]]
                            else:
                                pvw = [tex]
                            if km == 0:
                                if sw > 1:
                                    pvw.append(sweep_ev[(h, sw - 1)])
                                elif h > 0:
                                    pvw.append(sweep_ev[(h - 1, NSW)])
                                else:
                                    pvw.append(tok_p3_dve)

                            def mk_pv(b=beta, km=km, nk=nk):
                                def f(E):
                                    for j in range(NQS):
                                        ins = E.matmul(pso_(j),
                                                       ptp_(km, j * 128, (j + 1) * 128),
                                                       vh(b, km),
                                                       start=(km == 0 and j % 2 == 0),
                                                       stop=(km == nk - 1 and
                                                             (j % 2 == 1 or j == NQS - 1)),
                                                       skip_group_check=True)
                                    return ins
                                return f
                            pvq.append((mk_pv(), pvw, (k // 2) % 2))
                tpv = None
                for fn2, w2_, pr2 in pvq:
                    tpv = A("pe", fn2, waits=w2_, inc=("pe", 1))
                    pv_last[pr2] = tpv
                head_pe[h] = tpv
                sweep_pe = tpv
                t_ev = None
                for j in range(NQS):
                    rt = ((sw - 1) * QT + j * 128) // 128
                    A("dve", lambda E, j=j: E.reciprocal(w1(40 + j),
                                                         pso_l(j)),
                      waits=[sweep_pe], inc=("dve", 1), chain=True)
                    t_ev = A("dve", lambda E, j=j, rt=rt, h=h: E.tensor_scalar(
                        ac(rt, h * 128, (h + 1) * 128), pso_d(j),
                        w1(40 + j), None, op0=ALU.mult),
                        inc=("dve", 1), chain=True)
                sweep_ev[(h, sw)] = t_ev
            head_ev[h] = sweep_ev[(h, NSW)]
        tok_p4_dve = S_.tok("dve")
        tok_p4_pe = S_.tok("pe")
        tok_p4_act = S_.tok("act")

        tok_wo = wload(wo, waits=[tok_p4_pe])

        pairs = [(a, b) for a in range(ORT) for b in range(a, DC)]
        p5_ev = {}
        for idx, (a, b) in enumerate(pairs):
            par = idx % 2
            n_tr = 1 if a == b else 2
            waits = [tok_p4_dve, tok_consts]
            if idx >= 2:
                waits.append(p5_ev[idx - 2])
            else:
                waits.append(tok_p4_act)

            def mk_tr(a=a, b=b, par=par, n_tr=n_tr):
                def f(E):
                    ins = E.transpose(pbv(par, 0, 128), ac(a, b * 128, (b + 1) * 128), ids[:])
                    if n_tr == 2:
                        ins = E.transpose(pbv(par, 128, 256), ac(b, a * 128, (a + 1) * 128), ids[:])
                    return ins
                return f
            t_tr = A("pe", mk_tr(), waits=waits, inc=("pe", n_tr))
            t_e1 = A("dve", lambda E, a=a, b=b, par=par: E.tensor_scalar(
                acT(b, a * 128, (a + 1) * 128), pbv(par, 0, 128),
                bvs[:, b:b + 1], None, op0=ALU.add),
                waits=[t_tr], inc=("dve", 1), chain=True)
            if a != b:
                t_e1 = A("dve", lambda E, a=a, b=b, par=par: E.tensor_scalar(
                    acT(a, b * 128, (b + 1) * 128), pbv(par, 128, 256),
                    bvs[:, a:a + 1], None, op0=ALU.add),
                    inc=("dve", 1), chain=True)
            p5_ev[idx] = t_e1
        tok_p5_dve = S_.tok("dve")

        tok_bo = dma("sy", "bo", bo_b(0, D), bo_i,
                     waits=[tok_p4_dve, tok_p4_act])

        e1t = {}
        e2t = {}
        e3t = {}
        stt = {}
        xlt = {}
        NTOT6 = ORT * NVC

        def emit_xlt(n, extra):
            rt2, cc2 = n // NVC, n % NVC
            xlt[n] = dma("sy", f"px{n % 2}", p6x(n % 2),
                         xo[rt2 * 128:(rt2 + 1) * 128, cc2 * 512:(cc2 + 1) * 512],
                         waits=[tok_bo] + extra)

        emit_xlt(0, [])
        if NTOT6 > 1:
            emit_xlt(1, [])
        for rt in range(ORT):
            for cc in range(NVC):
                n = rt * NVC + cc
                mw = [tok_p3_dve, tok_vflat, *tok_wo, tok_p5_dve]
                if n >= 2:
                    mw += [e1t[n - 2], e2t[n - 2]]

                def mk_f(rt=rt, cc=cc, n=n):
                    def f(E):
                        for jc in range(JC):
                            ins = E.matmul(pf[n % 2][:, 0:512],
                                           gut(jc, rt * 128, (rt + 1) * 128),
                                           vflat(jc, cc * 512, (cc + 1) * 512),
                                           start=(jc == 0), stop=(jc == JC - 1))
                        return ins
                    return f
                t_f = A("pe", mk_f(), waits=mw, inc=("pe", 1))

                def mk_o(rt=rt, cc=cc, n=n):
                    def f(E):
                        for fc in range(DC):
                            ins = E.matmul(pf[2 + n % 2][:, 0:512],
                                           acT(fc, rt * 128, (rt + 1) * 128),
                                           wchunk(fc, cc * 512, (cc + 1) * 512),
                                           start=(fc == 0), stop=(fc == DC - 1))
                        return ins
                    return f
                t_o = A("pe", mk_o(), inc=("pe", 1))

                w1_ = [t_f, tok_p3_dve]
                if n >= 3:
                    w1_.append(stt[n - 3])
                e1t[n] = A("act", lambda E, rt=rt, n=n: E.activation(
                    out=p6t(n % 3), in_=pf[n % 2][:, 0:512], func=AF.Copy,
                    scale=gps[:, rt:rt + 1]), waits=w1_, inc=("act", 1),
                    chain=True)
                e2t[n] = A("dve", lambda E, n=n: E.tensor_tensor(
                    p6t(n % 3), p6t(n % 3), pf[2 + n % 2][:, 0:512], op=ALU.add),
                    waits=[e1t[n], t_o], inc=("dve", 1), chain=True)
                A("dve", lambda E, n=n: E.tensor_tensor(
                    p6t(n % 3), p6t(n % 3), p6x(n % 2), op=ALU.add),
                  waits=[xlt[n]], inc=("dve", 1), chain=True)
                e3t[n] = A("dve", lambda E, n=n, cc=cc: E.tensor_tensor(
                    p6t(n % 3), p6t(n % 3), bo_b(cc * 512, (cc + 1) * 512),
                    op=ALU.add), waits=[tok_bo], inc=("dve", 1), chain=True)
                if n + 2 < NTOT6:
                    emit_xlt(n + 2, [e3t[n]])
                stt[n] = dma("sy", f"o{n % 3}", out[rt * 128:(rt + 1) * 128, cc * 512:(cc + 1) * 512],
                             p6t(n % 3), waits=[e3t[n]])

        final = dict(S_.n)
        S_.add("gq", None, waits=[(s, v) for s, v in final.items()])

        semnames = set(S_.n.keys())
        for ename in S_.ENG:
            for fn, waits, inc in S_.ops[ename]:
                for sem, val in waits:
                    semnames.add(sem)
        sems = {name: e(nc.semaphore("s_" + name)) for name in sorted(semnames)}
        block = e(nc.Block())

        def emit_wm(engobj, ename):
            water = {}
            for fn, waits, inc in S_.ops[ename]:
                for sem, val in waits:
                    if water.get(sem, 0) >= val:
                        continue
                    engobj.wait_ge(sems[sem], val)
                    water[sem] = val
                if fn is None:
                    continue
                ins = fn(engobj)
                if inc is not None:
                    ins.then_inc(sems[inc[0]], inc[1])

        @block.sync
        def _(E):
            emit_wm(E, "sy")

        @block.gpsimd
        def _(E):
            emit_wm(E, "gq")

        @block.tensor
        def _(E):
            emit_wm(E, "pe")

        @block.vector
        def _(E):
            emit_wm(E, "dve")

        @block.scalar
        def _(E):
            emit_wm(E, "act")

    return nc


def host_inputs(cfg, core, ins):
    D, T = cfg["D"], cfg["T"]
    OWN, QT, NSW, NKS, DC, JC, S = (cfg["OWN"], cfg["QT"], cfg["NSW"],
                                    cfg["NKS"], cfg["DC"], cfg["JC"], cfg["S"])
    b, g = core // 4, core % 4
    x = ins["L0_attn_x"]
    xb = np.ascontiguousarray(x[b], dtype=np.float32)
    blocks = [4 * s + g for s in range(NSW)]
    xo = np.ascontiguousarray(
        np.concatenate([x[b][blk * QT:(blk + 1) * QT] for blk in blocks], axis=0),
        dtype=np.float32)
    H = cfg["H"]
    TPH, QPH = cfg["T"] // H, QT // H
    qrows = []
    for h in range(H):
        for s in range(NSW):
            base = h * TPH + (QT * (4 * s + g)) // H
            qrows.extend(range(base, base + QPH))
    xq = np.ascontiguousarray(x[b][np.array(qrows)], dtype=np.float32)
    masks = np.zeros((128, NKS * QT), dtype=ml_dtypes.bfloat16)
    kl = np.arange(128)[:, None]
    q = np.arange(QT)[None, :]
    for j in range(NKS):
        masks[:, j * QT:(j + 1) * QT] = (j * 128 + kl <= g * QT + q).astype(
            ml_dtypes.bfloat16)
    e_mat = np.zeros((S, 128 * JC), dtype=ml_dtypes.bfloat16)
    for jj in range(128 * JC):
        e_mat[jj // 16, jj] = 1
    U = np.asarray(ins["L0_ffn_U"], np.float32)
    m = {
        "xb": xb,
        "xo": xo,
        "xq": xq,
        "wq": ins["L0_attn_Wq"], "wk": ins["L0_attn_Wk"],
        "wv": ins["L0_attn_Wv"], "wo": ins["L0_attn_Wo"],
        "bq": np.ascontiguousarray(np.asarray(ins["L0_attn_bq"], np.float32).reshape(DC, 128).T),
        "bk": np.ascontiguousarray(np.asarray(ins["L0_attn_bk"], np.float32).reshape(DC, 128).T),
        "bv": np.ascontiguousarray(np.asarray(ins["L0_attn_bv"], np.float32).reshape(DC, 128).T),
        "bo_bc": np.ascontiguousarray(np.broadcast_to(
            np.asarray(ins["L0_attn_bo"], np.float32).reshape(1, D), (128, D))),
        "wr_t": np.ascontiguousarray(np.asarray(ins["L0_ffn_Wr"], np.float32).T),
        "u_flat": np.ascontiguousarray(U.transpose(1, 0, 2).reshape(D, -1)),
        "v_flat": np.ascontiguousarray(np.asarray(ins["L0_ffn_V"], np.float32).reshape(-1, D)),
        "attr_t": np.ascontiguousarray(np.asarray(ins["L0_ffn_attr"], np.float32).T),
        "wa_t": np.ascontiguousarray(np.asarray(ins["L0_ffn_Wa"], np.float32).reshape(1, S).T),
        "masks": masks,
        "ident": np.eye(128, dtype=ml_dtypes.bfloat16),
        "ones_f": np.ones((1, 128), np.float32),
        "e_mat": e_mat,
    }
    return {k: np.ascontiguousarray(np.asarray(v)) for k, v in m.items()}


def assemble(cfg, results):
    D, T, QT, NSW = cfg["D"], cfg["T"], cfg["QT"], cfg["NSW"]
    full = np.zeros((2, T, D), np.float32)
    for core in range(8):
        b, g = core // 4, core % 4
        o = np.asarray(results[core]["out"], np.float32)
        for s in range(NSW):
            blk = 4 * s + g
            full[b, blk * QT:(blk + 1) * QT] = o[s * QT:(s + 1) * QT]
    return full


def kernel(**inputs):
    cfg = mkcfg(FULL)
    nc = build_nc(cfg)
    in_maps = [host_inputs(cfg, c, inputs) for c in range(8)]
    res = run_bass_kernel_spmd(nc, in_maps, core_ids=list(range(8)))
    return assemble(cfg, res.results)
